# version 04
# baseline (speedup 1.0000x reference)
import math
import numpy as np
import ml_dtypes
import concourse.bass as bass
import concourse.mybir as mybir
from concourse.bass_utils import run_bass_kernel_spmd

F32 = mybir.dt.float32
BF16 = mybir.dt.bfloat16
U32 = mybir.dt.uint32
AF = mybir.ActivationFunctionType
ALU = mybir.AluOpType
AX = mybir.AxisListType

DM = 4096
B = 4
L = 2048
NCORE = 8
TPC = 1024
NH0, CD0, NB0 = 16, 32, 8
H1, KVH1, D1 = 32, 8, 128
CHUNK = 128
LN_EPS = 1e-5
RMS_EPS = 1e-5
BF = ml_dtypes.bfloat16
ENGS = ("tensor", "vector", "scalar", "gpsimd", "sync")


class Prog:

    def __init__(self, nc):
        self.nc = nc
        self.qs = {e: [] for e in ENGS}
        self.cnt = {}
        self.handles = {}
        self.lastwait = {e: {} for e in ENGS}

    def sem(self, name):
        if name not in self.handles:
            self.handles[name] = self.nc.alloc_semaphore(name)
            self.cnt[name] = 0
        return self.handles[name]

    def op(self, eng, fn, inc=None, k=1):
        if inc is not None:
            h = self.sem(inc)
            self.cnt[inc] += k
            val = self.cnt[inc]
            def run(e, fn=fn, h=h, k=k):
                fn(e).then_inc(h, k)
            self.qs[eng].append(run)
            return (inc, val)
        self.qs[eng].append(fn)
        return None

    def raw(self, eng, fn):
        self.qs[eng].append(fn)

    def barrier(self, tag=""):
        h = self.sem("bar")
        for e in ENGS:
            self.qs[e].append(lambda eng, h=h: eng.nop().then_inc(h, 1))
        self.cnt["bar"] += len(ENGS)
        tgt = self.cnt["bar"]
        for e in ENGS:
            self.qs[e].append(lambda eng, h=h, v=tgt: eng.wait_ge(h, v))
            self.lastwait[e]["bar"] = tgt

    def wait(self, eng, tok):
        if tok is None:
            return
        name, val = tok
        if val <= 0:
            return
        lw = self.lastwait[eng]
        if lw.get(name, -1) >= val:
            return
        lw[name] = val
        h = self.sem(name)
        self.qs[eng].append(lambda e, h=h, v=val: e.wait_ge(h, v))

    def tok(self, name):
        return (name, self.cnt.get(name, 0))

    def dma(self, eng, out, in_, inc, **kw):
        def run(e, o=out, i=in_, kw=kw):
            return e.dma_start(out=o() if callable(o) else o,
                               in_=i() if callable(i) else i, **kw)
        return self.op(eng, run, inc=inc, k=16)

    def emit(self):
        with self.nc.Block() as blk:
            for e in ENGS:
                q = self.qs[e]
                getattr(blk, e)(lambda eng, q=q: [f(eng) for f in q])


def _build_tables():
    dim, n = 3, 8
    masks = sorted(range(n), key=lambda m: (bin(m).count('1'), m))
    inv = {m: i for i, m in enumerate(masks)}
    C = np.zeros((n, n, n), dtype=np.float32)
    for i, a in enumerate(masks):
        for j, b_ in enumerate(masks):
            sw, at = (0, a >> 1)
            while at:
                sw += bin(at & b_).count('1')
                at >>= 1
            s = -1.0 if sw & 1 else 1.0
            C[i, j, inv[a ^ b_]] = s
    grades = np.array([bin(m).count('1') for m in masks])
    rev = np.where(grades * (grades - 1) // 2 % 2 == 1, -1.0, 1.0).astype(np.float32)
    return C, rev


def host_consts_L0():
    C, rev = _build_tables()
    biv = C[:, :, [4, 5, 6]]
    bivmat = np.ascontiguousarray(biv.transpose(2, 0, 1).reshape(3, 64)).astype(np.float32)
    revrep = np.repeat(rev, 8)
    revtile = np.tile(revrep[None, :], (16, 1)).astype(np.float32)
    scale = 1.0 / math.sqrt(CD0 * NB0)
    scdelta = np.tile((np.eye(8, dtype=np.float32).reshape(64) * revrep * scale)[None, :],
                      (16, 1)).astype(np.float32)
    return bivmat, revtile, scdelta


def host_rope_tables(positions):
    D = D1
    inv_freq = 1.0 / (10000.0 ** (np.arange(0, D, 2, dtype=np.float64) / D))
    ang = positions[:, None].astype(np.float64) * inv_freq[None, :]
    cos = np.concatenate([np.cos(ang), np.cos(ang)], axis=-1)
    sin = np.concatenate([np.sin(ang), np.sin(ang)], axis=-1)
    return (np.ascontiguousarray(cos.T).astype(np.float32),
            np.ascontiguousarray(sin.T).astype(np.float32))


def build_launch_A(ctx=None):
    if ctx is None:
        nc = bass.Bass("TRN2", target_bir_lowering=False, num_devices=NCORE)
        p = Prog(nc)
        x0 = nc.dram_tensor("x0", [L, DM], F32, kind="ExternalInput").ap()
        wT = {k: nc.dram_tensor(f"w{k}T", [DM, 2048], F32, kind="ExternalInput").ap()
              for k in "qkv"}
        bias = {k: nc.dram_tensor(f"b{k}", [2048], F32, kind="ExternalInput").ap()
                for k in "qkv"}
        bwT = nc.dram_tensor("bwT", [3, 16], F32, kind="ExternalInput").ap()
        bivmat = nc.dram_tensor("bivmat", [3, 64], F32, kind="ExternalInput").ap()
        revtile = nc.dram_tensor("revtile", [16, 64], F32, kind="ExternalInput").ap()
        scdelta = nc.dram_tensor("scdelta", [16, 64], F32, kind="ExternalInput").ap()
        ident = nc.dram_tensor("ident", [128, 128], BF16, kind="ExternalInput").ap()
        attn_out = nc.dram_tensor("attn_out", [L, 2048], BF16, kind="ExternalOutput").ap()
        attn_dst = lambda qt, g: (lambda: attn_out[qt * 128:(qt + 1) * 128,
                                                   g * 512:(g + 1) * 512])
    else:
        nc, p = ctx["nc"], ctx["p"]
        x0, wT, bias = ctx["x0"], ctx["wT"], ctx["bias"]
        bwT, bivmat, revtile, scdelta, ident = (ctx["bwT"], ctx["bivmat"],
                                                ctx["revtile"], ctx["scdelta"],
                                                ctx["ident"])
        attn_dst = ctx["attn_dst"]
    xT0d = nc.dram_tensor("xT0d", [32, 128, L], BF16, kind="Internal").ap()
    a_scr = nc.dram_tensor("a_scr", [16, 8, 8], BF16, kind="Internal").ap()

    al = ctx["al"] if ctx else nc.alloc_sbuf_tensor
    identb = al("identb", [128, 128], BF16)
    T_sb = al("T_sb", [128, 8 * 128], BF16)
    bcol = al("bcol", [128, 3 * 16], F32)
    xTh = al("xTh", [128, 32 * 1024], BF16)
    wst_f = al("wst_f", [128, 2 * DM], F32)
    wst_b = al("wst_b", [128, 2 * DM], BF16)
    qmodT = al("qmodT", [128, 4 * L], BF16)
    kTg = al("kTg", [128, 4 * L], BF16)
    vtok = al("vtok", [128, 16 * 512], BF16)
    qraw = al("qraw", [128, 1024], BF16)
    vtmp = al("vtmp", [128, 1024], BF16)
    Pb = al("Pb", [128, L], BF16)
    PT4 = al("PT4", [128, 16 * 512], BF16)
    avf = al("avf", [128, 512], BF16)
    outstg = al("outstg", [128, 4 * 512], BF16)
    negmax = al("negmax", [128, 1], F32)
    zrec = al("zrec", [128, 16], F32)
    wk_f = al("wk_f", [16, 64], F32)
    wk_b = al("wk_b", [16, 64], BF16)
    bw_sb = al("bw_sb", [3, 16], F32)
    biv_sb = al("biv_sb", [3, 64], F32)
    rev_sb = al("rev_sb", [16, 64], F32)
    scd_sb = al("scd_sb", [16, 64], F32)

    if ctx is None:
        ps_big = nc.alloc_psum_tensor("ps_big", [128, 2048], F32)
        ps_sm = nc.alloc_psum_tensor("ps_sm", [128, 512], F32)
        ps_tr = nc.alloc_psum_tensor("ps_tr", [128, 128], BF16)
    else:
        PS = ctx["PS"]
        ps_big = PS[:, 0:2048]
        ps_sm = PS[:, 2048:2560]
        ps_tr = PS.bitcast(BF16)[:, 5 * 1024: 5 * 1024 + 128]

    xf_t = xTh.bitcast(F32)[:, 0:DM]
    xb_t = xTh[:, 16384 // 2: 16384 // 2 + DM]
    stg = xTh[:, 24576 // 2: 24576 // 2 + DM]

    p.dma("sync", identb[:], ident, "ld")
    p.dma("sync", bw_sb[:], bwT, "ld")
    p.dma("sync", biv_sb[:], bivmat, "ld")
    p.dma("sync", rev_sb[:], revtile, "ld")
    p.dma("sync", scd_sb[:], scdelta, "ld")
    for j, k in enumerate("qkv"):
        for c in range(16):
            p.dma("sync", bcol[:, j * 16 + c: j * 16 + c + 1],
                  bias[k][c * 128:(c + 1) * 128].unsqueeze(1), "ld")
    p.wait("tensor", p.tok("ld"))
    mm0 = p.op("tensor", lambda e: e.matmul(out=ps_sm.bitcast(F32)[0:16, 0:64] if False else ps_sm[0:16, 0:64],
                                            lhsT=bw_sb[:], rhs=biv_sb[:],
                                            start=True, stop=True), inc="pe")
    p.wait("vector", mm0)
    dvx = p.op("vector", lambda e: e.tensor_tensor(out=wk_f[:], in0=ps_sm[0:16, 0:64],
                                                   in1=rev_sb[:], op=ALU.mult), inc="dv")
    p.wait("vector", dvx)
    dv0 = p.op("vector", lambda e: e.tensor_tensor(out=wk_f[:], in0=wk_f[:],
                                                   in1=scd_sb[:], op=ALU.add), inc="dv")
    p.wait("scalar", dv0)
    ac0 = p.op("scalar", lambda e: e.copy(out=wk_b[:], in_=wk_f[:]), inc="ac")
    p.wait("sync", ac0)
    wr0 = p.dma("sync", a_scr.rearrange("h i j -> h (i j)"), wk_b[:], "ascr")
    mz = p.op("vector", lambda e: e.memset(T_sb[:], 0.0), inc="dv")
    p.wait("sync", wr0)
    p.wait("sync", mz)
    for hh in range(8):
        for d in range(16):
            p.dma("sync", T_sb[d * 8:(d + 1) * 8,
                               hh * 128 + d * 8: hh * 128 + d * 8 + 8],
                  a_scr[hh], "tld")
    t_built = p.tok("tld")

    cast_tok = None
    tr_last = None
    st_tok = None
    for tt in range(16):
        if cast_tok is not None:
            p.wait("sync", cast_tok)
        ltok = p.dma("sync", xf_t, x0[tt * 128:(tt + 1) * 128, :], "ld")
        p.wait("scalar", ltok)
        if tr_last is not None:
            p.wait("scalar", tr_last)
        cast_tok = p.op("scalar", lambda e: e.copy(out=xb_t, in_=xf_t), inc="ac")
        p.wait("tensor", cast_tok)
        if st_tok is not None:
            p.wait("vector", st_tok)
        for c in range(32):
            tr_last = p.op("tensor", lambda e, c=c: e.transpose(
                out=ps_tr[:], in_=xb_t[:, c * 128:(c + 1) * 128], identity=identb[:]),
                inc="pe")
            p.wait("vector", tr_last)
            cpt = p.op("vector", lambda e, c=c: e.tensor_copy(
                out=stg[:, c * 128:(c + 1) * 128], in_=ps_tr[:]), inc="dv")
            p.wait("tensor", cpt)
        p.wait("sync", p.tok("dv"))
        st_tok = p.dma("sync", xT0d[:, :, tt * 128:(tt + 1) * 128],
                       stg.rearrange("p (c i) -> p c i", c=32), f"xtw{tt % 2}")
    p.wait("tensor", t_built)

    wbuf_user = [None, None]
    evac_tok = None
    qm_evac = None
    xTh_user = st_tok
    ext_prev = None
    av_done = None
    store_tok = None
    for g in range(4):
        for mh in range(2):
            if xTh_user is not None:
                p.wait("sync", xTh_user)
            xld = p.dma("sync", xTh.rearrange("p (c m) -> p c m", c=32),
                        xT0d[:, :, mh * 1024:(mh + 1) * 1024], "ld")
            for oc in range(12):
                kind, loc = ("q", oc) if oc < 4 else (("k", oc - 4) if oc < 8 else ("v", oc - 8))
                gch = g * 4 + loc
                db = oc % 2
                if wbuf_user[db] is not None:
                    p.wait("sync", wbuf_user[db])
                wld = p.dma("sync",
                            wst_f[:, db * DM:(db + 1) * DM].rearrange("p (c o) -> p c o", c=32),
                            wT[kind][:, gch * 128:(gch + 1) * 128]
                            .rearrange("(c p) o -> p c o", p=128), f"wld{db}")
                p.wait("gpsimd", wld)
                p.wait("gpsimd", wbuf_user[db])
                wct = p.op("gpsimd", lambda e, db=db: e.tensor_copy(
                    out=wst_b[:, db * DM:(db + 1) * DM],
                    in_=wst_f[:, db * DM:(db + 1) * DM]), inc="pool")
                p.wait("tensor", wct)
                p.wait("tensor", xld)
                if evac_tok is not None:
                    p.wait("tensor", evac_tok)
                for d in range(32):
                    for half in range(2):
                        mmt = p.op("tensor", lambda e, d=d, h=half, db=db: e.matmul(
                            out=ps_big[:, h * 512:(h + 1) * 512],
                            lhsT=wst_b[:, db * DM + d * 128: db * DM + (d + 1) * 128],
                            rhs=xTh[:, d * 1024 + h * 512: d * 1024 + (h + 1) * 512],
                            start=(d == 0), stop=(d == 31)),
                            inc="pe" if (d == 31 and half == 1) else None)
                wbuf_user[db] = mmt
                xTh_user = mmt
                p.wait("scalar", mmt)
                if kind == "q":
                    evac_tok = p.op("scalar", lambda e, gch=gch: e.activation(
                        out=qraw[:], in_=ps_big[:, 0:1024], func=AF.Identity,
                        bias=bcol[:, gch: gch + 1], scale=1.0), inc="ac")
                    p.wait("tensor", evac_tok)
                    if qm_evac is not None:
                        p.wait("tensor", qm_evac)
                    lhead = g * 2 + loc // 2
                    for half in range(2):
                        qmt = p.op("tensor", lambda e, h=half, lh=lhead: e.matmul(
                            out=ps_big[:, 1024 + h * 512: 1024 + (h + 1) * 512],
                            lhsT=T_sb[:, lh * 128:(lh + 1) * 128],
                            rhs=qraw[:, h * 512:(h + 1) * 512],
                            start=True, stop=True),
                            inc="pe" if half == 1 else None)
                    p.wait("scalar", qmt)
                    qm_evac = p.op("scalar", lambda e, loc=loc, mh=mh: e.copy(
                        out=qmodT[:, loc * L + mh * 1024: loc * L + (mh + 1) * 1024],
                        in_=ps_big[:, 1024:2048]), inc="ac")
                elif kind == "k":
                    evac_tok = p.op("scalar", lambda e, gch=gch, loc=loc, mh=mh: e.activation(
                        out=kTg[:, loc * L + mh * 1024: loc * L + (mh + 1) * 1024],
                        in_=ps_big[:, 0:1024], func=AF.Identity,
                        bias=bcol[:, 16 + gch: 16 + gch + 1], scale=1.0), inc="ac")
                else:
                    p.wait("scalar", p.tok("pe"))
                    evac_tok = p.op("scalar", lambda e, gch=gch: e.activation(
                        out=vtmp[:], in_=ps_big[:, 0:1024], func=AF.Identity,
                        bias=bcol[:, 32 + gch: 32 + gch + 1], scale=1.0), inc="ac")
                    p.wait("tensor", evac_tok)
                    for j in range(8):
                        trt = p.op("tensor", lambda e, j=j: e.transpose(
                            out=ps_tr[:], in_=vtmp[:, j * 128:(j + 1) * 128],
                            identity=identb[:]), inc="pe")
                        p.wait("vector", trt)
                        cpt = p.op("vector", lambda e, j=j, mh=mh, loc=loc: e.tensor_copy(
                            out=vtok[:, (mh * 8 + j) * 512 + loc * 128:
                                        (mh * 8 + j) * 512 + (loc + 1) * 128],
                            in_=ps_tr[:]), inc="dv")
                        p.wait("tensor", cpt)

        p.wait("tensor", p.tok("ac"))
        p.wait("tensor", p.tok("dv"))
        for qb in range(4):
            for hh2 in range(2):
                c0 = hh2 * 2
                for qi in range(4):
                    qt = qb * 4 + qi
                    p.wait("tensor", p.tok("ac"))
                    for cc in range(2):
                        for ms in range(4):
                            smt = p.op("tensor", lambda e, cc=cc, ms=ms, qt=qt, c0=c0: e.matmul(
                                out=ps_big[:, ms * 512:(ms + 1) * 512],
                                lhsT=qmodT[:, (c0 + cc) * L + qt * 128:
                                              (c0 + cc) * L + (qt + 1) * 128],
                                rhs=kTg[:, (c0 + cc) * L + ms * 512:
                                           (c0 + cc) * L + (ms + 1) * 512],
                                start=(cc == 0), stop=(cc == 1)),
                                inc="pe" if (cc == 1 and ms == 3) else None)
                    p.wait("vector", smt)
                    p.wait("vector", ext_prev)
                    rmt = p.op("vector", lambda e: e.reduce_max(
                        out=negmax[:], in_=ps_big[:], axis=AX.X, negate=True), inc="dv")
                    p.wait("scalar", rmt)
                    ext_prev = p.op("scalar", lambda e, qt=qt: e.activation(
                        out=Pb[:], in_=ps_big[:], func=AF.Exp,
                        bias=negmax[:], scale=1.0,
                        accum_out=zrec[:, qt:qt + 1]), inc="ac")
                    p.wait("vector", ext_prev)
                    p.op("vector", lambda e, qt=qt: e.reciprocal(
                        out=zrec[:, qt:qt + 1], in_=zrec[:, qt:qt + 1]), inc="dv")
                    p.wait("tensor", ext_prev)
                    p.wait("scalar", av_done)
                    for mc in range(16):
                        trt = p.op("tensor", lambda e, mc=mc: e.transpose(
                            out=ps_tr[:], in_=Pb[:, mc * 128:(mc + 1) * 128],
                            identity=identb[:]), inc="pe")
                        p.wait("scalar", trt)
                        cpt = p.op("scalar", lambda e, mc=mc, qi=qi: e.copy(
                            out=PT4[:, mc * 512 + qi * 128: mc * 512 + (qi + 1) * 128],
                            in_=ps_tr[:]), inc="ac")
                        p.wait("tensor", cpt)
                p.wait("scalar", p.tok("dv"))
                for cc2 in range(2):
                    p.wait("tensor", p.tok("ac"))
                    for mc in range(16):
                        avt = p.op("tensor", lambda e, mc=mc, hh2=hh2, cc2=cc2: e.matmul(
                            out=ps_sm[:],
                            lhsT=vtok[:, mc * 512 + hh2 * 256 + cc2 * 128:
                                         mc * 512 + hh2 * 256 + (cc2 + 1) * 128],
                            rhs=PT4[:, mc * 512:(mc + 1) * 512],
                            start=(mc == 0), stop=(mc == 15)),
                            inc="pe" if mc == 15 else None)
                    av_done = avt
                    p.wait("scalar", avt)
                    aet = p.op("scalar", lambda e: e.copy(out=avf[:], in_=ps_sm[:]), inc="ac")
                    p.wait("tensor", aet)
                    for qi in range(4):
                        trt = p.op("tensor", lambda e, qi=qi: e.transpose(
                            out=ps_tr[:], in_=avf[:, qi * 128:(qi + 1) * 128],
                            identity=identb[:]), inc="pe")
                        p.wait("scalar", trt)
                        p.wait("scalar", store_tok)
                        cpt = p.op("scalar", lambda e, qi=qi, qb=qb, hh2=hh2, cc2=cc2: e.activation(
                            out=outstg[:, qi * 512 + hh2 * 256 + cc2 * 128:
                                          qi * 512 + hh2 * 256 + (cc2 + 1) * 128],
                            in_=ps_tr[:], func=AF.Identity,
                            bias=0.0, scale=zrec[:, qb * 4 + qi: qb * 4 + qi + 1]), inc="ac")
                        p.wait("tensor", cpt)
            p.wait("sync", p.tok("ac"))
            for qi in range(4):
                store_tok = p.dma("sync", attn_dst(qb * 4 + qi, g),
                                  outstg[:, qi * 512:(qi + 1) * 512], f"ost{qb % 2}")
    p.wait("sync", p.tok("ost0"))
    p.wait("sync", p.tok("ost1"))
    if ctx is None:
        p.emit()
        return nc


def stage_A(inputs):
    bivmat, revtile, scdelta = host_consts_L0()
    ident = np.eye(128, dtype=BF)
    X = np.asarray(inputs["L0_attn_x"])
    WqT = np.ascontiguousarray(np.asarray(inputs["L0_attn_Wq"]).T)
    WkT = np.ascontiguousarray(np.asarray(inputs["L0_attn_Wk"]).T)
    WvT = np.ascontiguousarray(np.asarray(inputs["L0_attn_Wv"]).T)
    bw = np.asarray(inputs["L0_attn_bias_weight"])
    in_maps = []
    for c in range(NCORE):
        b, s = c // 2, c % 2
        sl = slice(s * 2048, (s + 1) * 2048)
        bwT_c = np.zeros((3, 16), np.float32)
        bwT_c[:, 0:8] = bw[s * 8:(s + 1) * 8, :].T
        in_maps.append({
            "x0": np.ascontiguousarray(X[b]),
            "wqT": np.ascontiguousarray(WqT[:, sl]),
            "wkT": np.ascontiguousarray(WkT[:, sl]),
            "wvT": np.ascontiguousarray(WvT[:, sl]),
            "bq": np.ascontiguousarray(np.asarray(inputs["L0_attn_bq"])[sl]),
            "bk": np.ascontiguousarray(np.asarray(inputs["L0_attn_bk"])[sl]),
            "bv": np.ascontiguousarray(np.asarray(inputs["L0_attn_bv"])[sl]),
            "bwT": bwT_c,
            "bivmat": bivmat, "revtile": revtile, "scdelta": scdelta,
            "ident": ident,
        })
    return in_maps




def build_launch_B(ctx=None):
    if ctx is None:
        nc = bass.Bass("TRN2", target_bir_lowering=False, num_devices=NCORE)
        p = Prog(nc)
        x0 = nc.dram_tensor("x0", [TPC, DM], F32, kind="ExternalInput").ap()
        attn = nc.dram_tensor("attn", [TPC, DM], BF16, kind="ExternalInput").ap()
        w0T = nc.dram_tensor("w0T", [DM, DM], F32, kind="ExternalInput").ap()
        bias0 = nc.dram_tensor("bias0", [1, DM], F32, kind="ExternalInput").ap()
        a0T = nc.dram_tensor("a0T", [DM, 256], F32, kind="ExternalInput").ap()
        b0cat = nc.dram_tensor("b0cat", [256, DM], F32, kind="ExternalInput").ap()
        coefr = nc.dram_tensor("coefr", [256], F32, kind="ExternalInput").ap()
        sclr = nc.dram_tensor("sclr", [256], F32, kind="ExternalInput").ap()
        ident = nc.dram_tensor("ident", [128, 128], BF16, kind="ExternalInput").ap()
        x2o = nc.dram_tensor("x2o", [TPC, DM], F32, kind="ExternalOutput").ap()
        attn_srcs = lambda tt: [(slice(0, DM),
                                 lambda tt=tt: attn[tt * 128:(tt + 1) * 128, :])]
        x2_dst = lambda tt: (lambda tt=tt: x2o[tt * 128:(tt + 1) * 128, :])
    else:
        nc, p = ctx["nc"], ctx["p"]
        x0 = ctx["x0own"]
        w0T, bias0, a0T, b0cat, coefr, sclr, ident = (
            ctx["w0T"], ctx["bias0"], ctx["a0T"], ctx["b0cat"], ctx["coefr"],
            ctx["sclr"], ctx["ident"])
        attn_srcs = ctx["attn_srcs"]
        x2_dst = ctx["x2_dst"]
    xT1d = nc.dram_tensor("xT1d", [32, 128, TPC], BF16, kind="Internal").ap()

    al = ctx["al"] if ctx else nc.alloc_sbuf_tensor
    identb = al("identb", [128, 128], BF16)
    z = al("z", [128, 8 * DM], F32)
    scr = al("scr", [128, 4352], F32)
    a0b = al("a0b", [128, 32 * 256], BF16)
    b0s = al("b0s", [128, 2 * 512], BF16)
    xaT = al("xaT", [128, 2 * TPC], BF16)
    xw = al("xw", [128, 4 * 1024], BF16)
    stg = xw[:, 0:DM]
    wfs = al("wfs", [128, 2 * 2048], F32)
    wbs = al("wbs", [128, 2 * 2048], BF16)
    x1tmp = al("x1tmp", [128, 512], F32)
    brb = al("brb", [1, 512], BF16)
    ones1b = al("ones1b", [1, 128], BF16)
    stat = al("stat", [128, 48], F32)
    wcol = al("wcol", [128, 2], F32)
    ccol = al("ccol", [128, 2], F32)
    scol = al("scol", [128, 2], F32)

    ps = ctx["PS"] if ctx else nc.alloc_psum_tensor("ps", [128, 4096], F32)
    ps_trv = ps.bitcast(BF16)[:, 0:128]
    ps_xa = ps[:, 512:768]

    b0f = scr[:, 0:1024]
    ZS, ZSS, RSTD, NEGM = 0, 8, 16, 24
    MEAN, VAR = 32, 40

    def ln_stats(tt, ztile, pfx):
        p.wait("vector", p.tok("ac"))
        p.wait("scalar", p.tok("dv"))
        t1 = p.op("vector", lambda e: e.reduce_sum(
            out=stat[:, ZS + tt:ZS + tt + 1], in_=ztile, axis=AX.X), inc="dv")
        t2 = p.op("scalar", lambda e: e.activation(
            out=stg[:, 0:DM], in_=ztile, func=AF.Square,
            bias=0.0, scale=1.0, accum_out=stat[:, ZSS + tt:ZSS + tt + 1]), inc="ac")
        p.wait("vector", t1)
        p.wait("vector", t2)
        t3 = p.op("vector", lambda e: e.tensor_scalar(
            out=stat[:, MEAN:MEAN + 1], in0=stat[:, ZS + tt:ZS + tt + 1],
            scalar1=1.0 / DM, scalar2=None, op0=ALU.mult), inc="dv")
        p.wait("vector", t3)
        t4 = p.op("vector", lambda e: e.tensor_scalar(
            out=stat[:, VAR:VAR + 1], in0=stat[:, ZSS + tt:ZSS + tt + 1],
            scalar1=1.0 / DM, scalar2=None, op0=ALU.mult), inc="dv")
        p.wait("scalar", t3)
        t5 = p.op("scalar", lambda e: e.activation(
            out=stat[:, MEAN + 1:MEAN + 2], in_=stat[:, MEAN:MEAN + 1],
            func=AF.Square, bias=0.0, scale=1.0), inc="ac")
        p.wait("vector", t4)
        p.wait("vector", t5)
        t6 = p.op("vector", lambda e: e.tensor_tensor(
            out=stat[:, VAR:VAR + 1], in0=stat[:, VAR:VAR + 1],
            in1=stat[:, MEAN + 1:MEAN + 2], op=ALU.subtract), inc="dv")
        p.wait("vector", t6)
        t6b = p.op("vector", lambda e: e.tensor_scalar(
            out=stat[:, VAR:VAR + 1], in0=stat[:, VAR:VAR + 1],
            scalar1=LN_EPS, scalar2=None, op0=ALU.add), inc="dv")
        p.wait("scalar", t6b)
        t7 = p.op("scalar", lambda e: e.activation(
            out=stat[:, VAR + 1:VAR + 2], in_=stat[:, VAR:VAR + 1],
            func=AF.Ln, bias=0.0, scale=1.0), inc="ac")
        p.wait("scalar", t7)
        t8 = p.op("scalar", lambda e: e.activation(
            out=stat[:, RSTD + tt:RSTD + tt + 1], in_=stat[:, VAR + 1:VAR + 2],
            func=AF.Exp, bias=0.0, scale=-0.5), inc="ac")
        p.wait("vector", t8)
        p.wait("scalar", t8)
        t9 = p.op("vector", lambda e: e.tensor_tensor(
            out=stat[:, NEGM + tt:NEGM + tt + 1], in0=stat[:, MEAN:MEAN + 1],
            in1=stat[:, RSTD + tt:RSTD + tt + 1], op=ALU.mult), inc="dv")
        p.wait("vector", t9)
        t10 = p.op("vector", lambda e: e.tensor_scalar(
            out=stat[:, NEGM + tt:NEGM + tt + 1], in0=stat[:, NEGM + tt:NEGM + tt + 1],
            scalar1=-1.0, scalar2=None, op0=ALU.mult), inc="dv")
        return t10

    p.dma("sync", identb[:], ident, "lc")
    mzt = p.op("vector", lambda e: e.memset(ones1b[:], 1.0), inc="dv")
    p.wait("scalar", p.tok("lc"))
    bct = mzt
    for rc in range(2):
        p.dma("sync", ccol[:, rc:rc + 1], coefr[rc * 128:(rc + 1) * 128].unsqueeze(1), "lcc")
        p.dma("sync", scol[:, rc:rc + 1], sclr[rc * 128:(rc + 1) * 128].unsqueeze(1), "lcc")
    p.wait("vector", p.tok("lcc"))
    wct = p.op("vector", lambda e: e.tensor_tensor(out=wcol[:], in0=ccol[:],
                                                   in1=scol[:], op=ALU.mult), inc="dv")
    p.wait("sync", bct)
    for half in range(2):
        t = p.dma("sync", scr[:, 0:DM].rearrange("p (c r) -> p c r", c=16),
                  a0T[half * 2048:(half + 1) * 2048, :]
                  .rearrange("(c p) r -> p c r", p=128), f"la{half}")
        p.wait("gpsimd", t)
        ct = p.op("gpsimd", lambda e, half=half: e.tensor_copy(
            out=a0b[:, half * 16 * 256:(half + 1) * 16 * 256], in_=scr[:, 0:DM]),
            inc="pool")
        p.wait("sync", ct)

    at_b = scr.bitcast(BF16)[:, 0:DM]
    xat = scr.bitcast(BF16)[:, DM:DM + 256]
    stg_user = None
    xa_ev = None
    for tt in range(8):
        p.wait("sync", p.tok("pool"))
        for dsl, srcfn in attn_srcs(tt):
            lt1 = p.dma("sync", at_b[:, dsl], srcfn, f"lat{tt % 2}")
        lt2 = p.dma("sync", z[:, tt * DM:(tt + 1) * DM],
                    x0[tt * 128:(tt + 1) * 128, :], f"lx{tt % 2}")
        p.wait("vector", lt1)
        p.wait("vector", lt2)
        zt = p.op("vector", lambda e, tt=tt: e.scalar_tensor_tensor(
            out=z[:, tt * DM:(tt + 1) * DM], in0=at_b, scalar=1.0,
            in1=z[:, tt * DM:(tt + 1) * DM], op0=ALU.mult, op1=ALU.add), inc="dv")
        p.wait("vector", zt)
        p.wait("scalar", zt)
        if stg_user is not None:
            p.wait("scalar", stg_user)
        sttok = ln_stats(tt, z[:, tt * DM:(tt + 1) * DM], "b")
        p.wait("scalar", sttok)
        x1t = p.op("scalar", lambda e, tt=tt: e.activation(
            out=at_b, in_=z[:, tt * DM:(tt + 1) * DM], func=AF.Identity,
            bias=stat[:, NEGM + tt:NEGM + tt + 1],
            scale=stat[:, RSTD + tt:RSTD + tt + 1]), inc="ac")
        p.wait("tensor", x1t)
        for c in range(32):
            trt = p.op("tensor", lambda e, c=c: e.transpose(
                out=ps_trv, in_=at_b[:, c * 128:(c + 1) * 128],
                identity=identb[:]), inc="pe")
            p.wait("vector", trt)
            cpt = p.op("vector", lambda e, c=c: e.tensor_copy(
                out=stg[:, c * 128:(c + 1) * 128], in_=ps_trv), inc="dv")
            p.wait("tensor", cpt)
        p.wait("sync", p.tok("dv"))
        stg_user = p.dma("sync",
                         xT1d[:, :, tt * 128:(tt + 1) * 128].transpose([1, 0, 2]),
                         stg.rearrange("p (c i) -> p c i", c=32), f"xtw{tt % 2}")
        p.wait("tensor", xa_ev)
        for d in range(32):
            xmt = p.op("tensor", lambda e, d=d: e.matmul(
                out=ps_xa, lhsT=stg[:, d * 128:(d + 1) * 128],
                rhs=a0b[:, d * 256:(d + 1) * 256],
                start=(d == 0), stop=(d == 31)), inc="pe" if d == 31 else None)
        p.wait("scalar", xmt)
        xa_ev = p.op("scalar", lambda e: e.copy(out=xat, in_=ps_xa), inc="ac")
        p.wait("tensor", xa_ev)
        for rc in range(2):
            trt = p.op("tensor", lambda e, rc=rc: e.transpose(
                out=ps_trv, in_=xat[:, rc * 128:(rc + 1) * 128], identity=identb[:]),
                inc="pe")
            p.wait("scalar", trt)
            xa_ev = p.op("scalar", lambda e, rc=rc, tt=tt: e.activation(
                out=xaT[:, rc * TPC + tt * 128: rc * TPC + (tt + 1) * 128],
                in_=ps_trv, func=AF.Identity, bias=0.0,
                scale=wcol[:, rc:rc + 1]), inc="ac")
            p.wait("tensor", xa_ev)
        p.wait("sync", p.tok("ac"))
    p.wait("sync", p.tok("xtw0"))
    p.wait("sync", p.tok("xtw1"))
    wu = [None, None]
    ev_tok = None
    xu = None
    prev_bmt = None
    for osp in range(8):
        p.wait("sync", p.tok("dv"))
        brl = p.dma("sync", scr[0:1, 3840:3840 + 512],
                    bias0[0:1, osp * 512:(osp + 1) * 512], f"brl{osp % 2}")
        p.wait("scalar", brl)
        p.wait("scalar", prev_bmt)
        brc = p.op("scalar", lambda e: e.copy(out=brb[:], in_=scr[0:1, 3840:3840 + 512]),
                   inc="ac")
        for dg in range(8):
            db = dg % 2
            if wu[db] is not None:
                p.wait("sync", wu[db])
            wl = p.dma("sync", wfs[:, db * 2048:(db + 1) * 2048]
                       .rearrange("p (c o) -> p c o", c=4),
                       w0T[dg * 512:(dg + 1) * 512, osp * 512:(osp + 1) * 512]
                       .rearrange("(c p) o -> p c o", p=128), f"wl{db}")
            p.wait("scalar", xu)
            xl = p.op("scalar", lambda e, dg=dg: e.dma_start(
                out=xw.rearrange("p (c m) -> p c m", c=4),
                in_=xT1d[dg * 4:(dg + 1) * 4].transpose([1, 0, 2])), inc="xl", k=16)
            p.wait("gpsimd", wl)
            p.wait("gpsimd", wu[db])
            wc = p.op("gpsimd", lambda e, db=db: e.tensor_copy(
                out=wbs[:, db * 2048:(db + 1) * 2048],
                in_=wfs[:, db * 2048:(db + 1) * 2048]), inc="pool")
            p.wait("tensor", wc)
            p.wait("tensor", xl)
            if dg < 2 and ev_tok is not None:
                p.wait("tensor", ev_tok)
            for dc in range(4):
                for tt in range(8):
                    mmt = p.op("tensor", lambda e, tt=tt, dc=dc, db=db, dg=dg: e.matmul(
                        out=ps[:, tt * 512:(tt + 1) * 512],
                        lhsT=xw[:, dc * 1024 + tt * 128: dc * 1024 + (tt + 1) * 128],
                        rhs=wbs[:, db * 2048 + dc * 512: db * 2048 + (dc + 1) * 512],
                        start=(dg == 0 and dc == 0), stop=False),
                        inc="pe" if (tt == 7 and dc == 3) else None)
            wu[db] = mmt
            xu = mmt
        p.wait("sync", p.tok("pe"))
        for rc in range(2):
            blt = p.dma("sync", b0f[:, rc * 512:(rc + 1) * 512],
                        b0cat[rc * 128:(rc + 1) * 128, osp * 512:(osp + 1) * 512],
                        f"lb{rc}")
            p.wait("gpsimd", blt)
            bctk = p.op("gpsimd", lambda e, rc=rc: e.tensor_copy(
                out=b0s[:, rc * 512:(rc + 1) * 512], in_=b0f[:, rc * 512:(rc + 1) * 512]),
                inc="pool")
            p.wait("tensor", bctk)
        for tt in range(8):
            for rc in range(2):
                p.op("tensor", lambda e, tt=tt, rc=rc, osp=osp: e.matmul(
                    out=ps[:, tt * 512:(tt + 1) * 512],
                    lhsT=xaT[:, rc * TPC + tt * 128: rc * TPC + (tt + 1) * 128],
                    rhs=b0s[:, rc * 512:(rc + 1) * 512],
                    start=False, stop=False))
            if tt == 0:
                p.wait("tensor", brc)
            bmt = p.op("tensor", lambda e, tt=tt: e.matmul(
                out=ps[:, tt * 512:(tt + 1) * 512],
                lhsT=ones1b[:], rhs=brb[:],
                start=False, stop=True), inc="pe")
            prev_bmt = bmt
            p.wait("scalar", bmt)
            x1o = p.op("scalar", lambda e, tt=tt, osp=osp: e.activation(
                out=x1tmp[:], in_=z[:, tt * DM + osp * 512: tt * DM + (osp + 1) * 512],
                func=AF.Identity, bias=stat[:, NEGM + tt:NEGM + tt + 1],
                scale=stat[:, RSTD + tt:RSTD + tt + 1]), inc="ac")
            p.wait("vector", x1o)
            ev_tok = p.op("vector", lambda e, tt=tt, osp=osp: e.tensor_tensor(
                out=z[:, tt * DM + osp * 512: tt * DM + (osp + 1) * 512],
                in0=ps[:, tt * 512:(tt + 1) * 512], in1=x1tmp[:], op=ALU.add), inc="dv")
            p.wait("scalar", ev_tok)
    out_f = scr[:, 0:DM]
    ot_prev = None
    for tt in range(8):
        p.wait("vector", p.tok("dv"))
        p.wait("scalar", p.tok("dv"))
        sttok = ln_stats(tt, z[:, tt * DM:(tt + 1) * DM], "b2")
        p.wait("scalar", sttok)
        if ot_prev is not None:
            p.wait("scalar", ot_prev)
        oft = p.op("scalar", lambda e, tt=tt: e.activation(
            out=out_f, in_=z[:, tt * DM:(tt + 1) * DM], func=AF.Identity,
            bias=stat[:, NEGM + tt:NEGM + tt + 1],
            scale=stat[:, RSTD + tt:RSTD + tt + 1]), inc="ac")
        p.wait("sync", oft)
        ot_prev = p.dma("sync", x2_dst(tt), out_f, f"ox{tt % 2}")
    p.wait("sync", p.tok("ox0"))
    p.wait("sync", p.tok("ox1"))
    if ctx is None:
        p.emit()
        return nc


def stage_B(inputs, attn_full, x0_full):
    ident = np.eye(128, dtype=BF)
    w0T = np.ascontiguousarray(np.asarray(inputs["L0_ffn_W"]).T)
    A = np.asarray(inputs["L0_ffn_A"])
    a0T = np.ascontiguousarray(A.reshape(256, DM).T)
    Bf = np.asarray(inputs["L0_ffn_Bf"])
    b0cat = np.ascontiguousarray(Bf.transpose(0, 2, 1).reshape(256, DM))
    coefr = np.ascontiguousarray(np.repeat(np.asarray(inputs["L0_ffn_coeffs"]), 16))
    sclr = np.ascontiguousarray(np.repeat(np.asarray(inputs["L0_ffn_scales"]), 16))
    bias0 = np.ascontiguousarray(np.asarray(inputs["L0_ffn_bias"])[None, :])
    in_maps = []
    for c in range(NCORE):
        b, s = c // 2, c % 2
        tsl = slice(s * 1024, (s + 1) * 1024)
        in_maps.append({
            "x0": np.ascontiguousarray(x0_full[b][tsl]),
            "attn": np.ascontiguousarray(attn_full[b][tsl]),
            "w0T": w0T, "bias0": bias0, "a0T": a0T, "b0cat": b0cat,
            "coefr": coefr, "sclr": sclr, "ident": ident,
        })
    return in_maps




def build_launch_C(ctx=None):
    if ctx is None:
        nc = bass.Bass("TRN2", target_bir_lowering=False, num_devices=NCORE)
        p = Prog(nc)
        x2f = nc.dram_tensor("x2f", [L, DM], F32, kind="ExternalInput").ap()
        wqT = nc.dram_tensor("wqT", [DM, 2048], F32, kind="ExternalInput").ap()
        wkT = nc.dram_tensor("wkT", [DM, 512], F32, kind="ExternalInput").ap()
        wvT = nc.dram_tensor("wvT", [DM, 512], F32, kind="ExternalInput").ap()
        wgT = nc.dram_tensor("wgT", [DM, 2048], F32, kind="ExternalInput").ap()
        winT = nc.dram_tensor("winT", [DM, 16], F32, kind="ExternalInput").ap()
        dtb = nc.dram_tensor("dtb", [1, 16], F32, kind="ExternalInput").ap()
        alog = nc.dram_tensor("alog", [1, 16], F32, kind="ExternalInput").ap()
        gnw = nc.dram_tensor("gnw", [1, 128], F32, kind="ExternalInput").ap()
        cosT = nc.dram_tensor("cosT", [128, L], F32, kind="ExternalInput").ap()
        sinT = nc.dram_tensor("sinT", [128, L], F32, kind="ExternalInput").ap()
        ident = nc.dram_tensor("ident", [128, 128], BF16, kind="ExternalInput").ap()
        identf = nc.dram_tensor("identf", [128, 128], F32, kind="ExternalInput").ap()
        trif = nc.dram_tensor("trif", [128, 128], F32, kind="ExternalInput").ap()
        ygo = nc.dram_tensor("ygo", [L, 2048], BF16, kind="ExternalOutput").ap()
        yg_dst = lambda n, kv: (lambda: ygo[n * 128:(n + 1) * 128,
                                            kv * 512:(kv + 1) * 512])
    else:
        nc, p = ctx["nc"], ctx["p"]
        x2f = ctx["x2f"]
        wqT, wkT, wvT, wgT, winT = (ctx["wq1T"], ctx["wk1T"], ctx["wv1T"],
                                    ctx["wg1T"], ctx["win1T"])
        dtb, alog, gnw, cosT, sinT = (ctx["dtb"], ctx["alog"], ctx["gnw"],
                                      ctx["cosT"], ctx["sinT"])
        ident, identf, trif = ctx["ident"], ctx["identf"], ctx["trif"]
        yg_dst = ctx["yg_dst"]
    NCh = 16
    xT2d = nc.dram_tensor("xT2d", [32, 128, L], BF16, kind="Internal").ap()

    al = ctx["al"] if ctx else nc.alloc_sbuf_tensor
    identb = al("identb", [128, 128], BF16)
    idenff = al("idenff", [128, 128], F32)
    tri_f = al("tri_f", [128, 128], F32)
    cos_s = al("cos_s", [128, L], F32)
    sin_s = al("sin_s", [128, L], F32)
    scr = al("scr", [128, 6144], F32)
    stg = al("stg", [128, DM], BF16)
    xTh = al("xTh", [128, 32 * 1024], BF16)
    wst_f = al("wst_f", [128, DM], F32)
    wst_b = al("wst_b", [128, DM], BF16)
    qfm = al("qfm", [128, 4 * L], BF16)
    gtok = al("gtok", [128, 16 * 512], BF16)
    kfm = al("kfm", [128, L], BF16)
    ktok = al("ktok", [128, 16 * 128], BF16)
    vtok = al("vtok", [128, 16 * 128], BF16)
    winb = al("winb", [128, 32 * 16], BF16)
    winlo = al("winlo", [128, 32 * 16], BF16)
    xlob = al("xlob", [128, DM], BF16)
    dtbb = al("dtbb", [1, 16], BF16)
    dtv = al("dtv", [128, 16 * 16], F32)
    Gv = al("Gv", [128, 16 * 16], F32)
    nGv = al("nGv", [128, 16 * 16], F32)
    Gl = al("Gl", [1, 16 * 16], F32)
    bcGl = al("bcGl", [128, 16 * 16], F32)
    edcv = al("edcv", [128, 16 * 16], F32)
    deccv = al("deccv", [128, 16 * 16], F32)
    egcv = al("egcv", [128, 16 * 16], F32)
    Aneg = al("Aneg", [128, 16], F32)
    gnb = al("gnb", [128, 128], F32)
    ones1b = al("ones1b", [1, 128], BF16)
    ones1f = al("ones1f", [1, 128], F32)
    onecf = al("onecf", [128, 1], F32)
    prevC = al("prevC", [128, 4 * 128], F32)
    prevb = al("prevb", [128, 4 * 128], BF16)
    T1 = al("T1", [128, 128], F32)
    dmask = al("dmask", [128, 128], F32)
    mt = al("mt", [128, 128], BF16)
    tlo = mt
    ytmp = T1[:, :]
    yjunk = mt
    xct = al("xct", [128, 128], BF16)
    kdt = al("kdt", [128, 128], BF16)
    ytile = al("ytile", [128, 128], F32)
    ygstg = al("ygstg", [128, 512], BF16)
    ccol = al("ccol", [128, 1], F32)
    alog_s = al("alog_s", [1, 16], F32)
    row16 = al("row16", [1, 16], F32)

    if ctx is None:
        ps_proj = nc.alloc_psum_tensor("ps_proj", [128, 1024], F32)
        ps_attn = nc.alloc_psum_tensor("ps_attn", [128, 2048], F32)
        ps_trf = nc.alloc_psum_tensor("ps_trf", [128, 128], F32)
        ps_tr = nc.alloc_psum_tensor("ps_tr", [128, 128], BF16)
    else:
        PS = ctx["PS"]
        ps_proj = PS[:, 0:1024]
        ps_attn = PS[:, 1024:3072]
        ps_trf = PS[:, 3072:3200]
        ps_tr = PS.bitcast(BF16)[:, 7 * 1024: 7 * 1024 + 128]
    ps_st = ps_attn[:, 0:128]
    ps_yA = ps_attn[:, 512:640]
    ps_yB = ps_attn[:, 1024:1152]
    ps_C = ps_attn[:, 1536:1664]
    ps_dt = ps_attn[:, 0:16]
    ps_G = ps_attn[:, 512:528]
    ps_gl = ps_attn[0:1, 1024:1040]
    ps_bc = ps_attn[:, 1536:1552]

    p.dma("sync", identb[:], ident, "lc")
    p.dma("sync", idenff[:], identf, "lc")
    p.dma("sync", tri_f[:], trif, "lc")
    p.dma("sync", cos_s[:], cosT, "lc")
    p.dma("sync", sin_s[:], sinT, "lc")
    p.dma("sync", alog_s[:], alog, "lc")
    p.dma("sync", scr[0:1, 0:16], dtb, "lc")
    p.dma("sync", scr[0:1, 16:144], gnw, "lc")
    wl = p.dma("sync", scr[:, 2048:2048 + 512].rearrange("p (c r) -> p c r", c=32),
               winT.rearrange("(c p) r -> p c r", p=128), "lc")
    mz1 = p.op("vector", lambda e: e.memset(ones1b[:], 1.0), inc="dv")
    mz2 = p.op("vector", lambda e: e.memset(ones1f[:], 1.0), inc="dv")
    mz3 = p.op("vector", lambda e: e.memset(onecf[:], 1.0), inc="dv")
    mz4 = p.op("vector", lambda e: e.memset(prevC[:], 0.0), inc="dv")
    mz5 = p.op("vector", lambda e: e.memset(prevb[:], 0.0), inc="dv")
    p.wait("scalar", p.tok("lc"))
    p.wait("scalar", p.tok("dv"))
    c1 = p.op("scalar", lambda e: e.copy(out=dtbb[:], in_=scr[0:1, 0:16]), inc="ac")
    c2 = p.op("scalar", lambda e: e.copy(out=winb[:], in_=scr[:, 2048:2048 + 512]), inc="ac")
    p.wait("vector", c2)
    c2b = p.op("vector", lambda e: e.tensor_tensor(out=winlo[:], in0=scr[:, 2048:2048 + 512],
                                                   in1=winb[:], op=ALU.subtract), inc="dv")
    c3 = p.op("scalar", lambda e: e.activation(out=row16[:], in_=alog_s[:],
                                               func=AF.Exp, bias=0.0, scale=1.0), inc="ac")
    p.wait("vector", c3)
    c4 = p.op("vector", lambda e: e.tensor_scalar(out=row16[:], in0=row16[:],
                                                  scalar1=-1.0, scalar2=None,
                                                  op0=ALU.mult), inc="dv")
    p.wait("tensor", c4)
    bc1 = p.op("tensor", lambda e: e.matmul(out=ps_attn[:, 0:16], lhsT=ones1f[:],
                                            rhs=row16[:], start=True, stop=True), inc="pe")
    p.wait("vector", bc1)
    an1 = p.op("vector", lambda e: e.tensor_copy(out=Aneg[:], in_=ps_attn[:, 0:16]), inc="dv")
    p.wait("scalar", p.tok("dv"))
    c5 = p.op("scalar", lambda e: e.copy(out=gnb[0:1, :],
                                         in_=scr[0:1, 16:144]), inc="ac")
    p.wait("tensor", c5)
    p.wait("tensor", an1)
    bc2 = p.op("tensor", lambda e: e.matmul(out=ps_attn[:, 0:128], lhsT=ones1f[:],
                                            rhs=gnb[0:1, :], start=True, stop=True), inc="pe")
    p.wait("vector", bc2)
    gn1 = p.op("vector", lambda e: e.tensor_copy(out=gnb[:], in_=ps_attn[:, 0:128]), inc="dv")

    xf = scr[:, 0:DM]
    xb = scr.bitcast(BF16)[:, 2 * DM: 3 * DM]
    dtr = scr[:, 5500:5516]
    sp1 = scr[:, 5520:5536]
    sp2 = scr[:, 5540:5556]
    gt = scr[:, 5560:5576]
    p.wait("sync", p.tok("ac"))
    cast_tok = None
    tr_last = None
    st_tok = None
    dt_ev = p.tok("dv")
    for tt in range(NCh):
        if cast_tok is not None:
            p.wait("sync", cast_tok)
        ltok = p.dma("sync", xf, x2f[tt * 128:(tt + 1) * 128, :], f"ldx{tt % 2}")
        p.wait("scalar", ltok)
        if tr_last is not None:
            p.wait("scalar", tr_last)
        cast_tok = p.op("scalar", lambda e: e.copy(out=xb, in_=xf), inc="ac")
        p.wait("vector", cast_tok)
        lo_tok = p.op("vector", lambda e: e.tensor_tensor(
            out=xlob[:], in0=xf, in1=xb, op=ALU.subtract), inc="dv")
        p.wait("tensor", cast_tok)
        p.wait("tensor", lo_tok)
        if st_tok is not None:
            p.wait("vector", st_tok)
        p.wait("tensor", dt_ev)
        for c in range(32):
            tr_last = p.op("tensor", lambda e, c=c: e.transpose(
                out=ps_tr[:], in_=xb[:, c * 128:(c + 1) * 128], identity=identb[:]),
                inc="pe")
            p.wait("vector", tr_last)
            cpt = p.op("vector", lambda e, c=c: e.tensor_copy(
                out=stg[:, c * 128:(c + 1) * 128], in_=ps_tr[:]), inc="dv")
            p.wait("tensor", cpt)
            trl = p.op("tensor", lambda e, c=c: e.transpose(
                out=ps_tr[:], in_=xlob[:, c * 128:(c + 1) * 128], identity=identb[:]),
                inc="pe")
            p.wait("vector", trl)
            cpl = p.op("vector", lambda e: e.tensor_copy(out=tlo[:], in_=ps_tr[:]),
                       inc="dv")
            p.wait("tensor", cpl)
            p.op("tensor", lambda e, c=c: e.matmul(
                out=ps_dt, lhsT=stg[:, c * 128:(c + 1) * 128],
                rhs=winb[:, c * 16:(c + 1) * 16],
                start=(c == 0), stop=False))
            p.op("tensor", lambda e, c=c: e.matmul(
                out=ps_dt, lhsT=tlo[:], rhs=winb[:, c * 16:(c + 1) * 16],
                start=False, stop=False))
            p.op("tensor", lambda e, c=c: e.matmul(
                out=ps_dt, lhsT=stg[:, c * 128:(c + 1) * 128],
                rhs=winlo[:, c * 16:(c + 1) * 16],
                start=False, stop=False))
        p.wait("sync", p.tok("dv"))
        st_tok = p.dma("sync",
                       xT2d[:, :, tt * 128:(tt + 1) * 128].transpose([1, 0, 2]),
                       stg.rearrange("p (c i) -> p c i", c=32), f"xtw{tt % 2}")
        dmt = p.op("tensor", lambda e: e.matmul(
            out=ps_dt, lhsT=ones1b[:], rhs=dtbb[:], start=False, stop=True), inc="pe")
        p.wait("scalar", dmt)
        p.wait("scalar", p.tok("dv"))
        e1 = p.op("scalar", lambda e: e.copy(out=dtr, in_=ps_dt), inc="ac")
        p.wait("scalar", e1)
        e2 = p.op("scalar", lambda e: e.activation(out=sp1, in_=dtr, func=AF.Abs,
                                                   bias=0.0, scale=1.0), inc="ac")
        p.wait("scalar", e2)
        e3 = p.op("scalar", lambda e: e.activation(out=sp1, in_=sp1, func=AF.Exp,
                                                   bias=0.0, scale=-1.0), inc="ac")
        p.wait("vector", e3)
        e4 = p.op("vector", lambda e: e.tensor_scalar(out=sp1, in0=sp1, scalar1=1.0,
                                                      scalar2=None, op0=ALU.add), inc="dv")
        p.wait("scalar", e4)
        e5 = p.op("scalar", lambda e: e.activation(out=sp1, in_=sp1, func=AF.Ln,
                                                   bias=0.0, scale=1.0), inc="ac")
        p.wait("vector", e5)
        e6 = p.op("vector", lambda e: e.tensor_scalar(out=sp2, in0=dtr, scalar1=0.0,
                                                      scalar2=None, op0=ALU.max), inc="dv")
        p.wait("vector", e6)
        e7 = p.op("vector", lambda e, tt=tt: e.tensor_tensor(
            out=dtv[:, tt * 16:(tt + 1) * 16], in0=sp1, in1=sp2, op=ALU.add), inc="dv")
        p.wait("vector", e7)
        e8 = p.op("vector", lambda e, tt=tt: e.tensor_tensor(
            out=gt, in0=dtv[:, tt * 16:(tt + 1) * 16], in1=Aneg[:], op=ALU.mult), inc="dv")
        p.wait("tensor", e8)
        g1 = p.op("tensor", lambda e: e.matmul(out=ps_G, lhsT=tri_f[:], rhs=gt,
                                               start=True, stop=True), inc="pe")
        g2 = p.op("tensor", lambda e: e.matmul(out=ps_gl, lhsT=onecf[:], rhs=gt,
                                               start=True, stop=True), inc="pe")
        p.wait("vector", g2)
        dt_ev = p.op("vector", lambda e, tt=tt: e.tensor_copy(
            out=Gv[:, tt * 16:(tt + 1) * 16], in_=ps_G), inc="dv")
        gl_ev = p.op("vector", lambda e, tt=tt: e.tensor_copy(
            out=Gl[0:1, tt * 16:(tt + 1) * 16], in_=ps_gl), inc="dv")
        p.wait("tensor", gl_ev)
        p.wait("sync", p.tok("ac"))
        p.wait("sync", p.tok("dv"))
    p.wait("vector", p.tok("dv"))
    ng = p.op("vector", lambda e: e.tensor_scalar(out=nGv[:], in0=Gv[:], scalar1=-1.0,
                                                  scalar2=None, op0=ALU.mult), inc="dv")
    for n in range(NCh):
        p.wait("tensor", p.tok("dv"))
        b1 = p.op("tensor", lambda e, n=n: e.matmul(
            out=ps_bc, lhsT=ones1f[:], rhs=Gl[0:1, n * 16:(n + 1) * 16],
            start=True, stop=True), inc="pe")
        p.wait("vector", b1)
        b2 = p.op("vector", lambda e, n=n: e.tensor_copy(
            out=bcGl[:, n * 16:(n + 1) * 16], in_=ps_bc), inc="dv")
        p.wait("tensor", b2)
    p.wait("scalar", p.tok("dv"))
    ec1 = p.op("scalar", lambda e: e.activation(out=deccv[:], in_=bcGl[:],
                                                func=AF.Exp, bias=0.0, scale=1.0), inc="ac")
    ec2 = p.op("scalar", lambda e: e.activation(out=egcv[:], in_=Gv[:],
                                                func=AF.Exp, bias=0.0, scale=1.0), inc="ac")
    p.wait("vector", ec2)
    ec3 = p.op("vector", lambda e: e.tensor_tensor(out=edcv[:], in0=bcGl[:],
                                                   in1=Gv[:], op=ALU.subtract), inc="dv")
    p.wait("scalar", ec3)
    ec4 = p.op("scalar", lambda e: e.activation(out=edcv[:], in_=edcv[:],
                                                func=AF.Exp, bias=0.0, scale=1.0), inc="ac")
    p.wait("tensor", p.tok("ac"))
    p.wait("vector", p.tok("ac"))

    def rope_evac(psrc, n_cols, dst, coff):
        rt = scr[:, 0:1024]
        t1s = scr[:, 1024:2048]
        r1 = p.op("scalar", lambda e: e.activation(
            out=rt[0:64, 0:n_cols], in_=psrc[64:128, :], func=AF.Identity,
            bias=0.0, scale=-1.0), inc="ac")
        r2 = p.op("scalar", lambda e: e.activation(
            out=rt[64:128, 0:n_cols], in_=psrc[0:64, :], func=AF.Identity,
            bias=0.0, scale=1.0), inc="ac")
        p.wait("vector", r2)
        m1 = p.op("vector", lambda e: e.tensor_tensor(
            out=t1s[:, 0:n_cols], in0=psrc, in1=cos_s[:, coff:coff + n_cols],
            op=ALU.mult), inc="dv")
        p.wait("vector", m1)
        m2 = p.op("vector", lambda e: e.tensor_tensor(
            out=rt[:, 0:n_cols], in0=rt[:, 0:n_cols],
            in1=sin_s[:, coff:coff + n_cols], op=ALU.mult), inc="dv")
        p.wait("vector", m2)
        m3 = p.op("vector", lambda e: e.tensor_tensor(
            out=dst, in0=t1s[:, 0:n_cols], in1=rt[:, 0:n_cols], op=ALU.add), inc="dv")
        return m3

    p.wait("sync", p.tok("xtw0"))
    p.wait("sync", p.tok("xtw1"))
    fillreg = {}
    p.raw("gpsimd", lambda e: fillreg.__setitem__("r", e.to_reg(-1e30)))
    wld_tok = None
    proj_ev = None
    yg_st = None
    prev_su = None
    prev_yn1 = None
    prev_cn = None
    prev_ye1 = None
    prev_ye2 = None
    for kv in range(4):
        for mh in range(2):
            p.wait("sync", p.tok("pe"))
            xld = p.dma("sync", xTh.rearrange("p (c m) -> p c m", c=32),
                        xT2d[:, :, mh * 1024:(mh + 1) * 1024].transpose([1, 0, 2]),
                        f"lxh{mh}")
            for oc in range(10):
                if oc < 4:
                    kind, wsrc, gch = "q", wqT, kv * 4 + oc
                elif oc < 8:
                    kind, wsrc, gch = "g", wgT, kv * 4 + (oc - 4)
                elif oc == 8:
                    kind, wsrc, gch = "k", wkT, kv
                else:
                    kind, wsrc, gch = "v", wvT, kv
                if wld_tok is not None:
                    p.wait("sync", p.tok("pe"))
                wld_tok = p.dma("sync", wst_f[:].rearrange("p (c o) -> p c o", c=32),
                                wsrc[:, gch * 128:(gch + 1) * 128]
                                .rearrange("(c p) o -> p c o", p=128), "wld")
                p.wait("gpsimd", wld_tok)
                p.wait("gpsimd", p.tok("pe"))
                wct = p.op("gpsimd", lambda e: e.tensor_copy(out=wst_b[:], in_=wst_f[:]),
                           inc="pool")
                p.wait("tensor", wct)
                p.wait("tensor", xld)
                p.wait("tensor", p.tok("ac"))
                p.wait("tensor", p.tok("dv"))
                for d in range(32):
                    for half in range(2):
                        pmt = p.op("tensor", lambda e, d=d, h=half: e.matmul(
                            out=ps_proj[:, h * 512:(h + 1) * 512],
                            lhsT=wst_b[:, d * 128:(d + 1) * 128],
                            rhs=xTh[:, d * 1024 + h * 512: d * 1024 + (h + 1) * 512],
                            start=(d == 0), stop=(d == 31)),
                            inc="pe" if (d == 31 and half == 1) else None)
                p.wait("scalar", pmt)
                p.wait("vector", pmt)
                if kind == "q":
                    loc = oc
                    rtok = rope_evac(ps_proj[:, 0:1024], 1024,
                                     qfm[:, loc * L + mh * 1024: loc * L + (mh + 1) * 1024],
                                     mh * 1024)
                    p.wait("tensor", rtok)
                elif kind == "k":
                    rtok = rope_evac(ps_proj[:, 0:1024], 1024,
                                     kfm[:, mh * 1024:(mh + 1) * 1024], mh * 1024)
                    p.wait("tensor", rtok)
                    for j in range(8):
                        trt = p.op("tensor", lambda e, j=j, mh=mh: e.transpose(
                            out=ps_tr[:], in_=kfm[:, mh * 1024 + j * 128:
                                                   mh * 1024 + (j + 1) * 128],
                            identity=identb[:]), inc="pe")
                        p.wait("vector", trt)
                        cpt = p.op("vector", lambda e, j=j, mh=mh: e.tensor_copy(
                            out=ktok[:, (mh * 8 + j) * 128:(mh * 8 + j + 1) * 128],
                            in_=ps_tr[:]), inc="dv")
                        p.wait("tensor", cpt)
                else:
                    vt = scr.bitcast(BF16)[:, 0:1024]
                    ev = p.op("scalar", lambda e: e.copy(out=vt, in_=ps_proj[:, 0:1024]),
                              inc="ac")
                    p.wait("tensor", ev)
                    for j in range(8):
                        trt = p.op("tensor", lambda e, j=j: e.transpose(
                            out=ps_tr[:], in_=vt[:, j * 128:(j + 1) * 128],
                            identity=identb[:]), inc="pe")
                        p.wait("vector", trt)
                        if kind == "g":
                            loc = oc - 4
                            p.wait("scalar", trt)
                            cpt = p.op("scalar", lambda e, j=j, mh=mh, loc=loc: e.activation(
                                out=gtok[:, (mh * 8 + j) * 512 + loc * 128:
                                            (mh * 8 + j) * 512 + (loc + 1) * 128],
                                in_=ps_tr[:], func=AF.Silu, bias=0.0, scale=1.0), inc="dv")
                        else:
                            cpt = p.op("vector", lambda e, j=j, mh=mh: e.tensor_copy(
                                out=vtok[:, (mh * 8 + j) * 128:(mh * 8 + j + 1) * 128],
                                in_=ps_tr[:]), inc="dv")
                        p.wait("tensor", cpt)
                    p.wait("scalar", p.tok("dv"))
        p.wait("tensor", p.tok("dv"))
        p.wait("tensor", p.tok("ac"))
        for n in range(NCh):
            for j in range(4):
                hh = kv * 4 + j
                gcol = n * 16 + hh
                smt = p.op("tensor", lambda e, n=n, j=j: e.matmul(
                    out=ps_st, lhsT=kfm[:, n * 128:(n + 1) * 128],
                    rhs=qfm[:, j * L + n * 128: j * L + (n + 1) * 128],
                    start=True, stop=True), inc="pe")
                t1t = p.op("scalar", lambda e, gcol=gcol: e.activation(
                    out=T1[:], in_=T1[:],
                    func=AF.Identity, bias=Gv[:, gcol:gcol + 1], scale=0.0), inc="ac")
                p.wait("tensor", t1t)
                t2t = p.op("tensor", lambda e: e.transpose(
                    out=ps_trf[:], in_=T1[:], identity=idenff[:]), inc="pe")
                p.wait("scalar", t2t)
                p.wait("scalar", prev_yn1)
                t3t = p.op("scalar", lambda e, gcol=gcol: e.activation(
                    out=dmask[:], in_=ps_trf[:], func=AF.Identity,
                    bias=nGv[:, gcol:gcol + 1], scale=1.0), inc="ac")
                p.wait("gpsimd", t3t)
                t4t = p.op("gpsimd", lambda e: e.affine_select(
                    out=dmask[:], in_=dmask[:], pattern=[[1, 128]],
                    compare_op=ALU.is_ge, fill=fillreg["r"], base=0,
                    channel_multiplier=-1), inc="pool")
                p.wait("scalar", t4t)
                t4b = p.op("scalar", lambda e: e.activation(
                    out=dmask[:], in_=dmask[:], func=AF.Exp,
                    bias=0.0, scale=1.0), inc="ac")
                p.wait("vector", t4b)
                p.wait("vector", smt)
                t5t = p.op("vector", lambda e: e.tensor_tensor(
                    out=mt[:], in0=ps_st, in1=dmask[:], op=ALU.mult), inc="dv")
                p.wait("scalar", prev_cn)
                xc1 = p.op("scalar", lambda e, n=n, gcol=gcol: e.activation(
                    out=xct[:], in_=vtok[:, n * 128:(n + 1) * 128], func=AF.Identity,
                    bias=0.0, scale=dtv[:, gcol:gcol + 1]), inc="ac")
                kd2 = p.op("scalar", lambda e, n=n, gcol=gcol: e.activation(
                    out=kdt[:], in_=ktok[:, n * 128:(n + 1) * 128], func=AF.Identity,
                    bias=0.0, scale=edcv[:, gcol:gcol + 1]), inc="ac")
                p.wait("tensor", t5t)
                p.wait("tensor", xc1)
                p.wait("tensor", prev_ye2)
                p.wait("tensor", prev_ye1)
                ya = p.op("tensor", lambda e: e.matmul(
                    out=ps_yA, lhsT=mt[:], rhs=xct[:], start=True, stop=True), inc="pe")
                yb = p.op("tensor", lambda e, j=j: e.matmul(
                    out=ps_yB, lhsT=qfm[:, j * L + n * 128: j * L + (n + 1) * 128],
                    rhs=prevb[:, j * 128:(j + 1) * 128], start=True, stop=True), inc="pe")
                p.wait("tensor", kd2)
                p.wait("tensor", prev_su)
                cn = p.op("tensor", lambda e: e.matmul(
                    out=ps_C, lhsT=kdt[:], rhs=xct[:], start=True, stop=True), inc="pe")
                p.wait("vector", cn)
                su = p.op("vector", lambda e, j=j, gcol=gcol: e.scalar_tensor_tensor(
                    out=prevC[:, j * 128:(j + 1) * 128],
                    in0=prevC[:, j * 128:(j + 1) * 128],
                    scalar=deccv[:, gcol:gcol + 1],
                    in1=ps_C, op0=ALU.mult, op1=ALU.add), inc="dv")
                p.wait("scalar", su)
                p.wait("scalar", yb)
                pc = p.op("scalar", lambda e, j=j: e.copy(
                    out=prevb[:, j * 128:(j + 1) * 128],
                    in_=prevC[:, j * 128:(j + 1) * 128]), inc="ac")
                p.wait("scalar", yb)
                ye1 = p.op("scalar", lambda e, gcol=gcol: e.activation(
                    out=ytmp[:], in_=ps_yB, func=AF.Identity, bias=0.0,
                    scale=egcv[:, gcol:gcol + 1]), inc="ac")
                p.wait("vector", ye1)
                p.wait("vector", ya)
                ye2 = p.op("vector", lambda e: e.tensor_tensor(
                    out=ytile[:], in0=ps_yA, in1=ytmp[:], op=ALU.add), inc="dv")
                p.wait("scalar", ye2)
                p.wait("scalar", prev_yn1)
                p.wait("scalar", cn)
                rs1 = p.op("scalar", lambda e: e.activation(
                    out=yjunk[:], in_=ytile[:], func=AF.Square, bias=0.0, scale=1.0,
                    accum_out=ccol[:, 0:1]), inc="ac")
                p.wait("vector", rs1)
                rs2 = p.op("vector", lambda e: e.tensor_scalar(
                    out=ccol[:, 0:1], in0=ccol[:, 0:1], scalar1=1.0 / 128.0,
                    scalar2=RMS_EPS, op0=ALU.mult, op1=ALU.add), inc="dv")
                p.wait("scalar", rs2)
                rs3 = p.op("scalar", lambda e: e.activation(
                    out=ccol[:, 0:1], in_=ccol[:, 0:1], func=AF.Ln,
                    bias=0.0, scale=1.0), inc="ac")
                p.wait("scalar", rs3)
                rs4 = p.op("scalar", lambda e: e.activation(
                    out=ccol[:, 0:1], in_=ccol[:, 0:1], func=AF.Exp,
                    bias=0.0, scale=-0.5), inc="ac")
                p.wait("vector", rs4)
                yn1 = p.op("vector", lambda e: e.scalar_tensor_tensor(
                    out=ytile[:], in0=ytile[:], scalar=ccol[:, 0:1], in1=gnb[:],
                    op0=ALU.mult, op1=ALU.mult), inc="dv")
                p.wait("vector", yn1)
                if yg_st is not None:
                    p.wait("vector", yg_st)
                yg1 = p.op("vector", lambda e, j=j, n=n: e.tensor_tensor(
                    out=ygstg[:, j * 128:(j + 1) * 128], in0=ytile[:],
                    in1=gtok[:, n * 512 + j * 128: n * 512 + (j + 1) * 128],
                    op=ALU.mult), inc="dv")
                prev_su, prev_yn1, prev_cn = su, yn1, cn
                prev_ye1, prev_ye2 = ye1, ye2
            p.wait("sync", p.tok("dv"))
            yg_st = p.dma("sync", yg_dst(n, kv), ygstg[:], f"yst{n % 2}")
    p.wait("sync", p.tok("yst0"))
    p.wait("sync", p.tok("yst1"))
    if ctx is None:
        p.emit()
        return nc


def stage_C(inputs, x2_full):
    ident = np.eye(128, dtype=BF)
    identf = np.eye(128, dtype=np.float32)
    trif = np.ascontiguousarray(np.triu(np.ones((128, 128), np.float32)))
    WqT = np.ascontiguousarray(np.asarray(inputs["L1_attn_Wq"]).T)
    WkT = np.ascontiguousarray(np.asarray(inputs["L1_attn_Wk"]).T)
    WvT = np.ascontiguousarray(np.asarray(inputs["L1_attn_Wv"]).T)
    WgT = np.ascontiguousarray(np.asarray(inputs["L1_attn_Wg"]).T)
    WinT = np.ascontiguousarray(np.asarray(inputs["L1_attn_Win"]).T)
    cos, sin = host_rope_tables(np.arange(L))
    in_maps = []
    for c in range(NCORE):
        b, s = c // 2, c % 2
        qsl = slice(s * 2048, (s + 1) * 2048)
        kvsl = slice(s * 512, (s + 1) * 512)
        hsl = slice(s * 16, (s + 1) * 16)
        in_maps.append({
            "x2f": np.ascontiguousarray(x2_full[b]),
            "wqT": np.ascontiguousarray(WqT[:, qsl]),
            "wkT": np.ascontiguousarray(WkT[:, kvsl]),
            "wvT": np.ascontiguousarray(WvT[:, kvsl]),
            "wgT": np.ascontiguousarray(WgT[:, qsl]),
            "winT": np.ascontiguousarray(WinT[:, hsl]),
            "dtb": np.ascontiguousarray(np.asarray(inputs["L1_attn_dt_bias"])[None, hsl]),
            "alog": np.ascontiguousarray(np.asarray(inputs["L1_attn_A_log_bias"])[None, hsl]),
            "gnw": np.ascontiguousarray(np.asarray(inputs["L1_attn_gnorm_w"])[None, :]),
            "cosT": cos, "sinT": sin,
            "ident": ident, "identf": identf, "trif": trif,
        })
    return in_maps




def build_launch_D(ctx=None):
    if ctx is None:
        nc = bass.Bass("TRN2", target_bir_lowering=False, num_devices=NCORE)
        p = Prog(nc)
        x2 = nc.dram_tensor("x2", [TPC, DM], F32, kind="ExternalInput").ap()
        yg = nc.dram_tensor("yg", [TPC, DM], BF16, kind="ExternalInput").ap()
        woT = nc.dram_tensor("woT", [DM, DM], F32, kind="ExternalInput").ap()
        w1T = nc.dram_tensor("w1T", [DM, DM], F32, kind="ExternalInput").ap()
        b1 = nc.dram_tensor("b1", [1, DM], F32, kind="ExternalInput").ap()
        art = nc.dram_tensor("art", [DM, 80], F32, kind="ExternalInput").ap()
        bpT = nc.dram_tensor("bpT", [8, 128, DM], F32, kind="ExternalInput").ap()
        ident = nc.dram_tensor("ident", [128, 128], BF16, kind="ExternalInput").ap()
        x4o = nc.dram_tensor("x4o", [TPC, DM], F32, kind="ExternalOutput").ap()
        x2_src = lambda tt: (lambda tt=tt: x2[tt * 128:(tt + 1) * 128, :])
        yg_srcs = lambda tt: [(slice(0, DM),
                               lambda tt=tt: yg[tt * 128:(tt + 1) * 128, :])]
    else:
        nc, p = ctx["nc"], ctx["p"]
        woT, w1T, b1, art, bpT, ident = (ctx["woT"], ctx["w1T"], ctx["b1"],
                                         ctx["art"], ctx["bpT"], ctx["ident"])
        x4o = ctx["x4o"]
        x2_src = ctx["x2_src"]
        yg_srcs = ctx["yg_srcs"]
    ygTd = nc.dram_tensor("ygTd", [32, 128, TPC], BF16, kind="Internal").ap()
    xT3d = nc.dram_tensor("xT3d", [32, 128, TPC], BF16, kind="Internal").ap()

    al = ctx["al"] if ctx else nc.alloc_sbuf_tensor
    identb = al("identb", [128, 128], BF16)
    z = al("z", [128, 8 * DM], F32)
    scr = al("scr", [128, 5248], F32)
    artb = al("artb", [128, 32 * 80], BF16)
    hgT = al("hgT", [128, 8 * 1024], BF16)
    xw = al("xw", [128, 4 * 1024], BF16)
    stg = xw[:, 0:DM]
    wfs = al("wfs", [128, 2 * 2048], F32)
    wbs = al("wbs", [128, 2 * 2048], BF16)
    bp_b = al("bp_b", [128, 2 * 512], BF16)
    brb = al("brb", [1, 512], BF16)
    ones1b = al("ones1b", [1, 128], BF16)
    stat = al("stat", [128, 48], F32)
    rh_f = al("rh_f", [128, 64], F32)
    lg = al("lg", [128, 16], F32)
    pex = al("pex", [128, 16], F32)
    s1 = al("s1", [128, 16], F32)
    mskd = al("mskd", [128, 16], F32)
    comb = al("comb", [128, 16], F32)
    het = al("het", [128, 64], BF16)
    ccol = al("ccol", [128, 8], F32)

    ps = ctx["PS"] if ctx else nc.alloc_psum_tensor("ps", [128, 4096], F32)
    ps_trv = ps.bitcast(BF16)[:, 0:128]
    ps_aux = ps[:, 512:592]

    ZS, ZSS, RSTD, NEGM, MEAN, VAR = 0, 8, 16, 24, 32, 40

    def ln_stats(tt, ztile):
        p.wait("vector", p.tok("ac"))
        p.wait("scalar", p.tok("dv"))
        t1 = p.op("vector", lambda e: e.reduce_sum(
            out=stat[:, ZS + tt:ZS + tt + 1], in_=ztile, axis=AX.X), inc="dv")
        t2 = p.op("scalar", lambda e: e.activation(
            out=stg[:, 0:DM], in_=ztile, func=AF.Square,
            bias=0.0, scale=1.0, accum_out=stat[:, ZSS + tt:ZSS + tt + 1]), inc="ac")
        p.wait("vector", t1)
        p.wait("vector", t2)
        t3 = p.op("vector", lambda e: e.tensor_scalar(
            out=stat[:, MEAN:MEAN + 1], in0=stat[:, ZS + tt:ZS + tt + 1],
            scalar1=1.0 / DM, scalar2=None, op0=ALU.mult), inc="dv")
        p.wait("vector", t3)
        t4 = p.op("vector", lambda e: e.tensor_scalar(
            out=stat[:, VAR:VAR + 1], in0=stat[:, ZSS + tt:ZSS + tt + 1],
            scalar1=1.0 / DM, scalar2=None, op0=ALU.mult), inc="dv")
        p.wait("scalar", t3)
        t5 = p.op("scalar", lambda e: e.activation(
            out=stat[:, MEAN + 1:MEAN + 2], in_=stat[:, MEAN:MEAN + 1],
            func=AF.Square, bias=0.0, scale=1.0), inc="ac")
        p.wait("vector", t4)
        p.wait("vector", t5)
        t6 = p.op("vector", lambda e: e.tensor_tensor(
            out=stat[:, VAR:VAR + 1], in0=stat[:, VAR:VAR + 1],
            in1=stat[:, MEAN + 1:MEAN + 2], op=ALU.subtract), inc="dv")
        p.wait("vector", t6)
        t6b = p.op("vector", lambda e: e.tensor_scalar(
            out=stat[:, VAR:VAR + 1], in0=stat[:, VAR:VAR + 1],
            scalar1=LN_EPS, scalar2=None, op0=ALU.add), inc="dv")
        p.wait("scalar", t6b)
        t7 = p.op("scalar", lambda e: e.activation(
            out=stat[:, VAR + 1:VAR + 2], in_=stat[:, VAR:VAR + 1],
            func=AF.Ln, bias=0.0, scale=1.0), inc="ac")
        p.wait("scalar", t7)
        t8 = p.op("scalar", lambda e: e.activation(
            out=stat[:, RSTD + tt:RSTD + tt + 1], in_=stat[:, VAR + 1:VAR + 2],
            func=AF.Exp, bias=0.0, scale=-0.5), inc="ac")
        p.wait("vector", t8)
        p.wait("scalar", t8)
        t9 = p.op("vector", lambda e: e.tensor_tensor(
            out=stat[:, NEGM + tt:NEGM + tt + 1], in0=stat[:, MEAN:MEAN + 1],
            in1=stat[:, RSTD + tt:RSTD + tt + 1], op=ALU.mult), inc="dv")
        p.wait("vector", t9)
        t10 = p.op("vector", lambda e: e.tensor_scalar(
            out=stat[:, NEGM + tt:NEGM + tt + 1], in0=stat[:, NEGM + tt:NEGM + tt + 1],
            scalar1=-1.0, scalar2=None, op0=ALU.mult), inc="dv")
        return t10

    p.dma("sync", identb[:], ident, "lc")
    mzt = p.op("vector", lambda e: e.memset(ones1b[:], 1.0), inc="dv")
    t = p.dma("sync", scr[:, 0:2560].rearrange("p (c r) -> p c r", c=32),
              art.rearrange("(c p) r -> p c r", p=128), "la")
    p.wait("gpsimd", t)
    ct = p.op("gpsimd", lambda e: e.tensor_copy(out=artb[:], in_=scr[:, 0:2560]),
              inc="pool")
    p.wait("sync", ct)

    ygt = scr.bitcast(BF16)[:, 0:DM]
    p.wait("tensor", p.tok("lc"))
    for tt in range(8):
        p.dma("sync", z[:, tt * DM:(tt + 1) * DM], x2_src(tt), f"lz{tt % 2}")
        for dsl, srcfn in yg_srcs(tt):
            lt = p.dma("sync", ygt[:, dsl], srcfn, f"ly{tt % 2}")
        p.wait("tensor", lt)
        p.wait("vector", p.tok("xtw0"))
        p.wait("vector", p.tok("xtw1"))
        for c in range(32):
            trt = p.op("tensor", lambda e, c=c: e.transpose(
                out=ps_trv, in_=ygt[:, c * 128:(c + 1) * 128], identity=identb[:]),
                inc="pe")
            p.wait("vector", trt)
            cpt = p.op("vector", lambda e, c=c: e.tensor_copy(
                out=stg[:, c * 128:(c + 1) * 128], in_=ps_trv), inc="dv")
            p.wait("tensor", cpt)
        p.wait("sync", p.tok("dv"))
        sttk = p.dma("sync", ygTd[:, :, tt * 128:(tt + 1) * 128].transpose([1, 0, 2]),
                     stg.rearrange("p (c i) -> p c i", c=32), f"xtw{tt % 2}")
        p.wait("sync", p.tok("ac"))
        p.wait("sync", p.tok("pe"))
    p.wait("sync", p.tok("xtw0"))
    p.wait("sync", p.tok("xtw1"))
    p.wait("scalar", p.tok("xtw0"))
    p.wait("scalar", p.tok("xtw1"))
    wu = [None, None]
    ev_tok = None
    xu = None
    for osp in range(8):
        for dg in range(8):
            db = dg % 2
            if wu[db] is not None:
                p.wait("sync", wu[db])
            wl = p.dma("sync", wfs[:, db * 2048:(db + 1) * 2048]
                       .rearrange("p (c o) -> p c o", c=4),
                       woT[dg * 512:(dg + 1) * 512, osp * 512:(osp + 1) * 512]
                       .rearrange("(c p) o -> p c o", p=128), f"wl{db}")
            p.wait("scalar", xu)
            xl = p.op("scalar", lambda e, dg=dg: e.dma_start(
                out=xw.rearrange("p (c m) -> p c m", c=4),
                in_=ygTd[dg * 4:(dg + 1) * 4].transpose([1, 0, 2])), inc="xl", k=16)
            p.wait("gpsimd", wl)
            p.wait("gpsimd", wu[db])
            wc = p.op("gpsimd", lambda e, db=db: e.tensor_copy(
                out=wbs[:, db * 2048:(db + 1) * 2048],
                in_=wfs[:, db * 2048:(db + 1) * 2048]), inc="pool")
            p.wait("tensor", wc)
            p.wait("tensor", xl)
            if dg < 2 and ev_tok is not None:
                p.wait("tensor", ev_tok)
            for dc in range(4):
                for tt in range(8):
                    mmt = p.op("tensor", lambda e, tt=tt, dc=dc, db=db, dg=dg: e.matmul(
                        out=ps[:, tt * 512:(tt + 1) * 512],
                        lhsT=xw[:, dc * 1024 + tt * 128: dc * 1024 + (tt + 1) * 128],
                        rhs=wbs[:, db * 2048 + dc * 512: db * 2048 + (dc + 1) * 512],
                        start=(dg == 0 and dc == 0), stop=(dg == 7 and dc == 3)),
                        inc="pe" if (tt == 7 and dc == 3) else None)
            wu[db] = mmt
            xu = mmt
        for tt in range(8):
            p.wait("vector", p.tok("pe"))
            p.wait("vector", p.tok("lz0"))
            p.wait("vector", p.tok("lz1"))
            ev_tok = p.op("vector", lambda e, tt=tt, osp=osp: e.tensor_tensor(
                out=z[:, tt * DM + osp * 512: tt * DM + (osp + 1) * 512],
                in0=ps[:, tt * 512:(tt + 1) * 512],
                in1=z[:, tt * DM + osp * 512: tt * DM + (osp + 1) * 512],
                op=ALU.add), inc="dv")
    x3b = scr.bitcast(BF16)[:, 0:DM]
    aux_ev = None
    stg_user = None
    for tt in range(8):
        p.wait("vector", p.tok("dv"))
        p.wait("scalar", stg_user)
        sttok = ln_stats(tt, z[:, tt * DM:(tt + 1) * DM])
        p.wait("scalar", sttok)
        p.wait("scalar", p.tok("pool"))
        x3t = p.op("scalar", lambda e, tt=tt: e.activation(
            out=x3b, in_=z[:, tt * DM:(tt + 1) * DM], func=AF.Identity,
            bias=stat[:, NEGM + tt:NEGM + tt + 1],
            scale=stat[:, RSTD + tt:RSTD + tt + 1]), inc="ac")
        p.wait("tensor", x3t)
        if stg_user is not None:
            p.wait("vector", stg_user)
        for c in range(32):
            trt = p.op("tensor", lambda e, c=c: e.transpose(
                out=ps_trv, in_=x3b[:, c * 128:(c + 1) * 128], identity=identb[:]),
                inc="pe")
            p.wait("vector", trt)
            cpt = p.op("vector", lambda e, c=c: e.tensor_copy(
                out=stg[:, c * 128:(c + 1) * 128], in_=ps_trv), inc="dv")
            p.wait("tensor", cpt)
        p.wait("sync", p.tok("dv"))
        stg_user = p.dma("sync", xT3d[:, :, tt * 128:(tt + 1) * 128].transpose([1, 0, 2]),
                         stg.rearrange("p (c i) -> p c i", c=32), f"xt3{tt % 2}")
        p.wait("tensor", aux_ev)
        for d in range(32):
            amt = p.op("tensor", lambda e, d=d: e.matmul(
                out=ps_aux, lhsT=stg[:, d * 128:(d + 1) * 128],
                rhs=artb[:, d * 80:(d + 1) * 80],
                start=(d == 0), stop=(d == 31)), inc="pe" if d == 31 else None)
        p.wait("scalar", amt)
        a1 = p.op("scalar", lambda e: e.copy(out=rh_f[:], in_=ps_aux[:, 0:64]), inc="ac")
        aux_ev = p.op("scalar", lambda e: e.copy(out=lg[:], in_=ps_aux[:, 64:80]), inc="ac")
        p.wait("vector", aux_ev)
        q1 = p.op("vector", lambda e: e.reduce_max(out=ccol[:, 0:1], in_=lg[:],
                                                   axis=AX.X, negate=True), inc="dv")
        p.wait("scalar", q1)
        q2 = p.op("scalar", lambda e: e.activation(out=pex[:], in_=lg[:], func=AF.Exp,
                                                   bias=ccol[:, 0:1], scale=1.0,
                                                   accum_out=ccol[:, 1:2]), inc="ac")
        p.wait("vector", q2)
        q3 = p.op("vector", lambda e: e.tensor_scalar(out=ccol[:, 2:3], in0=ccol[:, 0:1],
                                                      scalar1=-1.0, scalar2=None,
                                                      op0=ALU.mult), inc="dv")
        p.wait("scalar", q3)
        q4 = p.op("scalar", lambda e: e.activation(out=s1[:], in_=lg[:], func=AF.Sign,
                                                   bias=ccol[:, 2:3], scale=-1.0), inc="ac")
        p.wait("vector", q4)
        q5 = p.op("vector", lambda e: e.tensor_tensor(
            out=mskd[:], in0=lg[:], in1=s1[:], op=ALU.mult), inc="dv")
        q5b = p.op("vector", lambda e: e.tensor_scalar(
            out=s1[:], in0=s1[:], scalar1=-1.0, scalar2=64.0,
            op0=ALU.add, op1=ALU.mult), inc="dv")
        p.wait("vector", q5b)
        q6 = p.op("vector", lambda e: e.tensor_tensor(out=mskd[:], in0=mskd[:],
                                                      in1=s1[:], op=ALU.add), inc="dv")
        p.wait("vector", q6)
        q7 = p.op("vector", lambda e: e.reduce_max(out=ccol[:, 3:4], in_=mskd[:],
                                                   axis=AX.X), inc="dv")
        p.wait("scalar", q7)
        q8 = p.op("scalar", lambda e: e.activation(out=s1[:], in_=lg[:], func=AF.Sign,
                                                   bias=ccol[:, 3:4], scale=-1.0), inc="ac")
        p.wait("vector", q8)
        q9 = p.op("vector", lambda e: e.tensor_scalar(out=s1[:], in0=s1[:],
                                                      scalar1=-1.0, scalar2=1.0,
                                                      op0=ALU.mult, op1=ALU.add), inc="dv")
        p.wait("vector", q9)
        q10 = p.op("vector", lambda e: e.tensor_scalar(out=s1[:], in0=s1[:],
                                                       scalar1=1.0, scalar2=None,
                                                       op0=ALU.min), inc="dv")
        p.wait("vector", q10)
        q11 = p.op("vector", lambda e: e.tensor_tensor(out=pex[:], in0=pex[:],
                                                       in1=s1[:], op=ALU.mult), inc="dv")
        p.wait("vector", q11)
        q12 = p.op("vector", lambda e: e.reduce_sum(out=ccol[:, 4:5], in_=pex[:],
                                                    axis=AX.X), inc="dv")
        p.wait("vector", q12)
        q13 = p.op("vector", lambda e: e.scalar_tensor_tensor(
            out=ccol[:, 5:6], in0=ccol[:, 1:2], scalar=1e-6, in1=ccol[:, 4:5],
            op0=ALU.mult, op1=ALU.add), inc="dv")
        p.wait("vector", q13)
        q14 = p.op("vector", lambda e: e.reciprocal(out=ccol[:, 5:6], in_=ccol[:, 5:6]),
                   inc="dv")
        p.wait("vector", q14)
        q15 = p.op("vector", lambda e: e.tensor_scalar(out=ccol[:, 5:6], in0=ccol[:, 5:6],
                                                       scalar1=0.5, scalar2=None,
                                                       op0=ALU.mult), inc="dv")
        p.wait("vector", q15)
        q16 = p.op("vector", lambda e: e.tensor_scalar(out=comb[:], in0=pex[:],
                                                       scalar1=ccol[:, 5:6], scalar2=None,
                                                       op0=ALU.mult), inc="dv")
        p.wait("vector", a1)
        p.wait("vector", q16)
        for ee in range(16):
            he = p.op("vector", lambda e, ee=ee: e.tensor_scalar(
                out=het[:], in0=rh_f[:], scalar1=comb[:, ee:ee + 1], scalar2=None,
                op0=ALU.mult), inc="dv")
            p.wait("tensor", he)
            trt = p.op("tensor", lambda e: e.transpose(
                out=ps_trv[0:64, :], in_=het[:], identity=identb[:]), inc="pe")
            p.wait("scalar", trt)
            hct = p.op("scalar", lambda e, ee=ee, tt=tt: e.copy(
                out=hgT[(ee % 2) * 64:(ee % 2) * 64 + 64,
                        (ee // 2) * 1024 + tt * 128: (ee // 2) * 1024 + (tt + 1) * 128],
                in_=ps_trv[0:64, :]), inc="ac")
            p.wait("vector", hct)
            p.wait("tensor", hct)
    p.wait("sync", p.tok("xt30"))
    p.wait("sync", p.tok("xt31"))
    p.wait("scalar", p.tok("xt30"))
    p.wait("scalar", p.tok("xt31"))
    wu = [None, None]
    ev_tok = None
    bp_user = None
    xu = None
    prev_bmt = None
    for osp in range(8):
        p.wait("sync", p.tok("dv"))
        brl = p.dma("sync", scr[0:1, 4608:4608 + 512],
                    b1[0:1, osp * 512:(osp + 1) * 512], f"brl{osp % 2}")
        p.wait("scalar", brl)
        p.wait("scalar", prev_bmt)
        brc = p.op("scalar", lambda e: e.copy(out=brb[:], in_=scr[0:1, 4608:4608 + 512]),
                   inc="ac")
        for dg in range(8):
            db = dg % 2
            if wu[db] is not None:
                p.wait("sync", wu[db])
            wl = p.dma("sync", wfs[:, db * 2048:(db + 1) * 2048]
                       .rearrange("p (c o) -> p c o", c=4),
                       w1T[dg * 512:(dg + 1) * 512, osp * 512:(osp + 1) * 512]
                       .rearrange("(c p) o -> p c o", p=128), f"w2{db}")
            p.wait("scalar", xu)
            xl = p.op("scalar", lambda e, dg=dg: e.dma_start(
                out=xw.rearrange("p (c m) -> p c m", c=4),
                in_=xT3d[dg * 4:(dg + 1) * 4].transpose([1, 0, 2])), inc="x2l", k=16)
            p.wait("gpsimd", wl)
            p.wait("gpsimd", wu[db])
            wc = p.op("gpsimd", lambda e, db=db: e.tensor_copy(
                out=wbs[:, db * 2048:(db + 1) * 2048],
                in_=wfs[:, db * 2048:(db + 1) * 2048]), inc="pool")
            p.wait("tensor", wc)
            p.wait("tensor", xl)
            if dg < 2 and ev_tok is not None:
                p.wait("tensor", ev_tok)
            for dc in range(4):
                for tt in range(8):
                    mmt = p.op("tensor", lambda e, tt=tt, dc=dc, db=db, dg=dg: e.matmul(
                        out=ps[:, tt * 512:(tt + 1) * 512],
                        lhsT=xw[:, dc * 1024 + tt * 128: dc * 1024 + (tt + 1) * 128],
                        rhs=wbs[:, db * 2048 + dc * 512: db * 2048 + (dc + 1) * 512],
                        start=(dg == 0 and dc == 0), stop=False),
                        inc="pe" if (tt == 7 and dc == 3) else None)
            wu[db] = mmt
            xu = mmt
        p.wait("sync", p.tok("pe"))
        for fp in range(4):
            for ff in range(2):
                f = fp * 2 + ff
                blt = p.dma("sync", scr[:, ff * 512:(ff + 1) * 512],
                            bpT[f][:, osp * 512:(osp + 1) * 512], f"lb{f % 2}")
                p.wait("gpsimd", blt)
                bctk = p.op("gpsimd", lambda e, ff=ff: e.tensor_copy(
                    out=bp_b[:, ff * 512:(ff + 1) * 512],
                    in_=scr[:, ff * 512:(ff + 1) * 512]), inc="pool")
            p.wait("tensor", bctk)
            for tt in range(8):
                dmm = p.op("tensor", lambda e, tt=tt, fp=fp: e.matmul(
                    out=ps[:, tt * 512:(tt + 1) * 512],
                    lhsT=hgT[:, (fp * 2) * 1024 + tt * 128: (fp * 2) * 1024 + (tt + 1) * 128],
                    rhs=bp_b[:, 0:512],
                    start=False, stop=False), inc=None)
                dmm = p.op("tensor", lambda e, tt=tt, fp=fp: e.matmul(
                    out=ps[:, tt * 512:(tt + 1) * 512],
                    lhsT=hgT[:, (fp * 2 + 1) * 1024 + tt * 128: (fp * 2 + 1) * 1024 + (tt + 1) * 128],
                    rhs=bp_b[:, 512:1024],
                    start=False, stop=False),
                    inc="pe" if tt == 7 else None)
            p.wait("sync", dmm)
        for tt in range(8):
            if tt == 0:
                p.wait("tensor", brc)
            bmt = p.op("tensor", lambda e, tt=tt: e.matmul(
                out=ps[:, tt * 512:(tt + 1) * 512],
                lhsT=ones1b[:], rhs=brb[:],
                start=False, stop=True), inc="pe")
            prev_bmt = bmt
            p.wait("scalar", bmt)
            x3o = p.op("scalar", lambda e, tt=tt, osp=osp: e.activation(
                out=scr[:, 4096:4608], in_=z[:, tt * DM + osp * 512: tt * DM + (osp + 1) * 512],
                func=AF.Identity, bias=stat[:, NEGM + tt:NEGM + tt + 1],
                scale=stat[:, RSTD + tt:RSTD + tt + 1]), inc="ac")
            p.wait("vector", x3o)
            ev_tok = p.op("vector", lambda e, tt=tt, osp=osp: e.tensor_tensor(
                out=z[:, tt * DM + osp * 512: tt * DM + (osp + 1) * 512],
                in0=ps[:, tt * 512:(tt + 1) * 512], in1=scr[:, 4096:4608],
                op=ALU.add), inc="dv")
            p.wait("scalar", ev_tok)
    out_f = scr[:, 0:DM]
    ot_prev = None
    for tt in range(8):
        p.wait("vector", p.tok("dv"))
        p.wait("scalar", p.tok("dv"))
        p.wait("scalar", p.tok("pool"))
        sttok = ln_stats(tt, z[:, tt * DM:(tt + 1) * DM])
        p.wait("scalar", sttok)
        if ot_prev is not None:
            p.wait("scalar", ot_prev)
        oft = p.op("scalar", lambda e, tt=tt: e.activation(
            out=out_f, in_=z[:, tt * DM:(tt + 1) * DM], func=AF.Identity,
            bias=stat[:, NEGM + tt:NEGM + tt + 1],
            scale=stat[:, RSTD + tt:RSTD + tt + 1]), inc="ac")
        p.wait("sync", oft)
        ot_prev = p.dma("sync", x4o[tt * 128:(tt + 1) * 128, :], out_f, f"ox{tt % 2}")
    p.wait("sync", p.tok("ox0"))
    p.wait("sync", p.tok("ox1"))
    if ctx is None:
        p.emit()
        return nc


def stage_D(inputs, x2_full, yg_full):
    ident = np.eye(128, dtype=BF)
    woT = np.ascontiguousarray(np.asarray(inputs["L1_attn_Wo"]).T)
    w1T = np.ascontiguousarray(np.asarray(inputs["L1_ffn_base_w"]).T)
    A = np.asarray(inputs["L1_ffn_A"])
    rw = np.asarray(inputs["L1_ffn_router_w"])
    art = np.ascontiguousarray(np.concatenate([A.T, rw.T], axis=1))
    Bm = np.asarray(inputs["L1_ffn_B"])
    bpT = np.ascontiguousarray(
        np.stack([np.concatenate([Bm[2 * f].T, Bm[2 * f + 1].T], axis=0)
                  for f in range(8)]))
    b1 = np.ascontiguousarray(np.asarray(inputs["L1_ffn_base_b"])[None, :])
    in_maps = []
    for c in range(NCORE):
        b, s = c // 2, c % 2
        tsl = slice(s * 1024, (s + 1) * 1024)
        in_maps.append({
            "x2": np.ascontiguousarray(x2_full[b][tsl]),
            "yg": np.ascontiguousarray(yg_full[b][tsl]),
            "woT": woT, "w1T": w1T, "b1": b1, "art": art, "bpT": bpT,
            "ident": ident,
        })
    return in_maps




MAGIC = [0x5A5A0001, 0x5A5A0002, 0x5A5A0003]


def build_fused():
    nc = bass.Bass("TRN2", target_bir_lowering=False, num_devices=NCORE,
                   detect_race_conditions=False)
    p = Prog(nc)
    EI = lambda n, s, d: nc.dram_tensor(n, s, d, kind="ExternalInput").ap()
    x0 = EI("x0", [L, DM], F32)
    x0own = EI("x0own", [TPC, DM], F32)
    wT = {k: EI(f"w{k}T", [DM, 2048], F32) for k in "qkv"}
    bias = {k: EI(f"b{k}", [2048], F32) for k in "qkv"}
    bwT = EI("bwT", [3, 16], F32)
    bivmat = EI("bivmat", [3, 64], F32)
    revtile = EI("revtile", [16, 64], F32)
    scdelta = EI("scdelta", [16, 64], F32)
    ident = EI("ident", [128, 128], BF16)
    identf = EI("identf", [128, 128], F32)
    trif = EI("trif", [128, 128], F32)
    w0T = EI("w0T", [DM, DM], F32)
    bias0 = EI("bias0", [1, DM], F32)
    a0T = EI("a0T", [DM, 256], F32)
    b0cat = EI("b0cat", [256, DM], F32)
    coefr = EI("coefr", [256], F32)
    sclr = EI("sclr", [256], F32)
    wq1T = EI("wq1T", [DM, 2048], F32)
    wk1T = EI("wk1T", [DM, 512], F32)
    wv1T = EI("wv1T", [DM, 512], F32)
    wg1T = EI("wg1T", [DM, 2048], F32)
    win1T = EI("win1T", [DM, 16], F32)
    dtb = EI("dtb", [1, 16], F32)
    alog = EI("alog", [1, 16], F32)
    gnw = EI("gnw", [1, 128], F32)
    cosT = EI("cosT", [128, L], F32)
    sinT = EI("sinT", [128, L], F32)
    woT = EI("woT", [DM, DM], F32)
    w1T = EI("w1T", [DM, DM], F32)
    b1 = EI("b1", [1, DM], F32)
    art = EI("art", [DM, 80], F32)
    bpT = EI("bpT", [8, 128, DM], F32)
    svin = EI("sv", [1, 1], U32)
    x4o = nc.dram_tensor("x4o", [TPC, DM], F32, kind="ExternalOutput").ap()
    sh_attn = nc.dram_tensor("sh_attn", [2, 2, 1024, 2048], BF16,
                             kind="Internal", addr_space="Shared").ap()
    sh_x2 = nc.dram_tensor("sh_x2", [2, 1024, DM], F32,
                           kind="Internal", addr_space="Shared").ap()
    sh_yg = nc.dram_tensor("sh_yg", [2, 2, 1024, 2048], BF16,
                           kind="Internal", addr_space="Shared").ap()
    sh_flag = nc.dram_tensor("sh_flag", [3, 2, 1, 1], U32,
                             kind="Internal", addr_space="Shared").ap()
    PS = nc.alloc_psum_tensor("PS", [128, 4096], F32)

    sv_sb = nc.alloc_sbuf_tensor("sv_sb", [1, 1], U32)
    magic_sb = nc.alloc_sbuf_tensor("magic_sb", [1, 4], U32)
    flag_rx = nc.alloc_sbuf_tensor("flag_rx", [1, 4], U32)
    ARENA0 = (nc.sbuf_base + 512 + 31) // 32 * 32

    def mkal(prefix):
        off = [ARENA0]
        def al(name, shape, dtype):
            szmap = {F32: 4, BF16: 2, U32: 4}
            per = int(np.prod(shape[1:])) * szmap[dtype]
            t = nc.alloc_sbuf_tensor_at(f"{prefix}_{name}", shape, dtype, offset=off[0])
            off[0] += (per + 31) // 32 * 32
            assert off[0] <= nc.sbuf_top, f"{prefix}_{name} arena overflow {off[0]}"
            return t
        return al

    svh = {}
    lvt = p.dma("sync", sv_sb[:], svin, "lsv")
    p.wait("sync", lvt)
    p.raw("sync", lambda e: svh.__setitem__("v", e.value_load(sv_sb[0:1, 0:1])))
    for k in range(3):
        p.op("vector", lambda e, k=k: e.memset(magic_sb[0:1, k:k + 1], MAGIC[k]),
             inc="mgk")
    attn_loc = nc.dram_tensor("attn_loc", [L, 2048], BF16, kind="Internal").ap()
    attn_in = nc.dram_tensor("attn_in", [TPC, DM], BF16, kind="Internal").ap()
    x2_loc = nc.dram_tensor("x2_loc", [TPC, DM], F32, kind="Internal").ap()
    ygo_loc = nc.dram_tensor("ygo_loc", [L, 2048], BF16, kind="Internal").ap()
    yg_loc = nc.dram_tensor("yg_loc", [TPC, DM], BF16, kind="Internal").ap()

    def set_flag(k):
        p.wait("sync", p.tok("mgk"))
        p.dma("sync", lambda k=k: sh_flag[k][svh["v"]], magic_sb[0:1, k:k + 1],
              f"fw{k}")
        p.wait("sync", p.tok(f"fw{k}"))

    def poll_flag(k):
        psem = p.sem(f"poll{k}")
        def go(e, k=k, psem=psem):
            with e.register(f"pf{k}") as f, e.register(f"pc{k}") as cnt, \
                 e.register(f"pn{k}") as nd:
                e.reg_mov(f, 0)
                e.reg_mov(cnt, 0)
                e.reg_mov(nd, 1)
                with e.While(nd):
                    e.dma_start(out=flag_rx[0:1, k:k + 1],
                                in_=sh_flag[k][1 - svh["v"]]).then_inc(psem, 16)
                    e.reg_add(cnt, cnt, 16)
                    e.wait_ge(psem, cnt)
                    e.load(f, flag_rx[0:1, k:k + 1])
                    e.reg_alu(nd, f, MAGIC[k], ALU.not_equal)
        p.raw("sync", go)

    ctxA = dict(nc=nc, p=p, al=mkal("A"), PS=PS, x0=x0, wT=wT, bias=bias,
                bwT=bwT, bivmat=bivmat, revtile=revtile, scdelta=scdelta,
                ident=ident,
                attn_dst=lambda qt, g: (lambda qt=qt, g=g: attn_loc
                                        [qt * 128:(qt + 1) * 128,
                                         g * 512:(g + 1) * 512]))
    build_launch_A(ctxA)
    pub0 = p.dma("sync", lambda: sh_attn[svh["v"]],
                 attn_loc.rearrange("(a b) c -> a b c", b=1024), "pub0")
    p.wait("sync", pub0)
    set_flag(0)
    p.barrier("AB")
    poll_flag(0)
    g0 = p.dma("sync", attn_in[:, 0:2048], lambda: sh_attn[0][svh["v"]], "gat0")
    g1 = p.dma("sync", attn_in[:, 2048:4096], lambda: sh_attn[1][svh["v"]], "gat1")
    p.wait("sync", g0)
    p.wait("sync", g1)
    ctxB = dict(nc=nc, p=p, al=mkal("B"), PS=PS, x0own=x0own, w0T=w0T,
                bias0=bias0, a0T=a0T, b0cat=b0cat, coefr=coefr, sclr=sclr,
                ident=ident,
                attn_srcs=lambda tt: [(slice(0, DM),
                                       lambda tt=tt: attn_in[tt * 128:(tt + 1) * 128, :])],
                x2_dst=lambda tt: (lambda tt=tt: x2_loc[tt * 128:(tt + 1) * 128, :]))
    build_launch_B(ctxB)
    pub1 = p.dma("sync", lambda: sh_x2[svh["v"]], x2_loc, "pub1")
    p.wait("sync", pub1)
    set_flag(1)
    p.barrier("BC")
    poll_flag(1)
    ctxC = dict(nc=nc, p=p, al=mkal("C"), PS=PS,
                x2f=sh_x2.rearrange("a b c -> (a b) c"),
                wq1T=wq1T, wk1T=wk1T, wv1T=wv1T, wg1T=wg1T, win1T=win1T,
                dtb=dtb, alog=alog, gnw=gnw, cosT=cosT, sinT=sinT,
                ident=ident, identf=identf, trif=trif,
                yg_dst=lambda n, kv: (lambda n=n, kv=kv: ygo_loc
                                      [n * 128:(n + 1) * 128,
                                       kv * 512:(kv + 1) * 512]))
    build_launch_C(ctxC)
    pub2 = p.dma("sync", lambda: sh_yg[svh["v"]],
                 ygo_loc.rearrange("(a b) c -> a b c", b=1024), "pub2")
    p.wait("sync", pub2)
    set_flag(2)
    p.barrier("CD")
    poll_flag(2)
    g2 = p.dma("sync", yg_loc[:, 0:2048], lambda: sh_yg[0][svh["v"]], "gyg0")
    g3 = p.dma("sync", yg_loc[:, 2048:4096], lambda: sh_yg[1][svh["v"]], "gyg1")
    p.wait("sync", g2)
    p.wait("sync", g3)
    ctxD = dict(nc=nc, p=p, al=mkal("D"), PS=PS, woT=woT, w1T=w1T, b1=b1,
                art=art, bpT=bpT, ident=ident, x4o=x4o,
                x2_src=lambda tt: (lambda tt=tt: x2_loc[tt * 128:(tt + 1) * 128, :]),
                yg_srcs=lambda tt: [(slice(0, DM),
                                     lambda tt=tt: yg_loc[tt * 128:(tt + 1) * 128, :])])
    build_launch_D(ctxD)
    p.emit()
    return nc


def stage_fused(inputs):
    mA = stage_A(inputs)
    dummy_attn = np.zeros((B, L, DM), BF)
    dummy_x2 = np.zeros((B, L, DM), np.float32)
    dummy_yg = np.zeros((B, L, DM), BF)
    mB = stage_B(inputs, dummy_attn, inputs["L0_attn_x"])
    mC = stage_C(inputs, dummy_x2)
    mD = stage_D(inputs, dummy_x2, dummy_yg)
    in_maps = []
    for c in range(NCORE):
        m = {}
        m.update({k: v for k, v in mA[c].items()})
        m["x0own"] = mB[c]["x0"]
        for k in ["w0T", "bias0", "a0T", "b0cat", "coefr", "sclr"]:
            m[k] = mB[c][k]
        for k_new, k_old in [("wq1T", "wqT"), ("wk1T", "wkT"), ("wv1T", "wvT"),
                             ("wg1T", "wgT"), ("win1T", "winT")]:
            m[k_new] = mC[c][k_old]
        for k in ["dtb", "alog", "gnw", "cosT", "sinT", "identf", "trif"]:
            m[k] = mC[c][k]
        for k in ["woT", "w1T", "b1", "art", "bpT"]:
            m[k] = mD[c][k]
        m["sv"] = np.array([[c % 2]], dtype=np.uint32)
        in_maps.append(m)
    return in_maps


def kernel_fused(**inputs):
    inputs = {k: np.asarray(v) for k, v in inputs.items()}
    nc = _get_nc("F", build_fused)
    res = run_bass_kernel_spmd(nc, stage_fused(inputs), core_ids=list(range(NCORE)))
    out = np.concatenate(
        [np.concatenate([np.asarray(res.results[2 * b]["x4o"]),
                         np.asarray(res.results[2 * b + 1]["x4o"])], axis=0)[None]
         for b in range(B)], axis=0)
    return out


_NC_CACHE = {}


def _get_nc(name, builder):
    if name not in _NC_CACHE:
        _NC_CACHE[name] = builder()
    return _NC_CACHE[name]


def kernel(**inputs):
    return kernel_fused(**inputs)


def kernel_unfused(**inputs):
    inputs = {k: np.asarray(v) for k, v in inputs.items()}

    ncA = _get_nc("A", build_launch_A)
    resA = run_bass_kernel_spmd(ncA, stage_A(inputs), core_ids=list(range(NCORE)))
    attn_full = np.concatenate(
        [np.concatenate([np.asarray(resA.results[2 * b]["attn_out"]),
                         np.asarray(resA.results[2 * b + 1]["attn_out"])], axis=1)[None]
         for b in range(B)], axis=0)

    ncB = _get_nc("B", build_launch_B)
    resB = run_bass_kernel_spmd(ncB, stage_B(inputs, attn_full, inputs["L0_attn_x"]),
                                core_ids=list(range(NCORE)))
    x2_full = np.concatenate(
        [np.concatenate([np.asarray(resB.results[2 * b]["x2o"]),
                         np.asarray(resB.results[2 * b + 1]["x2o"])], axis=0)[None]
         for b in range(B)], axis=0)

    ncC = _get_nc("C", build_launch_C)
    resC = run_bass_kernel_spmd(ncC, stage_C(inputs, x2_full), core_ids=list(range(NCORE)))
    yg_full = np.concatenate(
        [np.concatenate([np.asarray(resC.results[2 * b]["ygo"]),
                         np.asarray(resC.results[2 * b + 1]["ygo"])], axis=1)[None]
         for b in range(B)], axis=0)

    ncD = _get_nc("D", build_launch_D)
    resD = run_bass_kernel_spmd(ncD, stage_D(inputs, x2_full, yg_full),
                                core_ids=list(range(NCORE)))
    out = np.concatenate(
        [np.concatenate([np.asarray(resD.results[2 * b]["x4o"]),
                         np.asarray(resD.results[2 * b + 1]["x4o"])], axis=0)[None]
         for b in range(B)], axis=0)
    return out


# version 05
# speedup vs baseline: 1.0397x; 1.0397x over previous
import math
import numpy as np
import ml_dtypes
import concourse.bass as bass
import concourse.mybir as mybir
from concourse.bass_utils import run_bass_kernel_spmd

F32 = mybir.dt.float32
BF16 = mybir.dt.bfloat16
U32 = mybir.dt.uint32
AF = mybir.ActivationFunctionType
ALU = mybir.AluOpType
AX = mybir.AxisListType

DM = 4096
B = 4
L = 2048
NCORE = 8
TPC = 1024
NH0, CD0, NB0 = 16, 32, 8
H1, KVH1, D1 = 32, 8, 128
CHUNK = 128
LN_EPS = 1e-5
RMS_EPS = 1e-5
BF = ml_dtypes.bfloat16
ENGS = ("tensor", "vector", "scalar", "gpsimd", "sync")


class Prog:

    def __init__(self, nc):
        self.nc = nc
        self.qs = {e: [] for e in ENGS}
        self.cnt = {}
        self.handles = {}
        self.lastwait = {e: {} for e in ENGS}

    def sem(self, name):
        if name not in self.handles:
            self.handles[name] = self.nc.alloc_semaphore(name)
            self.cnt[name] = 0
        return self.handles[name]

    def op(self, eng, fn, inc=None, k=1):
        if inc is not None:
            h = self.sem(inc)
            self.cnt[inc] += k
            val = self.cnt[inc]
            def run(e, fn=fn, h=h, k=k):
                fn(e).then_inc(h, k)
            self.qs[eng].append(run)
            return (inc, val)
        self.qs[eng].append(fn)
        return None

    def raw(self, eng, fn):
        self.qs[eng].append(fn)

    def barrier(self, tag=""):
        h = self.sem("bar")
        for e in ENGS:
            self.qs[e].append(lambda eng, h=h: eng.nop().then_inc(h, 1))
        self.cnt["bar"] += len(ENGS)
        tgt = self.cnt["bar"]
        for e in ENGS:
            self.qs[e].append(lambda eng, h=h, v=tgt: eng.wait_ge(h, v))
            self.lastwait[e]["bar"] = tgt

    def wait(self, eng, tok):
        if tok is None:
            return
        name, val = tok
        if val <= 0:
            return
        lw = self.lastwait[eng]
        if lw.get(name, -1) >= val:
            return
        lw[name] = val
        h = self.sem(name)
        self.qs[eng].append(lambda e, h=h, v=val: e.wait_ge(h, v))

    def tok(self, name):
        return (name, self.cnt.get(name, 0))

    def dma(self, eng, out, in_, inc, **kw):
        def run(e, o=out, i=in_, kw=kw):
            return e.dma_start(out=o() if callable(o) else o,
                               in_=i() if callable(i) else i, **kw)
        return self.op(eng, run, inc=inc, k=16)

    def emit(self):
        with self.nc.Block() as blk:
            for e in ENGS:
                q = self.qs[e]
                getattr(blk, e)(lambda eng, q=q: [f(eng) for f in q])


def _build_tables():
    dim, n = 3, 8
    masks = sorted(range(n), key=lambda m: (bin(m).count('1'), m))
    inv = {m: i for i, m in enumerate(masks)}
    C = np.zeros((n, n, n), dtype=np.float32)
    for i, a in enumerate(masks):
        for j, b_ in enumerate(masks):
            sw, at = (0, a >> 1)
            while at:
                sw += bin(at & b_).count('1')
                at >>= 1
            s = -1.0 if sw & 1 else 1.0
            C[i, j, inv[a ^ b_]] = s
    grades = np.array([bin(m).count('1') for m in masks])
    rev = np.where(grades * (grades - 1) // 2 % 2 == 1, -1.0, 1.0).astype(np.float32)
    return C, rev


def host_consts_L0():
    C, rev = _build_tables()
    biv = C[:, :, [4, 5, 6]]
    bivmat = np.ascontiguousarray(biv.transpose(2, 0, 1).reshape(3, 64)).astype(np.float32)
    revrep = np.repeat(rev, 8)
    revtile = np.tile(revrep[None, :], (16, 1)).astype(np.float32)
    scale = 1.0 / math.sqrt(CD0 * NB0)
    scdelta = np.tile((np.eye(8, dtype=np.float32).reshape(64) * revrep * scale)[None, :],
                      (16, 1)).astype(np.float32)
    return bivmat, revtile, scdelta


def host_rope_tables(positions):
    D = D1
    inv_freq = 1.0 / (10000.0 ** (np.arange(0, D, 2, dtype=np.float64) / D))
    ang = positions[:, None].astype(np.float64) * inv_freq[None, :]
    cos = np.concatenate([np.cos(ang), np.cos(ang)], axis=-1)
    sin = np.concatenate([np.sin(ang), np.sin(ang)], axis=-1)
    return (np.ascontiguousarray(cos.T).astype(np.float32),
            np.ascontiguousarray(sin.T).astype(np.float32))


def build_launch_A(ctx=None):
    if ctx is None:
        nc = bass.Bass("TRN2", target_bir_lowering=False, num_devices=NCORE)
        p = Prog(nc)
        x0 = nc.dram_tensor("x0", [L, DM], F32, kind="ExternalInput").ap()
        wT = {k: nc.dram_tensor(f"w{k}T", [DM, 2048], F32, kind="ExternalInput").ap()
              for k in "qkv"}
        bias = {k: nc.dram_tensor(f"b{k}", [2048], F32, kind="ExternalInput").ap()
                for k in "qkv"}
        bwT = nc.dram_tensor("bwT", [3, 16], F32, kind="ExternalInput").ap()
        bivmat = nc.dram_tensor("bivmat", [3, 64], F32, kind="ExternalInput").ap()
        revtile = nc.dram_tensor("revtile", [16, 64], F32, kind="ExternalInput").ap()
        scdelta = nc.dram_tensor("scdelta", [16, 64], F32, kind="ExternalInput").ap()
        ident = nc.dram_tensor("ident", [128, 128], BF16, kind="ExternalInput").ap()
        attn_out = nc.dram_tensor("attn_out", [L, 2048], BF16, kind="ExternalOutput").ap()
        attn_dst = lambda qt, g: (lambda: attn_out[qt * 128:(qt + 1) * 128,
                                                   g * 512:(g + 1) * 512])
    else:
        nc, p = ctx["nc"], ctx["p"]
        x0, wT, bias = ctx["x0"], ctx["wT"], ctx["bias"]
        bwT, bivmat, revtile, scdelta, ident = (ctx["bwT"], ctx["bivmat"],
                                                ctx["revtile"], ctx["scdelta"],
                                                ctx["ident"])
        attn_dst = ctx["attn_dst"]
    xT0d = nc.dram_tensor("xT0d", [32, 128, L], BF16, kind="Internal").ap()
    a_scr = nc.dram_tensor("a_scr", [16, 8, 8], BF16, kind="Internal").ap()

    al = ctx["al"] if ctx else nc.alloc_sbuf_tensor
    identb = al("identb", [128, 128], BF16)
    T_sb = al("T_sb", [128, 8 * 128], BF16)
    bcol = al("bcol", [128, 3 * 16], F32)
    xTh = al("xTh", [128, 32 * 1024], BF16)
    wst_f = al("wst_f", [128, 2 * DM], F32)
    wst_b = al("wst_b", [128, 2 * DM], BF16)
    qmodT = al("qmodT", [128, 4 * L], BF16)
    kTg = al("kTg", [128, 4 * L], BF16)
    vtok = al("vtok", [128, 16 * 512], BF16)
    qraw = al("qraw", [128, 1024], BF16)
    vtmp = al("vtmp", [128, 1024], BF16)
    Pb = al("Pb", [128, L], BF16)
    PT4 = al("PT4", [128, 16 * 512], BF16)
    avf = al("avf", [128, 512], BF16)
    outstg = al("outstg", [128, 4 * 512], BF16)
    negmax = al("negmax", [128, 1], F32)
    zrec = al("zrec", [128, 16], F32)
    wk_f = al("wk_f", [16, 64], F32)
    wk_b = al("wk_b", [16, 64], BF16)
    bw_sb = al("bw_sb", [3, 16], F32)
    biv_sb = al("biv_sb", [3, 64], F32)
    rev_sb = al("rev_sb", [16, 64], F32)
    scd_sb = al("scd_sb", [16, 64], F32)

    if ctx is None:
        ps_big = nc.alloc_psum_tensor("ps_big", [128, 2048], F32)
        ps_sm = nc.alloc_psum_tensor("ps_sm", [128, 512], F32)
        ps_tr = nc.alloc_psum_tensor("ps_tr", [128, 128], BF16)
    else:
        PS = ctx["PS"]
        ps_big = PS[:, 0:2048]
        ps_sm = PS[:, 2048:2560]
        ps_tr = PS.bitcast(BF16)[:, 5 * 1024: 5 * 1024 + 128]

    xf_t = xTh.bitcast(F32)[:, 0:DM]
    xb_t = xTh[:, 16384 // 2: 16384 // 2 + DM]
    stg = xTh[:, 24576 // 2: 24576 // 2 + DM]

    p.dma("sync", identb[:], ident, "ld")
    p.dma("sync", bw_sb[:], bwT, "ld")
    p.dma("sync", biv_sb[:], bivmat, "ld")
    p.dma("sync", rev_sb[:], revtile, "ld")
    p.dma("sync", scd_sb[:], scdelta, "ld")
    for j, k in enumerate("qkv"):
        for c in range(16):
            p.dma("sync", bcol[:, j * 16 + c: j * 16 + c + 1],
                  bias[k][c * 128:(c + 1) * 128].unsqueeze(1), "ld")
    p.wait("tensor", p.tok("ld"))
    mm0 = p.op("tensor", lambda e: e.matmul(out=ps_sm.bitcast(F32)[0:16, 0:64] if False else ps_sm[0:16, 0:64],
                                            lhsT=bw_sb[:], rhs=biv_sb[:],
                                            start=True, stop=True), inc="pe")
    p.wait("vector", mm0)
    dvx = p.op("vector", lambda e: e.tensor_tensor(out=wk_f[:], in0=ps_sm[0:16, 0:64],
                                                   in1=rev_sb[:], op=ALU.mult), inc="dv")
    p.wait("vector", dvx)
    dv0 = p.op("vector", lambda e: e.tensor_tensor(out=wk_f[:], in0=wk_f[:],
                                                   in1=scd_sb[:], op=ALU.add), inc="dv")
    p.wait("scalar", dv0)
    ac0 = p.op("scalar", lambda e: e.copy(out=wk_b[:], in_=wk_f[:]), inc="ac")
    p.wait("sync", ac0)
    wr0 = p.dma("sync", a_scr.rearrange("h i j -> h (i j)"), wk_b[:], "ascr")
    mz = p.op("vector", lambda e: e.memset(T_sb[:], 0.0), inc="dv")
    p.wait("sync", wr0)
    p.wait("sync", mz)
    for hh in range(8):
        for d in range(16):
            p.dma("sync", T_sb[d * 8:(d + 1) * 8,
                               hh * 128 + d * 8: hh * 128 + d * 8 + 8],
                  a_scr[hh], "tld")
    t_built = p.tok("tld")

    cast_tok = None
    tr_last = None
    st_tok = None
    for tt in range(16):
        if cast_tok is not None:
            p.wait("sync", cast_tok)
        ltok = p.dma("sync", xf_t, x0[tt * 128:(tt + 1) * 128, :], "ld")
        p.wait("scalar", ltok)
        if tr_last is not None:
            p.wait("scalar", tr_last)
        cast_tok = p.op("scalar", lambda e: e.copy(out=xb_t, in_=xf_t), inc="ac")
        p.wait("tensor", cast_tok)
        if st_tok is not None:
            p.wait("vector", st_tok)
        for c in range(32):
            tr_last = p.op("tensor", lambda e, c=c: e.transpose(
                out=ps_tr[:], in_=xb_t[:, c * 128:(c + 1) * 128], identity=identb[:]),
                inc="pe")
            p.wait("vector", tr_last)
            cpt = p.op("vector", lambda e, c=c: e.tensor_copy(
                out=stg[:, c * 128:(c + 1) * 128], in_=ps_tr[:]), inc="dv")
            p.wait("tensor", cpt)
        p.wait("sync", p.tok("dv"))
        st_tok = p.dma("sync", xT0d[:, :, tt * 128:(tt + 1) * 128],
                       stg.rearrange("p (c i) -> p c i", c=32), f"xtw{tt % 2}")
    p.wait("tensor", t_built)

    wbuf_user = [None, None]
    evac_tok = None
    qm_evac = None
    xTh_user = st_tok
    ext_prev = None
    av_done = None
    store_tok = None
    for g in range(4):
        for mh in range(2):
            if xTh_user is not None:
                p.wait("sync", xTh_user)
            xld = p.dma("sync", xTh.rearrange("p (c m) -> p c m", c=32),
                        xT0d[:, :, mh * 1024:(mh + 1) * 1024], "ld")
            for oc in range(12):
                kind, loc = ("q", oc) if oc < 4 else (("k", oc - 4) if oc < 8 else ("v", oc - 8))
                gch = g * 4 + loc
                db = oc % 2
                if wbuf_user[db] is not None:
                    p.wait("sync", wbuf_user[db])
                wld = p.dma("sync",
                            wst_f[:, db * DM:(db + 1) * DM].rearrange("p (c o) -> p c o", c=32),
                            wT[kind][:, gch * 128:(gch + 1) * 128]
                            .rearrange("(c p) o -> p c o", p=128), f"wld{db}")
                ceng = "gpsimd" if db == 0 else "vector"
                p.wait(ceng, wld)
                p.wait(ceng, wbuf_user[db])
                wct = p.op(ceng, lambda e, db=db: e.tensor_copy(
                    out=wst_b[:, db * DM:(db + 1) * DM],
                    in_=wst_f[:, db * DM:(db + 1) * DM]), inc="pool")
                p.wait("tensor", wct)
                p.wait("tensor", xld)
                if evac_tok is not None:
                    p.wait("tensor", evac_tok)
                for d in range(32):
                    for half in range(2):
                        mmt = p.op("tensor", lambda e, d=d, h=half, db=db: e.matmul(
                            out=ps_big[:, h * 512:(h + 1) * 512],
                            lhsT=wst_b[:, db * DM + d * 128: db * DM + (d + 1) * 128],
                            rhs=xTh[:, d * 1024 + h * 512: d * 1024 + (h + 1) * 512],
                            start=(d == 0), stop=(d == 31)),
                            inc="pe" if (d == 31 and half == 1) else None)
                wbuf_user[db] = mmt
                xTh_user = mmt
                p.wait("scalar", mmt)
                if kind == "q":
                    evac_tok = p.op("scalar", lambda e, gch=gch: e.activation(
                        out=qraw[:], in_=ps_big[:, 0:1024], func=AF.Identity,
                        bias=bcol[:, gch: gch + 1], scale=1.0), inc="ac")
                    p.wait("tensor", evac_tok)
                    if qm_evac is not None:
                        p.wait("tensor", qm_evac)
                    lhead = g * 2 + loc // 2
                    for half in range(2):
                        qmt = p.op("tensor", lambda e, h=half, lh=lhead: e.matmul(
                            out=ps_big[:, 1024 + h * 512: 1024 + (h + 1) * 512],
                            lhsT=T_sb[:, lh * 128:(lh + 1) * 128],
                            rhs=qraw[:, h * 512:(h + 1) * 512],
                            start=True, stop=True),
                            inc="pe" if half == 1 else None)
                    p.wait("scalar", qmt)
                    qm_evac = p.op("scalar", lambda e, loc=loc, mh=mh: e.copy(
                        out=qmodT[:, loc * L + mh * 1024: loc * L + (mh + 1) * 1024],
                        in_=ps_big[:, 1024:2048]), inc="ac")
                elif kind == "k":
                    evac_tok = p.op("scalar", lambda e, gch=gch, loc=loc, mh=mh: e.activation(
                        out=kTg[:, loc * L + mh * 1024: loc * L + (mh + 1) * 1024],
                        in_=ps_big[:, 0:1024], func=AF.Identity,
                        bias=bcol[:, 16 + gch: 16 + gch + 1], scale=1.0), inc="ac")
                else:
                    p.wait("scalar", p.tok("pe"))
                    evac_tok = p.op("scalar", lambda e, gch=gch: e.activation(
                        out=vtmp[:], in_=ps_big[:, 0:1024], func=AF.Identity,
                        bias=bcol[:, 32 + gch: 32 + gch + 1], scale=1.0), inc="ac")
                    p.wait("tensor", evac_tok)
                    for j in range(8):
                        trt = p.op("tensor", lambda e, j=j: e.transpose(
                            out=ps_tr[:], in_=vtmp[:, j * 128:(j + 1) * 128],
                            identity=identb[:]), inc="pe")
                        p.wait("vector", trt)
                        cpt = p.op("vector", lambda e, j=j, mh=mh, loc=loc: e.tensor_copy(
                            out=vtok[:, (mh * 8 + j) * 512 + loc * 128:
                                        (mh * 8 + j) * 512 + (loc + 1) * 128],
                            in_=ps_tr[:]), inc="dv")
                        p.wait("tensor", cpt)

        p.wait("tensor", p.tok("ac"))
        p.wait("tensor", p.tok("dv"))
        for qb in range(4):
            for hh2 in range(2):
                c0 = hh2 * 2
                for qi in range(4):
                    qt = qb * 4 + qi
                    p.wait("tensor", p.tok("ac"))
                    for cc in range(2):
                        for ms in range(4):
                            smt = p.op("tensor", lambda e, cc=cc, ms=ms, qt=qt, c0=c0: e.matmul(
                                out=ps_big[:, ms * 512:(ms + 1) * 512],
                                lhsT=qmodT[:, (c0 + cc) * L + qt * 128:
                                              (c0 + cc) * L + (qt + 1) * 128],
                                rhs=kTg[:, (c0 + cc) * L + ms * 512:
                                           (c0 + cc) * L + (ms + 1) * 512],
                                start=(cc == 0), stop=(cc == 1)),
                                inc="pe" if (cc == 1 and ms == 3) else None)
                    p.wait("vector", smt)
                    p.wait("vector", ext_prev)
                    rmt = p.op("vector", lambda e: e.reduce_max(
                        out=negmax[:], in_=ps_big[:], axis=AX.X, negate=True), inc="dv")
                    p.wait("scalar", rmt)
                    ext_prev = p.op("scalar", lambda e, qt=qt: e.activation(
                        out=Pb[:], in_=ps_big[:], func=AF.Exp,
                        bias=negmax[:], scale=1.0,
                        accum_out=zrec[:, qt:qt + 1]), inc="ac")
                    p.wait("vector", ext_prev)
                    p.op("vector", lambda e, qt=qt: e.reciprocal(
                        out=zrec[:, qt:qt + 1], in_=zrec[:, qt:qt + 1]), inc="dv")
                    p.wait("tensor", ext_prev)
                    p.wait("scalar", av_done)
                    for mc in range(16):
                        trt = p.op("tensor", lambda e, mc=mc: e.transpose(
                            out=ps_tr[:], in_=Pb[:, mc * 128:(mc + 1) * 128],
                            identity=identb[:]), inc="pe")
                        p.wait("scalar", trt)
                        cpt = p.op("scalar", lambda e, mc=mc, qi=qi: e.copy(
                            out=PT4[:, mc * 512 + qi * 128: mc * 512 + (qi + 1) * 128],
                            in_=ps_tr[:]), inc="ac")
                        p.wait("tensor", cpt)
                p.wait("scalar", p.tok("dv"))
                for cc2 in range(2):
                    p.wait("tensor", p.tok("ac"))
                    for mc in range(16):
                        avt = p.op("tensor", lambda e, mc=mc, hh2=hh2, cc2=cc2: e.matmul(
                            out=ps_sm[:],
                            lhsT=vtok[:, mc * 512 + hh2 * 256 + cc2 * 128:
                                         mc * 512 + hh2 * 256 + (cc2 + 1) * 128],
                            rhs=PT4[:, mc * 512:(mc + 1) * 512],
                            start=(mc == 0), stop=(mc == 15)),
                            inc="pe" if mc == 15 else None)
                    av_done = avt
                    p.wait("scalar", avt)
                    aet = p.op("scalar", lambda e: e.copy(out=avf[:], in_=ps_sm[:]), inc="ac")
                    p.wait("tensor", aet)
                    for qi in range(4):
                        trt = p.op("tensor", lambda e, qi=qi: e.transpose(
                            out=ps_tr[:], in_=avf[:, qi * 128:(qi + 1) * 128],
                            identity=identb[:]), inc="pe")
                        p.wait("scalar", trt)
                        p.wait("scalar", store_tok)
                        cpt = p.op("scalar", lambda e, qi=qi, qb=qb, hh2=hh2, cc2=cc2: e.activation(
                            out=outstg[:, qi * 512 + hh2 * 256 + cc2 * 128:
                                          qi * 512 + hh2 * 256 + (cc2 + 1) * 128],
                            in_=ps_tr[:], func=AF.Identity,
                            bias=0.0, scale=zrec[:, qb * 4 + qi: qb * 4 + qi + 1]), inc="ac")
                        p.wait("tensor", cpt)
            p.wait("sync", p.tok("ac"))
            for qi in range(4):
                store_tok = p.dma("sync", attn_dst(qb * 4 + qi, g),
                                  outstg[:, qi * 512:(qi + 1) * 512], f"ost{qb % 2}")
    p.wait("sync", p.tok("ost0"))
    p.wait("sync", p.tok("ost1"))
    if ctx is None:
        p.emit()
        return nc


def stage_A(inputs):
    bivmat, revtile, scdelta = host_consts_L0()
    ident = np.eye(128, dtype=BF)
    X = np.asarray(inputs["L0_attn_x"])
    WqT = np.ascontiguousarray(np.asarray(inputs["L0_attn_Wq"]).T)
    WkT = np.ascontiguousarray(np.asarray(inputs["L0_attn_Wk"]).T)
    WvT = np.ascontiguousarray(np.asarray(inputs["L0_attn_Wv"]).T)
    bw = np.asarray(inputs["L0_attn_bias_weight"])
    in_maps = []
    for c in range(NCORE):
        b, s = c // 2, c % 2
        sl = slice(s * 2048, (s + 1) * 2048)
        bwT_c = np.zeros((3, 16), np.float32)
        bwT_c[:, 0:8] = bw[s * 8:(s + 1) * 8, :].T
        in_maps.append({
            "x0": np.ascontiguousarray(X[b]),
            "wqT": np.ascontiguousarray(WqT[:, sl]),
            "wkT": np.ascontiguousarray(WkT[:, sl]),
            "wvT": np.ascontiguousarray(WvT[:, sl]),
            "bq": np.ascontiguousarray(np.asarray(inputs["L0_attn_bq"])[sl]),
            "bk": np.ascontiguousarray(np.asarray(inputs["L0_attn_bk"])[sl]),
            "bv": np.ascontiguousarray(np.asarray(inputs["L0_attn_bv"])[sl]),
            "bwT": bwT_c,
            "bivmat": bivmat, "revtile": revtile, "scdelta": scdelta,
            "ident": ident,
        })
    return in_maps




def build_launch_B(ctx=None):
    if ctx is None:
        nc = bass.Bass("TRN2", target_bir_lowering=False, num_devices=NCORE)
        p = Prog(nc)
        x0 = nc.dram_tensor("x0", [TPC, DM], F32, kind="ExternalInput").ap()
        attn = nc.dram_tensor("attn", [TPC, DM], BF16, kind="ExternalInput").ap()
        w0T = nc.dram_tensor("w0T", [DM, DM], F32, kind="ExternalInput").ap()
        bias0 = nc.dram_tensor("bias0", [1, DM], F32, kind="ExternalInput").ap()
        a0T = nc.dram_tensor("a0T", [DM, 256], F32, kind="ExternalInput").ap()
        b0cat = nc.dram_tensor("b0cat", [256, DM], F32, kind="ExternalInput").ap()
        coefr = nc.dram_tensor("coefr", [256], F32, kind="ExternalInput").ap()
        sclr = nc.dram_tensor("sclr", [256], F32, kind="ExternalInput").ap()
        ident = nc.dram_tensor("ident", [128, 128], BF16, kind="ExternalInput").ap()
        x2o = nc.dram_tensor("x2o", [TPC, DM], F32, kind="ExternalOutput").ap()
        attn_srcs = lambda tt: [(slice(0, DM),
                                 lambda tt=tt: attn[tt * 128:(tt + 1) * 128, :])]
        x2_dst = lambda tt: (lambda tt=tt: x2o[tt * 128:(tt + 1) * 128, :])
    else:
        nc, p = ctx["nc"], ctx["p"]
        x0 = ctx["x0own"]
        w0T, bias0, a0T, b0cat, coefr, sclr, ident = (
            ctx["w0T"], ctx["bias0"], ctx["a0T"], ctx["b0cat"], ctx["coefr"],
            ctx["sclr"], ctx["ident"])
        attn_srcs = ctx["attn_srcs"]
        x2_dst = ctx["x2_dst"]
    xT1d = nc.dram_tensor("xT1d", [32, 128, TPC], BF16, kind="Internal").ap()

    al = ctx["al"] if ctx else nc.alloc_sbuf_tensor
    identb = al("identb", [128, 128], BF16)
    z = al("z", [128, 8 * DM], F32)
    scr = al("scr", [128, 1024], F32)
    a0b = al("a0b", [128, 32 * 256], BF16)
    b0s = al("b0s", [128, 2 * 512], BF16)
    xaT = al("xaT", [128, 2 * TPC], BF16)
    xw = al("xw", [128, 2 * 4096], BF16)
    stg = xw[:, 0:DM]
    wfs = al("wfs", [128, 2 * 2048], F32)
    wbs = al("wbs", [128, 2 * 2048], BF16)
    x1tmp = al("x1tmp", [128, 512], F32)
    brb = al("brb", [1, 512], BF16)
    ones1b = al("ones1b", [1, 128], BF16)
    stat = al("stat", [128, 48], F32)
    wcol = al("wcol", [128, 2], F32)
    ccol = al("ccol", [128, 2], F32)
    scol = al("scol", [128, 2], F32)

    ps = ctx["PS"] if ctx else nc.alloc_psum_tensor("ps", [128, 4096], F32)
    ps_trv = ps.bitcast(BF16)[:, 0:128]
    ps_xa = ps[:, 512:768]

    b0f = scr[:, 0:1024]
    ZS, ZSS, RSTD, NEGM = 0, 8, 16, 24
    MEAN, VAR = 32, 40

    def ln_stats(tt, ztile, pfx):
        p.wait("vector", p.tok("ac"))
        p.wait("scalar", p.tok("dv"))
        t1 = p.op("vector", lambda e: e.reduce_sum(
            out=stat[:, ZS + tt:ZS + tt + 1], in_=ztile, axis=AX.X), inc="dv")
        t2 = p.op("scalar", lambda e: e.activation(
            out=stg[:, 0:DM], in_=ztile, func=AF.Square,
            bias=0.0, scale=1.0, accum_out=stat[:, ZSS + tt:ZSS + tt + 1]), inc="ac")
        p.wait("vector", t1)
        p.wait("vector", t2)
        t3 = p.op("vector", lambda e: e.tensor_scalar(
            out=stat[:, MEAN:MEAN + 1], in0=stat[:, ZS + tt:ZS + tt + 1],
            scalar1=1.0 / DM, scalar2=None, op0=ALU.mult), inc="dv")
        p.wait("vector", t3)
        t4 = p.op("vector", lambda e: e.tensor_scalar(
            out=stat[:, VAR:VAR + 1], in0=stat[:, ZSS + tt:ZSS + tt + 1],
            scalar1=1.0 / DM, scalar2=None, op0=ALU.mult), inc="dv")
        p.wait("scalar", t3)
        t5 = p.op("scalar", lambda e: e.activation(
            out=stat[:, MEAN + 1:MEAN + 2], in_=stat[:, MEAN:MEAN + 1],
            func=AF.Square, bias=0.0, scale=1.0), inc="ac")
        p.wait("vector", t4)
        p.wait("vector", t5)
        t6 = p.op("vector", lambda e: e.tensor_tensor(
            out=stat[:, VAR:VAR + 1], in0=stat[:, VAR:VAR + 1],
            in1=stat[:, MEAN + 1:MEAN + 2], op=ALU.subtract), inc="dv")
        p.wait("vector", t6)
        t6b = p.op("vector", lambda e: e.tensor_scalar(
            out=stat[:, VAR:VAR + 1], in0=stat[:, VAR:VAR + 1],
            scalar1=LN_EPS, scalar2=None, op0=ALU.add), inc="dv")
        p.wait("scalar", t6b)
        t7 = p.op("scalar", lambda e: e.activation(
            out=stat[:, VAR + 1:VAR + 2], in_=stat[:, VAR:VAR + 1],
            func=AF.Ln, bias=0.0, scale=1.0), inc="ac")
        p.wait("scalar", t7)
        t8 = p.op("scalar", lambda e: e.activation(
            out=stat[:, RSTD + tt:RSTD + tt + 1], in_=stat[:, VAR + 1:VAR + 2],
            func=AF.Exp, bias=0.0, scale=-0.5), inc="ac")
        p.wait("vector", t8)
        p.wait("scalar", t8)
        t9 = p.op("vector", lambda e: e.tensor_tensor(
            out=stat[:, NEGM + tt:NEGM + tt + 1], in0=stat[:, MEAN:MEAN + 1],
            in1=stat[:, RSTD + tt:RSTD + tt + 1], op=ALU.mult), inc="dv")
        p.wait("vector", t9)
        t10 = p.op("vector", lambda e: e.tensor_scalar(
            out=stat[:, NEGM + tt:NEGM + tt + 1], in0=stat[:, NEGM + tt:NEGM + tt + 1],
            scalar1=-1.0, scalar2=None, op0=ALU.mult), inc="dv")
        return t10

    p.dma("sync", identb[:], ident, "lc")
    mzt = p.op("vector", lambda e: e.memset(ones1b[:], 1.0), inc="dv")
    p.wait("scalar", p.tok("lc"))
    bct = mzt
    for rc in range(2):
        p.dma("sync", ccol[:, rc:rc + 1], coefr[rc * 128:(rc + 1) * 128].unsqueeze(1), "lcc")
        p.dma("sync", scol[:, rc:rc + 1], sclr[rc * 128:(rc + 1) * 128].unsqueeze(1), "lcc")
    p.wait("vector", p.tok("lcc"))
    wct = p.op("vector", lambda e: e.tensor_tensor(out=wcol[:], in0=ccol[:],
                                                   in1=scol[:], op=ALU.mult), inc="dv")
    p.wait("sync", bct)
    for half in range(2):
        t = p.dma("sync", wfs[:, 0:DM].rearrange("p (c r) -> p c r", c=16),
                  a0T[half * 2048:(half + 1) * 2048, :]
                  .rearrange("(c p) r -> p c r", p=128), f"la{half}")
        p.wait("gpsimd", t)
        ct = p.op("gpsimd", lambda e, half=half: e.tensor_copy(
            out=a0b[:, half * 16 * 256:(half + 1) * 16 * 256], in_=wfs[:, 0:DM]),
            inc="pool")
        p.wait("sync", ct)

    at_b = xw[:, DM:2 * DM]
    xat = scr.bitcast(BF16)[:, 0:256]
    stg_user = None
    xa_ev = None
    for tt in range(8):
        p.wait("sync", p.tok("pool"))
        for dsl, srcfn in attn_srcs(tt):
            lt1 = p.dma("sync", at_b[:, dsl], srcfn, f"lat{tt % 2}")
        lt2 = p.dma("sync", z[:, tt * DM:(tt + 1) * DM],
                    x0[tt * 128:(tt + 1) * 128, :], f"lx{tt % 2}")
        p.wait("vector", lt1)
        p.wait("vector", lt2)
        zt = p.op("vector", lambda e, tt=tt: e.scalar_tensor_tensor(
            out=z[:, tt * DM:(tt + 1) * DM], in0=at_b, scalar=1.0,
            in1=z[:, tt * DM:(tt + 1) * DM], op0=ALU.mult, op1=ALU.add), inc="dv")
        p.wait("vector", zt)
        p.wait("scalar", zt)
        if stg_user is not None:
            p.wait("scalar", stg_user)
        sttok = ln_stats(tt, z[:, tt * DM:(tt + 1) * DM], "b")
        p.wait("scalar", sttok)
        x1t = p.op("scalar", lambda e, tt=tt: e.activation(
            out=at_b, in_=z[:, tt * DM:(tt + 1) * DM], func=AF.Identity,
            bias=stat[:, NEGM + tt:NEGM + tt + 1],
            scale=stat[:, RSTD + tt:RSTD + tt + 1]), inc="ac")
        p.wait("tensor", x1t)
        for c in range(32):
            trt = p.op("tensor", lambda e, c=c: e.transpose(
                out=ps_trv, in_=at_b[:, c * 128:(c + 1) * 128],
                identity=identb[:]), inc="pe")
            p.wait("vector", trt)
            cpt = p.op("vector", lambda e, c=c: e.tensor_copy(
                out=stg[:, c * 128:(c + 1) * 128], in_=ps_trv), inc="dv")
            p.wait("tensor", cpt)
        p.wait("sync", p.tok("dv"))
        stg_user = p.dma("sync",
                         xT1d[:, :, tt * 128:(tt + 1) * 128].transpose([1, 0, 2]),
                         stg.rearrange("p (c i) -> p c i", c=32), f"xtw{tt % 2}")
        p.wait("tensor", xa_ev)
        for d in range(32):
            xmt = p.op("tensor", lambda e, d=d: e.matmul(
                out=ps_xa, lhsT=stg[:, d * 128:(d + 1) * 128],
                rhs=a0b[:, d * 256:(d + 1) * 256],
                start=(d == 0), stop=(d == 31)), inc="pe" if d == 31 else None)
        p.wait("scalar", xmt)
        xa_ev = p.op("scalar", lambda e: e.copy(out=xat, in_=ps_xa), inc="ac")
        p.wait("tensor", xa_ev)
        for rc in range(2):
            trt = p.op("tensor", lambda e, rc=rc: e.transpose(
                out=ps_trv, in_=xat[:, rc * 128:(rc + 1) * 128], identity=identb[:]),
                inc="pe")
            p.wait("scalar", trt)
            xa_ev = p.op("scalar", lambda e, rc=rc, tt=tt: e.activation(
                out=xaT[:, rc * TPC + tt * 128: rc * TPC + (tt + 1) * 128],
                in_=ps_trv, func=AF.Identity, bias=0.0,
                scale=wcol[:, rc:rc + 1]), inc="ac")
            p.wait("tensor", xa_ev)
        p.wait("sync", p.tok("ac"))
    p.wait("sync", p.tok("xtw0"))
    p.wait("sync", p.tok("xtw1"))
    wu = [None, None]
    ev_tok = None
    xu = None
    prev_bmt = None
    for osp in range(8):
        p.wait("sync", p.tok("dv"))
        brl = p.dma("sync", scr[0:1, 512:1024],
                    bias0[0:1, osp * 512:(osp + 1) * 512], f"brl{osp % 2}")
        p.wait("scalar", brl)
        p.wait("scalar", prev_bmt)
        brc = p.op("scalar", lambda e: e.copy(out=brb[:], in_=scr[0:1, 512:1024]),
                   inc="ac")
        for dg in range(8):
            db = dg % 2
            if wu[db] is not None:
                p.wait("sync", wu[db])
            wl = p.dma("sync", wfs[:, db * 2048:(db + 1) * 2048]
                       .rearrange("p (c o) -> p c o", c=4),
                       w0T[dg * 512:(dg + 1) * 512, osp * 512:(osp + 1) * 512]
                       .rearrange("(c p) o -> p c o", p=128), f"wl{db}")
            p.wait("scalar", wu[db])
            xl = p.op("scalar", lambda e, dg=dg, db=db: e.dma_start(
                out=xw[:, db * 4096:(db + 1) * 4096].rearrange("p (c m) -> p c m", c=4),
                in_=xT1d[dg * 4:(dg + 1) * 4].transpose([1, 0, 2])), inc=f"xl{db}", k=16)
            p.wait("gpsimd", wl)
            p.wait("gpsimd", wu[db])
            wc = p.op("gpsimd", lambda e, db=db: e.tensor_copy(
                out=wbs[:, db * 2048:(db + 1) * 2048],
                in_=wfs[:, db * 2048:(db + 1) * 2048]), inc="pool")
            p.wait("tensor", wc)
            p.wait("tensor", xl)
            if dg < 2 and ev_tok is not None:
                p.wait("tensor", ev_tok)
            for dc in range(4):
                for tt in range(8):
                    mmt = p.op("tensor", lambda e, tt=tt, dc=dc, db=db, dg=dg: e.matmul(
                        out=ps[:, tt * 512:(tt + 1) * 512],
                        lhsT=xw[:, db * 4096 + dc * 1024 + tt * 128:
                                   db * 4096 + dc * 1024 + (tt + 1) * 128],
                        rhs=wbs[:, db * 2048 + dc * 512: db * 2048 + (dc + 1) * 512],
                        start=(dg == 0 and dc == 0), stop=False),
                        inc="pe" if (tt == 7 and dc == 3) else None)
            wu[db] = mmt
        p.wait("sync", p.tok("pe"))
        p.wait("sync", brc)
        for rc in range(2):
            blt = p.dma("sync", b0f[:, rc * 512:(rc + 1) * 512],
                        b0cat[rc * 128:(rc + 1) * 128, osp * 512:(osp + 1) * 512],
                        f"lb{rc}")
            p.wait("gpsimd", blt)
            bctk = p.op("gpsimd", lambda e, rc=rc: e.tensor_copy(
                out=b0s[:, rc * 512:(rc + 1) * 512], in_=b0f[:, rc * 512:(rc + 1) * 512]),
                inc="pool")
            p.wait("tensor", bctk)
        for tt in range(8):
            for rc in range(2):
                p.op("tensor", lambda e, tt=tt, rc=rc, osp=osp: e.matmul(
                    out=ps[:, tt * 512:(tt + 1) * 512],
                    lhsT=xaT[:, rc * TPC + tt * 128: rc * TPC + (tt + 1) * 128],
                    rhs=b0s[:, rc * 512:(rc + 1) * 512],
                    start=False, stop=False))
            if tt == 0:
                p.wait("tensor", brc)
            bmt = p.op("tensor", lambda e, tt=tt: e.matmul(
                out=ps[:, tt * 512:(tt + 1) * 512],
                lhsT=ones1b[:], rhs=brb[:],
                start=False, stop=True), inc="pe")
            prev_bmt = bmt
            p.wait("scalar", bmt)
            x1o = p.op("scalar", lambda e, tt=tt, osp=osp: e.activation(
                out=x1tmp[:], in_=z[:, tt * DM + osp * 512: tt * DM + (osp + 1) * 512],
                func=AF.Identity, bias=stat[:, NEGM + tt:NEGM + tt + 1],
                scale=stat[:, RSTD + tt:RSTD + tt + 1]), inc="ac")
            p.wait("vector", x1o)
            ev_tok = p.op("vector", lambda e, tt=tt, osp=osp: e.tensor_tensor(
                out=z[:, tt * DM + osp * 512: tt * DM + (osp + 1) * 512],
                in0=ps[:, tt * 512:(tt + 1) * 512], in1=x1tmp[:], op=ALU.add), inc="dv")
            p.wait("scalar", ev_tok)
    ot_prev = [None, None]
    for tt in range(8):
        p.wait("vector", p.tok("dv"))
        p.wait("scalar", p.tok("dv"))
        sttok = ln_stats(tt, z[:, tt * DM:(tt + 1) * DM], "b2")
        p.wait("scalar", sttok)
        slots = [x1tmp[:], scr[:, 0:512]]
        for oq in range(8):
            sl = slots[oq % 2]
            p.wait("scalar", ot_prev[oq % 2])
            oft = p.op("scalar", lambda e, tt=tt, oq=oq, sl=sl: e.activation(
                out=sl, in_=z[:, tt * DM + oq * 512: tt * DM + (oq + 1) * 512],
                func=AF.Identity,
                bias=stat[:, NEGM + tt:NEGM + tt + 1],
                scale=stat[:, RSTD + tt:RSTD + tt + 1]), inc="ac")
            p.wait("sync", oft)
            ot_prev[oq % 2] = p.dma("sync",
                                    lambda tt=tt, oq=oq: x2_dst(tt)()[:, oq * 512:(oq + 1) * 512],
                                    sl, f"ox{oq % 2}")
    p.wait("sync", p.tok("ox0"))
    p.wait("sync", p.tok("ox1"))
    if ctx is None:
        p.emit()
        return nc


def stage_B(inputs, attn_full, x0_full):
    ident = np.eye(128, dtype=BF)
    w0T = np.ascontiguousarray(np.asarray(inputs["L0_ffn_W"]).T)
    A = np.asarray(inputs["L0_ffn_A"])
    a0T = np.ascontiguousarray(A.reshape(256, DM).T)
    Bf = np.asarray(inputs["L0_ffn_Bf"])
    b0cat = np.ascontiguousarray(Bf.transpose(0, 2, 1).reshape(256, DM))
    coefr = np.ascontiguousarray(np.repeat(np.asarray(inputs["L0_ffn_coeffs"]), 16))
    sclr = np.ascontiguousarray(np.repeat(np.asarray(inputs["L0_ffn_scales"]), 16))
    bias0 = np.ascontiguousarray(np.asarray(inputs["L0_ffn_bias"])[None, :])
    in_maps = []
    for c in range(NCORE):
        b, s = c // 2, c % 2
        tsl = slice(s * 1024, (s + 1) * 1024)
        in_maps.append({
            "x0": np.ascontiguousarray(x0_full[b][tsl]),
            "attn": np.ascontiguousarray(attn_full[b][tsl]),
            "w0T": w0T, "bias0": bias0, "a0T": a0T, "b0cat": b0cat,
            "coefr": coefr, "sclr": sclr, "ident": ident,
        })
    return in_maps




def build_launch_C(ctx=None):
    if ctx is None:
        nc = bass.Bass("TRN2", target_bir_lowering=False, num_devices=NCORE)
        p = Prog(nc)
        x2f = nc.dram_tensor("x2f", [L, DM], F32, kind="ExternalInput").ap()
        wqT = nc.dram_tensor("wqT", [DM, 2048], F32, kind="ExternalInput").ap()
        wkT = nc.dram_tensor("wkT", [DM, 512], F32, kind="ExternalInput").ap()
        wvT = nc.dram_tensor("wvT", [DM, 512], F32, kind="ExternalInput").ap()
        wgT = nc.dram_tensor("wgT", [DM, 2048], F32, kind="ExternalInput").ap()
        winT = nc.dram_tensor("winT", [DM, 16], F32, kind="ExternalInput").ap()
        dtb = nc.dram_tensor("dtb", [1, 16], F32, kind="ExternalInput").ap()
        alog = nc.dram_tensor("alog", [1, 16], F32, kind="ExternalInput").ap()
        gnw = nc.dram_tensor("gnw", [1, 128], F32, kind="ExternalInput").ap()
        cosT = nc.dram_tensor("cosT", [128, L], F32, kind="ExternalInput").ap()
        sinT = nc.dram_tensor("sinT", [128, L], F32, kind="ExternalInput").ap()
        ident = nc.dram_tensor("ident", [128, 128], BF16, kind="ExternalInput").ap()
        identf = nc.dram_tensor("identf", [128, 128], F32, kind="ExternalInput").ap()
        trif = nc.dram_tensor("trif", [128, 128], F32, kind="ExternalInput").ap()
        ygo = nc.dram_tensor("ygo", [L, 2048], BF16, kind="ExternalOutput").ap()
        yg_dst = lambda n, kv: (lambda: ygo[n * 128:(n + 1) * 128,
                                            kv * 512:(kv + 1) * 512])
    else:
        nc, p = ctx["nc"], ctx["p"]
        x2f = ctx["x2f"]
        wqT, wkT, wvT, wgT, winT = (ctx["wq1T"], ctx["wk1T"], ctx["wv1T"],
                                    ctx["wg1T"], ctx["win1T"])
        dtb, alog, gnw, cosT, sinT = (ctx["dtb"], ctx["alog"], ctx["gnw"],
                                      ctx["cosT"], ctx["sinT"])
        ident, identf, trif = ctx["ident"], ctx["identf"], ctx["trif"]
        yg_dst = ctx["yg_dst"]
    NCh = 16
    xT2d = nc.dram_tensor("xT2d", [32, 128, L], BF16, kind="Internal").ap()

    al = ctx["al"] if ctx else nc.alloc_sbuf_tensor
    identb = al("identb", [128, 128], BF16)
    idenff = al("idenff", [128, 128], F32)
    tri_f = al("tri_f", [128, 128], F32)
    cos_s = al("cos_s", [128, L], F32)
    sin_s = al("sin_s", [128, L], F32)
    scr = al("scr", [128, 6144], F32)
    stg = al("stg", [128, DM], BF16)
    xTh = al("xTh", [128, 32 * 1024], BF16)
    wst_f = al("wst_f", [128, DM], F32)
    wst_b = al("wst_b", [128, DM], BF16)
    qfm = al("qfm", [128, 4 * L], BF16)
    gtok = al("gtok", [128, 16 * 512], BF16)
    kfm = al("kfm", [128, L], BF16)
    ktok = al("ktok", [128, 16 * 128], BF16)
    vtok = al("vtok", [128, 16 * 128], BF16)
    winb = al("winb", [128, 32 * 16], BF16)
    winlo = al("winlo", [128, 32 * 16], BF16)
    xlob = al("xlob", [128, DM], BF16)
    dtbb = al("dtbb", [1, 16], BF16)
    dtv = al("dtv", [128, 16 * 16], F32)
    Gv = al("Gv", [128, 16 * 16], F32)
    nGv = al("nGv", [128, 16 * 16], F32)
    Gl = al("Gl", [1, 16 * 16], F32)
    bcGl = al("bcGl", [128, 16 * 16], F32)
    edcv = al("edcv", [128, 16 * 16], F32)
    deccv = al("deccv", [128, 16 * 16], F32)
    egcv = al("egcv", [128, 16 * 16], F32)
    Aneg = al("Aneg", [128, 16], F32)
    gnb = al("gnb", [128, 128], F32)
    ones1b = al("ones1b", [1, 128], BF16)
    ones1f = al("ones1f", [1, 128], F32)
    onecf = al("onecf", [128, 1], F32)
    prevC = al("prevC", [128, 4 * 128], F32)
    prevb = al("prevb", [128, 4 * 128], BF16)
    T1 = al("T1", [128, 128], F32)
    dmask = al("dmask", [128, 128], F32)
    mt = al("mt", [128, 128], BF16)
    tlo = mt
    ytmp = T1[:, :]
    yjunk = mt
    xct = al("xct", [128, 128], BF16)
    kdt = al("kdt", [128, 128], BF16)
    ytile = al("ytile", [128, 128], F32)
    ygstg = al("ygstg", [128, 512], BF16)
    ccol = al("ccol", [128, 1], F32)
    alog_s = al("alog_s", [1, 16], F32)
    row16 = al("row16", [1, 16], F32)

    if ctx is None:
        ps_proj = nc.alloc_psum_tensor("ps_proj", [128, 1024], F32)
        ps_attn = nc.alloc_psum_tensor("ps_attn", [128, 2048], F32)
        ps_trf = nc.alloc_psum_tensor("ps_trf", [128, 128], F32)
        ps_tr = nc.alloc_psum_tensor("ps_tr", [128, 128], BF16)
    else:
        PS = ctx["PS"]
        ps_proj = PS[:, 0:1024]
        ps_attn = PS[:, 1024:3072]
        ps_trf = PS[:, 3072:3200]
        ps_tr = PS.bitcast(BF16)[:, 7 * 1024: 7 * 1024 + 128]
    ps_st = ps_attn[:, 0:128]
    ps_yA = ps_attn[:, 512:640]
    ps_yB = ps_attn[:, 1024:1152]
    ps_C = ps_attn[:, 1536:1664]
    ps_dt = ps_attn[:, 0:16]
    ps_G = ps_attn[:, 512:528]
    ps_gl = ps_attn[0:1, 1024:1040]
    ps_bc = ps_attn[:, 1536:1552]

    p.dma("sync", identb[:], ident, "lc")
    p.dma("sync", idenff[:], identf, "lc")
    p.dma("sync", tri_f[:], trif, "lc")
    p.dma("sync", cos_s[:], cosT, "lc")
    p.dma("sync", sin_s[:], sinT, "lc")
    p.dma("sync", alog_s[:], alog, "lc")
    p.dma("sync", scr[0:1, 0:16], dtb, "lc")
    p.dma("sync", scr[0:1, 16:144], gnw, "lc")
    wl = p.dma("sync", scr[:, 2048:2048 + 512].rearrange("p (c r) -> p c r", c=32),
               winT.rearrange("(c p) r -> p c r", p=128), "lc")
    mz1 = p.op("vector", lambda e: e.memset(ones1b[:], 1.0), inc="dv")
    mz2 = p.op("vector", lambda e: e.memset(ones1f[:], 1.0), inc="dv")
    mz3 = p.op("vector", lambda e: e.memset(onecf[:], 1.0), inc="dv")
    mz4 = p.op("vector", lambda e: e.memset(prevC[:], 0.0), inc="dv")
    mz5 = p.op("vector", lambda e: e.memset(prevb[:], 0.0), inc="dv")
    p.wait("scalar", p.tok("lc"))
    p.wait("scalar", p.tok("dv"))
    c1 = p.op("scalar", lambda e: e.copy(out=dtbb[:], in_=scr[0:1, 0:16]), inc="ac")
    c2 = p.op("scalar", lambda e: e.copy(out=winb[:], in_=scr[:, 2048:2048 + 512]), inc="ac")
    p.wait("vector", c2)
    c2b = p.op("vector", lambda e: e.tensor_tensor(out=winlo[:], in0=scr[:, 2048:2048 + 512],
                                                   in1=winb[:], op=ALU.subtract), inc="dv")
    c3 = p.op("scalar", lambda e: e.activation(out=row16[:], in_=alog_s[:],
                                               func=AF.Exp, bias=0.0, scale=1.0), inc="ac")
    p.wait("vector", c3)
    c4 = p.op("vector", lambda e: e.tensor_scalar(out=row16[:], in0=row16[:],
                                                  scalar1=-1.0, scalar2=None,
                                                  op0=ALU.mult), inc="dv")
    p.wait("tensor", c4)
    bc1 = p.op("tensor", lambda e: e.matmul(out=ps_attn[:, 0:16], lhsT=ones1f[:],
                                            rhs=row16[:], start=True, stop=True), inc="pe")
    p.wait("vector", bc1)
    an1 = p.op("vector", lambda e: e.tensor_copy(out=Aneg[:], in_=ps_attn[:, 0:16]), inc="dv")
    p.wait("scalar", p.tok("dv"))
    c5 = p.op("scalar", lambda e: e.copy(out=gnb[0:1, :],
                                         in_=scr[0:1, 16:144]), inc="ac")
    p.wait("tensor", c5)
    p.wait("tensor", an1)
    bc2 = p.op("tensor", lambda e: e.matmul(out=ps_attn[:, 0:128], lhsT=ones1f[:],
                                            rhs=gnb[0:1, :], start=True, stop=True), inc="pe")
    p.wait("vector", bc2)
    gn1 = p.op("vector", lambda e: e.tensor_copy(out=gnb[:], in_=ps_attn[:, 0:128]), inc="dv")

    xf = scr[:, 0:DM]
    xb = scr.bitcast(BF16)[:, 2 * DM: 3 * DM]
    dtr = scr[:, 5500:5516]
    sp1 = scr[:, 5520:5536]
    sp2 = scr[:, 5540:5556]
    gt = scr[:, 5560:5576]
    p.wait("sync", p.tok("ac"))
    cast_tok = None
    tr_last = None
    st_tok = None
    dt_ev = p.tok("dv")
    for tt in range(NCh):
        if cast_tok is not None:
            p.wait("sync", cast_tok)
        ltok = p.dma("sync", xf, x2f[tt * 128:(tt + 1) * 128, :], f"ldx{tt % 2}")
        p.wait("scalar", ltok)
        if tr_last is not None:
            p.wait("scalar", tr_last)
        cast_tok = p.op("scalar", lambda e: e.copy(out=xb, in_=xf), inc="ac")
        p.wait("vector", cast_tok)
        lo_tok = p.op("vector", lambda e: e.tensor_tensor(
            out=xlob[:], in0=xf, in1=xb, op=ALU.subtract), inc="dv")
        p.wait("tensor", cast_tok)
        p.wait("tensor", lo_tok)
        if st_tok is not None:
            p.wait("vector", st_tok)
        p.wait("tensor", dt_ev)
        for c in range(32):
            tr_last = p.op("tensor", lambda e, c=c: e.transpose(
                out=ps_tr[:], in_=xb[:, c * 128:(c + 1) * 128], identity=identb[:]),
                inc="pe")
            p.wait("vector", tr_last)
            cpt = p.op("vector", lambda e, c=c: e.tensor_copy(
                out=stg[:, c * 128:(c + 1) * 128], in_=ps_tr[:]), inc="dv")
            p.wait("tensor", cpt)
            trl = p.op("tensor", lambda e, c=c: e.transpose(
                out=ps_tr[:], in_=xlob[:, c * 128:(c + 1) * 128], identity=identb[:]),
                inc="pe")
            p.wait("vector", trl)
            cpl = p.op("vector", lambda e: e.tensor_copy(out=tlo[:], in_=ps_tr[:]),
                       inc="dv")
            p.wait("tensor", cpl)
            p.op("tensor", lambda e, c=c: e.matmul(
                out=ps_dt, lhsT=stg[:, c * 128:(c + 1) * 128],
                rhs=winb[:, c * 16:(c + 1) * 16],
                start=(c == 0), stop=False))
            p.op("tensor", lambda e, c=c: e.matmul(
                out=ps_dt, lhsT=tlo[:], rhs=winb[:, c * 16:(c + 1) * 16],
                start=False, stop=False))
            p.op("tensor", lambda e, c=c: e.matmul(
                out=ps_dt, lhsT=stg[:, c * 128:(c + 1) * 128],
                rhs=winlo[:, c * 16:(c + 1) * 16],
                start=False, stop=False))
        p.wait("sync", p.tok("dv"))
        st_tok = p.dma("sync",
                       xT2d[:, :, tt * 128:(tt + 1) * 128].transpose([1, 0, 2]),
                       stg.rearrange("p (c i) -> p c i", c=32), f"xtw{tt % 2}")
        dmt = p.op("tensor", lambda e: e.matmul(
            out=ps_dt, lhsT=ones1b[:], rhs=dtbb[:], start=False, stop=True), inc="pe")
        p.wait("scalar", dmt)
        p.wait("scalar", p.tok("dv"))
        e1 = p.op("scalar", lambda e: e.copy(out=dtr, in_=ps_dt), inc="ac")
        p.wait("scalar", e1)
        e2 = p.op("scalar", lambda e: e.activation(out=sp1, in_=dtr, func=AF.Abs,
                                                   bias=0.0, scale=1.0), inc="ac")
        p.wait("scalar", e2)
        e3 = p.op("scalar", lambda e: e.activation(out=sp1, in_=sp1, func=AF.Exp,
                                                   bias=0.0, scale=-1.0), inc="ac")
        p.wait("vector", e3)
        e4 = p.op("vector", lambda e: e.tensor_scalar(out=sp1, in0=sp1, scalar1=1.0,
                                                      scalar2=None, op0=ALU.add), inc="dv")
        p.wait("scalar", e4)
        e5 = p.op("scalar", lambda e: e.activation(out=sp1, in_=sp1, func=AF.Ln,
                                                   bias=0.0, scale=1.0), inc="ac")
        p.wait("vector", e5)
        e6 = p.op("vector", lambda e: e.tensor_scalar(out=sp2, in0=dtr, scalar1=0.0,
                                                      scalar2=None, op0=ALU.max), inc="dv")
        p.wait("vector", e6)
        e7 = p.op("vector", lambda e, tt=tt: e.tensor_tensor(
            out=dtv[:, tt * 16:(tt + 1) * 16], in0=sp1, in1=sp2, op=ALU.add), inc="dv")
        p.wait("vector", e7)
        e8 = p.op("vector", lambda e, tt=tt: e.tensor_tensor(
            out=gt, in0=dtv[:, tt * 16:(tt + 1) * 16], in1=Aneg[:], op=ALU.mult), inc="dv")
        p.wait("tensor", e8)
        g1 = p.op("tensor", lambda e: e.matmul(out=ps_G, lhsT=tri_f[:], rhs=gt,
                                               start=True, stop=True), inc="pe")
        g2 = p.op("tensor", lambda e: e.matmul(out=ps_gl, lhsT=onecf[:], rhs=gt,
                                               start=True, stop=True), inc="pe")
        p.wait("vector", g2)
        dt_ev = p.op("vector", lambda e, tt=tt: e.tensor_copy(
            out=Gv[:, tt * 16:(tt + 1) * 16], in_=ps_G), inc="dv")
        gl_ev = p.op("vector", lambda e, tt=tt: e.tensor_copy(
            out=Gl[0:1, tt * 16:(tt + 1) * 16], in_=ps_gl), inc="dv")
        p.wait("tensor", gl_ev)
        p.wait("sync", p.tok("ac"))
        p.wait("sync", p.tok("dv"))
    p.wait("vector", p.tok("dv"))
    ng = p.op("vector", lambda e: e.tensor_scalar(out=nGv[:], in0=Gv[:], scalar1=-1.0,
                                                  scalar2=None, op0=ALU.mult), inc="dv")
    for n in range(NCh):
        p.wait("tensor", p.tok("dv"))
        b1 = p.op("tensor", lambda e, n=n: e.matmul(
            out=ps_bc, lhsT=ones1f[:], rhs=Gl[0:1, n * 16:(n + 1) * 16],
            start=True, stop=True), inc="pe")
        p.wait("vector", b1)
        b2 = p.op("vector", lambda e, n=n: e.tensor_copy(
            out=bcGl[:, n * 16:(n + 1) * 16], in_=ps_bc), inc="dv")
        p.wait("tensor", b2)
    p.wait("scalar", p.tok("dv"))
    ec1 = p.op("scalar", lambda e: e.activation(out=deccv[:], in_=bcGl[:],
                                                func=AF.Exp, bias=0.0, scale=1.0), inc="ac")
    ec2 = p.op("scalar", lambda e: e.activation(out=egcv[:], in_=Gv[:],
                                                func=AF.Exp, bias=0.0, scale=1.0), inc="ac")
    p.wait("vector", ec2)
    ec3 = p.op("vector", lambda e: e.tensor_tensor(out=edcv[:], in0=bcGl[:],
                                                   in1=Gv[:], op=ALU.subtract), inc="dv")
    p.wait("scalar", ec3)
    ec4 = p.op("scalar", lambda e: e.activation(out=edcv[:], in_=edcv[:],
                                                func=AF.Exp, bias=0.0, scale=1.0), inc="ac")
    p.wait("tensor", p.tok("ac"))
    p.wait("vector", p.tok("ac"))

    def rope_evac(psrc, n_cols, dst, coff):
        rt = scr[:, 0:1024]
        t1s = scr[:, 1024:2048]
        r1 = p.op("scalar", lambda e: e.activation(
            out=rt[0:64, 0:n_cols], in_=psrc[64:128, :], func=AF.Identity,
            bias=0.0, scale=-1.0), inc="ac")
        r2 = p.op("scalar", lambda e: e.activation(
            out=rt[64:128, 0:n_cols], in_=psrc[0:64, :], func=AF.Identity,
            bias=0.0, scale=1.0), inc="ac")
        p.wait("vector", r2)
        m1 = p.op("vector", lambda e: e.tensor_tensor(
            out=t1s[:, 0:n_cols], in0=psrc, in1=cos_s[:, coff:coff + n_cols],
            op=ALU.mult), inc="dv")
        p.wait("vector", m1)
        m2 = p.op("vector", lambda e: e.tensor_tensor(
            out=rt[:, 0:n_cols], in0=rt[:, 0:n_cols],
            in1=sin_s[:, coff:coff + n_cols], op=ALU.mult), inc="dv")
        p.wait("vector", m2)
        m3 = p.op("vector", lambda e: e.tensor_tensor(
            out=dst, in0=t1s[:, 0:n_cols], in1=rt[:, 0:n_cols], op=ALU.add), inc="dv")
        return m3

    p.wait("sync", p.tok("xtw0"))
    p.wait("sync", p.tok("xtw1"))
    fillreg = {}
    p.raw("gpsimd", lambda e: fillreg.__setitem__("r", e.to_reg(-1e30)))
    wld_tok = None
    proj_ev = None
    yg_st = None
    prev_su = None
    prev_yn1 = None
    prev_cn = None
    prev_ye1 = None
    prev_ye2 = None
    for kv in range(4):
        for mh in range(2):
            p.wait("sync", p.tok("pe"))
            xld = p.dma("sync", xTh.rearrange("p (c m) -> p c m", c=32),
                        xT2d[:, :, mh * 1024:(mh + 1) * 1024].transpose([1, 0, 2]),
                        f"lxh{mh}")
            for oc in range(10):
                if oc < 4:
                    kind, wsrc, gch = "q", wqT, kv * 4 + oc
                elif oc < 8:
                    kind, wsrc, gch = "g", wgT, kv * 4 + (oc - 4)
                elif oc == 8:
                    kind, wsrc, gch = "k", wkT, kv
                else:
                    kind, wsrc, gch = "v", wvT, kv
                if wld_tok is not None:
                    p.wait("sync", p.tok("pe"))
                wld_tok = p.dma("sync", wst_f[:].rearrange("p (c o) -> p c o", c=32),
                                wsrc[:, gch * 128:(gch + 1) * 128]
                                .rearrange("(c p) o -> p c o", p=128), "wld")
                p.wait("gpsimd", wld_tok)
                p.wait("gpsimd", p.tok("pe"))
                wct = p.op("gpsimd", lambda e: e.tensor_copy(out=wst_b[:], in_=wst_f[:]),
                           inc="pool")
                p.wait("tensor", wct)
                p.wait("tensor", xld)
                p.wait("tensor", p.tok("ac"))
                p.wait("tensor", p.tok("dv"))
                for d in range(32):
                    for half in range(2):
                        pmt = p.op("tensor", lambda e, d=d, h=half: e.matmul(
                            out=ps_proj[:, h * 512:(h + 1) * 512],
                            lhsT=wst_b[:, d * 128:(d + 1) * 128],
                            rhs=xTh[:, d * 1024 + h * 512: d * 1024 + (h + 1) * 512],
                            start=(d == 0), stop=(d == 31)),
                            inc="pe" if (d == 31 and half == 1) else None)
                p.wait("scalar", pmt)
                p.wait("vector", pmt)
                if kind == "q":
                    loc = oc
                    rtok = rope_evac(ps_proj[:, 0:1024], 1024,
                                     qfm[:, loc * L + mh * 1024: loc * L + (mh + 1) * 1024],
                                     mh * 1024)
                    p.wait("tensor", rtok)
                elif kind == "k":
                    rtok = rope_evac(ps_proj[:, 0:1024], 1024,
                                     kfm[:, mh * 1024:(mh + 1) * 1024], mh * 1024)
                    p.wait("tensor", rtok)
                    for j in range(8):
                        trt = p.op("tensor", lambda e, j=j, mh=mh: e.transpose(
                            out=ps_tr[:], in_=kfm[:, mh * 1024 + j * 128:
                                                   mh * 1024 + (j + 1) * 128],
                            identity=identb[:]), inc="pe")
                        p.wait("vector", trt)
                        cpt = p.op("vector", lambda e, j=j, mh=mh: e.tensor_copy(
                            out=ktok[:, (mh * 8 + j) * 128:(mh * 8 + j + 1) * 128],
                            in_=ps_tr[:]), inc="dv")
                        p.wait("tensor", cpt)
                else:
                    vt = scr.bitcast(BF16)[:, 0:1024]
                    ev = p.op("scalar", lambda e: e.copy(out=vt, in_=ps_proj[:, 0:1024]),
                              inc="ac")
                    p.wait("tensor", ev)
                    for j in range(8):
                        trt = p.op("tensor", lambda e, j=j: e.transpose(
                            out=ps_tr[:], in_=vt[:, j * 128:(j + 1) * 128],
                            identity=identb[:]), inc="pe")
                        p.wait("vector", trt)
                        if kind == "g":
                            loc = oc - 4
                            p.wait("scalar", trt)
                            cpt = p.op("scalar", lambda e, j=j, mh=mh, loc=loc: e.activation(
                                out=gtok[:, (mh * 8 + j) * 512 + loc * 128:
                                            (mh * 8 + j) * 512 + (loc + 1) * 128],
                                in_=ps_tr[:], func=AF.Silu, bias=0.0, scale=1.0), inc="dv")
                        else:
                            cpt = p.op("vector", lambda e, j=j, mh=mh: e.tensor_copy(
                                out=vtok[:, (mh * 8 + j) * 128:(mh * 8 + j + 1) * 128],
                                in_=ps_tr[:]), inc="dv")
                        p.wait("tensor", cpt)
                    p.wait("scalar", p.tok("dv"))
        p.wait("tensor", p.tok("dv"))
        p.wait("tensor", p.tok("ac"))
        for n in range(NCh):
            for j in range(4):
                hh = kv * 4 + j
                gcol = n * 16 + hh
                smt = p.op("tensor", lambda e, n=n, j=j: e.matmul(
                    out=ps_st, lhsT=kfm[:, n * 128:(n + 1) * 128],
                    rhs=qfm[:, j * L + n * 128: j * L + (n + 1) * 128],
                    start=True, stop=True), inc="pe")
                t1t = p.op("scalar", lambda e, gcol=gcol: e.activation(
                    out=T1[:], in_=T1[:],
                    func=AF.Identity, bias=Gv[:, gcol:gcol + 1], scale=0.0), inc="ac")
                p.wait("tensor", t1t)
                t2t = p.op("tensor", lambda e: e.transpose(
                    out=ps_trf[:], in_=T1[:], identity=idenff[:]), inc="pe")
                p.wait("scalar", t2t)
                p.wait("scalar", prev_yn1)
                t3t = p.op("scalar", lambda e, gcol=gcol: e.activation(
                    out=dmask[:], in_=ps_trf[:], func=AF.Identity,
                    bias=nGv[:, gcol:gcol + 1], scale=1.0), inc="ac")
                p.wait("gpsimd", t3t)
                t4t = p.op("gpsimd", lambda e: e.affine_select(
                    out=dmask[:], in_=dmask[:], pattern=[[1, 128]],
                    compare_op=ALU.is_ge, fill=fillreg["r"], base=0,
                    channel_multiplier=-1), inc="pool")
                p.wait("scalar", t4t)
                t4b = p.op("scalar", lambda e: e.activation(
                    out=dmask[:], in_=dmask[:], func=AF.Exp,
                    bias=0.0, scale=1.0), inc="ac")
                p.wait("vector", t4b)
                p.wait("vector", smt)
                t5t = p.op("vector", lambda e: e.tensor_tensor(
                    out=mt[:], in0=ps_st, in1=dmask[:], op=ALU.mult), inc="dv")
                p.wait("scalar", prev_cn)
                xc1 = p.op("scalar", lambda e, n=n, gcol=gcol: e.activation(
                    out=xct[:], in_=vtok[:, n * 128:(n + 1) * 128], func=AF.Identity,
                    bias=0.0, scale=dtv[:, gcol:gcol + 1]), inc="ac")
                kd2 = p.op("scalar", lambda e, n=n, gcol=gcol: e.activation(
                    out=kdt[:], in_=ktok[:, n * 128:(n + 1) * 128], func=AF.Identity,
                    bias=0.0, scale=edcv[:, gcol:gcol + 1]), inc="ac")
                p.wait("tensor", t5t)
                p.wait("tensor", xc1)
                p.wait("tensor", prev_ye2)
                p.wait("tensor", prev_ye1)
                ya = p.op("tensor", lambda e: e.matmul(
                    out=ps_yA, lhsT=mt[:], rhs=xct[:], start=True, stop=True), inc="pe")
                yb = p.op("tensor", lambda e, j=j: e.matmul(
                    out=ps_yB, lhsT=qfm[:, j * L + n * 128: j * L + (n + 1) * 128],
                    rhs=prevb[:, j * 128:(j + 1) * 128], start=True, stop=True), inc="pe")
                p.wait("tensor", kd2)
                p.wait("tensor", prev_su)
                cn = p.op("tensor", lambda e: e.matmul(
                    out=ps_C, lhsT=kdt[:], rhs=xct[:], start=True, stop=True), inc="pe")
                p.wait("vector", cn)
                su = p.op("vector", lambda e, j=j, gcol=gcol: e.scalar_tensor_tensor(
                    out=prevC[:, j * 128:(j + 1) * 128],
                    in0=prevC[:, j * 128:(j + 1) * 128],
                    scalar=deccv[:, gcol:gcol + 1],
                    in1=ps_C, op0=ALU.mult, op1=ALU.add), inc="dv")
                p.wait("scalar", su)
                p.wait("scalar", yb)
                pc = p.op("scalar", lambda e, j=j: e.copy(
                    out=prevb[:, j * 128:(j + 1) * 128],
                    in_=prevC[:, j * 128:(j + 1) * 128]), inc="ac")
                p.wait("scalar", yb)
                ye1 = p.op("scalar", lambda e, gcol=gcol: e.activation(
                    out=ytmp[:], in_=ps_yB, func=AF.Identity, bias=0.0,
                    scale=egcv[:, gcol:gcol + 1]), inc="ac")
                p.wait("vector", ye1)
                p.wait("vector", ya)
                ye2 = p.op("vector", lambda e: e.tensor_tensor(
                    out=ytile[:], in0=ps_yA, in1=ytmp[:], op=ALU.add), inc="dv")
                p.wait("scalar", ye2)
                p.wait("scalar", prev_yn1)
                p.wait("scalar", cn)
                rs1 = p.op("scalar", lambda e: e.activation(
                    out=yjunk[:], in_=ytile[:], func=AF.Square, bias=0.0, scale=1.0,
                    accum_out=ccol[:, 0:1]), inc="ac")
                p.wait("vector", rs1)
                rs2 = p.op("vector", lambda e: e.tensor_scalar(
                    out=ccol[:, 0:1], in0=ccol[:, 0:1], scalar1=1.0 / 128.0,
                    scalar2=RMS_EPS, op0=ALU.mult, op1=ALU.add), inc="dv")
                p.wait("scalar", rs2)
                rs3 = p.op("scalar", lambda e: e.activation(
                    out=ccol[:, 0:1], in_=ccol[:, 0:1], func=AF.Ln,
                    bias=0.0, scale=1.0), inc="ac")
                p.wait("scalar", rs3)
                rs4 = p.op("scalar", lambda e: e.activation(
                    out=ccol[:, 0:1], in_=ccol[:, 0:1], func=AF.Exp,
                    bias=0.0, scale=-0.5), inc="ac")
                p.wait("vector", rs4)
                yn1 = p.op("vector", lambda e: e.scalar_tensor_tensor(
                    out=ytile[:], in0=ytile[:], scalar=ccol[:, 0:1], in1=gnb[:],
                    op0=ALU.mult, op1=ALU.mult), inc="dv")
                p.wait("vector", yn1)
                if yg_st is not None:
                    p.wait("vector", yg_st)
                yg1 = p.op("vector", lambda e, j=j, n=n: e.tensor_tensor(
                    out=ygstg[:, j * 128:(j + 1) * 128], in0=ytile[:],
                    in1=gtok[:, n * 512 + j * 128: n * 512 + (j + 1) * 128],
                    op=ALU.mult), inc="dv")
                prev_su, prev_yn1, prev_cn = su, yn1, cn
                prev_ye1, prev_ye2 = ye1, ye2
            p.wait("sync", p.tok("dv"))
            yg_st = p.dma("sync", yg_dst(n, kv), ygstg[:], f"yst{n % 2}")
    p.wait("sync", p.tok("yst0"))
    p.wait("sync", p.tok("yst1"))
    if ctx is None:
        p.emit()
        return nc


def stage_C(inputs, x2_full):
    ident = np.eye(128, dtype=BF)
    identf = np.eye(128, dtype=np.float32)
    trif = np.ascontiguousarray(np.triu(np.ones((128, 128), np.float32)))
    WqT = np.ascontiguousarray(np.asarray(inputs["L1_attn_Wq"]).T)
    WkT = np.ascontiguousarray(np.asarray(inputs["L1_attn_Wk"]).T)
    WvT = np.ascontiguousarray(np.asarray(inputs["L1_attn_Wv"]).T)
    WgT = np.ascontiguousarray(np.asarray(inputs["L1_attn_Wg"]).T)
    WinT = np.ascontiguousarray(np.asarray(inputs["L1_attn_Win"]).T)
    cos, sin = host_rope_tables(np.arange(L))
    in_maps = []
    for c in range(NCORE):
        b, s = c // 2, c % 2
        qsl = slice(s * 2048, (s + 1) * 2048)
        kvsl = slice(s * 512, (s + 1) * 512)
        hsl = slice(s * 16, (s + 1) * 16)
        in_maps.append({
            "x2f": np.ascontiguousarray(x2_full[b]),
            "wqT": np.ascontiguousarray(WqT[:, qsl]),
            "wkT": np.ascontiguousarray(WkT[:, kvsl]),
            "wvT": np.ascontiguousarray(WvT[:, kvsl]),
            "wgT": np.ascontiguousarray(WgT[:, qsl]),
            "winT": np.ascontiguousarray(WinT[:, hsl]),
            "dtb": np.ascontiguousarray(np.asarray(inputs["L1_attn_dt_bias"])[None, hsl]),
            "alog": np.ascontiguousarray(np.asarray(inputs["L1_attn_A_log_bias"])[None, hsl]),
            "gnw": np.ascontiguousarray(np.asarray(inputs["L1_attn_gnorm_w"])[None, :]),
            "cosT": cos, "sinT": sin,
            "ident": ident, "identf": identf, "trif": trif,
        })
    return in_maps




def build_launch_D(ctx=None):
    if ctx is None:
        nc = bass.Bass("TRN2", target_bir_lowering=False, num_devices=NCORE)
        p = Prog(nc)
        x2 = nc.dram_tensor("x2", [TPC, DM], F32, kind="ExternalInput").ap()
        yg = nc.dram_tensor("yg", [TPC, DM], BF16, kind="ExternalInput").ap()
        woT = nc.dram_tensor("woT", [DM, DM], F32, kind="ExternalInput").ap()
        w1T = nc.dram_tensor("w1T", [DM, DM], F32, kind="ExternalInput").ap()
        b1 = nc.dram_tensor("b1", [1, DM], F32, kind="ExternalInput").ap()
        art = nc.dram_tensor("art", [DM, 80], F32, kind="ExternalInput").ap()
        bpT = nc.dram_tensor("bpT", [8, 128, DM], F32, kind="ExternalInput").ap()
        ident = nc.dram_tensor("ident", [128, 128], BF16, kind="ExternalInput").ap()
        x4o = nc.dram_tensor("x4o", [TPC, DM], F32, kind="ExternalOutput").ap()
        x2_src = lambda tt: (lambda tt=tt: x2[tt * 128:(tt + 1) * 128, :])
        yg_srcs = lambda tt: [(slice(0, DM),
                               lambda tt=tt: yg[tt * 128:(tt + 1) * 128, :])]
    else:
        nc, p = ctx["nc"], ctx["p"]
        woT, w1T, b1, art, bpT, ident = (ctx["woT"], ctx["w1T"], ctx["b1"],
                                         ctx["art"], ctx["bpT"], ctx["ident"])
        x4o = ctx["x4o"]
        x2_src = ctx["x2_src"]
        yg_srcs = ctx["yg_srcs"]
    ygTd = nc.dram_tensor("ygTd", [32, 128, TPC], BF16, kind="Internal").ap()
    xT3d = nc.dram_tensor("xT3d", [32, 128, TPC], BF16, kind="Internal").ap()

    al = ctx["al"] if ctx else nc.alloc_sbuf_tensor
    identb = al("identb", [128, 128], BF16)
    z = al("z", [128, 8 * DM], F32)
    scr = al("scr", [128, 2688], F32)
    artb = al("artb", [128, 32 * 80], BF16)
    hgT = al("hgT", [128, 8 * 1024], BF16)
    xw = al("xw", [128, 2 * 4096], BF16)
    stg = xw[:, 0:DM]
    xgt_alias = xw[:, DM:2 * DM]
    wfs = al("wfs", [128, 2 * 2048], F32)
    wbs = al("wbs", [128, 2 * 2048], BF16)
    bp_b = al("bp_b", [128, 2 * 512], BF16)
    brb = al("brb", [1, 512], BF16)
    ones1b = al("ones1b", [1, 128], BF16)
    stat = al("stat", [128, 48], F32)
    rh_f = al("rh_f", [128, 64], F32)
    lg = al("lg", [128, 16], F32)
    pex = al("pex", [128, 16], F32)
    s1 = al("s1", [128, 16], F32)
    mskd = al("mskd", [128, 16], F32)
    comb = al("comb", [128, 16], F32)
    het = al("het", [128, 64], BF16)
    ccol = al("ccol", [128, 8], F32)

    ps = ctx["PS"] if ctx else nc.alloc_psum_tensor("ps", [128, 4096], F32)
    ps_trv = ps.bitcast(BF16)[:, 0:128]
    ps_aux = ps[:, 512:592]

    ZS, ZSS, RSTD, NEGM, MEAN, VAR = 0, 8, 16, 24, 32, 40

    def ln_stats(tt, ztile):
        p.wait("vector", p.tok("ac"))
        p.wait("scalar", p.tok("dv"))
        t1 = p.op("vector", lambda e: e.reduce_sum(
            out=stat[:, ZS + tt:ZS + tt + 1], in_=ztile, axis=AX.X), inc="dv")
        t2 = p.op("scalar", lambda e: e.activation(
            out=stg[:, 0:DM], in_=ztile, func=AF.Square,
            bias=0.0, scale=1.0, accum_out=stat[:, ZSS + tt:ZSS + tt + 1]), inc="ac")
        p.wait("vector", t1)
        p.wait("vector", t2)
        t3 = p.op("vector", lambda e: e.tensor_scalar(
            out=stat[:, MEAN:MEAN + 1], in0=stat[:, ZS + tt:ZS + tt + 1],
            scalar1=1.0 / DM, scalar2=None, op0=ALU.mult), inc="dv")
        p.wait("vector", t3)
        t4 = p.op("vector", lambda e: e.tensor_scalar(
            out=stat[:, VAR:VAR + 1], in0=stat[:, ZSS + tt:ZSS + tt + 1],
            scalar1=1.0 / DM, scalar2=None, op0=ALU.mult), inc="dv")
        p.wait("scalar", t3)
        t5 = p.op("scalar", lambda e: e.activation(
            out=stat[:, MEAN + 1:MEAN + 2], in_=stat[:, MEAN:MEAN + 1],
            func=AF.Square, bias=0.0, scale=1.0), inc="ac")
        p.wait("vector", t4)
        p.wait("vector", t5)
        t6 = p.op("vector", lambda e: e.tensor_tensor(
            out=stat[:, VAR:VAR + 1], in0=stat[:, VAR:VAR + 1],
            in1=stat[:, MEAN + 1:MEAN + 2], op=ALU.subtract), inc="dv")
        p.wait("vector", t6)
        t6b = p.op("vector", lambda e: e.tensor_scalar(
            out=stat[:, VAR:VAR + 1], in0=stat[:, VAR:VAR + 1],
            scalar1=LN_EPS, scalar2=None, op0=ALU.add), inc="dv")
        p.wait("scalar", t6b)
        t7 = p.op("scalar", lambda e: e.activation(
            out=stat[:, VAR + 1:VAR + 2], in_=stat[:, VAR:VAR + 1],
            func=AF.Ln, bias=0.0, scale=1.0), inc="ac")
        p.wait("scalar", t7)
        t8 = p.op("scalar", lambda e: e.activation(
            out=stat[:, RSTD + tt:RSTD + tt + 1], in_=stat[:, VAR + 1:VAR + 2],
            func=AF.Exp, bias=0.0, scale=-0.5), inc="ac")
        p.wait("vector", t8)
        p.wait("scalar", t8)
        t9 = p.op("vector", lambda e: e.tensor_tensor(
            out=stat[:, NEGM + tt:NEGM + tt + 1], in0=stat[:, MEAN:MEAN + 1],
            in1=stat[:, RSTD + tt:RSTD + tt + 1], op=ALU.mult), inc="dv")
        p.wait("vector", t9)
        t10 = p.op("vector", lambda e: e.tensor_scalar(
            out=stat[:, NEGM + tt:NEGM + tt + 1], in0=stat[:, NEGM + tt:NEGM + tt + 1],
            scalar1=-1.0, scalar2=None, op0=ALU.mult), inc="dv")
        return t10

    p.dma("sync", identb[:], ident, "lc")
    mzt = p.op("vector", lambda e: e.memset(ones1b[:], 1.0), inc="dv")
    t = p.dma("sync", scr[:, 0:2560].rearrange("p (c r) -> p c r", c=32),
              art.rearrange("(c p) r -> p c r", p=128), "la")
    p.wait("gpsimd", t)
    ct = p.op("gpsimd", lambda e: e.tensor_copy(out=artb[:], in_=scr[:, 0:2560]),
              inc="pool")
    p.wait("sync", ct)

    ygt = xgt_alias
    p.wait("tensor", p.tok("lc"))
    for tt in range(8):
        p.dma("sync", z[:, tt * DM:(tt + 1) * DM], x2_src(tt), f"lz{tt % 2}")
        for dsl, srcfn in yg_srcs(tt):
            lt = p.dma("sync", ygt[:, dsl], srcfn, f"ly{tt % 2}")
        p.wait("tensor", lt)
        p.wait("vector", p.tok("xtw0"))
        p.wait("vector", p.tok("xtw1"))
        for c in range(32):
            trt = p.op("tensor", lambda e, c=c: e.transpose(
                out=ps_trv, in_=ygt[:, c * 128:(c + 1) * 128], identity=identb[:]),
                inc="pe")
            p.wait("vector", trt)
            cpt = p.op("vector", lambda e, c=c: e.tensor_copy(
                out=stg[:, c * 128:(c + 1) * 128], in_=ps_trv), inc="dv")
            p.wait("tensor", cpt)
        p.wait("sync", p.tok("dv"))
        sttk = p.dma("sync", ygTd[:, :, tt * 128:(tt + 1) * 128].transpose([1, 0, 2]),
                     stg.rearrange("p (c i) -> p c i", c=32), f"xtw{tt % 2}")
        p.wait("sync", p.tok("ac"))
        p.wait("sync", p.tok("pe"))
    p.wait("sync", p.tok("xtw0"))
    p.wait("sync", p.tok("xtw1"))
    p.wait("scalar", p.tok("xtw0"))
    p.wait("scalar", p.tok("xtw1"))
    wu = [None, None]
    ev_tok = None
    xu = None
    for osp in range(8):
        for dg in range(8):
            db = dg % 2
            if wu[db] is not None:
                p.wait("sync", wu[db])
            wl = p.dma("sync", wfs[:, db * 2048:(db + 1) * 2048]
                       .rearrange("p (c o) -> p c o", c=4),
                       woT[dg * 512:(dg + 1) * 512, osp * 512:(osp + 1) * 512]
                       .rearrange("(c p) o -> p c o", p=128), f"wl{db}")
            p.wait("scalar", wu[db])
            xl = p.op("scalar", lambda e, dg=dg, db=db: e.dma_start(
                out=xw[:, db * 4096:(db + 1) * 4096].rearrange("p (c m) -> p c m", c=4),
                in_=ygTd[dg * 4:(dg + 1) * 4].transpose([1, 0, 2])), inc=f"xl{db}", k=16)
            p.wait("gpsimd", wl)
            p.wait("gpsimd", wu[db])
            wc = p.op("gpsimd", lambda e, db=db: e.tensor_copy(
                out=wbs[:, db * 2048:(db + 1) * 2048],
                in_=wfs[:, db * 2048:(db + 1) * 2048]), inc="pool")
            p.wait("tensor", wc)
            p.wait("tensor", xl)
            if dg < 2 and ev_tok is not None:
                p.wait("tensor", ev_tok)
            for dc in range(4):
                for tt in range(8):
                    mmt = p.op("tensor", lambda e, tt=tt, dc=dc, db=db, dg=dg: e.matmul(
                        out=ps[:, tt * 512:(tt + 1) * 512],
                        lhsT=xw[:, db * 4096 + dc * 1024 + tt * 128:
                                   db * 4096 + dc * 1024 + (tt + 1) * 128],
                        rhs=wbs[:, db * 2048 + dc * 512: db * 2048 + (dc + 1) * 512],
                        start=(dg == 0 and dc == 0), stop=(dg == 7 and dc == 3)),
                        inc="pe" if (tt == 7 and dc == 3) else None)
            wu[db] = mmt
        for tt in range(8):
            p.wait("vector", p.tok("pe"))
            p.wait("vector", p.tok("lz0"))
            p.wait("vector", p.tok("lz1"))
            ev_tok = p.op("vector", lambda e, tt=tt, osp=osp: e.tensor_tensor(
                out=z[:, tt * DM + osp * 512: tt * DM + (osp + 1) * 512],
                in0=ps[:, tt * 512:(tt + 1) * 512],
                in1=z[:, tt * DM + osp * 512: tt * DM + (osp + 1) * 512],
                op=ALU.add), inc="dv")
    x3b = xgt_alias
    aux_ev = None
    stg_user = None
    for tt in range(8):
        p.wait("vector", p.tok("dv"))
        p.wait("scalar", stg_user)
        sttok = ln_stats(tt, z[:, tt * DM:(tt + 1) * DM])
        p.wait("scalar", sttok)
        p.wait("scalar", p.tok("pool"))
        x3t = p.op("scalar", lambda e, tt=tt: e.activation(
            out=x3b, in_=z[:, tt * DM:(tt + 1) * DM], func=AF.Identity,
            bias=stat[:, NEGM + tt:NEGM + tt + 1],
            scale=stat[:, RSTD + tt:RSTD + tt + 1]), inc="ac")
        p.wait("tensor", x3t)
        if stg_user is not None:
            p.wait("vector", stg_user)
        for c in range(32):
            trt = p.op("tensor", lambda e, c=c: e.transpose(
                out=ps_trv, in_=x3b[:, c * 128:(c + 1) * 128], identity=identb[:]),
                inc="pe")
            p.wait("vector", trt)
            cpt = p.op("vector", lambda e, c=c: e.tensor_copy(
                out=stg[:, c * 128:(c + 1) * 128], in_=ps_trv), inc="dv")
            p.wait("tensor", cpt)
        p.wait("sync", p.tok("dv"))
        stg_user = p.dma("sync", xT3d[:, :, tt * 128:(tt + 1) * 128].transpose([1, 0, 2]),
                         stg.rearrange("p (c i) -> p c i", c=32), f"xt3{tt % 2}")
        p.wait("tensor", aux_ev)
        for d in range(32):
            amt = p.op("tensor", lambda e, d=d: e.matmul(
                out=ps_aux, lhsT=stg[:, d * 128:(d + 1) * 128],
                rhs=artb[:, d * 80:(d + 1) * 80],
                start=(d == 0), stop=(d == 31)), inc="pe" if d == 31 else None)
        p.wait("scalar", amt)
        a1 = p.op("scalar", lambda e: e.copy(out=rh_f[:], in_=ps_aux[:, 0:64]), inc="ac")
        aux_ev = p.op("scalar", lambda e: e.copy(out=lg[:], in_=ps_aux[:, 64:80]), inc="ac")
        p.wait("vector", aux_ev)
        q1 = p.op("vector", lambda e: e.reduce_max(out=ccol[:, 0:1], in_=lg[:],
                                                   axis=AX.X, negate=True), inc="dv")
        p.wait("scalar", q1)
        q2 = p.op("scalar", lambda e: e.activation(out=pex[:], in_=lg[:], func=AF.Exp,
                                                   bias=ccol[:, 0:1], scale=1.0,
                                                   accum_out=ccol[:, 1:2]), inc="ac")
        p.wait("vector", q2)
        q3 = p.op("vector", lambda e: e.tensor_scalar(out=ccol[:, 2:3], in0=ccol[:, 0:1],
                                                      scalar1=-1.0, scalar2=None,
                                                      op0=ALU.mult), inc="dv")
        p.wait("scalar", q3)
        q4 = p.op("scalar", lambda e: e.activation(out=s1[:], in_=lg[:], func=AF.Sign,
                                                   bias=ccol[:, 2:3], scale=-1.0), inc="ac")
        p.wait("vector", q4)
        q5 = p.op("vector", lambda e: e.tensor_tensor(
            out=mskd[:], in0=lg[:], in1=s1[:], op=ALU.mult), inc="dv")
        q5b = p.op("vector", lambda e: e.tensor_scalar(
            out=s1[:], in0=s1[:], scalar1=-1.0, scalar2=64.0,
            op0=ALU.add, op1=ALU.mult), inc="dv")
        p.wait("vector", q5b)
        q6 = p.op("vector", lambda e: e.tensor_tensor(out=mskd[:], in0=mskd[:],
                                                      in1=s1[:], op=ALU.add), inc="dv")
        p.wait("vector", q6)
        q7 = p.op("vector", lambda e: e.reduce_max(out=ccol[:, 3:4], in_=mskd[:],
                                                   axis=AX.X), inc="dv")
        p.wait("scalar", q7)
        q8 = p.op("scalar", lambda e: e.activation(out=s1[:], in_=lg[:], func=AF.Sign,
                                                   bias=ccol[:, 3:4], scale=-1.0), inc="ac")
        p.wait("vector", q8)
        q9 = p.op("vector", lambda e: e.tensor_scalar(out=s1[:], in0=s1[:],
                                                      scalar1=-1.0, scalar2=1.0,
                                                      op0=ALU.mult, op1=ALU.add), inc="dv")
        p.wait("vector", q9)
        q10 = p.op("vector", lambda e: e.tensor_scalar(out=s1[:], in0=s1[:],
                                                       scalar1=1.0, scalar2=None,
                                                       op0=ALU.min), inc="dv")
        p.wait("vector", q10)
        q11 = p.op("vector", lambda e: e.tensor_tensor(out=pex[:], in0=pex[:],
                                                       in1=s1[:], op=ALU.mult), inc="dv")
        p.wait("vector", q11)
        q12 = p.op("vector", lambda e: e.reduce_sum(out=ccol[:, 4:5], in_=pex[:],
                                                    axis=AX.X), inc="dv")
        p.wait("vector", q12)
        q13 = p.op("vector", lambda e: e.scalar_tensor_tensor(
            out=ccol[:, 5:6], in0=ccol[:, 1:2], scalar=1e-6, in1=ccol[:, 4:5],
            op0=ALU.mult, op1=ALU.add), inc="dv")
        p.wait("vector", q13)
        q14 = p.op("vector", lambda e: e.reciprocal(out=ccol[:, 5:6], in_=ccol[:, 5:6]),
                   inc="dv")
        p.wait("vector", q14)
        q15 = p.op("vector", lambda e: e.tensor_scalar(out=ccol[:, 5:6], in0=ccol[:, 5:6],
                                                       scalar1=0.5, scalar2=None,
                                                       op0=ALU.mult), inc="dv")
        p.wait("vector", q15)
        q16 = p.op("vector", lambda e: e.tensor_scalar(out=comb[:], in0=pex[:],
                                                       scalar1=ccol[:, 5:6], scalar2=None,
                                                       op0=ALU.mult), inc="dv")
        p.wait("vector", a1)
        p.wait("vector", q16)
        for ee in range(16):
            he = p.op("vector", lambda e, ee=ee: e.tensor_scalar(
                out=het[:], in0=rh_f[:], scalar1=comb[:, ee:ee + 1], scalar2=None,
                op0=ALU.mult), inc="dv")
            p.wait("tensor", he)
            trt = p.op("tensor", lambda e: e.transpose(
                out=ps_trv[0:64, :], in_=het[:], identity=identb[:]), inc="pe")
            p.wait("scalar", trt)
            hct = p.op("scalar", lambda e, ee=ee, tt=tt: e.copy(
                out=hgT[(ee % 2) * 64:(ee % 2) * 64 + 64,
                        (ee // 2) * 1024 + tt * 128: (ee // 2) * 1024 + (tt + 1) * 128],
                in_=ps_trv[0:64, :]), inc="ac")
            p.wait("vector", hct)
            p.wait("tensor", hct)
    p.wait("sync", p.tok("xt30"))
    p.wait("sync", p.tok("xt31"))
    p.wait("scalar", p.tok("xt30"))
    p.wait("scalar", p.tok("xt31"))
    wu = [None, None]
    ev_tok = None
    bp_user = None
    xu = None
    prev_bmt = None
    for osp in range(8):
        p.wait("sync", p.tok("dv"))
        brl = p.dma("sync", scr[0:1, 1536:1536 + 512],
                    b1[0:1, osp * 512:(osp + 1) * 512], f"brl{osp % 2}")
        p.wait("scalar", brl)
        p.wait("scalar", prev_bmt)
        brc = p.op("scalar", lambda e: e.copy(out=brb[:], in_=scr[0:1, 1536:1536 + 512]),
                   inc="ac")
        for dg in range(8):
            db = dg % 2
            if wu[db] is not None:
                p.wait("sync", wu[db])
            wl = p.dma("sync", wfs[:, db * 2048:(db + 1) * 2048]
                       .rearrange("p (c o) -> p c o", c=4),
                       w1T[dg * 512:(dg + 1) * 512, osp * 512:(osp + 1) * 512]
                       .rearrange("(c p) o -> p c o", p=128), f"w2{db}")
            p.wait("scalar", wu[db])
            xl = p.op("scalar", lambda e, dg=dg, db=db: e.dma_start(
                out=xw[:, db * 4096:(db + 1) * 4096].rearrange("p (c m) -> p c m", c=4),
                in_=xT3d[dg * 4:(dg + 1) * 4].transpose([1, 0, 2])), inc=f"x2l{db}", k=16)
            p.wait("gpsimd", wl)
            p.wait("gpsimd", wu[db])
            wc = p.op("gpsimd", lambda e, db=db: e.tensor_copy(
                out=wbs[:, db * 2048:(db + 1) * 2048],
                in_=wfs[:, db * 2048:(db + 1) * 2048]), inc="pool")
            p.wait("tensor", wc)
            p.wait("tensor", xl)
            if dg < 2 and ev_tok is not None:
                p.wait("tensor", ev_tok)
            for dc in range(4):
                for tt in range(8):
                    mmt = p.op("tensor", lambda e, tt=tt, dc=dc, db=db, dg=dg: e.matmul(
                        out=ps[:, tt * 512:(tt + 1) * 512],
                        lhsT=xw[:, db * 4096 + dc * 1024 + tt * 128:
                                   db * 4096 + dc * 1024 + (tt + 1) * 128],
                        rhs=wbs[:, db * 2048 + dc * 512: db * 2048 + (dc + 1) * 512],
                        start=(dg == 0 and dc == 0), stop=False),
                        inc="pe" if (tt == 7 and dc == 3) else None)
            wu[db] = mmt
        p.wait("sync", p.tok("pe"))
        for fp in range(4):
            for ff in range(2):
                f = fp * 2 + ff
                blt = p.dma("sync", scr[:, ff * 512:(ff + 1) * 512],
                            bpT[f][:, osp * 512:(osp + 1) * 512], f"lb{f % 2}")
                p.wait("gpsimd", blt)
                bctk = p.op("gpsimd", lambda e, ff=ff: e.tensor_copy(
                    out=bp_b[:, ff * 512:(ff + 1) * 512],
                    in_=scr[:, ff * 512:(ff + 1) * 512]), inc="pool")
            p.wait("tensor", bctk)
            for tt in range(8):
                dmm = p.op("tensor", lambda e, tt=tt, fp=fp: e.matmul(
                    out=ps[:, tt * 512:(tt + 1) * 512],
                    lhsT=hgT[:, (fp * 2) * 1024 + tt * 128: (fp * 2) * 1024 + (tt + 1) * 128],
                    rhs=bp_b[:, 0:512],
                    start=False, stop=False), inc=None)
                dmm = p.op("tensor", lambda e, tt=tt, fp=fp: e.matmul(
                    out=ps[:, tt * 512:(tt + 1) * 512],
                    lhsT=hgT[:, (fp * 2 + 1) * 1024 + tt * 128: (fp * 2 + 1) * 1024 + (tt + 1) * 128],
                    rhs=bp_b[:, 512:1024],
                    start=False, stop=False),
                    inc="pe" if tt == 7 else None)
            p.wait("sync", dmm)
        for tt in range(8):
            if tt == 0:
                p.wait("tensor", brc)
            bmt = p.op("tensor", lambda e, tt=tt: e.matmul(
                out=ps[:, tt * 512:(tt + 1) * 512],
                lhsT=ones1b[:], rhs=brb[:],
                start=False, stop=True), inc="pe")
            prev_bmt = bmt
            p.wait("scalar", bmt)
            x3o = p.op("scalar", lambda e, tt=tt, osp=osp: e.activation(
                out=scr[:, 1024:1536], in_=z[:, tt * DM + osp * 512: tt * DM + (osp + 1) * 512],
                func=AF.Identity, bias=stat[:, NEGM + tt:NEGM + tt + 1],
                scale=stat[:, RSTD + tt:RSTD + tt + 1]), inc="ac")
            p.wait("vector", x3o)
            ev_tok = p.op("vector", lambda e, tt=tt, osp=osp: e.tensor_tensor(
                out=z[:, tt * DM + osp * 512: tt * DM + (osp + 1) * 512],
                in0=ps[:, tt * 512:(tt + 1) * 512], in1=scr[:, 1024:1536],
                op=ALU.add), inc="dv")
            p.wait("scalar", ev_tok)
    out_f = None
    ot_prev = None
    for tt in range(8):
        p.wait("vector", p.tok("dv"))
        p.wait("scalar", p.tok("dv"))
        p.wait("scalar", p.tok("pool"))
        sttok = ln_stats(tt, z[:, tt * DM:(tt + 1) * DM])
        p.wait("scalar", sttok)
        slots = [scr[:, 0:512], scr[:, 512:1024]]
        for oq in range(8):
            sl = slots[oq % 2]
            p.wait("scalar", ot_prev[oq % 2] if isinstance(ot_prev, list) else ot_prev)
            oft = p.op("scalar", lambda e, tt=tt, oq=oq, sl=sl: e.activation(
                out=sl, in_=z[:, tt * DM + oq * 512: tt * DM + (oq + 1) * 512],
                func=AF.Identity,
                bias=stat[:, NEGM + tt:NEGM + tt + 1],
                scale=stat[:, RSTD + tt:RSTD + tt + 1]), inc="ac")
            p.wait("sync", oft)
            tokx = p.dma("sync",
                         x4o[tt * 128:(tt + 1) * 128, oq * 512:(oq + 1) * 512],
                         sl, f"ox{oq % 2}")
            if not isinstance(ot_prev, list):
                ot_prev = [None, None]
            ot_prev[oq % 2] = tokx
    p.wait("sync", p.tok("ox0"))
    p.wait("sync", p.tok("ox1"))
    if ctx is None:
        p.emit()
        return nc


def stage_D(inputs, x2_full, yg_full):
    ident = np.eye(128, dtype=BF)
    woT = np.ascontiguousarray(np.asarray(inputs["L1_attn_Wo"]).T)
    w1T = np.ascontiguousarray(np.asarray(inputs["L1_ffn_base_w"]).T)
    A = np.asarray(inputs["L1_ffn_A"])
    rw = np.asarray(inputs["L1_ffn_router_w"])
    art = np.ascontiguousarray(np.concatenate([A.T, rw.T], axis=1))
    Bm = np.asarray(inputs["L1_ffn_B"])
    bpT = np.ascontiguousarray(
        np.stack([np.concatenate([Bm[2 * f].T, Bm[2 * f + 1].T], axis=0)
                  for f in range(8)]))
    b1 = np.ascontiguousarray(np.asarray(inputs["L1_ffn_base_b"])[None, :])
    in_maps = []
    for c in range(NCORE):
        b, s = c // 2, c % 2
        tsl = slice(s * 1024, (s + 1) * 1024)
        in_maps.append({
            "x2": np.ascontiguousarray(x2_full[b][tsl]),
            "yg": np.ascontiguousarray(yg_full[b][tsl]),
            "woT": woT, "w1T": w1T, "b1": b1, "art": art, "bpT": bpT,
            "ident": ident,
        })
    return in_maps




MAGIC = [0x5A5A0001, 0x5A5A0002, 0x5A5A0003]


def build_fused():
    nc = bass.Bass("TRN2", target_bir_lowering=False, num_devices=NCORE,
                   detect_race_conditions=False)
    p = Prog(nc)
    EI = lambda n, s, d: nc.dram_tensor(n, s, d, kind="ExternalInput").ap()
    x0 = EI("x0", [L, DM], F32)
    x0own = EI("x0own", [TPC, DM], F32)
    wT = {k: EI(f"w{k}T", [DM, 2048], F32) for k in "qkv"}
    bias = {k: EI(f"b{k}", [2048], F32) for k in "qkv"}
    bwT = EI("bwT", [3, 16], F32)
    bivmat = EI("bivmat", [3, 64], F32)
    revtile = EI("revtile", [16, 64], F32)
    scdelta = EI("scdelta", [16, 64], F32)
    ident = EI("ident", [128, 128], BF16)
    identf = EI("identf", [128, 128], F32)
    trif = EI("trif", [128, 128], F32)
    w0T = EI("w0T", [DM, DM], F32)
    bias0 = EI("bias0", [1, DM], F32)
    a0T = EI("a0T", [DM, 256], F32)
    b0cat = EI("b0cat", [256, DM], F32)
    coefr = EI("coefr", [256], F32)
    sclr = EI("sclr", [256], F32)
    wq1T = EI("wq1T", [DM, 2048], F32)
    wk1T = EI("wk1T", [DM, 512], F32)
    wv1T = EI("wv1T", [DM, 512], F32)
    wg1T = EI("wg1T", [DM, 2048], F32)
    win1T = EI("win1T", [DM, 16], F32)
    dtb = EI("dtb", [1, 16], F32)
    alog = EI("alog", [1, 16], F32)
    gnw = EI("gnw", [1, 128], F32)
    cosT = EI("cosT", [128, L], F32)
    sinT = EI("sinT", [128, L], F32)
    woT = EI("woT", [DM, DM], F32)
    w1T = EI("w1T", [DM, DM], F32)
    b1 = EI("b1", [1, DM], F32)
    art = EI("art", [DM, 80], F32)
    bpT = EI("bpT", [8, 128, DM], F32)
    svin = EI("sv", [1, 1], U32)
    x4o = nc.dram_tensor("x4o", [TPC, DM], F32, kind="ExternalOutput").ap()
    sh_attn = nc.dram_tensor("sh_attn", [2, 2, 1024, 2048], BF16,
                             kind="Internal", addr_space="Shared").ap()
    sh_x2 = nc.dram_tensor("sh_x2", [2, 1024, DM], F32,
                           kind="Internal", addr_space="Shared").ap()
    sh_yg = nc.dram_tensor("sh_yg", [2, 2, 1024, 2048], BF16,
                           kind="Internal", addr_space="Shared").ap()
    sh_flag = nc.dram_tensor("sh_flag", [3, 2, 1, 1], U32,
                             kind="Internal", addr_space="Shared").ap()
    PS = nc.alloc_psum_tensor("PS", [128, 4096], F32)

    sv_sb = nc.alloc_sbuf_tensor("sv_sb", [1, 1], U32)
    magic_sb = nc.alloc_sbuf_tensor("magic_sb", [1, 4], U32)
    flag_rx = nc.alloc_sbuf_tensor("flag_rx", [1, 4], U32)
    ARENA0 = (nc.sbuf_base + 512 + 31) // 32 * 32

    def mkal(prefix):
        off = [ARENA0]
        def al(name, shape, dtype):
            szmap = {F32: 4, BF16: 2, U32: 4}
            per = int(np.prod(shape[1:])) * szmap[dtype]
            t = nc.alloc_sbuf_tensor_at(f"{prefix}_{name}", shape, dtype, offset=off[0])
            off[0] += (per + 31) // 32 * 32
            assert off[0] <= nc.sbuf_top, f"{prefix}_{name} arena overflow {off[0]}"
            return t
        return al

    svh = {}
    lvt = p.dma("sync", sv_sb[:], svin, "lsv")
    p.wait("sync", lvt)
    p.raw("sync", lambda e: svh.__setitem__("v", e.value_load(sv_sb[0:1, 0:1])))
    for k in range(3):
        p.op("vector", lambda e, k=k: e.memset(magic_sb[0:1, k:k + 1], MAGIC[k]),
             inc="mgk")
    attn_loc = nc.dram_tensor("attn_loc", [L, 2048], BF16, kind="Internal").ap()
    attn_in = nc.dram_tensor("attn_in", [TPC, DM], BF16, kind="Internal").ap()
    x2_loc = nc.dram_tensor("x2_loc", [TPC, DM], F32, kind="Internal").ap()
    ygo_loc = nc.dram_tensor("ygo_loc", [L, 2048], BF16, kind="Internal").ap()
    yg_loc = nc.dram_tensor("yg_loc", [TPC, DM], BF16, kind="Internal").ap()

    def set_flag(k):
        p.wait("sync", p.tok("mgk"))
        p.dma("sync", lambda k=k: sh_flag[k][svh["v"]], magic_sb[0:1, k:k + 1],
              f"fw{k}")
        p.wait("sync", p.tok(f"fw{k}"))

    def poll_flag(k):
        psem = p.sem(f"poll{k}")
        def go(e, k=k, psem=psem):
            with e.register(f"pf{k}") as f, e.register(f"pc{k}") as cnt, \
                 e.register(f"pn{k}") as nd:
                e.reg_mov(f, 0)
                e.reg_mov(cnt, 0)
                e.reg_mov(nd, 1)
                with e.While(nd):
                    e.dma_start(out=flag_rx[0:1, k:k + 1],
                                in_=sh_flag[k][1 - svh["v"]]).then_inc(psem, 16)
                    e.reg_add(cnt, cnt, 16)
                    e.wait_ge(psem, cnt)
                    e.load(f, flag_rx[0:1, k:k + 1])
                    e.reg_alu(nd, f, MAGIC[k], ALU.not_equal)
        p.raw("sync", go)

    ctxA = dict(nc=nc, p=p, al=mkal("A"), PS=PS, x0=x0, wT=wT, bias=bias,
                bwT=bwT, bivmat=bivmat, revtile=revtile, scdelta=scdelta,
                ident=ident,
                attn_dst=lambda qt, g: (lambda qt=qt, g=g: attn_loc
                                        [qt * 128:(qt + 1) * 128,
                                         g * 512:(g + 1) * 512]))
    build_launch_A(ctxA)
    pub0 = p.dma("sync", lambda: sh_attn[svh["v"]],
                 attn_loc.rearrange("(a b) c -> a b c", b=1024), "pub0")
    p.wait("sync", pub0)
    set_flag(0)
    p.barrier("AB")
    poll_flag(0)
    g0 = p.dma("sync", attn_in[:, 0:2048], lambda: sh_attn[0][svh["v"]], "gat0")
    g1 = p.dma("sync", attn_in[:, 2048:4096], lambda: sh_attn[1][svh["v"]], "gat1")
    p.wait("sync", g0)
    p.wait("sync", g1)
    ctxB = dict(nc=nc, p=p, al=mkal("B"), PS=PS, x0own=x0own, w0T=w0T,
                bias0=bias0, a0T=a0T, b0cat=b0cat, coefr=coefr, sclr=sclr,
                ident=ident,
                attn_srcs=lambda tt: [(slice(0, DM),
                                       lambda tt=tt: attn_in[tt * 128:(tt + 1) * 128, :])],
                x2_dst=lambda tt: (lambda tt=tt: x2_loc[tt * 128:(tt + 1) * 128, :]))
    build_launch_B(ctxB)
    pub1 = p.dma("sync", lambda: sh_x2[svh["v"]], x2_loc, "pub1")
    p.wait("sync", pub1)
    set_flag(1)
    p.barrier("BC")
    poll_flag(1)
    ctxC = dict(nc=nc, p=p, al=mkal("C"), PS=PS,
                x2f=sh_x2.rearrange("a b c -> (a b) c"),
                wq1T=wq1T, wk1T=wk1T, wv1T=wv1T, wg1T=wg1T, win1T=win1T,
                dtb=dtb, alog=alog, gnw=gnw, cosT=cosT, sinT=sinT,
                ident=ident, identf=identf, trif=trif,
                yg_dst=lambda n, kv: (lambda n=n, kv=kv: ygo_loc
                                      [n * 128:(n + 1) * 128,
                                       kv * 512:(kv + 1) * 512]))
    build_launch_C(ctxC)
    pub2 = p.dma("sync", lambda: sh_yg[svh["v"]],
                 ygo_loc.rearrange("(a b) c -> a b c", b=1024), "pub2")
    p.wait("sync", pub2)
    set_flag(2)
    p.barrier("CD")
    poll_flag(2)
    g2 = p.dma("sync", yg_loc[:, 0:2048], lambda: sh_yg[0][svh["v"]], "gyg0")
    g3 = p.dma("sync", yg_loc[:, 2048:4096], lambda: sh_yg[1][svh["v"]], "gyg1")
    p.wait("sync", g2)
    p.wait("sync", g3)
    ctxD = dict(nc=nc, p=p, al=mkal("D"), PS=PS, woT=woT, w1T=w1T, b1=b1,
                art=art, bpT=bpT, ident=ident, x4o=x4o,
                x2_src=lambda tt: (lambda tt=tt: x2_loc[tt * 128:(tt + 1) * 128, :]),
                yg_srcs=lambda tt: [(slice(0, DM),
                                     lambda tt=tt: yg_loc[tt * 128:(tt + 1) * 128, :])])
    build_launch_D(ctxD)
    p.emit()
    return nc


def stage_fused(inputs):
    mA = stage_A(inputs)
    dummy_attn = np.zeros((B, L, DM), BF)
    dummy_x2 = np.zeros((B, L, DM), np.float32)
    dummy_yg = np.zeros((B, L, DM), BF)
    mB = stage_B(inputs, dummy_attn, inputs["L0_attn_x"])
    mC = stage_C(inputs, dummy_x2)
    mD = stage_D(inputs, dummy_x2, dummy_yg)
    in_maps = []
    for c in range(NCORE):
        m = {}
        m.update({k: v for k, v in mA[c].items()})
        m["x0own"] = mB[c]["x0"]
        for k in ["w0T", "bias0", "a0T", "b0cat", "coefr", "sclr"]:
            m[k] = mB[c][k]
        for k_new, k_old in [("wq1T", "wqT"), ("wk1T", "wkT"), ("wv1T", "wvT"),
                             ("wg1T", "wgT"), ("win1T", "winT")]:
            m[k_new] = mC[c][k_old]
        for k in ["dtb", "alog", "gnw", "cosT", "sinT", "identf", "trif"]:
            m[k] = mC[c][k]
        for k in ["woT", "w1T", "b1", "art", "bpT"]:
            m[k] = mD[c][k]
        m["sv"] = np.array([[c % 2]], dtype=np.uint32)
        in_maps.append(m)
    return in_maps


def kernel_fused(**inputs):
    inputs = {k: np.asarray(v) for k, v in inputs.items()}
    nc = _get_nc("F", build_fused)
    res = run_bass_kernel_spmd(nc, stage_fused(inputs), core_ids=list(range(NCORE)))
    out = np.concatenate(
        [np.concatenate([np.asarray(res.results[2 * b]["x4o"]),
                         np.asarray(res.results[2 * b + 1]["x4o"])], axis=0)[None]
         for b in range(B)], axis=0)
    return out


_NC_CACHE = {}


def _get_nc(name, builder):
    if name not in _NC_CACHE:
        _NC_CACHE[name] = builder()
    return _NC_CACHE[name]


def kernel(**inputs):
    return kernel_fused(**inputs)


def kernel_unfused(**inputs):
    inputs = {k: np.asarray(v) for k, v in inputs.items()}

    ncA = _get_nc("A", build_launch_A)
    resA = run_bass_kernel_spmd(ncA, stage_A(inputs), core_ids=list(range(NCORE)))
    attn_full = np.concatenate(
        [np.concatenate([np.asarray(resA.results[2 * b]["attn_out"]),
                         np.asarray(resA.results[2 * b + 1]["attn_out"])], axis=1)[None]
         for b in range(B)], axis=0)

    ncB = _get_nc("B", build_launch_B)
    resB = run_bass_kernel_spmd(ncB, stage_B(inputs, attn_full, inputs["L0_attn_x"]),
                                core_ids=list(range(NCORE)))
    x2_full = np.concatenate(
        [np.concatenate([np.asarray(resB.results[2 * b]["x2o"]),
                         np.asarray(resB.results[2 * b + 1]["x2o"])], axis=0)[None]
         for b in range(B)], axis=0)

    ncC = _get_nc("C", build_launch_C)
    resC = run_bass_kernel_spmd(ncC, stage_C(inputs, x2_full), core_ids=list(range(NCORE)))
    yg_full = np.concatenate(
        [np.concatenate([np.asarray(resC.results[2 * b]["ygo"]),
                         np.asarray(resC.results[2 * b + 1]["ygo"])], axis=1)[None]
         for b in range(B)], axis=0)

    ncD = _get_nc("D", build_launch_D)
    resD = run_bass_kernel_spmd(ncD, stage_D(inputs, x2_full, yg_full),
                                core_ids=list(range(NCORE)))
    out = np.concatenate(
        [np.concatenate([np.asarray(resD.results[2 * b]["x4o"]),
                         np.asarray(resD.results[2 * b + 1]["x4o"])], axis=0)[None]
         for b in range(B)], axis=0)
    return out


# version 06 (final)
# speedup vs baseline: 1.1358x; 1.0925x over previous
import math
import numpy as np
import ml_dtypes
import concourse.bass as bass
import concourse.mybir as mybir
from concourse.bass_utils import run_bass_kernel_spmd

F32 = mybir.dt.float32
BF16 = mybir.dt.bfloat16
U32 = mybir.dt.uint32
AF = mybir.ActivationFunctionType
ALU = mybir.AluOpType
AX = mybir.AxisListType

DM = 4096
B = 4
L = 2048
NCORE = 8
TPC = 1024
NH0, CD0, NB0 = 16, 32, 8
H1, KVH1, D1 = 32, 8, 128
CHUNK = 128
LN_EPS = 1e-5
RMS_EPS = 1e-5
BF = ml_dtypes.bfloat16
ENGS = ("tensor", "vector", "scalar", "gpsimd", "sync")


class Prog:

    def __init__(self, nc):
        self.nc = nc
        self.qs = {e: [] for e in ENGS}
        self.cnt = {}
        self.handles = {}
        self.lastwait = {e: {} for e in ENGS}

    def sem(self, name):
        if name not in self.handles:
            self.handles[name] = self.nc.alloc_semaphore(name)
            self.cnt[name] = 0
        return self.handles[name]

    def op(self, eng, fn, inc=None, k=1):
        if inc is not None:
            h = self.sem(inc)
            self.cnt[inc] += k
            val = self.cnt[inc]
            def run(e, fn=fn, h=h, k=k):
                fn(e).then_inc(h, k)
            self.qs[eng].append(run)
            return (inc, val)
        self.qs[eng].append(fn)
        return None

    def raw(self, eng, fn):
        self.qs[eng].append(fn)

    def barrier(self, tag=""):
        h = self.sem("bar")
        for e in ENGS:
            self.qs[e].append(lambda eng, h=h: eng.nop().then_inc(h, 1))
        self.cnt["bar"] += len(ENGS)
        tgt = self.cnt["bar"]
        for e in ENGS:
            self.qs[e].append(lambda eng, h=h, v=tgt: eng.wait_ge(h, v))
            self.lastwait[e]["bar"] = tgt

    def wait(self, eng, tok):
        if tok is None:
            return
        name, val = tok
        if val <= 0:
            return
        lw = self.lastwait[eng]
        if lw.get(name, -1) >= val:
            return
        lw[name] = val
        h = self.sem(name)
        self.qs[eng].append(lambda e, h=h, v=val: e.wait_ge(h, v))

    def tok(self, name):
        return (name, self.cnt.get(name, 0))

    def dma(self, eng, out, in_, inc, **kw):
        def run(e, o=out, i=in_, kw=kw):
            return e.dma_start(out=o() if callable(o) else o,
                               in_=i() if callable(i) else i, **kw)
        return self.op(eng, run, inc=inc, k=16)

    def emit(self):
        with self.nc.Block() as blk:
            for e in ENGS:
                q = self.qs[e]
                getattr(blk, e)(lambda eng, q=q: [f(eng) for f in q])


def _build_tables():
    dim, n = 3, 8
    masks = sorted(range(n), key=lambda m: (bin(m).count('1'), m))
    inv = {m: i for i, m in enumerate(masks)}
    C = np.zeros((n, n, n), dtype=np.float32)
    for i, a in enumerate(masks):
        for j, b_ in enumerate(masks):
            sw, at = (0, a >> 1)
            while at:
                sw += bin(at & b_).count('1')
                at >>= 1
            s = -1.0 if sw & 1 else 1.0
            C[i, j, inv[a ^ b_]] = s
    grades = np.array([bin(m).count('1') for m in masks])
    rev = np.where(grades * (grades - 1) // 2 % 2 == 1, -1.0, 1.0).astype(np.float32)
    return C, rev


def host_consts_L0():
    C, rev = _build_tables()
    biv = C[:, :, [4, 5, 6]]
    bivmat = np.ascontiguousarray(biv.transpose(2, 0, 1).reshape(3, 64)).astype(np.float32)
    revrep = np.repeat(rev, 8)
    revtile = np.tile(revrep[None, :], (16, 1)).astype(np.float32)
    scale = 1.0 / math.sqrt(CD0 * NB0)
    scdelta = np.tile((np.eye(8, dtype=np.float32).reshape(64) * revrep * scale)[None, :],
                      (16, 1)).astype(np.float32)
    return bivmat, revtile, scdelta


def host_rope_tables(positions):
    D = D1
    inv_freq = 1.0 / (10000.0 ** (np.arange(0, D, 2, dtype=np.float64) / D))
    ang = positions[:, None].astype(np.float64) * inv_freq[None, :]
    cos = np.concatenate([np.cos(ang), np.cos(ang)], axis=-1)
    sin = np.concatenate([np.sin(ang), np.sin(ang)], axis=-1)
    return (np.ascontiguousarray(cos.T).astype(np.float32),
            np.ascontiguousarray(sin.T).astype(np.float32))


def build_launch_A(ctx=None):
    if ctx is None:
        nc = bass.Bass("TRN2", target_bir_lowering=False, num_devices=NCORE)
        p = Prog(nc)
        x0 = nc.dram_tensor("x0", [L, DM], F32, kind="ExternalInput").ap()
        wT = {k: nc.dram_tensor(f"w{k}T", [DM, 2048], F32, kind="ExternalInput").ap()
              for k in "qkv"}
        bias = {k: nc.dram_tensor(f"b{k}", [2048], F32, kind="ExternalInput").ap()
                for k in "qkv"}
        bwT = nc.dram_tensor("bwT", [3, 16], F32, kind="ExternalInput").ap()
        bivmat = nc.dram_tensor("bivmat", [3, 64], F32, kind="ExternalInput").ap()
        revtile = nc.dram_tensor("revtile", [16, 64], F32, kind="ExternalInput").ap()
        scdelta = nc.dram_tensor("scdelta", [16, 64], F32, kind="ExternalInput").ap()
        ident = nc.dram_tensor("ident", [128, 128], BF16, kind="ExternalInput").ap()
        attn_out = nc.dram_tensor("attn_out", [L, 2048], BF16, kind="ExternalOutput").ap()
        attn_dst = lambda qt, g: (lambda: attn_out[qt * 128:(qt + 1) * 128,
                                                   g * 512:(g + 1) * 512])
    else:
        nc, p = ctx["nc"], ctx["p"]
        x0, wT, bias = ctx["x0"], ctx["wT"], ctx["bias"]
        bwT, bivmat, revtile, scdelta, ident = (ctx["bwT"], ctx["bivmat"],
                                                ctx["revtile"], ctx["scdelta"],
                                                ctx["ident"])
        attn_dst = ctx["attn_dst"]
    xT0d = nc.dram_tensor("xT0d", [32, 128, L], BF16, kind="Internal").ap()
    a_scr = nc.dram_tensor("a_scr", [16, 8, 8], BF16, kind="Internal").ap()

    al = ctx["al"] if ctx else nc.alloc_sbuf_tensor
    identb = al("identb", [128, 128], BF16)
    T_sb = al("T_sb", [128, 8 * 128], BF16)
    bcol = al("bcol", [128, 3 * 16], F32)
    xTh = al("xTh", [128, 32 * 1024], BF16)
    wst_f = al("wst_f", [128, 2 * DM], F32)
    wst_b = al("wst_b", [128, 2 * DM], BF16)
    qmodT = al("qmodT", [128, 4 * L], BF16)
    kTg = al("kTg", [128, 4 * L], BF16)
    vtok = al("vtok", [128, 16 * 512], BF16)
    qraw = al("qraw", [128, 1024], BF16)
    vtmp = al("vtmp", [128, 1024], BF16)
    Pb = al("Pb", [128, L], BF16)
    PT4 = al("PT4", [128, 16 * 512], BF16)
    avf = al("avf", [128, 512], BF16)
    outstg = al("outstg", [128, 4 * 512], BF16)
    negmax = al("negmax", [128, 1], F32)
    zrec = al("zrec", [128, 16], F32)
    wk_f = al("wk_f", [16, 64], F32)
    wk_b = al("wk_b", [16, 64], BF16)
    bw_sb = al("bw_sb", [3, 16], F32)
    biv_sb = al("biv_sb", [3, 64], F32)
    rev_sb = al("rev_sb", [16, 64], F32)
    scd_sb = al("scd_sb", [16, 64], F32)

    if ctx is None:
        ps_big = nc.alloc_psum_tensor("ps_big", [128, 2048], F32)
        ps_sm = nc.alloc_psum_tensor("ps_sm", [128, 512], F32)
        ps_tr = nc.alloc_psum_tensor("ps_tr", [128, 128], BF16)
    else:
        PS = ctx["PS"]
        ps_big = PS[:, 0:2048]
        ps_sm = PS[:, 2048:2560]
        ps_tr = PS.bitcast(BF16)[:, 5 * 1024: 5 * 1024 + 128]

    xf_t = xTh.bitcast(F32)[:, 0:DM]
    xb_t = xTh[:, 16384 // 2: 16384 // 2 + DM]
    stg = xTh[:, 24576 // 2: 24576 // 2 + DM]

    p.dma("sync", identb[:], ident, "ld")
    p.dma("sync", bw_sb[:], bwT, "ld")
    p.dma("sync", biv_sb[:], bivmat, "ld")
    p.dma("sync", rev_sb[:], revtile, "ld")
    p.dma("sync", scd_sb[:], scdelta, "ld")
    for j, k in enumerate("qkv"):
        for c in range(16):
            p.dma("sync", bcol[:, j * 16 + c: j * 16 + c + 1],
                  bias[k][c * 128:(c + 1) * 128].unsqueeze(1), "ld")
    p.wait("tensor", p.tok("ld"))
    mm0 = p.op("tensor", lambda e: e.matmul(out=ps_sm.bitcast(F32)[0:16, 0:64] if False else ps_sm[0:16, 0:64],
                                            lhsT=bw_sb[:], rhs=biv_sb[:],
                                            start=True, stop=True), inc="pe")
    p.wait("vector", mm0)
    dvx = p.op("vector", lambda e: e.tensor_tensor(out=wk_f[:], in0=ps_sm[0:16, 0:64],
                                                   in1=rev_sb[:], op=ALU.mult), inc="dv")
    p.wait("vector", dvx)
    dv0 = p.op("vector", lambda e: e.tensor_tensor(out=wk_f[:], in0=wk_f[:],
                                                   in1=scd_sb[:], op=ALU.add), inc="dv")
    p.wait("scalar", dv0)
    ac0 = p.op("scalar", lambda e: e.copy(out=wk_b[:], in_=wk_f[:]), inc="ac")
    p.wait("sync", ac0)
    wr0 = p.dma("sync", a_scr.rearrange("h i j -> h (i j)"), wk_b[:], "ascr")
    mz = p.op("vector", lambda e: e.memset(T_sb[:], 0.0), inc="dv")
    p.wait("sync", wr0)
    p.wait("sync", mz)
    for hh in range(8):
        for d in range(16):
            p.dma("sync", T_sb[d * 8:(d + 1) * 8,
                               hh * 128 + d * 8: hh * 128 + d * 8 + 8],
                  a_scr[hh], "tld")
    t_built = p.tok("tld")

    cast_tok = None
    tr_last = None
    st_tok = None
    for tt in range(16):
        if cast_tok is not None:
            p.wait("sync", cast_tok)
        ltok = p.dma("sync", xf_t, x0[tt * 128:(tt + 1) * 128, :], "ld")
        p.wait("scalar", ltok)
        if tr_last is not None:
            p.wait("scalar", tr_last)
        cast_tok = p.op("scalar", lambda e: e.copy(out=xb_t, in_=xf_t), inc="ac")
        p.wait("tensor", cast_tok)
        if st_tok is not None:
            p.wait("vector", st_tok)
        for c in range(32):
            tr_last = p.op("tensor", lambda e, c=c: e.transpose(
                out=ps_tr[:], in_=xb_t[:, c * 128:(c + 1) * 128], identity=identb[:]),
                inc="pe")
            p.wait("vector", tr_last)
            cpt = p.op("vector", lambda e, c=c: e.tensor_copy(
                out=stg[:, c * 128:(c + 1) * 128], in_=ps_tr[:]), inc="dv")
            p.wait("tensor", cpt)
        p.wait("sync", p.tok("dv"))
        st_tok = p.dma("sync", xT0d[:, :, tt * 128:(tt + 1) * 128],
                       stg.rearrange("p (c i) -> p c i", c=32), f"xtw{tt % 2}")
    p.wait("tensor", t_built)

    wbuf_user = [None, None]
    evac_tok = None
    qm_evac = None
    xTh_user = st_tok
    ext_prev = None
    av_done = None
    store_tok = None
    for g in range(4):
        for mh in range(2):
            if xTh_user is not None:
                p.wait("sync", xTh_user)
            xld = p.dma("sync", xTh.rearrange("p (c m) -> p c m", c=32),
                        xT0d[:, :, mh * 1024:(mh + 1) * 1024], "ld")
            for oc in range(12):
                kind, loc = ("q", oc) if oc < 4 else (("k", oc - 4) if oc < 8 else ("v", oc - 8))
                gch = g * 4 + loc
                db = oc % 2
                p.wait("gpsimd", wbuf_user[db])
                wct = p.dma("gpsimd",
                            wst_b[:, db * DM:(db + 1) * DM].rearrange("p (c o) -> p c o", c=32),
                            wT[kind][:, gch * 128:(gch + 1) * 128]
                            .rearrange("(c p) o -> p c o", p=128), f"wld{db}")
                p.wait("tensor", wct)
                p.wait("tensor", xld)
                if evac_tok is not None:
                    p.wait("tensor", evac_tok)
                for d in range(32):
                    for half in range(2):
                        mmt = p.op("tensor", lambda e, d=d, h=half, db=db: e.matmul(
                            out=ps_big[:, h * 512:(h + 1) * 512],
                            lhsT=wst_b[:, db * DM + d * 128: db * DM + (d + 1) * 128],
                            rhs=xTh[:, d * 1024 + h * 512: d * 1024 + (h + 1) * 512],
                            start=(d == 0), stop=(d == 31)),
                            inc="pe" if (d == 31 and half == 1) else None)
                wbuf_user[db] = mmt
                xTh_user = mmt
                p.wait("scalar", mmt)
                if kind == "q":
                    evac_tok = p.op("scalar", lambda e, gch=gch: e.activation(
                        out=qraw[:], in_=ps_big[:, 0:1024], func=AF.Identity,
                        bias=bcol[:, gch: gch + 1], scale=1.0), inc="ac")
                    p.wait("tensor", evac_tok)
                    if qm_evac is not None:
                        p.wait("tensor", qm_evac)
                    lhead = g * 2 + loc // 2
                    for half in range(2):
                        qmt = p.op("tensor", lambda e, h=half, lh=lhead: e.matmul(
                            out=ps_big[:, 1024 + h * 512: 1024 + (h + 1) * 512],
                            lhsT=T_sb[:, lh * 128:(lh + 1) * 128],
                            rhs=qraw[:, h * 512:(h + 1) * 512],
                            start=True, stop=True),
                            inc="pe" if half == 1 else None)
                    p.wait("scalar", qmt)
                    qm_evac = p.op("scalar", lambda e, loc=loc, mh=mh: e.copy(
                        out=qmodT[:, loc * L + mh * 1024: loc * L + (mh + 1) * 1024],
                        in_=ps_big[:, 1024:2048]), inc="ac")
                elif kind == "k":
                    evac_tok = p.op("scalar", lambda e, gch=gch, loc=loc, mh=mh: e.activation(
                        out=kTg[:, loc * L + mh * 1024: loc * L + (mh + 1) * 1024],
                        in_=ps_big[:, 0:1024], func=AF.Identity,
                        bias=bcol[:, 16 + gch: 16 + gch + 1], scale=1.0), inc="ac")
                else:
                    p.wait("scalar", p.tok("pe"))
                    evac_tok = p.op("scalar", lambda e, gch=gch: e.activation(
                        out=vtmp[:], in_=ps_big[:, 0:1024], func=AF.Identity,
                        bias=bcol[:, 32 + gch: 32 + gch + 1], scale=1.0), inc="ac")
                    p.wait("tensor", evac_tok)
                    for j in range(8):
                        trt = p.op("tensor", lambda e, j=j: e.transpose(
                            out=ps_tr[:], in_=vtmp[:, j * 128:(j + 1) * 128],
                            identity=identb[:]), inc="pe")
                        p.wait("vector", trt)
                        cpt = p.op("vector", lambda e, j=j, mh=mh, loc=loc: e.tensor_copy(
                            out=vtok[:, (mh * 8 + j) * 512 + loc * 128:
                                        (mh * 8 + j) * 512 + (loc + 1) * 128],
                            in_=ps_tr[:]), inc="dv")
                        p.wait("tensor", cpt)

        p.wait("tensor", p.tok("ac"))
        p.wait("tensor", p.tok("dv"))
        for qb in range(4):
            for hh2 in range(2):
                c0 = hh2 * 2
                for qi in range(4):
                    qt = qb * 4 + qi
                    p.wait("tensor", p.tok("ac"))
                    for cc in range(2):
                        for ms in range(4):
                            smt = p.op("tensor", lambda e, cc=cc, ms=ms, qt=qt, c0=c0: e.matmul(
                                out=ps_big[:, ms * 512:(ms + 1) * 512],
                                lhsT=qmodT[:, (c0 + cc) * L + qt * 128:
                                              (c0 + cc) * L + (qt + 1) * 128],
                                rhs=kTg[:, (c0 + cc) * L + ms * 512:
                                           (c0 + cc) * L + (ms + 1) * 512],
                                start=(cc == 0), stop=(cc == 1)),
                                inc="pe" if (cc == 1 and ms == 3) else None)
                    p.wait("vector", smt)
                    p.wait("vector", ext_prev)
                    rmt = p.op("vector", lambda e: e.reduce_max(
                        out=negmax[:], in_=ps_big[:], axis=AX.X, negate=True), inc="dv")
                    p.wait("scalar", rmt)
                    ext_prev = p.op("scalar", lambda e, qt=qt: e.activation(
                        out=Pb[:], in_=ps_big[:], func=AF.Exp,
                        bias=negmax[:], scale=1.0,
                        accum_out=zrec[:, qt:qt + 1]), inc="ac")
                    p.wait("vector", ext_prev)
                    p.op("vector", lambda e, qt=qt: e.reciprocal(
                        out=zrec[:, qt:qt + 1], in_=zrec[:, qt:qt + 1]), inc="dv")
                    p.wait("tensor", ext_prev)
                    p.wait("scalar", av_done)
                    for mc in range(16):
                        trt = p.op("tensor", lambda e, mc=mc: e.transpose(
                            out=ps_tr[:], in_=Pb[:, mc * 128:(mc + 1) * 128],
                            identity=identb[:]), inc="pe")
                        p.wait("scalar", trt)
                        cpt = p.op("scalar", lambda e, mc=mc, qi=qi: e.copy(
                            out=PT4[:, mc * 512 + qi * 128: mc * 512 + (qi + 1) * 128],
                            in_=ps_tr[:]), inc="ac")
                        p.wait("tensor", cpt)
                p.wait("scalar", p.tok("dv"))
                for cc2 in range(2):
                    p.wait("tensor", p.tok("ac"))
                    for mc in range(16):
                        avt = p.op("tensor", lambda e, mc=mc, hh2=hh2, cc2=cc2: e.matmul(
                            out=ps_sm[:],
                            lhsT=vtok[:, mc * 512 + hh2 * 256 + cc2 * 128:
                                         mc * 512 + hh2 * 256 + (cc2 + 1) * 128],
                            rhs=PT4[:, mc * 512:(mc + 1) * 512],
                            start=(mc == 0), stop=(mc == 15)),
                            inc="pe" if mc == 15 else None)
                    av_done = avt
                    p.wait("scalar", avt)
                    aet = p.op("scalar", lambda e: e.copy(out=avf[:], in_=ps_sm[:]), inc="ac")
                    p.wait("tensor", aet)
                    for qi in range(4):
                        trt = p.op("tensor", lambda e, qi=qi: e.transpose(
                            out=ps_tr[:], in_=avf[:, qi * 128:(qi + 1) * 128],
                            identity=identb[:]), inc="pe")
                        p.wait("scalar", trt)
                        p.wait("scalar", store_tok)
                        cpt = p.op("scalar", lambda e, qi=qi, qb=qb, hh2=hh2, cc2=cc2: e.activation(
                            out=outstg[:, qi * 512 + hh2 * 256 + cc2 * 128:
                                          qi * 512 + hh2 * 256 + (cc2 + 1) * 128],
                            in_=ps_tr[:], func=AF.Identity,
                            bias=0.0, scale=zrec[:, qb * 4 + qi: qb * 4 + qi + 1]), inc="ac")
                        p.wait("tensor", cpt)
            p.wait("sync", p.tok("ac"))
            for qi in range(4):
                store_tok = p.dma("sync", attn_dst(qb * 4 + qi, g),
                                  outstg[:, qi * 512:(qi + 1) * 512], f"ost{qb % 2}")
    p.wait("sync", p.tok("ost0"))
    p.wait("sync", p.tok("ost1"))
    if ctx is None:
        p.emit()
        return nc


def stage_A(inputs):
    bivmat, revtile, scdelta = host_consts_L0()
    ident = np.eye(128, dtype=BF)
    X = np.asarray(inputs["L0_attn_x"])
    WqT = np.ascontiguousarray(np.asarray(inputs["L0_attn_Wq"]).T)
    WkT = np.ascontiguousarray(np.asarray(inputs["L0_attn_Wk"]).T)
    WvT = np.ascontiguousarray(np.asarray(inputs["L0_attn_Wv"]).T)
    bw = np.asarray(inputs["L0_attn_bias_weight"])
    in_maps = []
    for c in range(NCORE):
        b, s = c // 2, c % 2
        sl = slice(s * 2048, (s + 1) * 2048)
        bwT_c = np.zeros((3, 16), np.float32)
        bwT_c[:, 0:8] = bw[s * 8:(s + 1) * 8, :].T
        in_maps.append({
            "x0": np.ascontiguousarray(X[b]),
            "wqT": np.ascontiguousarray(WqT[:, sl]),
            "wkT": np.ascontiguousarray(WkT[:, sl]),
            "wvT": np.ascontiguousarray(WvT[:, sl]),
            "bq": np.ascontiguousarray(np.asarray(inputs["L0_attn_bq"])[sl]),
            "bk": np.ascontiguousarray(np.asarray(inputs["L0_attn_bk"])[sl]),
            "bv": np.ascontiguousarray(np.asarray(inputs["L0_attn_bv"])[sl]),
            "bwT": bwT_c,
            "bivmat": bivmat, "revtile": revtile, "scdelta": scdelta,
            "ident": ident,
        })
    return in_maps




def build_launch_B(ctx=None):
    if ctx is None:
        nc = bass.Bass("TRN2", target_bir_lowering=False, num_devices=NCORE)
        p = Prog(nc)
        x0 = nc.dram_tensor("x0", [TPC, DM], F32, kind="ExternalInput").ap()
        attn = nc.dram_tensor("attn", [TPC, DM], BF16, kind="ExternalInput").ap()
        w0T = nc.dram_tensor("w0T", [DM, DM], F32, kind="ExternalInput").ap()
        bias0 = nc.dram_tensor("bias0", [1, DM], F32, kind="ExternalInput").ap()
        a0T = nc.dram_tensor("a0T", [DM, 256], F32, kind="ExternalInput").ap()
        b0cat = nc.dram_tensor("b0cat", [256, DM], F32, kind="ExternalInput").ap()
        coefr = nc.dram_tensor("coefr", [256], F32, kind="ExternalInput").ap()
        sclr = nc.dram_tensor("sclr", [256], F32, kind="ExternalInput").ap()
        ident = nc.dram_tensor("ident", [128, 128], BF16, kind="ExternalInput").ap()
        x2o = nc.dram_tensor("x2o", [TPC, DM], F32, kind="ExternalOutput").ap()
        attn_srcs = lambda tt: [(slice(0, DM),
                                 lambda tt=tt: attn[tt * 128:(tt + 1) * 128, :])]
        x2_dst = lambda tt: (lambda tt=tt: x2o[tt * 128:(tt + 1) * 128, :])
    else:
        nc, p = ctx["nc"], ctx["p"]
        x0 = ctx["x0own"]
        w0T, bias0, a0T, b0cat, coefr, sclr, ident = (
            ctx["w0T"], ctx["bias0"], ctx["a0T"], ctx["b0cat"], ctx["coefr"],
            ctx["sclr"], ctx["ident"])
        attn_srcs = ctx["attn_srcs"]
        x2_dst = ctx["x2_dst"]
    xT1d = nc.dram_tensor("xT1d", [32, 128, TPC], BF16, kind="Internal").ap()

    al = ctx["al"] if ctx else nc.alloc_sbuf_tensor
    identb = al("identb", [128, 128], BF16)
    z = al("z", [128, 8 * DM], F32)
    scr = al("scr", [128, 1024], F32)
    a0b = al("a0b", [128, 32 * 256], BF16)
    b0s = al("b0s", [128, 2 * 512], BF16)
    xaT = al("xaT", [128, 2 * TPC], BF16)
    xw = al("xw", [128, 2 * 4096], BF16)
    stg = xw[:, 0:DM]
    wfs = al("wfs", [128, 2 * 2048], F32)
    wbs = al("wbs", [128, 2 * 2048], BF16)
    x1tmp = al("x1tmp", [128, 512], F32)
    brb = al("brb", [1, 512], BF16)
    ones1b = al("ones1b", [1, 128], BF16)
    stat = al("stat", [128, 48], F32)
    wcol = al("wcol", [128, 2], F32)
    ccol = al("ccol", [128, 2], F32)
    scol = al("scol", [128, 2], F32)

    ps = ctx["PS"] if ctx else nc.alloc_psum_tensor("ps", [128, 4096], F32)
    ps_trv = ps.bitcast(BF16)[:, 0:128]
    ps_xa = ps[:, 512:768]

    b0f = scr[:, 0:1024]
    ZS, ZSS, RSTD, NEGM = 0, 8, 16, 24
    MEAN, VAR = 32, 40

    def ln_stats(tt, ztile, pfx):
        p.wait("vector", p.tok("ac"))
        p.wait("scalar", p.tok("dv"))
        t1 = p.op("vector", lambda e: e.reduce_sum(
            out=stat[:, ZS + tt:ZS + tt + 1], in_=ztile, axis=AX.X), inc="dv")
        t2 = p.op("scalar", lambda e: e.activation(
            out=stg[:, 0:DM], in_=ztile, func=AF.Square,
            bias=0.0, scale=1.0, accum_out=stat[:, ZSS + tt:ZSS + tt + 1]), inc="ac")
        p.wait("vector", t1)
        p.wait("vector", t2)
        t3 = p.op("vector", lambda e: e.tensor_scalar(
            out=stat[:, MEAN:MEAN + 1], in0=stat[:, ZS + tt:ZS + tt + 1],
            scalar1=1.0 / DM, scalar2=None, op0=ALU.mult), inc="dv")
        p.wait("vector", t3)
        t4 = p.op("vector", lambda e: e.tensor_scalar(
            out=stat[:, VAR:VAR + 1], in0=stat[:, ZSS + tt:ZSS + tt + 1],
            scalar1=1.0 / DM, scalar2=None, op0=ALU.mult), inc="dv")
        p.wait("scalar", t3)
        t5 = p.op("scalar", lambda e: e.activation(
            out=stat[:, MEAN + 1:MEAN + 2], in_=stat[:, MEAN:MEAN + 1],
            func=AF.Square, bias=0.0, scale=1.0), inc="ac")
        p.wait("vector", t4)
        p.wait("vector", t5)
        t6 = p.op("vector", lambda e: e.tensor_tensor(
            out=stat[:, VAR:VAR + 1], in0=stat[:, VAR:VAR + 1],
            in1=stat[:, MEAN + 1:MEAN + 2], op=ALU.subtract), inc="dv")
        p.wait("vector", t6)
        t6b = p.op("vector", lambda e: e.tensor_scalar(
            out=stat[:, VAR:VAR + 1], in0=stat[:, VAR:VAR + 1],
            scalar1=LN_EPS, scalar2=None, op0=ALU.add), inc="dv")
        p.wait("scalar", t6b)
        t7 = p.op("scalar", lambda e: e.activation(
            out=stat[:, VAR + 1:VAR + 2], in_=stat[:, VAR:VAR + 1],
            func=AF.Ln, bias=0.0, scale=1.0), inc="ac")
        p.wait("scalar", t7)
        t8 = p.op("scalar", lambda e: e.activation(
            out=stat[:, RSTD + tt:RSTD + tt + 1], in_=stat[:, VAR + 1:VAR + 2],
            func=AF.Exp, bias=0.0, scale=-0.5), inc="ac")
        p.wait("vector", t8)
        p.wait("scalar", t8)
        t9 = p.op("vector", lambda e: e.tensor_tensor(
            out=stat[:, NEGM + tt:NEGM + tt + 1], in0=stat[:, MEAN:MEAN + 1],
            in1=stat[:, RSTD + tt:RSTD + tt + 1], op=ALU.mult), inc="dv")
        p.wait("vector", t9)
        t10 = p.op("vector", lambda e: e.tensor_scalar(
            out=stat[:, NEGM + tt:NEGM + tt + 1], in0=stat[:, NEGM + tt:NEGM + tt + 1],
            scalar1=-1.0, scalar2=None, op0=ALU.mult), inc="dv")
        return t10

    p.dma("sync", identb[:], ident, "lc")
    mzt = p.op("vector", lambda e: e.memset(ones1b[:], 1.0), inc="dv")
    p.wait("scalar", p.tok("lc"))
    bct = mzt
    for rc in range(2):
        p.dma("sync", ccol[:, rc:rc + 1], coefr[rc * 128:(rc + 1) * 128].unsqueeze(1), "lcc")
        p.dma("sync", scol[:, rc:rc + 1], sclr[rc * 128:(rc + 1) * 128].unsqueeze(1), "lcc")
    p.wait("vector", p.tok("lcc"))
    wct = p.op("vector", lambda e: e.tensor_tensor(out=wcol[:], in0=ccol[:],
                                                   in1=scol[:], op=ALU.mult), inc="dv")
    p.wait("sync", bct)
    for half in range(2):
        ct = p.dma("gpsimd",
                   a0b[:, half * 16 * 256:(half + 1) * 16 * 256]
                   .rearrange("p (c r) -> p c r", c=16),
                   a0T[half * 2048:(half + 1) * 2048, :]
                   .rearrange("(c p) r -> p c r", p=128), f"la{half}")
        p.wait("tensor", ct)

    at_b = xw[:, DM:2 * DM]
    xat = scr.bitcast(BF16)[:, 0:256]
    stg_user = None
    xa_ev = None
    for tt in range(8):
        p.wait("sync", p.tok("pool"))
        for dsl, srcfn in attn_srcs(tt):
            lt1 = p.dma("sync", at_b[:, dsl], srcfn, f"lat{tt % 2}")
        lt2 = p.dma("sync", z[:, tt * DM:(tt + 1) * DM],
                    x0[tt * 128:(tt + 1) * 128, :], f"lx{tt % 2}")
        p.wait("vector", lt1)
        p.wait("vector", lt2)
        zt = p.op("vector", lambda e, tt=tt: e.scalar_tensor_tensor(
            out=z[:, tt * DM:(tt + 1) * DM], in0=at_b, scalar=1.0,
            in1=z[:, tt * DM:(tt + 1) * DM], op0=ALU.mult, op1=ALU.add), inc="dv")
        p.wait("vector", zt)
        p.wait("scalar", zt)
        if stg_user is not None:
            p.wait("scalar", stg_user)
        sttok = ln_stats(tt, z[:, tt * DM:(tt + 1) * DM], "b")
        p.wait("scalar", sttok)
        x1t = p.op("scalar", lambda e, tt=tt: e.activation(
            out=at_b, in_=z[:, tt * DM:(tt + 1) * DM], func=AF.Identity,
            bias=stat[:, NEGM + tt:NEGM + tt + 1],
            scale=stat[:, RSTD + tt:RSTD + tt + 1]), inc="ac")
        p.wait("tensor", x1t)
        for c in range(32):
            trt = p.op("tensor", lambda e, c=c: e.transpose(
                out=ps_trv, in_=at_b[:, c * 128:(c + 1) * 128],
                identity=identb[:]), inc="pe")
            p.wait("vector", trt)
            cpt = p.op("vector", lambda e, c=c: e.tensor_copy(
                out=stg[:, c * 128:(c + 1) * 128], in_=ps_trv), inc="dv")
            p.wait("tensor", cpt)
        p.wait("sync", p.tok("dv"))
        stg_user = p.dma("sync",
                         xT1d[:, :, tt * 128:(tt + 1) * 128].transpose([1, 0, 2]),
                         stg.rearrange("p (c i) -> p c i", c=32), f"xtw{tt % 2}")
        p.wait("tensor", xa_ev)
        for d in range(32):
            xmt = p.op("tensor", lambda e, d=d: e.matmul(
                out=ps_xa, lhsT=stg[:, d * 128:(d + 1) * 128],
                rhs=a0b[:, d * 256:(d + 1) * 256],
                start=(d == 0), stop=(d == 31)), inc="pe" if d == 31 else None)
        p.wait("scalar", xmt)
        xa_ev = p.op("scalar", lambda e: e.copy(out=xat, in_=ps_xa), inc="ac")
        p.wait("tensor", xa_ev)
        for rc in range(2):
            trt = p.op("tensor", lambda e, rc=rc: e.transpose(
                out=ps_trv, in_=xat[:, rc * 128:(rc + 1) * 128], identity=identb[:]),
                inc="pe")
            p.wait("scalar", trt)
            xa_ev = p.op("scalar", lambda e, rc=rc, tt=tt: e.activation(
                out=xaT[:, rc * TPC + tt * 128: rc * TPC + (tt + 1) * 128],
                in_=ps_trv, func=AF.Identity, bias=0.0,
                scale=wcol[:, rc:rc + 1]), inc="ac")
            p.wait("tensor", xa_ev)
        p.wait("sync", p.tok("ac"))
    p.wait("sync", p.tok("xtw0"))
    p.wait("sync", p.tok("xtw1"))
    wu = [None, None]
    ev_tok = None
    xu = None
    prev_bmt = None
    for osp in range(8):
        p.wait("sync", p.tok("dv"))
        brl = p.dma("sync", scr[0:1, 512:1024],
                    bias0[0:1, osp * 512:(osp + 1) * 512], f"brl{osp % 2}")
        p.wait("scalar", brl)
        p.wait("scalar", prev_bmt)
        brc = p.op("scalar", lambda e: e.copy(out=brb[:], in_=scr[0:1, 512:1024]),
                   inc="ac")
        for dg in range(8):
            db = dg % 2
            p.wait("scalar", wu[db])
            xl = p.op("scalar", lambda e, dg=dg, db=db: e.dma_start(
                out=xw[:, db * 4096:(db + 1) * 4096].rearrange("p (c m) -> p c m", c=4),
                in_=xT1d[dg * 4:(dg + 1) * 4].transpose([1, 0, 2])), inc=f"xl{db}", k=16)
            p.wait("gpsimd", wu[db])
            wc = p.dma("gpsimd", wbs[:, db * 2048:(db + 1) * 2048]
                       .rearrange("p (c o) -> p c o", c=4),
                       w0T[dg * 512:(dg + 1) * 512, osp * 512:(osp + 1) * 512]
                       .rearrange("(c p) o -> p c o", p=128), f"wl{db}")
            p.wait("tensor", wc)
            p.wait("tensor", xl)
            if dg < 2 and ev_tok is not None:
                p.wait("tensor", ev_tok)
            for dc in range(4):
                for tt in range(8):
                    mmt = p.op("tensor", lambda e, tt=tt, dc=dc, db=db, dg=dg: e.matmul(
                        out=ps[:, tt * 512:(tt + 1) * 512],
                        lhsT=xw[:, db * 4096 + dc * 1024 + tt * 128:
                                   db * 4096 + dc * 1024 + (tt + 1) * 128],
                        rhs=wbs[:, db * 2048 + dc * 512: db * 2048 + (dc + 1) * 512],
                        start=(dg == 0 and dc == 0), stop=False),
                        inc="pe" if (tt == 7 and dc == 3) else None)
            wu[db] = mmt
        p.wait("gpsimd", p.tok("pe"))
        p.wait("sync", brc)
        for rc in range(2):
            bctk = p.dma("gpsimd", b0s[:, rc * 512:(rc + 1) * 512],
                         b0cat[rc * 128:(rc + 1) * 128, osp * 512:(osp + 1) * 512],
                         f"lb{rc}")
            p.wait("tensor", bctk)
        for tt in range(8):
            for rc in range(2):
                p.op("tensor", lambda e, tt=tt, rc=rc, osp=osp: e.matmul(
                    out=ps[:, tt * 512:(tt + 1) * 512],
                    lhsT=xaT[:, rc * TPC + tt * 128: rc * TPC + (tt + 1) * 128],
                    rhs=b0s[:, rc * 512:(rc + 1) * 512],
                    start=False, stop=False))
            if tt == 0:
                p.wait("tensor", brc)
            bmt = p.op("tensor", lambda e, tt=tt: e.matmul(
                out=ps[:, tt * 512:(tt + 1) * 512],
                lhsT=ones1b[:], rhs=brb[:],
                start=False, stop=True), inc="pe")
            prev_bmt = bmt
            p.wait("scalar", bmt)
            x1o = p.op("scalar", lambda e, tt=tt, osp=osp: e.activation(
                out=x1tmp[:], in_=z[:, tt * DM + osp * 512: tt * DM + (osp + 1) * 512],
                func=AF.Identity, bias=stat[:, NEGM + tt:NEGM + tt + 1],
                scale=stat[:, RSTD + tt:RSTD + tt + 1]), inc="ac")
            p.wait("vector", x1o)
            ev_tok = p.op("vector", lambda e, tt=tt, osp=osp: e.tensor_tensor(
                out=z[:, tt * DM + osp * 512: tt * DM + (osp + 1) * 512],
                in0=ps[:, tt * 512:(tt + 1) * 512], in1=x1tmp[:], op=ALU.add), inc="dv")
            p.wait("scalar", ev_tok)
    ot_prev = [None, None]
    for tt in range(8):
        p.wait("vector", p.tok("dv"))
        p.wait("scalar", p.tok("dv"))
        sttok = ln_stats(tt, z[:, tt * DM:(tt + 1) * DM], "b2")
        p.wait("scalar", sttok)
        slots = [x1tmp[:], scr[:, 0:512]]
        for oq in range(8):
            sl = slots[oq % 2]
            p.wait("scalar", ot_prev[oq % 2])
            oft = p.op("scalar", lambda e, tt=tt, oq=oq, sl=sl: e.activation(
                out=sl, in_=z[:, tt * DM + oq * 512: tt * DM + (oq + 1) * 512],
                func=AF.Identity,
                bias=stat[:, NEGM + tt:NEGM + tt + 1],
                scale=stat[:, RSTD + tt:RSTD + tt + 1]), inc="ac")
            p.wait("sync", oft)
            ot_prev[oq % 2] = p.dma("sync",
                                    lambda tt=tt, oq=oq: x2_dst(tt)()[:, oq * 512:(oq + 1) * 512],
                                    sl, f"ox{oq % 2}")
    p.wait("sync", p.tok("ox0"))
    p.wait("sync", p.tok("ox1"))
    if ctx is None:
        p.emit()
        return nc


def stage_B(inputs, attn_full, x0_full):
    ident = np.eye(128, dtype=BF)
    w0T = np.ascontiguousarray(np.asarray(inputs["L0_ffn_W"]).T)
    A = np.asarray(inputs["L0_ffn_A"])
    a0T = np.ascontiguousarray(A.reshape(256, DM).T)
    Bf = np.asarray(inputs["L0_ffn_Bf"])
    b0cat = np.ascontiguousarray(Bf.transpose(0, 2, 1).reshape(256, DM))
    coefr = np.ascontiguousarray(np.repeat(np.asarray(inputs["L0_ffn_coeffs"]), 16))
    sclr = np.ascontiguousarray(np.repeat(np.asarray(inputs["L0_ffn_scales"]), 16))
    bias0 = np.ascontiguousarray(np.asarray(inputs["L0_ffn_bias"])[None, :])
    in_maps = []
    for c in range(NCORE):
        b, s = c // 2, c % 2
        tsl = slice(s * 1024, (s + 1) * 1024)
        in_maps.append({
            "x0": np.ascontiguousarray(x0_full[b][tsl]),
            "attn": np.ascontiguousarray(attn_full[b][tsl]),
            "w0T": w0T, "bias0": bias0, "a0T": a0T, "b0cat": b0cat,
            "coefr": coefr, "sclr": sclr, "ident": ident,
        })
    return in_maps




def build_launch_C(ctx=None):
    if ctx is None:
        nc = bass.Bass("TRN2", target_bir_lowering=False, num_devices=NCORE)
        p = Prog(nc)
        x2f = nc.dram_tensor("x2f", [L, DM], F32, kind="ExternalInput").ap()
        wqT = nc.dram_tensor("wqT", [DM, 2048], F32, kind="ExternalInput").ap()
        wkT = nc.dram_tensor("wkT", [DM, 512], F32, kind="ExternalInput").ap()
        wvT = nc.dram_tensor("wvT", [DM, 512], F32, kind="ExternalInput").ap()
        wgT = nc.dram_tensor("wgT", [DM, 2048], F32, kind="ExternalInput").ap()
        winT = nc.dram_tensor("winT", [DM, 16], F32, kind="ExternalInput").ap()
        dtb = nc.dram_tensor("dtb", [1, 16], F32, kind="ExternalInput").ap()
        alog = nc.dram_tensor("alog", [1, 16], F32, kind="ExternalInput").ap()
        gnw = nc.dram_tensor("gnw", [1, 128], F32, kind="ExternalInput").ap()
        cosT = nc.dram_tensor("cosT", [128, L], F32, kind="ExternalInput").ap()
        sinT = nc.dram_tensor("sinT", [128, L], F32, kind="ExternalInput").ap()
        ident = nc.dram_tensor("ident", [128, 128], BF16, kind="ExternalInput").ap()
        identf = nc.dram_tensor("identf", [128, 128], F32, kind="ExternalInput").ap()
        trif = nc.dram_tensor("trif", [128, 128], F32, kind="ExternalInput").ap()
        ygo = nc.dram_tensor("ygo", [L, 2048], BF16, kind="ExternalOutput").ap()
        yg_dst = lambda n, kv: (lambda: ygo[n * 128:(n + 1) * 128,
                                            kv * 512:(kv + 1) * 512])
    else:
        nc, p = ctx["nc"], ctx["p"]
        x2f = ctx["x2f"]
        wqT, wkT, wvT, wgT, winT = (ctx["wq1T"], ctx["wk1T"], ctx["wv1T"],
                                    ctx["wg1T"], ctx["win1T"])
        dtb, alog, gnw, cosT, sinT = (ctx["dtb"], ctx["alog"], ctx["gnw"],
                                      ctx["cosT"], ctx["sinT"])
        ident, identf, trif = ctx["ident"], ctx["identf"], ctx["trif"]
        yg_dst = ctx["yg_dst"]
    NCh = 16
    xT2d = nc.dram_tensor("xT2d", [32, 128, L], BF16, kind="Internal").ap()

    al = ctx["al"] if ctx else nc.alloc_sbuf_tensor
    identb = al("identb", [128, 128], BF16)
    idenff = al("idenff", [128, 128], F32)
    tri_f = al("tri_f", [128, 128], F32)
    cos_s = al("cos_s", [128, L], F32)
    sin_s = al("sin_s", [128, L], F32)
    scr = al("scr", [128, 6144], F32)
    stg = al("stg", [128, DM], BF16)
    xTh = al("xTh", [128, 32 * 1024], BF16)
    wst_f = al("wst_f", [128, DM], F32)
    wst_b = al("wst_b", [128, DM], BF16)
    qfm = al("qfm", [128, 4 * L], BF16)
    gtok = al("gtok", [128, 16 * 512], BF16)
    kfm = al("kfm", [128, L], BF16)
    ktok = al("ktok", [128, 16 * 128], BF16)
    vtok = al("vtok", [128, 16 * 128], BF16)
    winb = al("winb", [128, 32 * 16], BF16)
    winlo = al("winlo", [128, 32 * 16], BF16)
    xlob = al("xlob", [128, DM], BF16)
    dtbb = al("dtbb", [1, 16], BF16)
    dtv = al("dtv", [128, 16 * 16], F32)
    Gv = al("Gv", [128, 16 * 16], F32)
    nGv = al("nGv", [128, 16 * 16], F32)
    Gl = al("Gl", [1, 16 * 16], F32)
    bcGl = al("bcGl", [128, 16 * 16], F32)
    edcv = al("edcv", [128, 16 * 16], F32)
    deccv = al("deccv", [128, 16 * 16], F32)
    egcv = al("egcv", [128, 16 * 16], F32)
    Aneg = al("Aneg", [128, 16], F32)
    gnb = al("gnb", [128, 128], F32)
    ones1b = al("ones1b", [1, 128], BF16)
    ones1f = al("ones1f", [1, 128], F32)
    onecf = al("onecf", [128, 1], F32)
    prevC = al("prevC", [128, 4 * 128], F32)
    prevb = al("prevb", [128, 4 * 128], BF16)
    T1 = al("T1", [128, 128], F32)
    dmask = al("dmask", [128, 128], F32)
    mt = al("mt", [128, 128], BF16)
    tlo = mt
    ytmp = T1[:, :]
    yjunk = mt
    xct = al("xct", [128, 128], BF16)
    kdt = al("kdt", [128, 128], BF16)
    ytile = al("ytile", [128, 128], F32)
    ygstg = al("ygstg", [128, 512], BF16)
    ccol = al("ccol", [128, 1], F32)
    alog_s = al("alog_s", [1, 16], F32)
    row16 = al("row16", [1, 16], F32)

    if ctx is None:
        ps_proj = nc.alloc_psum_tensor("ps_proj", [128, 1024], F32)
        ps_attn = nc.alloc_psum_tensor("ps_attn", [128, 2048], F32)
        ps_trf = nc.alloc_psum_tensor("ps_trf", [128, 128], F32)
        ps_tr = nc.alloc_psum_tensor("ps_tr", [128, 128], BF16)
    else:
        PS = ctx["PS"]
        ps_proj = PS[:, 0:1024]
        ps_attn = PS[:, 1024:3072]
        ps_trf = PS[:, 3072:3200]
        ps_tr = PS.bitcast(BF16)[:, 7 * 1024: 7 * 1024 + 128]
    ps_st = ps_attn[:, 0:128]
    ps_yA = ps_attn[:, 512:640]
    ps_yB = ps_attn[:, 1024:1152]
    ps_C = ps_attn[:, 1536:1664]
    ps_dt = ps_attn[:, 0:16]
    ps_G = ps_attn[:, 512:528]
    ps_gl = ps_attn[0:1, 1024:1040]
    ps_bc = ps_attn[:, 1536:1552]

    p.dma("sync", identb[:], ident, "lc")
    p.dma("sync", idenff[:], identf, "lc")
    p.dma("sync", tri_f[:], trif, "lc")
    p.dma("sync", cos_s[:], cosT, "lc")
    p.dma("sync", sin_s[:], sinT, "lc")
    p.dma("sync", alog_s[:], alog, "lc")
    p.dma("sync", scr[0:1, 0:16], dtb, "lc")
    p.dma("sync", scr[0:1, 16:144], gnw, "lc")
    wl = p.dma("sync", scr[:, 2048:2048 + 512].rearrange("p (c r) -> p c r", c=32),
               winT.rearrange("(c p) r -> p c r", p=128), "lc")
    mz1 = p.op("vector", lambda e: e.memset(ones1b[:], 1.0), inc="dv")
    mz2 = p.op("vector", lambda e: e.memset(ones1f[:], 1.0), inc="dv")
    mz3 = p.op("vector", lambda e: e.memset(onecf[:], 1.0), inc="dv")
    mz4 = p.op("vector", lambda e: e.memset(prevC[:], 0.0), inc="dv")
    mz5 = p.op("vector", lambda e: e.memset(prevb[:], 0.0), inc="dv")
    p.wait("scalar", p.tok("lc"))
    p.wait("scalar", p.tok("dv"))
    c1 = p.op("scalar", lambda e: e.copy(out=dtbb[:], in_=scr[0:1, 0:16]), inc="ac")
    c2 = p.op("scalar", lambda e: e.copy(out=winb[:], in_=scr[:, 2048:2048 + 512]), inc="ac")
    p.wait("vector", c2)
    c2b = p.op("vector", lambda e: e.tensor_tensor(out=winlo[:], in0=scr[:, 2048:2048 + 512],
                                                   in1=winb[:], op=ALU.subtract), inc="dv")
    c3 = p.op("scalar", lambda e: e.activation(out=row16[:], in_=alog_s[:],
                                               func=AF.Exp, bias=0.0, scale=1.0), inc="ac")
    p.wait("vector", c3)
    c4 = p.op("vector", lambda e: e.tensor_scalar(out=row16[:], in0=row16[:],
                                                  scalar1=-1.0, scalar2=None,
                                                  op0=ALU.mult), inc="dv")
    p.wait("tensor", c4)
    bc1 = p.op("tensor", lambda e: e.matmul(out=ps_attn[:, 0:16], lhsT=ones1f[:],
                                            rhs=row16[:], start=True, stop=True), inc="pe")
    p.wait("vector", bc1)
    an1 = p.op("vector", lambda e: e.tensor_copy(out=Aneg[:], in_=ps_attn[:, 0:16]), inc="dv")
    p.wait("scalar", p.tok("dv"))
    c5 = p.op("scalar", lambda e: e.copy(out=gnb[0:1, :],
                                         in_=scr[0:1, 16:144]), inc="ac")
    p.wait("tensor", c5)
    p.wait("tensor", an1)
    bc2 = p.op("tensor", lambda e: e.matmul(out=ps_attn[:, 0:128], lhsT=ones1f[:],
                                            rhs=gnb[0:1, :], start=True, stop=True), inc="pe")
    p.wait("vector", bc2)
    gn1 = p.op("vector", lambda e: e.tensor_copy(out=gnb[:], in_=ps_attn[:, 0:128]), inc="dv")

    xf = scr[:, 0:DM]
    xb = scr.bitcast(BF16)[:, 2 * DM: 3 * DM]
    dtr = scr[:, 5500:5516]
    sp1 = scr[:, 5520:5536]
    sp2 = scr[:, 5540:5556]
    gt = scr[:, 5560:5576]
    p.wait("sync", p.tok("ac"))
    cast_tok = None
    tr_last = None
    st_tok = None
    dt_ev = p.tok("dv")
    for tt in range(NCh):
        if cast_tok is not None:
            p.wait("sync", cast_tok)
        ltok = p.dma("sync", xf, x2f[tt * 128:(tt + 1) * 128, :], f"ldx{tt % 2}")
        p.wait("scalar", ltok)
        if tr_last is not None:
            p.wait("scalar", tr_last)
        cast_tok = p.op("scalar", lambda e: e.copy(out=xb, in_=xf), inc="ac")
        p.wait("vector", cast_tok)
        lo_tok = p.op("vector", lambda e: e.tensor_tensor(
            out=xlob[:], in0=xf, in1=xb, op=ALU.subtract), inc="dv")
        p.wait("tensor", cast_tok)
        p.wait("tensor", lo_tok)
        if st_tok is not None:
            p.wait("vector", st_tok)
        p.wait("tensor", dt_ev)
        for c in range(32):
            tr_last = p.op("tensor", lambda e, c=c: e.transpose(
                out=ps_tr[:], in_=xb[:, c * 128:(c + 1) * 128], identity=identb[:]),
                inc="pe")
            p.wait("vector", tr_last)
            cpt = p.op("vector", lambda e, c=c: e.tensor_copy(
                out=stg[:, c * 128:(c + 1) * 128], in_=ps_tr[:]), inc="dv")
            p.wait("tensor", cpt)
            trl = p.op("tensor", lambda e, c=c: e.transpose(
                out=ps_tr[:], in_=xlob[:, c * 128:(c + 1) * 128], identity=identb[:]),
                inc="pe")
            p.wait("vector", trl)
            cpl = p.op("vector", lambda e: e.tensor_copy(out=tlo[:], in_=ps_tr[:]),
                       inc="dv")
            p.wait("tensor", cpl)
            p.op("tensor", lambda e, c=c: e.matmul(
                out=ps_dt, lhsT=stg[:, c * 128:(c + 1) * 128],
                rhs=winb[:, c * 16:(c + 1) * 16],
                start=(c == 0), stop=False))
            p.op("tensor", lambda e, c=c: e.matmul(
                out=ps_dt, lhsT=tlo[:], rhs=winb[:, c * 16:(c + 1) * 16],
                start=False, stop=False))
            p.op("tensor", lambda e, c=c: e.matmul(
                out=ps_dt, lhsT=stg[:, c * 128:(c + 1) * 128],
                rhs=winlo[:, c * 16:(c + 1) * 16],
                start=False, stop=False))
        p.wait("sync", p.tok("dv"))
        st_tok = p.dma("sync",
                       xT2d[:, :, tt * 128:(tt + 1) * 128].transpose([1, 0, 2]),
                       stg.rearrange("p (c i) -> p c i", c=32), f"xtw{tt % 2}")
        dmt = p.op("tensor", lambda e: e.matmul(
            out=ps_dt, lhsT=ones1b[:], rhs=dtbb[:], start=False, stop=True), inc="pe")
        p.wait("scalar", dmt)
        p.wait("scalar", p.tok("dv"))
        e1 = p.op("scalar", lambda e: e.copy(out=dtr, in_=ps_dt), inc="ac")
        p.wait("scalar", e1)
        e2 = p.op("scalar", lambda e: e.activation(out=sp1, in_=dtr, func=AF.Abs,
                                                   bias=0.0, scale=1.0), inc="ac")
        p.wait("scalar", e2)
        e3 = p.op("scalar", lambda e: e.activation(out=sp1, in_=sp1, func=AF.Exp,
                                                   bias=0.0, scale=-1.0), inc="ac")
        p.wait("vector", e3)
        e4 = p.op("vector", lambda e: e.tensor_scalar(out=sp1, in0=sp1, scalar1=1.0,
                                                      scalar2=None, op0=ALU.add), inc="dv")
        p.wait("scalar", e4)
        e5 = p.op("scalar", lambda e: e.activation(out=sp1, in_=sp1, func=AF.Ln,
                                                   bias=0.0, scale=1.0), inc="ac")
        p.wait("vector", e5)
        e6 = p.op("vector", lambda e: e.tensor_scalar(out=sp2, in0=dtr, scalar1=0.0,
                                                      scalar2=None, op0=ALU.max), inc="dv")
        p.wait("vector", e6)
        e7 = p.op("vector", lambda e, tt=tt: e.tensor_tensor(
            out=dtv[:, tt * 16:(tt + 1) * 16], in0=sp1, in1=sp2, op=ALU.add), inc="dv")
        p.wait("vector", e7)
        e8 = p.op("vector", lambda e, tt=tt: e.tensor_tensor(
            out=gt, in0=dtv[:, tt * 16:(tt + 1) * 16], in1=Aneg[:], op=ALU.mult), inc="dv")
        p.wait("tensor", e8)
        g1 = p.op("tensor", lambda e: e.matmul(out=ps_G, lhsT=tri_f[:], rhs=gt,
                                               start=True, stop=True), inc="pe")
        g2 = p.op("tensor", lambda e: e.matmul(out=ps_gl, lhsT=onecf[:], rhs=gt,
                                               start=True, stop=True), inc="pe")
        p.wait("vector", g2)
        dt_ev = p.op("vector", lambda e, tt=tt: e.tensor_copy(
            out=Gv[:, tt * 16:(tt + 1) * 16], in_=ps_G), inc="dv")
        gl_ev = p.op("vector", lambda e, tt=tt: e.tensor_copy(
            out=Gl[0:1, tt * 16:(tt + 1) * 16], in_=ps_gl), inc="dv")
        p.wait("tensor", gl_ev)
        p.wait("sync", p.tok("ac"))
        p.wait("sync", p.tok("dv"))
    p.wait("vector", p.tok("dv"))
    ng = p.op("vector", lambda e: e.tensor_scalar(out=nGv[:], in0=Gv[:], scalar1=-1.0,
                                                  scalar2=None, op0=ALU.mult), inc="dv")
    for n in range(NCh):
        p.wait("tensor", p.tok("dv"))
        b1 = p.op("tensor", lambda e, n=n: e.matmul(
            out=ps_bc, lhsT=ones1f[:], rhs=Gl[0:1, n * 16:(n + 1) * 16],
            start=True, stop=True), inc="pe")
        p.wait("vector", b1)
        b2 = p.op("vector", lambda e, n=n: e.tensor_copy(
            out=bcGl[:, n * 16:(n + 1) * 16], in_=ps_bc), inc="dv")
        p.wait("tensor", b2)
    p.wait("scalar", p.tok("dv"))
    ec1 = p.op("scalar", lambda e: e.activation(out=deccv[:], in_=bcGl[:],
                                                func=AF.Exp, bias=0.0, scale=1.0), inc="ac")
    ec2 = p.op("scalar", lambda e: e.activation(out=egcv[:], in_=Gv[:],
                                                func=AF.Exp, bias=0.0, scale=1.0), inc="ac")
    p.wait("vector", ec2)
    ec3 = p.op("vector", lambda e: e.tensor_tensor(out=edcv[:], in0=bcGl[:],
                                                   in1=Gv[:], op=ALU.subtract), inc="dv")
    p.wait("scalar", ec3)
    ec4 = p.op("scalar", lambda e: e.activation(out=edcv[:], in_=edcv[:],
                                                func=AF.Exp, bias=0.0, scale=1.0), inc="ac")
    p.wait("tensor", p.tok("ac"))
    p.wait("vector", p.tok("ac"))

    def rope_evac(psrc, n_cols, dst, coff):
        rt = scr[:, 0:1024]
        t1s = scr[:, 1024:2048]
        r1 = p.op("scalar", lambda e: e.activation(
            out=rt[0:64, 0:n_cols], in_=psrc[64:128, :], func=AF.Identity,
            bias=0.0, scale=-1.0), inc="ac")
        r2 = p.op("scalar", lambda e: e.activation(
            out=rt[64:128, 0:n_cols], in_=psrc[0:64, :], func=AF.Identity,
            bias=0.0, scale=1.0), inc="ac")
        p.wait("vector", r2)
        m1 = p.op("vector", lambda e: e.tensor_tensor(
            out=t1s[:, 0:n_cols], in0=psrc, in1=cos_s[:, coff:coff + n_cols],
            op=ALU.mult), inc="dv")
        p.wait("vector", m1)
        m2 = p.op("vector", lambda e: e.tensor_tensor(
            out=rt[:, 0:n_cols], in0=rt[:, 0:n_cols],
            in1=sin_s[:, coff:coff + n_cols], op=ALU.mult), inc="dv")
        p.wait("vector", m2)
        m3 = p.op("vector", lambda e: e.tensor_tensor(
            out=dst, in0=t1s[:, 0:n_cols], in1=rt[:, 0:n_cols], op=ALU.add), inc="dv")
        return m3

    p.wait("sync", p.tok("xtw0"))
    p.wait("sync", p.tok("xtw1"))
    fillreg = {}
    p.raw("gpsimd", lambda e: fillreg.__setitem__("r", e.to_reg(-1e30)))
    wld_tok = None
    proj_ev = None
    yg_st = None
    prev_su = None
    prev_yn1 = None
    prev_cn = None
    prev_ye1 = None
    prev_ye2 = None
    for kv in range(4):
        for mh in range(2):
            p.wait("sync", p.tok("pe"))
            xld = p.dma("sync", xTh.rearrange("p (c m) -> p c m", c=32),
                        xT2d[:, :, mh * 1024:(mh + 1) * 1024].transpose([1, 0, 2]),
                        f"lxh{mh}")
            for oc in range(10):
                if oc < 4:
                    kind, wsrc, gch = "q", wqT, kv * 4 + oc
                elif oc < 8:
                    kind, wsrc, gch = "g", wgT, kv * 4 + (oc - 4)
                elif oc == 8:
                    kind, wsrc, gch = "k", wkT, kv
                else:
                    kind, wsrc, gch = "v", wvT, kv
                if wld_tok is not None:
                    p.wait("gpsimd", p.tok("pe"))
                wld_tok = wct = p.dma("gpsimd", wst_b[:].rearrange("p (c o) -> p c o", c=32),
                                      wsrc[:, gch * 128:(gch + 1) * 128]
                                      .rearrange("(c p) o -> p c o", p=128), "wld")
                p.wait("tensor", wct)
                p.wait("tensor", xld)
                p.wait("tensor", p.tok("ac"))
                p.wait("tensor", p.tok("dv"))
                for d in range(32):
                    for half in range(2):
                        pmt = p.op("tensor", lambda e, d=d, h=half: e.matmul(
                            out=ps_proj[:, h * 512:(h + 1) * 512],
                            lhsT=wst_b[:, d * 128:(d + 1) * 128],
                            rhs=xTh[:, d * 1024 + h * 512: d * 1024 + (h + 1) * 512],
                            start=(d == 0), stop=(d == 31)),
                            inc="pe" if (d == 31 and half == 1) else None)
                p.wait("scalar", pmt)
                p.wait("vector", pmt)
                if kind == "q":
                    loc = oc
                    rtok = rope_evac(ps_proj[:, 0:1024], 1024,
                                     qfm[:, loc * L + mh * 1024: loc * L + (mh + 1) * 1024],
                                     mh * 1024)
                    p.wait("tensor", rtok)
                elif kind == "k":
                    rtok = rope_evac(ps_proj[:, 0:1024], 1024,
                                     kfm[:, mh * 1024:(mh + 1) * 1024], mh * 1024)
                    p.wait("tensor", rtok)
                    for j in range(8):
                        trt = p.op("tensor", lambda e, j=j, mh=mh: e.transpose(
                            out=ps_tr[:], in_=kfm[:, mh * 1024 + j * 128:
                                                   mh * 1024 + (j + 1) * 128],
                            identity=identb[:]), inc="pe")
                        p.wait("vector", trt)
                        cpt = p.op("vector", lambda e, j=j, mh=mh: e.tensor_copy(
                            out=ktok[:, (mh * 8 + j) * 128:(mh * 8 + j + 1) * 128],
                            in_=ps_tr[:]), inc="dv")
                        p.wait("tensor", cpt)
                else:
                    vt = scr.bitcast(BF16)[:, 0:1024]
                    ev = p.op("scalar", lambda e: e.copy(out=vt, in_=ps_proj[:, 0:1024]),
                              inc="ac")
                    p.wait("tensor", ev)
                    for j in range(8):
                        trt = p.op("tensor", lambda e, j=j: e.transpose(
                            out=ps_tr[:], in_=vt[:, j * 128:(j + 1) * 128],
                            identity=identb[:]), inc="pe")
                        p.wait("vector", trt)
                        if kind == "g":
                            loc = oc - 4
                            p.wait("scalar", trt)
                            cpt = p.op("scalar", lambda e, j=j, mh=mh, loc=loc: e.activation(
                                out=gtok[:, (mh * 8 + j) * 512 + loc * 128:
                                            (mh * 8 + j) * 512 + (loc + 1) * 128],
                                in_=ps_tr[:], func=AF.Silu, bias=0.0, scale=1.0), inc="dv")
                        else:
                            cpt = p.op("vector", lambda e, j=j, mh=mh: e.tensor_copy(
                                out=vtok[:, (mh * 8 + j) * 128:(mh * 8 + j + 1) * 128],
                                in_=ps_tr[:]), inc="dv")
                        p.wait("tensor", cpt)
                    p.wait("scalar", p.tok("dv"))
        p.wait("tensor", p.tok("dv"))
        p.wait("tensor", p.tok("ac"))
        for n in range(NCh):
            for j in range(4):
                hh = kv * 4 + j
                gcol = n * 16 + hh
                smt = p.op("tensor", lambda e, n=n, j=j: e.matmul(
                    out=ps_st, lhsT=kfm[:, n * 128:(n + 1) * 128],
                    rhs=qfm[:, j * L + n * 128: j * L + (n + 1) * 128],
                    start=True, stop=True), inc="pe")
                t1t = p.op("scalar", lambda e, gcol=gcol: e.activation(
                    out=T1[:], in_=T1[:],
                    func=AF.Identity, bias=Gv[:, gcol:gcol + 1], scale=0.0), inc="ac")
                p.wait("tensor", t1t)
                t2t = p.op("tensor", lambda e: e.transpose(
                    out=ps_trf[:], in_=T1[:], identity=idenff[:]), inc="pe")
                p.wait("scalar", t2t)
                p.wait("scalar", prev_yn1)
                t3t = p.op("scalar", lambda e, gcol=gcol: e.activation(
                    out=dmask[:], in_=ps_trf[:], func=AF.Identity,
                    bias=nGv[:, gcol:gcol + 1], scale=1.0), inc="ac")
                p.wait("gpsimd", t3t)
                t4t = p.op("gpsimd", lambda e: e.affine_select(
                    out=dmask[:], in_=dmask[:], pattern=[[1, 128]],
                    compare_op=ALU.is_ge, fill=fillreg["r"], base=0,
                    channel_multiplier=-1), inc="pool")
                p.wait("scalar", t4t)
                t4b = p.op("scalar", lambda e: e.activation(
                    out=dmask[:], in_=dmask[:], func=AF.Exp,
                    bias=0.0, scale=1.0), inc="ac")
                p.wait("vector", t4b)
                p.wait("vector", smt)
                t5t = p.op("vector", lambda e: e.tensor_tensor(
                    out=mt[:], in0=ps_st, in1=dmask[:], op=ALU.mult), inc="dv")
                p.wait("scalar", prev_cn)
                xc1 = p.op("scalar", lambda e, n=n, gcol=gcol: e.activation(
                    out=xct[:], in_=vtok[:, n * 128:(n + 1) * 128], func=AF.Identity,
                    bias=0.0, scale=dtv[:, gcol:gcol + 1]), inc="ac")
                kd2 = p.op("scalar", lambda e, n=n, gcol=gcol: e.activation(
                    out=kdt[:], in_=ktok[:, n * 128:(n + 1) * 128], func=AF.Identity,
                    bias=0.0, scale=edcv[:, gcol:gcol + 1]), inc="ac")
                p.wait("tensor", t5t)
                p.wait("tensor", xc1)
                p.wait("tensor", prev_ye2)
                p.wait("tensor", prev_ye1)
                ya = p.op("tensor", lambda e: e.matmul(
                    out=ps_yA, lhsT=mt[:], rhs=xct[:], start=True, stop=True), inc="pe")
                yb = p.op("tensor", lambda e, j=j: e.matmul(
                    out=ps_yB, lhsT=qfm[:, j * L + n * 128: j * L + (n + 1) * 128],
                    rhs=prevb[:, j * 128:(j + 1) * 128], start=True, stop=True), inc="pe")
                p.wait("tensor", kd2)
                p.wait("tensor", prev_su)
                cn = p.op("tensor", lambda e: e.matmul(
                    out=ps_C, lhsT=kdt[:], rhs=xct[:], start=True, stop=True), inc="pe")
                p.wait("vector", cn)
                su = p.op("vector", lambda e, j=j, gcol=gcol: e.scalar_tensor_tensor(
                    out=prevC[:, j * 128:(j + 1) * 128],
                    in0=prevC[:, j * 128:(j + 1) * 128],
                    scalar=deccv[:, gcol:gcol + 1],
                    in1=ps_C, op0=ALU.mult, op1=ALU.add), inc="dv")
                p.wait("scalar", su)
                p.wait("scalar", yb)
                pc = p.op("scalar", lambda e, j=j: e.copy(
                    out=prevb[:, j * 128:(j + 1) * 128],
                    in_=prevC[:, j * 128:(j + 1) * 128]), inc="ac")
                p.wait("scalar", yb)
                ye1 = p.op("scalar", lambda e, gcol=gcol: e.activation(
                    out=ytmp[:], in_=ps_yB, func=AF.Identity, bias=0.0,
                    scale=egcv[:, gcol:gcol + 1]), inc="ac")
                p.wait("vector", ye1)
                p.wait("vector", ya)
                ye2 = p.op("vector", lambda e: e.tensor_tensor(
                    out=ytile[:], in0=ps_yA, in1=ytmp[:], op=ALU.add), inc="dv")
                p.wait("scalar", ye2)
                p.wait("scalar", prev_yn1)
                p.wait("scalar", cn)
                rs1 = p.op("scalar", lambda e: e.activation(
                    out=yjunk[:], in_=ytile[:], func=AF.Square, bias=0.0, scale=1.0,
                    accum_out=ccol[:, 0:1]), inc="ac")
                p.wait("vector", rs1)
                rs2 = p.op("vector", lambda e: e.tensor_scalar(
                    out=ccol[:, 0:1], in0=ccol[:, 0:1], scalar1=1.0 / 128.0,
                    scalar2=RMS_EPS, op0=ALU.mult, op1=ALU.add), inc="dv")
                p.wait("scalar", rs2)
                rs3 = p.op("scalar", lambda e: e.activation(
                    out=ccol[:, 0:1], in_=ccol[:, 0:1], func=AF.Ln,
                    bias=0.0, scale=1.0), inc="ac")
                p.wait("scalar", rs3)
                rs4 = p.op("scalar", lambda e: e.activation(
                    out=ccol[:, 0:1], in_=ccol[:, 0:1], func=AF.Exp,
                    bias=0.0, scale=-0.5), inc="ac")
                p.wait("vector", rs4)
                yn1 = p.op("vector", lambda e: e.scalar_tensor_tensor(
                    out=ytile[:], in0=ytile[:], scalar=ccol[:, 0:1], in1=gnb[:],
                    op0=ALU.mult, op1=ALU.mult), inc="dv")
                p.wait("vector", yn1)
                if yg_st is not None:
                    p.wait("vector", yg_st)
                yg1 = p.op("vector", lambda e, j=j, n=n: e.tensor_tensor(
                    out=ygstg[:, j * 128:(j + 1) * 128], in0=ytile[:],
                    in1=gtok[:, n * 512 + j * 128: n * 512 + (j + 1) * 128],
                    op=ALU.mult), inc="dv")
                prev_su, prev_yn1, prev_cn = su, yn1, cn
                prev_ye1, prev_ye2 = ye1, ye2
            p.wait("sync", p.tok("dv"))
            yg_st = p.dma("sync", yg_dst(n, kv), ygstg[:], f"yst{n % 2}")
    p.wait("sync", p.tok("yst0"))
    p.wait("sync", p.tok("yst1"))
    if ctx is None:
        p.emit()
        return nc


def stage_C(inputs, x2_full):
    ident = np.eye(128, dtype=BF)
    identf = np.eye(128, dtype=np.float32)
    trif = np.ascontiguousarray(np.triu(np.ones((128, 128), np.float32)))
    WqT = np.ascontiguousarray(np.asarray(inputs["L1_attn_Wq"]).T)
    WkT = np.ascontiguousarray(np.asarray(inputs["L1_attn_Wk"]).T)
    WvT = np.ascontiguousarray(np.asarray(inputs["L1_attn_Wv"]).T)
    WgT = np.ascontiguousarray(np.asarray(inputs["L1_attn_Wg"]).T)
    WinT = np.ascontiguousarray(np.asarray(inputs["L1_attn_Win"]).T)
    cos, sin = host_rope_tables(np.arange(L))
    in_maps = []
    for c in range(NCORE):
        b, s = c // 2, c % 2
        qsl = slice(s * 2048, (s + 1) * 2048)
        kvsl = slice(s * 512, (s + 1) * 512)
        hsl = slice(s * 16, (s + 1) * 16)
        in_maps.append({
            "x2f": np.ascontiguousarray(x2_full[b]),
            "wqT": np.ascontiguousarray(WqT[:, qsl]),
            "wkT": np.ascontiguousarray(WkT[:, kvsl]),
            "wvT": np.ascontiguousarray(WvT[:, kvsl]),
            "wgT": np.ascontiguousarray(WgT[:, qsl]),
            "winT": np.ascontiguousarray(WinT[:, hsl]),
            "dtb": np.ascontiguousarray(np.asarray(inputs["L1_attn_dt_bias"])[None, hsl]),
            "alog": np.ascontiguousarray(np.asarray(inputs["L1_attn_A_log_bias"])[None, hsl]),
            "gnw": np.ascontiguousarray(np.asarray(inputs["L1_attn_gnorm_w"])[None, :]),
            "cosT": cos, "sinT": sin,
            "ident": ident, "identf": identf, "trif": trif,
        })
    return in_maps




def build_launch_D(ctx=None):
    if ctx is None:
        nc = bass.Bass("TRN2", target_bir_lowering=False, num_devices=NCORE)
        p = Prog(nc)
        x2 = nc.dram_tensor("x2", [TPC, DM], F32, kind="ExternalInput").ap()
        yg = nc.dram_tensor("yg", [TPC, DM], BF16, kind="ExternalInput").ap()
        woT = nc.dram_tensor("woT", [DM, DM], F32, kind="ExternalInput").ap()
        w1T = nc.dram_tensor("w1T", [DM, DM], F32, kind="ExternalInput").ap()
        b1 = nc.dram_tensor("b1", [1, DM], F32, kind="ExternalInput").ap()
        art = nc.dram_tensor("art", [DM, 80], F32, kind="ExternalInput").ap()
        bpT = nc.dram_tensor("bpT", [8, 128, DM], F32, kind="ExternalInput").ap()
        ident = nc.dram_tensor("ident", [128, 128], BF16, kind="ExternalInput").ap()
        x4o = nc.dram_tensor("x4o", [TPC, DM], F32, kind="ExternalOutput").ap()
        x2_src = lambda tt: (lambda tt=tt: x2[tt * 128:(tt + 1) * 128, :])
        yg_srcs = lambda tt: [(slice(0, DM),
                               lambda tt=tt: yg[tt * 128:(tt + 1) * 128, :])]
    else:
        nc, p = ctx["nc"], ctx["p"]
        woT, w1T, b1, art, bpT, ident = (ctx["woT"], ctx["w1T"], ctx["b1"],
                                         ctx["art"], ctx["bpT"], ctx["ident"])
        x4o = ctx["x4o"]
        x2_src = ctx["x2_src"]
        yg_srcs = ctx["yg_srcs"]
    ygTd = nc.dram_tensor("ygTd", [32, 128, TPC], BF16, kind="Internal").ap()
    xT3d = nc.dram_tensor("xT3d", [32, 128, TPC], BF16, kind="Internal").ap()

    al = ctx["al"] if ctx else nc.alloc_sbuf_tensor
    identb = al("identb", [128, 128], BF16)
    z = al("z", [128, 8 * DM], F32)
    scr = al("scr", [128, 2688], F32)
    artb = al("artb", [128, 32 * 80], BF16)
    hgT = al("hgT", [128, 8 * 1024], BF16)
    xw = al("xw", [128, 2 * 4096], BF16)
    stg = xw[:, 0:DM]
    xgt_alias = xw[:, DM:2 * DM]
    wfs = al("wfs", [128, 2 * 2048], F32)
    wbs = al("wbs", [128, 2 * 2048], BF16)
    bp_b = al("bp_b", [128, 2 * 512], BF16)
    brb = al("brb", [1, 512], BF16)
    ones1b = al("ones1b", [1, 128], BF16)
    stat = al("stat", [128, 48], F32)
    rh_f = al("rh_f", [128, 64], F32)
    lg = al("lg", [128, 16], F32)
    pex = al("pex", [128, 16], F32)
    s1 = al("s1", [128, 16], F32)
    mskd = al("mskd", [128, 16], F32)
    comb = al("comb", [128, 16], F32)
    het = al("het", [128, 64], BF16)
    ccol = al("ccol", [128, 8], F32)

    ps = ctx["PS"] if ctx else nc.alloc_psum_tensor("ps", [128, 4096], F32)
    ps_trv = ps.bitcast(BF16)[:, 0:128]
    ps_aux = ps[:, 512:592]

    ZS, ZSS, RSTD, NEGM, MEAN, VAR = 0, 8, 16, 24, 32, 40

    def ln_stats(tt, ztile):
        p.wait("vector", p.tok("ac"))
        p.wait("scalar", p.tok("dv"))
        t1 = p.op("vector", lambda e: e.reduce_sum(
            out=stat[:, ZS + tt:ZS + tt + 1], in_=ztile, axis=AX.X), inc="dv")
        t2 = p.op("scalar", lambda e: e.activation(
            out=stg[:, 0:DM], in_=ztile, func=AF.Square,
            bias=0.0, scale=1.0, accum_out=stat[:, ZSS + tt:ZSS + tt + 1]), inc="ac")
        p.wait("vector", t1)
        p.wait("vector", t2)
        t3 = p.op("vector", lambda e: e.tensor_scalar(
            out=stat[:, MEAN:MEAN + 1], in0=stat[:, ZS + tt:ZS + tt + 1],
            scalar1=1.0 / DM, scalar2=None, op0=ALU.mult), inc="dv")
        p.wait("vector", t3)
        t4 = p.op("vector", lambda e: e.tensor_scalar(
            out=stat[:, VAR:VAR + 1], in0=stat[:, ZSS + tt:ZSS + tt + 1],
            scalar1=1.0 / DM, scalar2=None, op0=ALU.mult), inc="dv")
        p.wait("scalar", t3)
        t5 = p.op("scalar", lambda e: e.activation(
            out=stat[:, MEAN + 1:MEAN + 2], in_=stat[:, MEAN:MEAN + 1],
            func=AF.Square, bias=0.0, scale=1.0), inc="ac")
        p.wait("vector", t4)
        p.wait("vector", t5)
        t6 = p.op("vector", lambda e: e.tensor_tensor(
            out=stat[:, VAR:VAR + 1], in0=stat[:, VAR:VAR + 1],
            in1=stat[:, MEAN + 1:MEAN + 2], op=ALU.subtract), inc="dv")
        p.wait("vector", t6)
        t6b = p.op("vector", lambda e: e.tensor_scalar(
            out=stat[:, VAR:VAR + 1], in0=stat[:, VAR:VAR + 1],
            scalar1=LN_EPS, scalar2=None, op0=ALU.add), inc="dv")
        p.wait("scalar", t6b)
        t7 = p.op("scalar", lambda e: e.activation(
            out=stat[:, VAR + 1:VAR + 2], in_=stat[:, VAR:VAR + 1],
            func=AF.Ln, bias=0.0, scale=1.0), inc="ac")
        p.wait("scalar", t7)
        t8 = p.op("scalar", lambda e: e.activation(
            out=stat[:, RSTD + tt:RSTD + tt + 1], in_=stat[:, VAR + 1:VAR + 2],
            func=AF.Exp, bias=0.0, scale=-0.5), inc="ac")
        p.wait("vector", t8)
        p.wait("scalar", t8)
        t9 = p.op("vector", lambda e: e.tensor_tensor(
            out=stat[:, NEGM + tt:NEGM + tt + 1], in0=stat[:, MEAN:MEAN + 1],
            in1=stat[:, RSTD + tt:RSTD + tt + 1], op=ALU.mult), inc="dv")
        p.wait("vector", t9)
        t10 = p.op("vector", lambda e: e.tensor_scalar(
            out=stat[:, NEGM + tt:NEGM + tt + 1], in0=stat[:, NEGM + tt:NEGM + tt + 1],
            scalar1=-1.0, scalar2=None, op0=ALU.mult), inc="dv")
        return t10

    p.dma("sync", identb[:], ident, "lc")
    mzt = p.op("vector", lambda e: e.memset(ones1b[:], 1.0), inc="dv")
    ct = p.dma("gpsimd", artb[:].rearrange("p (c r) -> p c r", c=32),
               art.rearrange("(c p) r -> p c r", p=128), "la")
    p.wait("tensor", ct)

    ygt = xgt_alias
    p.wait("tensor", p.tok("lc"))
    for tt in range(8):
        p.dma("sync", z[:, tt * DM:(tt + 1) * DM], x2_src(tt), f"lz{tt % 2}")
        for dsl, srcfn in yg_srcs(tt):
            lt = p.dma("sync", ygt[:, dsl], srcfn, f"ly{tt % 2}")
        p.wait("tensor", lt)
        p.wait("vector", p.tok("xtw0"))
        p.wait("vector", p.tok("xtw1"))
        for c in range(32):
            trt = p.op("tensor", lambda e, c=c: e.transpose(
                out=ps_trv, in_=ygt[:, c * 128:(c + 1) * 128], identity=identb[:]),
                inc="pe")
            p.wait("vector", trt)
            cpt = p.op("vector", lambda e, c=c: e.tensor_copy(
                out=stg[:, c * 128:(c + 1) * 128], in_=ps_trv), inc="dv")
            p.wait("tensor", cpt)
        p.wait("sync", p.tok("dv"))
        sttk = p.dma("sync", ygTd[:, :, tt * 128:(tt + 1) * 128].transpose([1, 0, 2]),
                     stg.rearrange("p (c i) -> p c i", c=32), f"xtw{tt % 2}")
        p.wait("sync", p.tok("ac"))
        p.wait("sync", p.tok("pe"))
    p.wait("sync", p.tok("xtw0"))
    p.wait("sync", p.tok("xtw1"))
    p.wait("scalar", p.tok("xtw0"))
    p.wait("scalar", p.tok("xtw1"))
    wu = [None, None]
    ev_tok = None
    xu = None
    for osp in range(8):
        for dg in range(8):
            db = dg % 2
            p.wait("scalar", wu[db])
            xl = p.op("scalar", lambda e, dg=dg, db=db: e.dma_start(
                out=xw[:, db * 4096:(db + 1) * 4096].rearrange("p (c m) -> p c m", c=4),
                in_=ygTd[dg * 4:(dg + 1) * 4].transpose([1, 0, 2])), inc=f"xl{db}", k=16)
            p.wait("gpsimd", wu[db])
            wc = p.dma("gpsimd", wbs[:, db * 2048:(db + 1) * 2048]
                       .rearrange("p (c o) -> p c o", c=4),
                       woT[dg * 512:(dg + 1) * 512, osp * 512:(osp + 1) * 512]
                       .rearrange("(c p) o -> p c o", p=128), f"wl{db}")
            p.wait("tensor", wc)
            p.wait("tensor", xl)
            if dg < 2 and ev_tok is not None:
                p.wait("tensor", ev_tok)
            for dc in range(4):
                for tt in range(8):
                    mmt = p.op("tensor", lambda e, tt=tt, dc=dc, db=db, dg=dg: e.matmul(
                        out=ps[:, tt * 512:(tt + 1) * 512],
                        lhsT=xw[:, db * 4096 + dc * 1024 + tt * 128:
                                   db * 4096 + dc * 1024 + (tt + 1) * 128],
                        rhs=wbs[:, db * 2048 + dc * 512: db * 2048 + (dc + 1) * 512],
                        start=(dg == 0 and dc == 0), stop=(dg == 7 and dc == 3)),
                        inc="pe" if (tt == 7 and dc == 3) else None)
            wu[db] = mmt
        for tt in range(8):
            p.wait("vector", p.tok("pe"))
            p.wait("vector", p.tok("lz0"))
            p.wait("vector", p.tok("lz1"))
            ev_tok = p.op("vector", lambda e, tt=tt, osp=osp: e.tensor_tensor(
                out=z[:, tt * DM + osp * 512: tt * DM + (osp + 1) * 512],
                in0=ps[:, tt * 512:(tt + 1) * 512],
                in1=z[:, tt * DM + osp * 512: tt * DM + (osp + 1) * 512],
                op=ALU.add), inc="dv")
    x3b = xgt_alias
    aux_ev = None
    stg_user = None
    for tt in range(8):
        p.wait("vector", p.tok("dv"))
        p.wait("scalar", stg_user)
        sttok = ln_stats(tt, z[:, tt * DM:(tt + 1) * DM])
        p.wait("scalar", sttok)
        p.wait("scalar", p.tok("pool"))
        x3t = p.op("scalar", lambda e, tt=tt: e.activation(
            out=x3b, in_=z[:, tt * DM:(tt + 1) * DM], func=AF.Identity,
            bias=stat[:, NEGM + tt:NEGM + tt + 1],
            scale=stat[:, RSTD + tt:RSTD + tt + 1]), inc="ac")
        p.wait("tensor", x3t)
        if stg_user is not None:
            p.wait("vector", stg_user)
        for c in range(32):
            trt = p.op("tensor", lambda e, c=c: e.transpose(
                out=ps_trv, in_=x3b[:, c * 128:(c + 1) * 128], identity=identb[:]),
                inc="pe")
            p.wait("vector", trt)
            cpt = p.op("vector", lambda e, c=c: e.tensor_copy(
                out=stg[:, c * 128:(c + 1) * 128], in_=ps_trv), inc="dv")
            p.wait("tensor", cpt)
        p.wait("sync", p.tok("dv"))
        stg_user = p.dma("sync", xT3d[:, :, tt * 128:(tt + 1) * 128].transpose([1, 0, 2]),
                         stg.rearrange("p (c i) -> p c i", c=32), f"xt3{tt % 2}")
        p.wait("tensor", aux_ev)
        for d in range(32):
            amt = p.op("tensor", lambda e, d=d: e.matmul(
                out=ps_aux, lhsT=stg[:, d * 128:(d + 1) * 128],
                rhs=artb[:, d * 80:(d + 1) * 80],
                start=(d == 0), stop=(d == 31)), inc="pe" if d == 31 else None)
        p.wait("scalar", amt)
        a1 = p.op("scalar", lambda e: e.copy(out=rh_f[:], in_=ps_aux[:, 0:64]), inc="ac")
        aux_ev = p.op("scalar", lambda e: e.copy(out=lg[:], in_=ps_aux[:, 64:80]), inc="ac")
        p.wait("vector", aux_ev)
        q1 = p.op("vector", lambda e: e.reduce_max(out=ccol[:, 0:1], in_=lg[:],
                                                   axis=AX.X, negate=True), inc="dv")
        p.wait("scalar", q1)
        q2 = p.op("scalar", lambda e: e.activation(out=pex[:], in_=lg[:], func=AF.Exp,
                                                   bias=ccol[:, 0:1], scale=1.0,
                                                   accum_out=ccol[:, 1:2]), inc="ac")
        p.wait("vector", q2)
        q3 = p.op("vector", lambda e: e.tensor_scalar(out=ccol[:, 2:3], in0=ccol[:, 0:1],
                                                      scalar1=-1.0, scalar2=None,
                                                      op0=ALU.mult), inc="dv")
        p.wait("scalar", q3)
        q4 = p.op("scalar", lambda e: e.activation(out=s1[:], in_=lg[:], func=AF.Sign,
                                                   bias=ccol[:, 2:3], scale=-1.0), inc="ac")
        p.wait("vector", q4)
        q5 = p.op("vector", lambda e: e.tensor_tensor(
            out=mskd[:], in0=lg[:], in1=s1[:], op=ALU.mult), inc="dv")
        q5b = p.op("vector", lambda e: e.tensor_scalar(
            out=s1[:], in0=s1[:], scalar1=-1.0, scalar2=64.0,
            op0=ALU.add, op1=ALU.mult), inc="dv")
        p.wait("vector", q5b)
        q6 = p.op("vector", lambda e: e.tensor_tensor(out=mskd[:], in0=mskd[:],
                                                      in1=s1[:], op=ALU.add), inc="dv")
        p.wait("vector", q6)
        q7 = p.op("vector", lambda e: e.reduce_max(out=ccol[:, 3:4], in_=mskd[:],
                                                   axis=AX.X), inc="dv")
        p.wait("scalar", q7)
        q8 = p.op("scalar", lambda e: e.activation(out=s1[:], in_=lg[:], func=AF.Sign,
                                                   bias=ccol[:, 3:4], scale=-1.0), inc="ac")
        p.wait("vector", q8)
        q9 = p.op("vector", lambda e: e.tensor_scalar(out=s1[:], in0=s1[:],
                                                      scalar1=-1.0, scalar2=1.0,
                                                      op0=ALU.mult, op1=ALU.add), inc="dv")
        p.wait("vector", q9)
        q10 = p.op("vector", lambda e: e.tensor_scalar(out=s1[:], in0=s1[:],
                                                       scalar1=1.0, scalar2=None,
                                                       op0=ALU.min), inc="dv")
        p.wait("vector", q10)
        q11 = p.op("vector", lambda e: e.tensor_tensor(out=pex[:], in0=pex[:],
                                                       in1=s1[:], op=ALU.mult), inc="dv")
        p.wait("vector", q11)
        q12 = p.op("vector", lambda e: e.reduce_sum(out=ccol[:, 4:5], in_=pex[:],
                                                    axis=AX.X), inc="dv")
        p.wait("vector", q12)
        q13 = p.op("vector", lambda e: e.scalar_tensor_tensor(
            out=ccol[:, 5:6], in0=ccol[:, 1:2], scalar=1e-6, in1=ccol[:, 4:5],
            op0=ALU.mult, op1=ALU.add), inc="dv")
        p.wait("vector", q13)
        q14 = p.op("vector", lambda e: e.reciprocal(out=ccol[:, 5:6], in_=ccol[:, 5:6]),
                   inc="dv")
        p.wait("vector", q14)
        q15 = p.op("vector", lambda e: e.tensor_scalar(out=ccol[:, 5:6], in0=ccol[:, 5:6],
                                                       scalar1=0.5, scalar2=None,
                                                       op0=ALU.mult), inc="dv")
        p.wait("vector", q15)
        q16 = p.op("vector", lambda e: e.tensor_scalar(out=comb[:], in0=pex[:],
                                                       scalar1=ccol[:, 5:6], scalar2=None,
                                                       op0=ALU.mult), inc="dv")
        p.wait("vector", a1)
        p.wait("vector", q16)
        for ee in range(16):
            he = p.op("vector", lambda e, ee=ee: e.tensor_scalar(
                out=het[:], in0=rh_f[:], scalar1=comb[:, ee:ee + 1], scalar2=None,
                op0=ALU.mult), inc="dv")
            p.wait("tensor", he)
            trt = p.op("tensor", lambda e: e.transpose(
                out=ps_trv[0:64, :], in_=het[:], identity=identb[:]), inc="pe")
            p.wait("scalar", trt)
            hct = p.op("scalar", lambda e, ee=ee, tt=tt: e.copy(
                out=hgT[(ee % 2) * 64:(ee % 2) * 64 + 64,
                        (ee // 2) * 1024 + tt * 128: (ee // 2) * 1024 + (tt + 1) * 128],
                in_=ps_trv[0:64, :]), inc="ac")
            p.wait("vector", hct)
            p.wait("tensor", hct)
    p.wait("sync", p.tok("xt30"))
    p.wait("sync", p.tok("xt31"))
    p.wait("scalar", p.tok("xt30"))
    p.wait("scalar", p.tok("xt31"))
    wu = [None, None]
    ev_tok = None
    bp_user = None
    xu = None
    prev_bmt = None
    for osp in range(8):
        p.wait("sync", p.tok("dv"))
        brl = p.dma("sync", scr[0:1, 1536:1536 + 512],
                    b1[0:1, osp * 512:(osp + 1) * 512], f"brl{osp % 2}")
        p.wait("scalar", brl)
        p.wait("scalar", prev_bmt)
        brc = p.op("scalar", lambda e: e.copy(out=brb[:], in_=scr[0:1, 1536:1536 + 512]),
                   inc="ac")
        for dg in range(8):
            db = dg % 2
            p.wait("scalar", wu[db])
            xl = p.op("scalar", lambda e, dg=dg, db=db: e.dma_start(
                out=xw[:, db * 4096:(db + 1) * 4096].rearrange("p (c m) -> p c m", c=4),
                in_=xT3d[dg * 4:(dg + 1) * 4].transpose([1, 0, 2])), inc=f"x2l{db}", k=16)
            p.wait("gpsimd", wu[db])
            wc = p.dma("gpsimd", wbs[:, db * 2048:(db + 1) * 2048]
                       .rearrange("p (c o) -> p c o", c=4),
                       w1T[dg * 512:(dg + 1) * 512, osp * 512:(osp + 1) * 512]
                       .rearrange("(c p) o -> p c o", p=128), f"w2{db}")
            p.wait("tensor", wc)
            p.wait("tensor", xl)
            if dg < 2 and ev_tok is not None:
                p.wait("tensor", ev_tok)
            for dc in range(4):
                for tt in range(8):
                    mmt = p.op("tensor", lambda e, tt=tt, dc=dc, db=db, dg=dg: e.matmul(
                        out=ps[:, tt * 512:(tt + 1) * 512],
                        lhsT=xw[:, db * 4096 + dc * 1024 + tt * 128:
                                   db * 4096 + dc * 1024 + (tt + 1) * 128],
                        rhs=wbs[:, db * 2048 + dc * 512: db * 2048 + (dc + 1) * 512],
                        start=(dg == 0 and dc == 0), stop=False),
                        inc="pe" if (tt == 7 and dc == 3) else None)
            wu[db] = mmt
        p.wait("gpsimd", p.tok("pe"))
        for fp in range(4):
            bts = []
            for ff in range(2):
                f = fp * 2 + ff
                bts.append(p.dma("gpsimd", bp_b[:, ff * 512:(ff + 1) * 512],
                                 bpT[f][:, osp * 512:(osp + 1) * 512], f"lb{f % 2}"))
            p.wait("tensor", bts[0])
            p.wait("tensor", bts[1])
            for tt in range(8):
                dmm = p.op("tensor", lambda e, tt=tt, fp=fp: e.matmul(
                    out=ps[:, tt * 512:(tt + 1) * 512],
                    lhsT=hgT[:, (fp * 2) * 1024 + tt * 128: (fp * 2) * 1024 + (tt + 1) * 128],
                    rhs=bp_b[:, 0:512],
                    start=False, stop=False), inc=None)
                dmm = p.op("tensor", lambda e, tt=tt, fp=fp: e.matmul(
                    out=ps[:, tt * 512:(tt + 1) * 512],
                    lhsT=hgT[:, (fp * 2 + 1) * 1024 + tt * 128: (fp * 2 + 1) * 1024 + (tt + 1) * 128],
                    rhs=bp_b[:, 512:1024],
                    start=False, stop=False),
                    inc="pe" if tt == 7 else None)
            p.wait("gpsimd", dmm)
        for tt in range(8):
            if tt == 0:
                p.wait("tensor", brc)
            bmt = p.op("tensor", lambda e, tt=tt: e.matmul(
                out=ps[:, tt * 512:(tt + 1) * 512],
                lhsT=ones1b[:], rhs=brb[:],
                start=False, stop=True), inc="pe")
            prev_bmt = bmt
            p.wait("scalar", bmt)
            x3o = p.op("scalar", lambda e, tt=tt, osp=osp: e.activation(
                out=scr[:, 1024:1536], in_=z[:, tt * DM + osp * 512: tt * DM + (osp + 1) * 512],
                func=AF.Identity, bias=stat[:, NEGM + tt:NEGM + tt + 1],
                scale=stat[:, RSTD + tt:RSTD + tt + 1]), inc="ac")
            p.wait("vector", x3o)
            ev_tok = p.op("vector", lambda e, tt=tt, osp=osp: e.tensor_tensor(
                out=z[:, tt * DM + osp * 512: tt * DM + (osp + 1) * 512],
                in0=ps[:, tt * 512:(tt + 1) * 512], in1=scr[:, 1024:1536],
                op=ALU.add), inc="dv")
            p.wait("scalar", ev_tok)
    out_f = None
    ot_prev = None
    for tt in range(8):
        p.wait("vector", p.tok("dv"))
        p.wait("scalar", p.tok("dv"))
        p.wait("scalar", p.tok("pool"))
        sttok = ln_stats(tt, z[:, tt * DM:(tt + 1) * DM])
        p.wait("scalar", sttok)
        slots = [scr[:, 0:512], scr[:, 512:1024]]
        for oq in range(8):
            sl = slots[oq % 2]
            p.wait("scalar", ot_prev[oq % 2] if isinstance(ot_prev, list) else ot_prev)
            oft = p.op("scalar", lambda e, tt=tt, oq=oq, sl=sl: e.activation(
                out=sl, in_=z[:, tt * DM + oq * 512: tt * DM + (oq + 1) * 512],
                func=AF.Identity,
                bias=stat[:, NEGM + tt:NEGM + tt + 1],
                scale=stat[:, RSTD + tt:RSTD + tt + 1]), inc="ac")
            p.wait("sync", oft)
            tokx = p.dma("sync",
                         x4o[tt * 128:(tt + 1) * 128, oq * 512:(oq + 1) * 512],
                         sl, f"ox{oq % 2}")
            if not isinstance(ot_prev, list):
                ot_prev = [None, None]
            ot_prev[oq % 2] = tokx
    p.wait("sync", p.tok("ox0"))
    p.wait("sync", p.tok("ox1"))
    if ctx is None:
        p.emit()
        return nc


def stage_D(inputs, x2_full, yg_full):
    ident = np.eye(128, dtype=BF)
    woT = np.ascontiguousarray(np.asarray(inputs["L1_attn_Wo"]).T)
    w1T = np.ascontiguousarray(np.asarray(inputs["L1_ffn_base_w"]).T)
    A = np.asarray(inputs["L1_ffn_A"])
    rw = np.asarray(inputs["L1_ffn_router_w"])
    art = np.ascontiguousarray(np.concatenate([A.T, rw.T], axis=1))
    Bm = np.asarray(inputs["L1_ffn_B"])
    bpT = np.ascontiguousarray(
        np.stack([np.concatenate([Bm[2 * f].T, Bm[2 * f + 1].T], axis=0)
                  for f in range(8)]))
    b1 = np.ascontiguousarray(np.asarray(inputs["L1_ffn_base_b"])[None, :])
    in_maps = []
    for c in range(NCORE):
        b, s = c // 2, c % 2
        tsl = slice(s * 1024, (s + 1) * 1024)
        in_maps.append({
            "x2": np.ascontiguousarray(x2_full[b][tsl]),
            "yg": np.ascontiguousarray(yg_full[b][tsl]),
            "woT": woT, "w1T": w1T, "b1": b1, "art": art, "bpT": bpT,
            "ident": ident,
        })
    return in_maps




MAGIC = [0x5A5A0001, 0x5A5A0002, 0x5A5A0003]


def build_fused():
    nc = bass.Bass("TRN2", target_bir_lowering=False, num_devices=NCORE,
                   detect_race_conditions=False)
    p = Prog(nc)
    EI = lambda n, s, d: nc.dram_tensor(n, s, d, kind="ExternalInput").ap()
    x0 = EI("x0", [L, DM], F32)
    x0own = EI("x0own", [TPC, DM], F32)
    wT = {k: EI(f"w{k}T", [DM, 2048], F32) for k in "qkv"}
    bias = {k: EI(f"b{k}", [2048], F32) for k in "qkv"}
    bwT = EI("bwT", [3, 16], F32)
    bivmat = EI("bivmat", [3, 64], F32)
    revtile = EI("revtile", [16, 64], F32)
    scdelta = EI("scdelta", [16, 64], F32)
    ident = EI("ident", [128, 128], BF16)
    identf = EI("identf", [128, 128], F32)
    trif = EI("trif", [128, 128], F32)
    w0T = EI("w0T", [DM, DM], F32)
    bias0 = EI("bias0", [1, DM], F32)
    a0T = EI("a0T", [DM, 256], F32)
    b0cat = EI("b0cat", [256, DM], F32)
    coefr = EI("coefr", [256], F32)
    sclr = EI("sclr", [256], F32)
    wq1T = EI("wq1T", [DM, 2048], F32)
    wk1T = EI("wk1T", [DM, 512], F32)
    wv1T = EI("wv1T", [DM, 512], F32)
    wg1T = EI("wg1T", [DM, 2048], F32)
    win1T = EI("win1T", [DM, 16], F32)
    dtb = EI("dtb", [1, 16], F32)
    alog = EI("alog", [1, 16], F32)
    gnw = EI("gnw", [1, 128], F32)
    cosT = EI("cosT", [128, L], F32)
    sinT = EI("sinT", [128, L], F32)
    woT = EI("woT", [DM, DM], F32)
    w1T = EI("w1T", [DM, DM], F32)
    b1 = EI("b1", [1, DM], F32)
    art = EI("art", [DM, 80], F32)
    bpT = EI("bpT", [8, 128, DM], F32)
    svin = EI("sv", [1, 1], U32)
    x4o = nc.dram_tensor("x4o", [TPC, DM], F32, kind="ExternalOutput").ap()
    sh_attn = nc.dram_tensor("sh_attn", [2, 2, 1024, 2048], BF16,
                             kind="Internal", addr_space="Shared").ap()
    sh_x2 = nc.dram_tensor("sh_x2", [2, 1024, DM], F32,
                           kind="Internal", addr_space="Shared").ap()
    sh_yg = nc.dram_tensor("sh_yg", [2, 2, 1024, 2048], BF16,
                           kind="Internal", addr_space="Shared").ap()
    sh_flag = nc.dram_tensor("sh_flag", [3, 2, 1, 1], U32,
                             kind="Internal", addr_space="Shared").ap()
    PS = nc.alloc_psum_tensor("PS", [128, 4096], F32)

    sv_sb = nc.alloc_sbuf_tensor("sv_sb", [1, 1], U32)
    magic_sb = nc.alloc_sbuf_tensor("magic_sb", [1, 4], U32)
    flag_rx = nc.alloc_sbuf_tensor("flag_rx", [1, 4], U32)
    ARENA0 = (nc.sbuf_base + 512 + 31) // 32 * 32

    def mkal(prefix):
        off = [ARENA0]
        def al(name, shape, dtype):
            szmap = {F32: 4, BF16: 2, U32: 4}
            per = int(np.prod(shape[1:])) * szmap[dtype]
            t = nc.alloc_sbuf_tensor_at(f"{prefix}_{name}", shape, dtype, offset=off[0])
            off[0] += (per + 31) // 32 * 32
            assert off[0] <= nc.sbuf_top, f"{prefix}_{name} arena overflow {off[0]}"
            return t
        return al

    svh = {}
    lvt = p.dma("sync", sv_sb[:], svin, "lsv")
    p.wait("sync", lvt)
    p.raw("sync", lambda e: svh.__setitem__("v", e.value_load(sv_sb[0:1, 0:1])))
    for k in range(3):
        p.op("vector", lambda e, k=k: e.memset(magic_sb[0:1, k:k + 1], MAGIC[k]),
             inc="mgk")
    attn_loc = nc.dram_tensor("attn_loc", [L, 2048], BF16, kind="Internal").ap()
    attn_in = nc.dram_tensor("attn_in", [TPC, DM], BF16, kind="Internal").ap()
    x2_loc = nc.dram_tensor("x2_loc", [TPC, DM], F32, kind="Internal").ap()
    ygo_loc = nc.dram_tensor("ygo_loc", [L, 2048], BF16, kind="Internal").ap()
    yg_loc = nc.dram_tensor("yg_loc", [TPC, DM], BF16, kind="Internal").ap()

    def set_flag(k):
        p.wait("sync", p.tok("mgk"))
        p.dma("sync", lambda k=k: sh_flag[k][svh["v"]], magic_sb[0:1, k:k + 1],
              f"fw{k}")
        p.wait("sync", p.tok(f"fw{k}"))

    def poll_flag(k):
        psem = p.sem(f"poll{k}")
        def go(e, k=k, psem=psem):
            with e.register(f"pf{k}") as f, e.register(f"pc{k}") as cnt, \
                 e.register(f"pn{k}") as nd:
                e.reg_mov(f, 0)
                e.reg_mov(cnt, 0)
                e.reg_mov(nd, 1)
                with e.While(nd):
                    e.dma_start(out=flag_rx[0:1, k:k + 1],
                                in_=sh_flag[k][1 - svh["v"]]).then_inc(psem, 16)
                    e.reg_add(cnt, cnt, 16)
                    e.wait_ge(psem, cnt)
                    e.load(f, flag_rx[0:1, k:k + 1])
                    e.reg_alu(nd, f, MAGIC[k], ALU.not_equal)
        p.raw("sync", go)

    ctxA = dict(nc=nc, p=p, al=mkal("A"), PS=PS, x0=x0, wT=wT, bias=bias,
                bwT=bwT, bivmat=bivmat, revtile=revtile, scdelta=scdelta,
                ident=ident,
                attn_dst=lambda qt, g: (lambda qt=qt, g=g: attn_loc
                                        [qt * 128:(qt + 1) * 128,
                                         g * 512:(g + 1) * 512]))
    build_launch_A(ctxA)
    pub0 = p.dma("sync", lambda: sh_attn[svh["v"]],
                 attn_loc.rearrange("(a b) c -> a b c", b=1024), "pub0")
    p.wait("sync", pub0)
    set_flag(0)
    p.barrier("AB")
    poll_flag(0)
    g0 = p.dma("sync", attn_in[:, 0:2048], lambda: sh_attn[0][svh["v"]], "gat0")
    g1 = p.dma("sync", attn_in[:, 2048:4096], lambda: sh_attn[1][svh["v"]], "gat1")
    p.wait("sync", g0)
    p.wait("sync", g1)
    ctxB = dict(nc=nc, p=p, al=mkal("B"), PS=PS, x0own=x0own, w0T=w0T,
                bias0=bias0, a0T=a0T, b0cat=b0cat, coefr=coefr, sclr=sclr,
                ident=ident,
                attn_srcs=lambda tt: [(slice(0, DM),
                                       lambda tt=tt: attn_in[tt * 128:(tt + 1) * 128, :])],
                x2_dst=lambda tt: (lambda tt=tt: x2_loc[tt * 128:(tt + 1) * 128, :]))
    build_launch_B(ctxB)
    pub1 = p.dma("sync", lambda: sh_x2[svh["v"]], x2_loc, "pub1")
    p.wait("sync", pub1)
    set_flag(1)
    p.barrier("BC")
    poll_flag(1)
    ctxC = dict(nc=nc, p=p, al=mkal("C"), PS=PS,
                x2f=sh_x2.rearrange("a b c -> (a b) c"),
                wq1T=wq1T, wk1T=wk1T, wv1T=wv1T, wg1T=wg1T, win1T=win1T,
                dtb=dtb, alog=alog, gnw=gnw, cosT=cosT, sinT=sinT,
                ident=ident, identf=identf, trif=trif,
                yg_dst=lambda n, kv: (lambda n=n, kv=kv: ygo_loc
                                      [n * 128:(n + 1) * 128,
                                       kv * 512:(kv + 1) * 512]))
    build_launch_C(ctxC)
    pub2 = p.dma("sync", lambda: sh_yg[svh["v"]],
                 ygo_loc.rearrange("(a b) c -> a b c", b=1024), "pub2")
    p.wait("sync", pub2)
    set_flag(2)
    p.barrier("CD")
    poll_flag(2)
    g2 = p.dma("sync", yg_loc[:, 0:2048], lambda: sh_yg[0][svh["v"]], "gyg0")
    g3 = p.dma("sync", yg_loc[:, 2048:4096], lambda: sh_yg[1][svh["v"]], "gyg1")
    p.wait("sync", g2)
    p.wait("sync", g3)
    ctxD = dict(nc=nc, p=p, al=mkal("D"), PS=PS, woT=woT, w1T=w1T, b1=b1,
                art=art, bpT=bpT, ident=ident, x4o=x4o,
                x2_src=lambda tt: (lambda tt=tt: x2_loc[tt * 128:(tt + 1) * 128, :]),
                yg_srcs=lambda tt: [(slice(0, DM),
                                     lambda tt=tt: yg_loc[tt * 128:(tt + 1) * 128, :])])
    build_launch_D(ctxD)
    p.emit()
    return nc


def stage_fused(inputs):
    mA = stage_A(inputs)
    dummy_attn = np.zeros((B, L, DM), BF)
    dummy_x2 = np.zeros((B, L, DM), np.float32)
    dummy_yg = np.zeros((B, L, DM), BF)
    mB = stage_B(inputs, dummy_attn, inputs["L0_attn_x"])
    mC = stage_C(inputs, dummy_x2)
    mD = stage_D(inputs, dummy_x2, dummy_yg)
    in_maps = []
    for c in range(NCORE):
        m = {}
        m.update({k: v for k, v in mA[c].items()})
        m["x0own"] = mB[c]["x0"]
        for k in ["w0T", "bias0", "a0T", "b0cat", "coefr", "sclr"]:
            m[k] = mB[c][k]
        for k_new, k_old in [("wq1T", "wqT"), ("wk1T", "wkT"), ("wv1T", "wvT"),
                             ("wg1T", "wgT"), ("win1T", "winT")]:
            m[k_new] = mC[c][k_old]
        for k in ["dtb", "alog", "gnw", "cosT", "sinT", "identf", "trif"]:
            m[k] = mC[c][k]
        for k in ["woT", "w1T", "b1", "art", "bpT"]:
            m[k] = mD[c][k]
        m["sv"] = np.array([[c % 2]], dtype=np.uint32)
        in_maps.append(m)
    return in_maps


def kernel_fused(**inputs):
    inputs = {k: np.asarray(v) for k, v in inputs.items()}
    nc = _get_nc("F", build_fused)
    res = run_bass_kernel_spmd(nc, stage_fused(inputs), core_ids=list(range(NCORE)))
    out = np.concatenate(
        [np.concatenate([np.asarray(res.results[2 * b]["x4o"]),
                         np.asarray(res.results[2 * b + 1]["x4o"])], axis=0)[None]
         for b in range(B)], axis=0)
    return out


_NC_CACHE = {}


def _get_nc(name, builder):
    if name not in _NC_CACHE:
        _NC_CACHE[name] = builder()
    return _NC_CACHE[name]


def kernel(**inputs):
    return kernel_fused(**inputs)


def kernel_unfused(**inputs):
    inputs = {k: np.asarray(v) for k, v in inputs.items()}

    ncA = _get_nc("A", build_launch_A)
    resA = run_bass_kernel_spmd(ncA, stage_A(inputs), core_ids=list(range(NCORE)))
    attn_full = np.concatenate(
        [np.concatenate([np.asarray(resA.results[2 * b]["attn_out"]),
                         np.asarray(resA.results[2 * b + 1]["attn_out"])], axis=1)[None]
         for b in range(B)], axis=0)

    ncB = _get_nc("B", build_launch_B)
    resB = run_bass_kernel_spmd(ncB, stage_B(inputs, attn_full, inputs["L0_attn_x"]),
                                core_ids=list(range(NCORE)))
    x2_full = np.concatenate(
        [np.concatenate([np.asarray(resB.results[2 * b]["x2o"]),
                         np.asarray(resB.results[2 * b + 1]["x2o"])], axis=0)[None]
         for b in range(B)], axis=0)

    ncC = _get_nc("C", build_launch_C)
    resC = run_bass_kernel_spmd(ncC, stage_C(inputs, x2_full), core_ids=list(range(NCORE)))
    yg_full = np.concatenate(
        [np.concatenate([np.asarray(resC.results[2 * b]["ygo"]),
                         np.asarray(resC.results[2 * b + 1]["ygo"])], axis=1)[None]
         for b in range(B)], axis=0)

    ncD = _get_nc("D", build_launch_D)
    resD = run_bass_kernel_spmd(ncD, stage_D(inputs, x2_full, yg_full),
                                core_ids=list(range(NCORE)))
    out = np.concatenate(
        [np.concatenate([np.asarray(resD.results[2 * b]["x4o"]),
                         np.asarray(resD.results[2 * b + 1]["x4o"])], axis=0)[None]
         for b in range(B)], axis=0)
    return out


# version 07
# speedup vs baseline: 1.2119x; 1.0669x over previous
import math
import numpy as np
import ml_dtypes
import concourse.bass as bass
import concourse.mybir as mybir
from concourse.bass_utils import run_bass_kernel_spmd

F32 = mybir.dt.float32
BF16 = mybir.dt.bfloat16
U32 = mybir.dt.uint32
AF = mybir.ActivationFunctionType
ALU = mybir.AluOpType
AX = mybir.AxisListType

DM = 4096
B = 4
L = 2048
NCORE = 8
TPC = 1024
NH0, CD0, NB0 = 16, 32, 8
H1, KVH1, D1 = 32, 8, 128
CHUNK = 128
LN_EPS = 1e-5
RMS_EPS = 1e-5
BF = ml_dtypes.bfloat16
ENGS = ("tensor", "vector", "scalar", "gpsimd", "sync")


class Prog:

    def __init__(self, nc):
        self.nc = nc
        self.qs = {e: [] for e in ENGS}
        self.cnt = {}
        self.handles = {}
        self.lastwait = {e: {} for e in ENGS}

    def sem(self, name):
        if name not in self.handles:
            self.handles[name] = self.nc.alloc_semaphore(name)
            self.cnt[name] = 0
        return self.handles[name]

    def op(self, eng, fn, inc=None, k=1):
        if inc is not None:
            h = self.sem(inc)
            self.cnt[inc] += k
            val = self.cnt[inc]
            def run(e, fn=fn, h=h, k=k):
                fn(e).then_inc(h, k)
            self.qs[eng].append(run)
            return (inc, val)
        self.qs[eng].append(fn)
        return None

    def raw(self, eng, fn):
        self.qs[eng].append(fn)

    def barrier(self, tag=""):
        h = self.sem("bar")
        for e in ENGS:
            self.qs[e].append(lambda eng, h=h: eng.nop().then_inc(h, 1))
        self.cnt["bar"] += len(ENGS)
        tgt = self.cnt["bar"]
        for e in ENGS:
            self.qs[e].append(lambda eng, h=h, v=tgt: eng.wait_ge(h, v))
            self.lastwait[e]["bar"] = tgt

    def wait(self, eng, tok):
        if tok is None:
            return
        name, val = tok
        if val <= 0:
            return
        lw = self.lastwait[eng]
        if lw.get(name, -1) >= val:
            return
        lw[name] = val
        h = self.sem(name)
        self.qs[eng].append(lambda e, h=h, v=val: e.wait_ge(h, v))

    def tok(self, name):
        return (name, self.cnt.get(name, 0))

    def dma(self, eng, out, in_, inc, **kw):
        def run(e, o=out, i=in_, kw=kw):
            return e.dma_start(out=o() if callable(o) else o,
                               in_=i() if callable(i) else i, **kw)
        return self.op(eng, run, inc=inc, k=16)

    def emit(self):
        with self.nc.Block() as blk:
            for e in ENGS:
                q = self.qs[e]
                getattr(blk, e)(lambda eng, q=q: [f(eng) for f in q])


def _build_tables():
    dim, n = 3, 8
    masks = sorted(range(n), key=lambda m: (bin(m).count('1'), m))
    inv = {m: i for i, m in enumerate(masks)}
    C = np.zeros((n, n, n), dtype=np.float32)
    for i, a in enumerate(masks):
        for j, b_ in enumerate(masks):
            sw, at = (0, a >> 1)
            while at:
                sw += bin(at & b_).count('1')
                at >>= 1
            s = -1.0 if sw & 1 else 1.0
            C[i, j, inv[a ^ b_]] = s
    grades = np.array([bin(m).count('1') for m in masks])
    rev = np.where(grades * (grades - 1) // 2 % 2 == 1, -1.0, 1.0).astype(np.float32)
    return C, rev


def host_consts_L0():
    C, rev = _build_tables()
    biv = C[:, :, [4, 5, 6]]
    bivmat = np.ascontiguousarray(biv.transpose(2, 0, 1).reshape(3, 64)).astype(np.float32)
    revrep = np.repeat(rev, 8)
    revtile = np.tile(revrep[None, :], (16, 1)).astype(np.float32)
    scale = 1.0 / math.sqrt(CD0 * NB0)
    scdelta = np.tile((np.eye(8, dtype=np.float32).reshape(64) * revrep * scale)[None, :],
                      (16, 1)).astype(np.float32)
    return bivmat, revtile, scdelta


def host_rope_tables(positions):
    D = D1
    inv_freq = 1.0 / (10000.0 ** (np.arange(0, D, 2, dtype=np.float64) / D))
    ang = positions[:, None].astype(np.float64) * inv_freq[None, :]
    cos = np.concatenate([np.cos(ang), np.cos(ang)], axis=-1)
    sin = np.concatenate([np.sin(ang), np.sin(ang)], axis=-1)
    return (np.ascontiguousarray(cos.T).astype(np.float32),
            np.ascontiguousarray(sin.T).astype(np.float32))


def build_launch_A(ctx=None):
    if ctx is None:
        nc = bass.Bass("TRN2", target_bir_lowering=False, num_devices=NCORE)
        p = Prog(nc)
        x0 = nc.dram_tensor("x0", [L, DM], F32, kind="ExternalInput").ap()
        wT = {k: nc.dram_tensor(f"w{k}T", [DM, 2048], F32, kind="ExternalInput").ap()
              for k in "qkv"}
        bias = {k: nc.dram_tensor(f"b{k}", [2048], F32, kind="ExternalInput").ap()
                for k in "qkv"}
        bwT = nc.dram_tensor("bwT", [3, 16], F32, kind="ExternalInput").ap()
        bivmat = nc.dram_tensor("bivmat", [3, 64], F32, kind="ExternalInput").ap()
        revtile = nc.dram_tensor("revtile", [16, 64], F32, kind="ExternalInput").ap()
        scdelta = nc.dram_tensor("scdelta", [16, 64], F32, kind="ExternalInput").ap()
        ident = nc.dram_tensor("ident", [128, 128], BF16, kind="ExternalInput").ap()
        attn_out = nc.dram_tensor("attn_out", [L, 2048], BF16, kind="ExternalOutput").ap()
        attn_dst = lambda qt, g: (lambda: attn_out[qt * 128:(qt + 1) * 128,
                                                   g * 512:(g + 1) * 512])
    else:
        nc, p = ctx["nc"], ctx["p"]
        x0, wT, bias = ctx["x0"], ctx["wT"], ctx["bias"]
        bwT, bivmat, revtile, scdelta, ident = (ctx["bwT"], ctx["bivmat"],
                                                ctx["revtile"], ctx["scdelta"],
                                                ctx["ident"])
        attn_dst = ctx["attn_dst"]
    xT0d = nc.dram_tensor("xT0d", [32, 128, L], BF16, kind="Internal").ap()
    a_scr = nc.dram_tensor("a_scr", [16, 8, 8], BF16, kind="Internal").ap()

    al = ctx["al"] if ctx else nc.alloc_sbuf_tensor
    identb = al("identb", [128, 128], BF16)
    T_sb = al("T_sb", [128, 8 * 128], BF16)
    bcol = al("bcol", [128, 3 * 16], F32)
    xTh = al("xTh", [128, 32 * 1024], BF16)
    wst_f = al("wst_f", [128, 2 * DM], F32)
    wst_b = al("wst_b", [128, 2 * DM], BF16)
    qmodT = al("qmodT", [128, 4 * L], BF16)
    kTg = al("kTg", [128, 4 * L], BF16)
    vtok = al("vtok", [128, 16 * 512], BF16)
    qraw = al("qraw", [128, 1024], BF16)
    vtmp = al("vtmp", [128, 1024], BF16)
    Pb = al("Pb", [128, L], BF16)
    PT4 = al("PT4", [128, 16 * 512], BF16)
    avf = al("avf", [128, 512], BF16)
    outstg = al("outstg", [128, 4 * 512], BF16)
    negmax = al("negmax", [128, 1], F32)
    zrec = al("zrec", [128, 16], F32)
    wk_f = al("wk_f", [16, 64], F32)
    wk_b = al("wk_b", [16, 64], BF16)
    bw_sb = al("bw_sb", [3, 16], F32)
    biv_sb = al("biv_sb", [3, 64], F32)
    rev_sb = al("rev_sb", [16, 64], F32)
    scd_sb = al("scd_sb", [16, 64], F32)

    if ctx is None:
        ps_big = nc.alloc_psum_tensor("ps_big", [128, 2048], F32)
        ps_sm = nc.alloc_psum_tensor("ps_sm", [128, 512], F32)
        ps_tr = nc.alloc_psum_tensor("ps_tr", [128, 128], BF16)
    else:
        PS = ctx["PS"]
        ps_big = PS[:, 0:2048]
        ps_sm = PS[:, 2048:2560]
        ps_tr = PS.bitcast(BF16)[:, 5 * 1024: 5 * 1024 + 128]

    xf_t = xTh.bitcast(F32)[:, 0:DM]
    xb_t = xTh[:, 16384 // 2: 16384 // 2 + DM]
    stg = xTh[:, 24576 // 2: 24576 // 2 + DM]

    p.dma("sync", identb[:], ident, "ld")
    p.dma("sync", bw_sb[:], bwT, "ld")
    p.dma("sync", biv_sb[:], bivmat, "ld")
    p.dma("sync", rev_sb[:], revtile, "ld")
    p.dma("sync", scd_sb[:], scdelta, "ld")
    for j, k in enumerate("qkv"):
        for c in range(16):
            p.dma("sync", bcol[:, j * 16 + c: j * 16 + c + 1],
                  bias[k][c * 128:(c + 1) * 128].unsqueeze(1), "ld")
    p.wait("tensor", p.tok("ld"))
    mm0 = p.op("tensor", lambda e: e.matmul(out=ps_sm.bitcast(F32)[0:16, 0:64] if False else ps_sm[0:16, 0:64],
                                            lhsT=bw_sb[:], rhs=biv_sb[:],
                                            start=True, stop=True), inc="pe")
    p.wait("vector", mm0)
    dvx = p.op("vector", lambda e: e.tensor_tensor(out=wk_f[:], in0=ps_sm[0:16, 0:64],
                                                   in1=rev_sb[:], op=ALU.mult), inc="dv")
    p.wait("vector", dvx)
    dv0 = p.op("vector", lambda e: e.tensor_tensor(out=wk_f[:], in0=wk_f[:],
                                                   in1=scd_sb[:], op=ALU.add), inc="dv")
    p.wait("scalar", dv0)
    ac0 = p.op("scalar", lambda e: e.copy(out=wk_b[:], in_=wk_f[:]), inc="ac")
    p.wait("sync", ac0)
    wr0 = p.dma("sync", a_scr.rearrange("h i j -> h (i j)"), wk_b[:], "ascr")
    mz = p.op("vector", lambda e: e.memset(T_sb[:], 0.0), inc="dv")
    p.wait("sync", wr0)
    p.wait("sync", mz)
    for hh in range(8):
        for d in range(16):
            p.dma("sync", T_sb[d * 8:(d + 1) * 8,
                               hh * 128 + d * 8: hh * 128 + d * 8 + 8],
                  a_scr[hh], "tld")
    t_built = p.tok("tld")

    cast_tok = None
    tr_last = None
    st_tok = None
    for tt in range(16):
        if cast_tok is not None:
            p.wait("sync", cast_tok)
        ltok = p.dma("sync", xf_t, x0[tt * 128:(tt + 1) * 128, :], "ld")
        p.wait("scalar", ltok)
        if tr_last is not None:
            p.wait("scalar", tr_last)
        cast_tok = p.op("scalar", lambda e: e.copy(out=xb_t, in_=xf_t), inc="ac")
        p.wait("tensor", cast_tok)
        if st_tok is not None:
            p.wait("vector", st_tok)
        for c in range(32):
            tr_last = p.op("tensor", lambda e, c=c: e.transpose(
                out=ps_tr[:], in_=xb_t[:, c * 128:(c + 1) * 128], identity=identb[:]),
                inc="pe")
            p.wait("vector", tr_last)
            cpt = p.op("vector", lambda e, c=c: e.tensor_copy(
                out=stg[:, c * 128:(c + 1) * 128], in_=ps_tr[:]), inc="dv")
            p.wait("tensor", cpt)
        p.wait("sync", p.tok("dv"))
        st_tok = p.dma("sync", xT0d[:, :, tt * 128:(tt + 1) * 128],
                       stg.rearrange("p (c i) -> p c i", c=32), f"xtw{tt % 2}")
    p.wait("tensor", t_built)

    wbuf_user = [None, None]
    evac_tok = None
    qm_evac = None
    xTh_user = st_tok
    ext_prev = None
    av_done = None
    store_tok = None
    for g in range(4):
        for mh in range(2):
            if xTh_user is not None:
                p.wait("sync", xTh_user)
            xld = p.dma("sync", xTh.rearrange("p (c m) -> p c m", c=32),
                        xT0d[:, :, mh * 1024:(mh + 1) * 1024], "ld")
            for oc in range(12):
                kind, loc = ("q", oc) if oc < 4 else (("k", oc - 4) if oc < 8 else ("v", oc - 8))
                gch = g * 4 + loc
                db = oc % 2
                p.wait("gpsimd", wbuf_user[db])
                wct = p.dma("gpsimd",
                            wst_b[:, db * DM:(db + 1) * DM].rearrange("p (c o) -> p c o", c=32),
                            wT[kind][:, gch * 128:(gch + 1) * 128]
                            .rearrange("(c p) o -> p c o", p=128), f"wld{db}")
                p.wait("tensor", wct)
                p.wait("tensor", xld)
                if evac_tok is not None:
                    p.wait("tensor", evac_tok)
                for d in range(32):
                    for half in range(2):
                        mmt = p.op("tensor", lambda e, d=d, h=half, db=db: e.matmul(
                            out=ps_big[:, h * 512:(h + 1) * 512],
                            lhsT=wst_b[:, db * DM + d * 128: db * DM + (d + 1) * 128],
                            rhs=xTh[:, d * 1024 + h * 512: d * 1024 + (h + 1) * 512],
                            start=(d == 0), stop=(d == 31)),
                            inc="pe" if (d == 31 and half == 1) else None)
                wbuf_user[db] = mmt
                xTh_user = mmt
                p.wait("scalar", mmt)
                if kind == "q":
                    evac_tok = p.op("scalar", lambda e, gch=gch: e.activation(
                        out=qraw[:], in_=ps_big[:, 0:1024], func=AF.Identity,
                        bias=bcol[:, gch: gch + 1], scale=1.0), inc="ac")
                    p.wait("tensor", evac_tok)
                    if qm_evac is not None:
                        p.wait("tensor", qm_evac)
                    lhead = g * 2 + loc // 2
                    for half in range(2):
                        qmt = p.op("tensor", lambda e, h=half, lh=lhead: e.matmul(
                            out=ps_big[:, 1024 + h * 512: 1024 + (h + 1) * 512],
                            lhsT=T_sb[:, lh * 128:(lh + 1) * 128],
                            rhs=qraw[:, h * 512:(h + 1) * 512],
                            start=True, stop=True),
                            inc="pe" if half == 1 else None)
                    p.wait("scalar", qmt)
                    qm_evac = p.op("scalar", lambda e, loc=loc, mh=mh: e.copy(
                        out=qmodT[:, loc * L + mh * 1024: loc * L + (mh + 1) * 1024],
                        in_=ps_big[:, 1024:2048]), inc="ac")
                elif kind == "k":
                    evac_tok = p.op("scalar", lambda e, gch=gch, loc=loc, mh=mh: e.activation(
                        out=kTg[:, loc * L + mh * 1024: loc * L + (mh + 1) * 1024],
                        in_=ps_big[:, 0:1024], func=AF.Identity,
                        bias=bcol[:, 16 + gch: 16 + gch + 1], scale=1.0), inc="ac")
                else:
                    p.wait("scalar", p.tok("pe"))
                    evac_tok = p.op("scalar", lambda e, gch=gch: e.activation(
                        out=vtmp[:], in_=ps_big[:, 0:1024], func=AF.Identity,
                        bias=bcol[:, 32 + gch: 32 + gch + 1], scale=1.0), inc="ac")
                    p.wait("tensor", evac_tok)
                    for j in range(8):
                        trt = p.op("tensor", lambda e, j=j: e.transpose(
                            out=ps_tr[:], in_=vtmp[:, j * 128:(j + 1) * 128],
                            identity=identb[:]), inc="pe")
                        p.wait("vector", trt)
                        cpt = p.op("vector", lambda e, j=j, mh=mh, loc=loc: e.tensor_copy(
                            out=vtok[:, (mh * 8 + j) * 512 + loc * 128:
                                        (mh * 8 + j) * 512 + (loc + 1) * 128],
                            in_=ps_tr[:]), inc="dv")
                        p.wait("tensor", cpt)

        p.wait("tensor", p.tok("ac"))
        p.wait("tensor", p.tok("dv"))
        for qb in range(4):
            for hh2 in range(2):
                c0 = hh2 * 2
                for qi in range(4):
                    qt = qb * 4 + qi
                    p.wait("tensor", p.tok("ac"))
                    for cc in range(2):
                        for ms in range(4):
                            smt = p.op("tensor", lambda e, cc=cc, ms=ms, qt=qt, c0=c0: e.matmul(
                                out=ps_big[:, ms * 512:(ms + 1) * 512],
                                lhsT=qmodT[:, (c0 + cc) * L + qt * 128:
                                              (c0 + cc) * L + (qt + 1) * 128],
                                rhs=kTg[:, (c0 + cc) * L + ms * 512:
                                           (c0 + cc) * L + (ms + 1) * 512],
                                start=(cc == 0), stop=(cc == 1)),
                                inc="pe" if (cc == 1 and ms == 3) else None)
                    p.wait("vector", smt)
                    p.wait("vector", ext_prev)
                    rmt = p.op("vector", lambda e: e.reduce_max(
                        out=negmax[:], in_=ps_big[:], axis=AX.X, negate=True), inc="dv")
                    p.wait("scalar", rmt)
                    ext_prev = p.op("scalar", lambda e, qt=qt: e.activation(
                        out=Pb[:], in_=ps_big[:], func=AF.Exp,
                        bias=negmax[:], scale=1.0,
                        accum_out=zrec[:, qt:qt + 1]), inc="ac")
                    p.wait("vector", ext_prev)
                    p.op("vector", lambda e, qt=qt: e.reciprocal(
                        out=zrec[:, qt:qt + 1], in_=zrec[:, qt:qt + 1]), inc="dv")
                    p.wait("tensor", ext_prev)
                    p.wait("scalar", av_done)
                    for mc in range(16):
                        trt = p.op("tensor", lambda e, mc=mc: e.transpose(
                            out=ps_tr[:], in_=Pb[:, mc * 128:(mc + 1) * 128],
                            identity=identb[:]), inc="pe")
                        p.wait("scalar", trt)
                        cpt = p.op("scalar", lambda e, mc=mc, qi=qi: e.copy(
                            out=PT4[:, mc * 512 + qi * 128: mc * 512 + (qi + 1) * 128],
                            in_=ps_tr[:]), inc="ac")
                        p.wait("tensor", cpt)
                p.wait("scalar", p.tok("dv"))
                for cc2 in range(2):
                    p.wait("tensor", p.tok("ac"))
                    for mc in range(16):
                        avt = p.op("tensor", lambda e, mc=mc, hh2=hh2, cc2=cc2: e.matmul(
                            out=ps_sm[:],
                            lhsT=vtok[:, mc * 512 + hh2 * 256 + cc2 * 128:
                                         mc * 512 + hh2 * 256 + (cc2 + 1) * 128],
                            rhs=PT4[:, mc * 512:(mc + 1) * 512],
                            start=(mc == 0), stop=(mc == 15)),
                            inc="pe" if mc == 15 else None)
                    av_done = avt
                    p.wait("scalar", avt)
                    aet = p.op("scalar", lambda e: e.copy(out=avf[:], in_=ps_sm[:]), inc="ac")
                    p.wait("tensor", aet)
                    for qi in range(4):
                        trt = p.op("tensor", lambda e, qi=qi: e.transpose(
                            out=ps_tr[:], in_=avf[:, qi * 128:(qi + 1) * 128],
                            identity=identb[:]), inc="pe")
                        p.wait("scalar", trt)
                        p.wait("scalar", store_tok)
                        cpt = p.op("scalar", lambda e, qi=qi, qb=qb, hh2=hh2, cc2=cc2: e.activation(
                            out=outstg[:, qi * 512 + hh2 * 256 + cc2 * 128:
                                          qi * 512 + hh2 * 256 + (cc2 + 1) * 128],
                            in_=ps_tr[:], func=AF.Identity,
                            bias=0.0, scale=zrec[:, qb * 4 + qi: qb * 4 + qi + 1]), inc="ac")
                        p.wait("tensor", cpt)
            p.wait("sync", p.tok("ac"))
            for qi in range(4):
                store_tok = p.dma("sync", attn_dst(qb * 4 + qi, g),
                                  outstg[:, qi * 512:(qi + 1) * 512], f"ost{qb % 2}")
    p.wait("sync", p.tok("ost0"))
    p.wait("sync", p.tok("ost1"))
    if ctx is None:
        p.emit()
        return nc


def stage_A(inputs):
    bivmat, revtile, scdelta = host_consts_L0()
    ident = np.eye(128, dtype=BF)
    X = np.asarray(inputs["L0_attn_x"])
    WqT = np.ascontiguousarray(np.asarray(inputs["L0_attn_Wq"]).T)
    WkT = np.ascontiguousarray(np.asarray(inputs["L0_attn_Wk"]).T)
    WvT = np.ascontiguousarray(np.asarray(inputs["L0_attn_Wv"]).T)
    bw = np.asarray(inputs["L0_attn_bias_weight"])
    in_maps = []
    for c in range(NCORE):
        b, s = c // 2, c % 2
        sl = slice(s * 2048, (s + 1) * 2048)
        bwT_c = np.zeros((3, 16), np.float32)
        bwT_c[:, 0:8] = bw[s * 8:(s + 1) * 8, :].T
        in_maps.append({
            "x0": np.ascontiguousarray(X[b]),
            "wqT": np.ascontiguousarray(WqT[:, sl]),
            "wkT": np.ascontiguousarray(WkT[:, sl]),
            "wvT": np.ascontiguousarray(WvT[:, sl]),
            "bq": np.ascontiguousarray(np.asarray(inputs["L0_attn_bq"])[sl]),
            "bk": np.ascontiguousarray(np.asarray(inputs["L0_attn_bk"])[sl]),
            "bv": np.ascontiguousarray(np.asarray(inputs["L0_attn_bv"])[sl]),
            "bwT": bwT_c,
            "bivmat": bivmat, "revtile": revtile, "scdelta": scdelta,
            "ident": ident,
        })
    return in_maps




def build_launch_B(ctx=None):
    if ctx is None:
        nc = bass.Bass("TRN2", target_bir_lowering=False, num_devices=NCORE)
        p = Prog(nc)
        x0 = nc.dram_tensor("x0", [TPC, DM], F32, kind="ExternalInput").ap()
        attn = nc.dram_tensor("attn", [TPC, DM], BF16, kind="ExternalInput").ap()
        w0T = nc.dram_tensor("w0T", [DM, DM], F32, kind="ExternalInput").ap()
        bias0 = nc.dram_tensor("bias0", [1, DM], F32, kind="ExternalInput").ap()
        a0T = nc.dram_tensor("a0T", [DM, 256], F32, kind="ExternalInput").ap()
        b0cat = nc.dram_tensor("b0cat", [256, DM], F32, kind="ExternalInput").ap()
        coefr = nc.dram_tensor("coefr", [256], F32, kind="ExternalInput").ap()
        sclr = nc.dram_tensor("sclr", [256], F32, kind="ExternalInput").ap()
        ident = nc.dram_tensor("ident", [128, 128], BF16, kind="ExternalInput").ap()
        x2o = nc.dram_tensor("x2o", [TPC, DM], F32, kind="ExternalOutput").ap()
        attn_srcs = lambda tt: [(slice(0, DM),
                                 lambda tt=tt: attn[tt * 128:(tt + 1) * 128, :])]
        x2_dst = lambda tt: (lambda tt=tt: x2o[tt * 128:(tt + 1) * 128, :])
    else:
        nc, p = ctx["nc"], ctx["p"]
        x0 = ctx["x0own"]
        w0T, bias0, a0T, b0cat, coefr, sclr, ident = (
            ctx["w0T"], ctx["bias0"], ctx["a0T"], ctx["b0cat"], ctx["coefr"],
            ctx["sclr"], ctx["ident"])
        attn_srcs = ctx["attn_srcs"]
        x2_dst = ctx["x2_dst"]
    xT1d = nc.dram_tensor("xT1d", [32, 128, TPC], BF16, kind="Internal").ap()

    al = ctx["al"] if ctx else nc.alloc_sbuf_tensor
    identb = al("identb", [128, 128], BF16)
    z = al("z", [128, 8 * DM], F32)
    scr = al("scr", [128, 1024], F32)
    a0b = al("a0b", [128, 32 * 256], BF16)
    b0s = al("b0s", [128, 2 * 512], BF16)
    xaT = al("xaT", [128, 2 * TPC], BF16)
    xw = al("xw", [128, 2 * 4096], BF16)
    stg = xw[:, 0:DM]
    wfs = al("wfs", [128, 2 * 2048], F32)
    wbs = al("wbs", [128, 2 * 2048], BF16)
    x1tmp = al("x1tmp", [128, 512], F32)
    brb = al("brb", [1, 512], BF16)
    ones1b = al("ones1b", [1, 128], BF16)
    stat = al("stat", [128, 48], F32)
    wcol = al("wcol", [128, 2], F32)
    ccol = al("ccol", [128, 2], F32)
    scol = al("scol", [128, 2], F32)

    ps = ctx["PS"] if ctx else nc.alloc_psum_tensor("ps", [128, 4096], F32)
    ps_trv = ps.bitcast(BF16)[:, 0:128]
    ps_xa = ps[:, 512:768]

    b0f = scr[:, 0:1024]
    ZS, ZSS, RSTD, NEGM = 0, 8, 16, 24
    MEAN, VAR = 32, 40

    def ln_stats(tt, ztile, pfx):
        p.wait("vector", p.tok("ac"))
        p.wait("scalar", p.tok("dv"))
        t1 = p.op("vector", lambda e: e.reduce_sum(
            out=stat[:, ZS + tt:ZS + tt + 1], in_=ztile, axis=AX.X), inc="dv")
        t2 = p.op("scalar", lambda e: e.activation(
            out=stg[:, 0:DM], in_=ztile, func=AF.Square,
            bias=0.0, scale=1.0, accum_out=stat[:, ZSS + tt:ZSS + tt + 1]), inc="ac")
        p.wait("vector", t1)
        p.wait("vector", t2)
        t3 = p.op("vector", lambda e: e.tensor_scalar(
            out=stat[:, MEAN:MEAN + 1], in0=stat[:, ZS + tt:ZS + tt + 1],
            scalar1=1.0 / DM, scalar2=None, op0=ALU.mult), inc="dv")
        p.wait("vector", t3)
        t4 = p.op("vector", lambda e: e.tensor_scalar(
            out=stat[:, VAR:VAR + 1], in0=stat[:, ZSS + tt:ZSS + tt + 1],
            scalar1=1.0 / DM, scalar2=None, op0=ALU.mult), inc="dv")
        p.wait("scalar", t3)
        t5 = p.op("scalar", lambda e: e.activation(
            out=stat[:, MEAN + 1:MEAN + 2], in_=stat[:, MEAN:MEAN + 1],
            func=AF.Square, bias=0.0, scale=1.0), inc="ac")
        p.wait("vector", t4)
        p.wait("vector", t5)
        t6 = p.op("vector", lambda e: e.tensor_tensor(
            out=stat[:, VAR:VAR + 1], in0=stat[:, VAR:VAR + 1],
            in1=stat[:, MEAN + 1:MEAN + 2], op=ALU.subtract), inc="dv")
        p.wait("vector", t6)
        t6b = p.op("vector", lambda e: e.tensor_scalar(
            out=stat[:, VAR:VAR + 1], in0=stat[:, VAR:VAR + 1],
            scalar1=LN_EPS, scalar2=None, op0=ALU.add), inc="dv")
        p.wait("scalar", t6b)
        t7 = p.op("scalar", lambda e: e.activation(
            out=stat[:, VAR + 1:VAR + 2], in_=stat[:, VAR:VAR + 1],
            func=AF.Ln, bias=0.0, scale=1.0), inc="ac")
        p.wait("scalar", t7)
        t8 = p.op("scalar", lambda e: e.activation(
            out=stat[:, RSTD + tt:RSTD + tt + 1], in_=stat[:, VAR + 1:VAR + 2],
            func=AF.Exp, bias=0.0, scale=-0.5), inc="ac")
        p.wait("vector", t8)
        p.wait("scalar", t8)
        t9 = p.op("vector", lambda e: e.tensor_tensor(
            out=stat[:, NEGM + tt:NEGM + tt + 1], in0=stat[:, MEAN:MEAN + 1],
            in1=stat[:, RSTD + tt:RSTD + tt + 1], op=ALU.mult), inc="dv")
        p.wait("vector", t9)
        t10 = p.op("vector", lambda e: e.tensor_scalar(
            out=stat[:, NEGM + tt:NEGM + tt + 1], in0=stat[:, NEGM + tt:NEGM + tt + 1],
            scalar1=-1.0, scalar2=None, op0=ALU.mult), inc="dv")
        return t10

    p.dma("sync", identb[:], ident, "lc")
    mzt = p.op("vector", lambda e: e.memset(ones1b[:], 1.0), inc="dv")
    p.wait("scalar", p.tok("lc"))
    bct = mzt
    for rc in range(2):
        p.dma("sync", ccol[:, rc:rc + 1], coefr[rc * 128:(rc + 1) * 128].unsqueeze(1), "lcc")
        p.dma("sync", scol[:, rc:rc + 1], sclr[rc * 128:(rc + 1) * 128].unsqueeze(1), "lcc")
    p.wait("vector", p.tok("lcc"))
    wct = p.op("vector", lambda e: e.tensor_tensor(out=wcol[:], in0=ccol[:],
                                                   in1=scol[:], op=ALU.mult), inc="dv")
    p.wait("sync", bct)
    for half in range(2):
        ct = p.dma("gpsimd",
                   a0b[:, half * 16 * 256:(half + 1) * 16 * 256]
                   .rearrange("p (c r) -> p c r", c=16),
                   a0T[half * 2048:(half + 1) * 2048, :]
                   .rearrange("(c p) r -> p c r", p=128), f"la{half}")
        p.wait("tensor", ct)

    at_b = xw[:, DM:2 * DM]
    xat = scr.bitcast(BF16)[:, 0:256]
    stg_user = None
    xa_ev = None
    for tt in range(8):
        p.wait("sync", p.tok("pool"))
        for dsl, srcfn in attn_srcs(tt):
            lt1 = p.dma("sync", at_b[:, dsl], srcfn, f"lat{tt % 2}")
        lt2 = p.dma("sync", z[:, tt * DM:(tt + 1) * DM],
                    x0[tt * 128:(tt + 1) * 128, :], f"lx{tt % 2}")
        p.wait("vector", lt1)
        p.wait("vector", lt2)
        zt = p.op("vector", lambda e, tt=tt: e.scalar_tensor_tensor(
            out=z[:, tt * DM:(tt + 1) * DM], in0=at_b, scalar=1.0,
            in1=z[:, tt * DM:(tt + 1) * DM], op0=ALU.mult, op1=ALU.add), inc="dv")
        p.wait("vector", zt)
        p.wait("scalar", zt)
        if stg_user is not None:
            p.wait("scalar", stg_user)
        sttok = ln_stats(tt, z[:, tt * DM:(tt + 1) * DM], "b")
        p.wait("scalar", sttok)
        x1t = p.op("scalar", lambda e, tt=tt: e.activation(
            out=at_b, in_=z[:, tt * DM:(tt + 1) * DM], func=AF.Identity,
            bias=stat[:, NEGM + tt:NEGM + tt + 1],
            scale=stat[:, RSTD + tt:RSTD + tt + 1]), inc="ac")
        p.wait("tensor", x1t)
        for c in range(32):
            trt = p.op("tensor", lambda e, c=c: e.transpose(
                out=ps_trv, in_=at_b[:, c * 128:(c + 1) * 128],
                identity=identb[:]), inc="pe")
            p.wait("vector", trt)
            cpt = p.op("vector", lambda e, c=c: e.tensor_copy(
                out=stg[:, c * 128:(c + 1) * 128], in_=ps_trv), inc="dv")
            p.wait("tensor", cpt)
        p.wait("sync", p.tok("dv"))
        stg_user = p.dma("sync",
                         xT1d[:, :, tt * 128:(tt + 1) * 128].transpose([1, 0, 2]),
                         stg.rearrange("p (c i) -> p c i", c=32), f"xtw{tt % 2}")
        p.wait("tensor", xa_ev)
        for d in range(32):
            xmt = p.op("tensor", lambda e, d=d: e.matmul(
                out=ps_xa, lhsT=stg[:, d * 128:(d + 1) * 128],
                rhs=a0b[:, d * 256:(d + 1) * 256],
                start=(d == 0), stop=(d == 31)), inc="pe" if d == 31 else None)
        p.wait("scalar", xmt)
        xa_ev = p.op("scalar", lambda e: e.copy(out=xat, in_=ps_xa), inc="ac")
        p.wait("tensor", xa_ev)
        for rc in range(2):
            trt = p.op("tensor", lambda e, rc=rc: e.transpose(
                out=ps_trv, in_=xat[:, rc * 128:(rc + 1) * 128], identity=identb[:]),
                inc="pe")
            p.wait("scalar", trt)
            xa_ev = p.op("scalar", lambda e, rc=rc, tt=tt: e.activation(
                out=xaT[:, rc * TPC + tt * 128: rc * TPC + (tt + 1) * 128],
                in_=ps_trv, func=AF.Identity, bias=0.0,
                scale=wcol[:, rc:rc + 1]), inc="ac")
            p.wait("tensor", xa_ev)
        p.wait("sync", p.tok("ac"))
    p.wait("sync", p.tok("xtw0"))
    p.wait("sync", p.tok("xtw1"))
    wu = [None, None]
    ev_tok = None
    xu = None
    prev_bmt = None
    for osp in range(8):
        p.wait("sync", p.tok("dv"))
        brl = p.dma("sync", scr[0:1, 512:1024],
                    bias0[0:1, osp * 512:(osp + 1) * 512], f"brl{osp % 2}")
        p.wait("scalar", brl)
        p.wait("scalar", prev_bmt)
        brc = p.op("scalar", lambda e: e.copy(out=brb[:], in_=scr[0:1, 512:1024]),
                   inc="ac")
        for dg in range(8):
            db = dg % 2
            p.wait("scalar", wu[db])
            xl = p.op("scalar", lambda e, dg=dg, db=db: e.dma_start(
                out=xw[:, db * 4096:(db + 1) * 4096].rearrange("p (c m) -> p c m", c=4),
                in_=xT1d[dg * 4:(dg + 1) * 4].transpose([1, 0, 2])), inc=f"xl{db}", k=16)
            p.wait("gpsimd", wu[db])
            wc = p.dma("gpsimd", wbs[:, db * 2048:(db + 1) * 2048]
                       .rearrange("p (c o) -> p c o", c=4),
                       w0T[dg * 512:(dg + 1) * 512, osp * 512:(osp + 1) * 512]
                       .rearrange("(c p) o -> p c o", p=128), f"wl{db}")
            p.wait("tensor", wc)
            p.wait("tensor", xl)
            if dg < 2 and ev_tok is not None:
                p.wait("tensor", ev_tok)
            for dc in range(4):
                for tt in range(8):
                    mmt = p.op("tensor", lambda e, tt=tt, dc=dc, db=db, dg=dg: e.matmul(
                        out=ps[:, tt * 512:(tt + 1) * 512],
                        lhsT=xw[:, db * 4096 + dc * 1024 + tt * 128:
                                   db * 4096 + dc * 1024 + (tt + 1) * 128],
                        rhs=wbs[:, db * 2048 + dc * 512: db * 2048 + (dc + 1) * 512],
                        start=(dg == 0 and dc == 0), stop=False),
                        inc="pe" if (tt == 7 and dc == 3) else None)
            wu[db] = mmt
        p.wait("gpsimd", p.tok("pe"))
        p.wait("sync", brc)
        for rc in range(2):
            bctk = p.dma("gpsimd", b0s[:, rc * 512:(rc + 1) * 512],
                         b0cat[rc * 128:(rc + 1) * 128, osp * 512:(osp + 1) * 512],
                         f"lb{rc}")
            p.wait("tensor", bctk)
        for tt in range(8):
            for rc in range(2):
                p.op("tensor", lambda e, tt=tt, rc=rc, osp=osp: e.matmul(
                    out=ps[:, tt * 512:(tt + 1) * 512],
                    lhsT=xaT[:, rc * TPC + tt * 128: rc * TPC + (tt + 1) * 128],
                    rhs=b0s[:, rc * 512:(rc + 1) * 512],
                    start=False, stop=False))
            if tt == 0:
                p.wait("tensor", brc)
            bmt = p.op("tensor", lambda e, tt=tt: e.matmul(
                out=ps[:, tt * 512:(tt + 1) * 512],
                lhsT=ones1b[:], rhs=brb[:],
                start=False, stop=True), inc="pe")
            prev_bmt = bmt
            p.wait("scalar", bmt)
            x1o = p.op("scalar", lambda e, tt=tt, osp=osp: e.activation(
                out=x1tmp[:], in_=z[:, tt * DM + osp * 512: tt * DM + (osp + 1) * 512],
                func=AF.Identity, bias=stat[:, NEGM + tt:NEGM + tt + 1],
                scale=stat[:, RSTD + tt:RSTD + tt + 1]), inc="ac")
            p.wait("vector", x1o)
            ev_tok = p.op("vector", lambda e, tt=tt, osp=osp: e.tensor_tensor(
                out=z[:, tt * DM + osp * 512: tt * DM + (osp + 1) * 512],
                in0=ps[:, tt * 512:(tt + 1) * 512], in1=x1tmp[:], op=ALU.add), inc="dv")
            p.wait("scalar", ev_tok)
    ot_prev = [None, None]
    for tt in range(8):
        p.wait("vector", p.tok("dv"))
        p.wait("scalar", p.tok("dv"))
        sttok = ln_stats(tt, z[:, tt * DM:(tt + 1) * DM], "b2")
        p.wait("scalar", sttok)
        slots = [x1tmp[:], scr[:, 0:512]]
        for oq in range(8):
            sl = slots[oq % 2]
            p.wait("scalar", ot_prev[oq % 2])
            oft = p.op("scalar", lambda e, tt=tt, oq=oq, sl=sl: e.activation(
                out=sl, in_=z[:, tt * DM + oq * 512: tt * DM + (oq + 1) * 512],
                func=AF.Identity,
                bias=stat[:, NEGM + tt:NEGM + tt + 1],
                scale=stat[:, RSTD + tt:RSTD + tt + 1]), inc="ac")
            p.wait("sync", oft)
            ot_prev[oq % 2] = p.dma("sync",
                                    lambda tt=tt, oq=oq: x2_dst(tt)()[:, oq * 512:(oq + 1) * 512],
                                    sl, f"ox{oq % 2}")
    p.wait("sync", p.tok("ox0"))
    p.wait("sync", p.tok("ox1"))
    if ctx is None:
        p.emit()
        return nc


def stage_B(inputs, attn_full, x0_full):
    ident = np.eye(128, dtype=BF)
    w0T = np.ascontiguousarray(np.asarray(inputs["L0_ffn_W"]).T)
    A = np.asarray(inputs["L0_ffn_A"])
    a0T = np.ascontiguousarray(A.reshape(256, DM).T)
    Bf = np.asarray(inputs["L0_ffn_Bf"])
    b0cat = np.ascontiguousarray(Bf.transpose(0, 2, 1).reshape(256, DM))
    coefr = np.ascontiguousarray(np.repeat(np.asarray(inputs["L0_ffn_coeffs"]), 16))
    sclr = np.ascontiguousarray(np.repeat(np.asarray(inputs["L0_ffn_scales"]), 16))
    bias0 = np.ascontiguousarray(np.asarray(inputs["L0_ffn_bias"])[None, :])
    in_maps = []
    for c in range(NCORE):
        b, s = c // 2, c % 2
        tsl = slice(s * 1024, (s + 1) * 1024)
        in_maps.append({
            "x0": np.ascontiguousarray(x0_full[b][tsl]),
            "attn": np.ascontiguousarray(attn_full[b][tsl]),
            "w0T": w0T, "bias0": bias0, "a0T": a0T, "b0cat": b0cat,
            "coefr": coefr, "sclr": sclr, "ident": ident,
        })
    return in_maps




def build_launch_C(ctx=None):
    if ctx is None:
        nc = bass.Bass("TRN2", target_bir_lowering=False, num_devices=NCORE)
        p = Prog(nc)
        x2f = nc.dram_tensor("x2f", [L, DM], F32, kind="ExternalInput").ap()
        wqT = nc.dram_tensor("wqT", [DM, 2048], F32, kind="ExternalInput").ap()
        wkT = nc.dram_tensor("wkT", [DM, 512], F32, kind="ExternalInput").ap()
        wvT = nc.dram_tensor("wvT", [DM, 512], F32, kind="ExternalInput").ap()
        wgT = nc.dram_tensor("wgT", [DM, 2048], F32, kind="ExternalInput").ap()
        winT = nc.dram_tensor("winT", [DM, 16], F32, kind="ExternalInput").ap()
        dtb = nc.dram_tensor("dtb", [1, 16], F32, kind="ExternalInput").ap()
        alog = nc.dram_tensor("alog", [1, 16], F32, kind="ExternalInput").ap()
        gnw = nc.dram_tensor("gnw", [1, 128], F32, kind="ExternalInput").ap()
        cosT = nc.dram_tensor("cosT", [128, L], F32, kind="ExternalInput").ap()
        sinT = nc.dram_tensor("sinT", [128, L], F32, kind="ExternalInput").ap()
        ident = nc.dram_tensor("ident", [128, 128], BF16, kind="ExternalInput").ap()
        identf = nc.dram_tensor("identf", [128, 128], F32, kind="ExternalInput").ap()
        trif = nc.dram_tensor("trif", [128, 128], F32, kind="ExternalInput").ap()
        ygo = nc.dram_tensor("ygo", [L, 2048], BF16, kind="ExternalOutput").ap()
        yg_dst = lambda n, kv: (lambda: ygo[n * 128:(n + 1) * 128,
                                            kv * 512:(kv + 1) * 512])
    else:
        nc, p = ctx["nc"], ctx["p"]
        x2f = ctx["x2f"]
        wqT, wkT, wvT, wgT, winT = (ctx["wq1T"], ctx["wk1T"], ctx["wv1T"],
                                    ctx["wg1T"], ctx["win1T"])
        dtb, alog, gnw, cosT, sinT = (ctx["dtb"], ctx["alog"], ctx["gnw"],
                                      ctx["cosT"], ctx["sinT"])
        ident, identf, trif = ctx["ident"], ctx["identf"], ctx["trif"]
        yg_dst = ctx["yg_dst"]
    NCh = 16
    xT2d = nc.dram_tensor("xT2d", [32, 128, L], BF16, kind="Internal").ap()

    al = ctx["al"] if ctx else nc.alloc_sbuf_tensor
    identb = al("identb", [128, 128], BF16)
    idenff = al("idenff", [128, 128], F32)
    tri_f = al("tri_f", [128, 128], F32)
    cos_s = al("cos_s", [128, L], F32)
    sin_s = al("sin_s", [128, L], F32)
    scr = al("scr", [128, 6144], F32)
    stg = al("stg", [128, DM], BF16)
    xTh = al("xTh", [128, 32 * 1024], BF16)
    wst_b = al("wst_b", [128, 2 * DM], BF16)
    qfm = al("qfm", [128, 4 * L], BF16)
    gtok = al("gtok", [128, 16 * 512], BF16)
    kfm = al("kfm", [128, L], BF16)
    ktok = al("ktok", [128, 16 * 128], BF16)
    vtok = al("vtok", [128, 16 * 128], BF16)
    winb = al("winb", [128, 32 * 16], BF16)
    winlo = al("winlo", [128, 32 * 16], BF16)
    xlob = al("xlob", [128, DM], BF16)
    dtbb = al("dtbb", [1, 16], BF16)
    dtv = al("dtv", [128, 16 * 16], F32)
    Gv = al("Gv", [128, 16 * 16], F32)
    nGv = al("nGv", [128, 16 * 16], F32)
    Gl = al("Gl", [1, 16 * 16], F32)
    bcGl = al("bcGl", [128, 16 * 16], F32)
    edcv = al("edcv", [128, 16 * 16], F32)
    deccv = al("deccv", [128, 16 * 16], F32)
    egcv = al("egcv", [128, 16 * 16], F32)
    Aneg = al("Aneg", [128, 16], F32)
    gnb = al("gnb", [128, 128], F32)
    ones1b = al("ones1b", [1, 128], BF16)
    ones1f = al("ones1f", [1, 128], F32)
    onecf = al("onecf", [128, 1], F32)
    prevC = al("prevC", [128, 4 * 128], F32)
    prevb = al("prevb", [128, 4 * 128], BF16)
    T1 = al("T1", [128, 128], F32)
    dmask = al("dmask", [128, 128], F32)
    mt = al("mt", [128, 128], BF16)
    tlo = mt
    ytmp = T1[:, :]
    yjunk = mt
    xct = al("xct", [128, 128], BF16)
    kdt = al("kdt", [128, 128], BF16)
    ytile = al("ytile", [128, 128], F32)
    ygstg = al("ygstg", [128, 512], BF16)
    ccol = al("ccol", [128, 1], F32)
    alog_s = al("alog_s", [1, 16], F32)
    row16 = al("row16", [1, 16], F32)

    if ctx is None:
        ps_proj = nc.alloc_psum_tensor("ps_proj", [128, 1024], F32)
        ps_attn = nc.alloc_psum_tensor("ps_attn", [128, 2048], F32)
        ps_trf = nc.alloc_psum_tensor("ps_trf", [128, 128], F32)
        ps_tr = nc.alloc_psum_tensor("ps_tr", [128, 128], BF16)
    else:
        PS = ctx["PS"]
        ps_proj = PS[:, 0:1024]
        ps_attn = PS[:, 1024:3072]
        ps_trf = PS[:, 3072:3200]
        ps_tr = PS.bitcast(BF16)[:, 7 * 1024: 7 * 1024 + 128]
    ps_st = ps_attn[:, 0:128]
    ps_yA = ps_attn[:, 512:640]
    ps_yB = ps_attn[:, 1024:1152]
    ps_C = ps_attn[:, 1536:1664]
    ps_dt = ps_attn[:, 0:16]
    ps_G = ps_attn[:, 512:528]
    ps_gl = ps_attn[0:1, 1024:1040]
    ps_bc = ps_attn[:, 1536:1552]

    p.dma("sync", identb[:], ident, "lc")
    p.dma("sync", idenff[:], identf, "lc")
    p.dma("sync", tri_f[:], trif, "lc")
    p.dma("sync", cos_s[:], cosT, "lc")
    p.dma("sync", sin_s[:], sinT, "lc")
    p.dma("sync", alog_s[:], alog, "lc")
    p.dma("sync", scr[0:1, 0:16], dtb, "lc")
    p.dma("sync", scr[0:1, 16:144], gnw, "lc")
    wl = p.dma("sync", scr[:, 2048:2048 + 512].rearrange("p (c r) -> p c r", c=32),
               winT.rearrange("(c p) r -> p c r", p=128), "lc")
    mz1 = p.op("vector", lambda e: e.memset(ones1b[:], 1.0), inc="dv")
    mz2 = p.op("vector", lambda e: e.memset(ones1f[:], 1.0), inc="dv")
    mz3 = p.op("vector", lambda e: e.memset(onecf[:], 1.0), inc="dv")
    mz4 = p.op("vector", lambda e: e.memset(prevC[:], 0.0), inc="dv")
    mz5 = p.op("vector", lambda e: e.memset(prevb[:], 0.0), inc="dv")
    p.wait("scalar", p.tok("lc"))
    p.wait("scalar", p.tok("dv"))
    c1 = p.op("scalar", lambda e: e.copy(out=dtbb[:], in_=scr[0:1, 0:16]), inc="ac")
    c2 = p.op("scalar", lambda e: e.copy(out=winb[:], in_=scr[:, 2048:2048 + 512]), inc="ac")
    p.wait("vector", c2)
    c2b = p.op("vector", lambda e: e.tensor_tensor(out=winlo[:], in0=scr[:, 2048:2048 + 512],
                                                   in1=winb[:], op=ALU.subtract), inc="dv")
    c3 = p.op("scalar", lambda e: e.activation(out=row16[:], in_=alog_s[:],
                                               func=AF.Exp, bias=0.0, scale=1.0), inc="ac")
    p.wait("vector", c3)
    c4 = p.op("vector", lambda e: e.tensor_scalar(out=row16[:], in0=row16[:],
                                                  scalar1=-1.0, scalar2=None,
                                                  op0=ALU.mult), inc="dv")
    p.wait("tensor", c4)
    bc1 = p.op("tensor", lambda e: e.matmul(out=ps_attn[:, 0:16], lhsT=ones1f[:],
                                            rhs=row16[:], start=True, stop=True), inc="pe")
    p.wait("vector", bc1)
    an1 = p.op("vector", lambda e: e.tensor_copy(out=Aneg[:], in_=ps_attn[:, 0:16]), inc="dv")
    p.wait("scalar", p.tok("dv"))
    c5 = p.op("scalar", lambda e: e.copy(out=gnb[0:1, :],
                                         in_=scr[0:1, 16:144]), inc="ac")
    p.wait("tensor", c5)
    p.wait("tensor", an1)
    bc2 = p.op("tensor", lambda e: e.matmul(out=ps_attn[:, 0:128], lhsT=ones1f[:],
                                            rhs=gnb[0:1, :], start=True, stop=True), inc="pe")
    p.wait("vector", bc2)
    gn1 = p.op("vector", lambda e: e.tensor_copy(out=gnb[:], in_=ps_attn[:, 0:128]), inc="dv")

    xf = scr[:, 0:DM]
    xb = scr.bitcast(BF16)[:, 2 * DM: 3 * DM]
    dtr = scr[:, 5500:5516]
    sp1 = scr[:, 5520:5536]
    sp2 = scr[:, 5540:5556]
    gt = scr[:, 5560:5576]
    p.wait("sync", p.tok("ac"))
    cast_tok = None
    tr_last = None
    st_tok = None
    dt_ev = p.tok("dv")
    for tt in range(NCh):
        if cast_tok is not None:
            p.wait("sync", cast_tok)
        ltok = p.dma("sync", xf, x2f[tt * 128:(tt + 1) * 128, :], f"ldx{tt % 2}")
        p.wait("scalar", ltok)
        if tr_last is not None:
            p.wait("scalar", tr_last)
        cast_tok = p.op("scalar", lambda e: e.copy(out=xb, in_=xf), inc="ac")
        p.wait("vector", cast_tok)
        lo_tok = p.op("vector", lambda e: e.tensor_tensor(
            out=xlob[:], in0=xf, in1=xb, op=ALU.subtract), inc="dv")
        p.wait("tensor", cast_tok)
        p.wait("tensor", lo_tok)
        if st_tok is not None:
            p.wait("vector", st_tok)
        p.wait("tensor", dt_ev)
        for c in range(32):
            tr_last = p.op("tensor", lambda e, c=c: e.transpose(
                out=ps_tr[:], in_=xb[:, c * 128:(c + 1) * 128], identity=identb[:]),
                inc="pe")
            p.wait("vector", tr_last)
            cpt = p.op("vector", lambda e, c=c: e.tensor_copy(
                out=stg[:, c * 128:(c + 1) * 128], in_=ps_tr[:]), inc="dv")
            p.wait("tensor", cpt)
            trl = p.op("tensor", lambda e, c=c: e.transpose(
                out=ps_tr[:], in_=xlob[:, c * 128:(c + 1) * 128], identity=identb[:]),
                inc="pe")
            p.wait("vector", trl)
            cpl = p.op("vector", lambda e: e.tensor_copy(out=tlo[:], in_=ps_tr[:]),
                       inc="dv")
            p.wait("tensor", cpl)
            p.op("tensor", lambda e, c=c: e.matmul(
                out=ps_dt, lhsT=stg[:, c * 128:(c + 1) * 128],
                rhs=winb[:, c * 16:(c + 1) * 16],
                start=(c == 0), stop=False))
            p.op("tensor", lambda e, c=c: e.matmul(
                out=ps_dt, lhsT=tlo[:], rhs=winb[:, c * 16:(c + 1) * 16],
                start=False, stop=False))
            p.op("tensor", lambda e, c=c: e.matmul(
                out=ps_dt, lhsT=stg[:, c * 128:(c + 1) * 128],
                rhs=winlo[:, c * 16:(c + 1) * 16],
                start=False, stop=False))
        p.wait("sync", p.tok("dv"))
        st_tok = p.dma("sync",
                       xT2d[:, :, tt * 128:(tt + 1) * 128].transpose([1, 0, 2]),
                       stg.rearrange("p (c i) -> p c i", c=32), f"xtw{tt % 2}")
        dmt = p.op("tensor", lambda e: e.matmul(
            out=ps_dt, lhsT=ones1b[:], rhs=dtbb[:], start=False, stop=True), inc="pe")
        p.wait("scalar", dmt)
        p.wait("scalar", p.tok("dv"))
        e1 = p.op("scalar", lambda e: e.copy(out=dtr, in_=ps_dt), inc="ac")
        p.wait("scalar", e1)
        e2 = p.op("scalar", lambda e: e.activation(out=sp1, in_=dtr, func=AF.Abs,
                                                   bias=0.0, scale=1.0), inc="ac")
        p.wait("scalar", e2)
        e3 = p.op("scalar", lambda e: e.activation(out=sp1, in_=sp1, func=AF.Exp,
                                                   bias=0.0, scale=-1.0), inc="ac")
        p.wait("vector", e3)
        e4 = p.op("vector", lambda e: e.tensor_scalar(out=sp1, in0=sp1, scalar1=1.0,
                                                      scalar2=None, op0=ALU.add), inc="dv")
        p.wait("scalar", e4)
        e5 = p.op("scalar", lambda e: e.activation(out=sp1, in_=sp1, func=AF.Ln,
                                                   bias=0.0, scale=1.0), inc="ac")
        p.wait("vector", e5)
        e6 = p.op("vector", lambda e: e.tensor_scalar(out=sp2, in0=dtr, scalar1=0.0,
                                                      scalar2=None, op0=ALU.max), inc="dv")
        p.wait("vector", e6)
        e7 = p.op("vector", lambda e, tt=tt: e.tensor_tensor(
            out=dtv[:, tt * 16:(tt + 1) * 16], in0=sp1, in1=sp2, op=ALU.add), inc="dv")
        p.wait("vector", e7)
        e8 = p.op("vector", lambda e, tt=tt: e.tensor_tensor(
            out=gt, in0=dtv[:, tt * 16:(tt + 1) * 16], in1=Aneg[:], op=ALU.mult), inc="dv")
        p.wait("tensor", e8)
        g1 = p.op("tensor", lambda e: e.matmul(out=ps_G, lhsT=tri_f[:], rhs=gt,
                                               start=True, stop=True), inc="pe")
        g2 = p.op("tensor", lambda e: e.matmul(out=ps_gl, lhsT=onecf[:], rhs=gt,
                                               start=True, stop=True), inc="pe")
        p.wait("vector", g2)
        dt_ev = p.op("vector", lambda e, tt=tt: e.tensor_copy(
            out=Gv[:, tt * 16:(tt + 1) * 16], in_=ps_G), inc="dv")
        gl_ev = p.op("vector", lambda e, tt=tt: e.tensor_copy(
            out=Gl[0:1, tt * 16:(tt + 1) * 16], in_=ps_gl), inc="dv")
        p.wait("tensor", gl_ev)
        p.wait("sync", p.tok("ac"))
        p.wait("sync", p.tok("dv"))
    p.wait("vector", p.tok("dv"))
    ng = p.op("vector", lambda e: e.tensor_scalar(out=nGv[:], in0=Gv[:], scalar1=-1.0,
                                                  scalar2=None, op0=ALU.mult), inc="dv")
    for n in range(NCh):
        p.wait("tensor", p.tok("dv"))
        b1 = p.op("tensor", lambda e, n=n: e.matmul(
            out=ps_bc, lhsT=ones1f[:], rhs=Gl[0:1, n * 16:(n + 1) * 16],
            start=True, stop=True), inc="pe")
        p.wait("vector", b1)
        b2 = p.op("vector", lambda e, n=n: e.tensor_copy(
            out=bcGl[:, n * 16:(n + 1) * 16], in_=ps_bc), inc="dv")
        p.wait("tensor", b2)
    p.wait("scalar", p.tok("dv"))
    ec1 = p.op("scalar", lambda e: e.activation(out=deccv[:], in_=bcGl[:],
                                                func=AF.Exp, bias=0.0, scale=1.0), inc="ac")
    ec2 = p.op("scalar", lambda e: e.activation(out=egcv[:], in_=Gv[:],
                                                func=AF.Exp, bias=0.0, scale=1.0), inc="ac")
    p.wait("vector", ec2)
    ec3 = p.op("vector", lambda e: e.tensor_tensor(out=edcv[:], in0=bcGl[:],
                                                   in1=Gv[:], op=ALU.subtract), inc="dv")
    p.wait("scalar", ec3)
    ec4 = p.op("scalar", lambda e: e.activation(out=edcv[:], in_=edcv[:],
                                                func=AF.Exp, bias=0.0, scale=1.0), inc="ac")
    p.wait("tensor", p.tok("ac"))
    p.wait("vector", p.tok("ac"))

    def rope_evac(psrc, n_cols, dst, coff):
        rt = scr[:, 0:1024]
        t1s = scr[:, 1024:2048]
        r1 = p.op("scalar", lambda e: e.activation(
            out=rt[0:64, 0:n_cols], in_=psrc[64:128, :], func=AF.Identity,
            bias=0.0, scale=-1.0), inc="ac")
        r2 = p.op("scalar", lambda e: e.activation(
            out=rt[64:128, 0:n_cols], in_=psrc[0:64, :], func=AF.Identity,
            bias=0.0, scale=1.0), inc="ac")
        p.wait("vector", r2)
        m1 = p.op("vector", lambda e: e.tensor_tensor(
            out=t1s[:, 0:n_cols], in0=psrc, in1=cos_s[:, coff:coff + n_cols],
            op=ALU.mult), inc="dv")
        p.wait("vector", m1)
        m2 = p.op("vector", lambda e: e.tensor_tensor(
            out=rt[:, 0:n_cols], in0=rt[:, 0:n_cols],
            in1=sin_s[:, coff:coff + n_cols], op=ALU.mult), inc="dv")
        p.wait("vector", m2)
        m3 = p.op("vector", lambda e: e.tensor_tensor(
            out=dst, in0=t1s[:, 0:n_cols], in1=rt[:, 0:n_cols], op=ALU.add), inc="dv")
        return m3

    p.wait("sync", p.tok("xtw0"))
    p.wait("sync", p.tok("xtw1"))
    fillreg = {}
    p.raw("gpsimd", lambda e: fillreg.__setitem__("r", e.to_reg(-1e30)))
    wb_user = [None, None]
    wch = [0]
    proj_ev = None
    yg_st = None
    prev_su = None
    prev_yn1 = None
    prev_cn = None
    prev_ye1 = None
    prev_ye2 = None
    for kv in range(4):
        for mh in range(2):
            p.wait("sync", p.tok("pe"))
            xld = p.dma("sync", xTh.rearrange("p (c m) -> p c m", c=32),
                        xT2d[:, :, mh * 1024:(mh + 1) * 1024].transpose([1, 0, 2]),
                        f"lxh{mh}")
            for oc in range(10):
                if oc < 4:
                    kind, wsrc, gch = "q", wqT, kv * 4 + oc
                elif oc < 8:
                    kind, wsrc, gch = "g", wgT, kv * 4 + (oc - 4)
                elif oc == 8:
                    kind, wsrc, gch = "k", wkT, kv
                else:
                    kind, wsrc, gch = "v", wvT, kv
                db = wch[0] % 2
                wch[0] += 1
                p.wait("gpsimd", wb_user[db])
                wct = p.dma("gpsimd", wst_b[:, db * DM:(db + 1) * DM]
                            .rearrange("p (c o) -> p c o", c=32),
                            wsrc[:, gch * 128:(gch + 1) * 128]
                            .rearrange("(c p) o -> p c o", p=128), f"wld{db}")
                p.wait("tensor", wct)
                p.wait("tensor", xld)
                p.wait("tensor", p.tok("ac"))
                p.wait("tensor", p.tok("dv"))
                for d in range(32):
                    for half in range(2):
                        pmt = p.op("tensor", lambda e, d=d, h=half, db=db: e.matmul(
                            out=ps_proj[:, h * 512:(h + 1) * 512],
                            lhsT=wst_b[:, db * DM + d * 128: db * DM + (d + 1) * 128],
                            rhs=xTh[:, d * 1024 + h * 512: d * 1024 + (h + 1) * 512],
                            start=(d == 0), stop=(d == 31)),
                            inc="pe" if (d == 31 and half == 1) else None)
                wb_user[db] = pmt
                p.wait("scalar", pmt)
                p.wait("vector", pmt)
                if kind == "q":
                    loc = oc
                    rtok = rope_evac(ps_proj[:, 0:1024], 1024,
                                     qfm[:, loc * L + mh * 1024: loc * L + (mh + 1) * 1024],
                                     mh * 1024)
                    p.wait("tensor", rtok)
                elif kind == "k":
                    rtok = rope_evac(ps_proj[:, 0:1024], 1024,
                                     kfm[:, mh * 1024:(mh + 1) * 1024], mh * 1024)
                    p.wait("tensor", rtok)
                    for j in range(8):
                        trt = p.op("tensor", lambda e, j=j, mh=mh: e.transpose(
                            out=ps_tr[:], in_=kfm[:, mh * 1024 + j * 128:
                                                   mh * 1024 + (j + 1) * 128],
                            identity=identb[:]), inc="pe")
                        p.wait("vector", trt)
                        cpt = p.op("vector", lambda e, j=j, mh=mh: e.tensor_copy(
                            out=ktok[:, (mh * 8 + j) * 128:(mh * 8 + j + 1) * 128],
                            in_=ps_tr[:]), inc="dv")
                        p.wait("tensor", cpt)
                else:
                    vt = scr.bitcast(BF16)[:, 0:1024]
                    ev = p.op("scalar", lambda e: e.copy(out=vt, in_=ps_proj[:, 0:1024]),
                              inc="ac")
                    p.wait("tensor", ev)
                    for j in range(8):
                        trt = p.op("tensor", lambda e, j=j: e.transpose(
                            out=ps_tr[:], in_=vt[:, j * 128:(j + 1) * 128],
                            identity=identb[:]), inc="pe")
                        p.wait("vector", trt)
                        if kind == "g":
                            loc = oc - 4
                            p.wait("scalar", trt)
                            cpt = p.op("scalar", lambda e, j=j, mh=mh, loc=loc: e.activation(
                                out=gtok[:, (mh * 8 + j) * 512 + loc * 128:
                                            (mh * 8 + j) * 512 + (loc + 1) * 128],
                                in_=ps_tr[:], func=AF.Silu, bias=0.0, scale=1.0), inc="dv")
                        else:
                            cpt = p.op("vector", lambda e, j=j, mh=mh: e.tensor_copy(
                                out=vtok[:, (mh * 8 + j) * 128:(mh * 8 + j + 1) * 128],
                                in_=ps_tr[:]), inc="dv")
                        p.wait("tensor", cpt)
                    p.wait("scalar", p.tok("dv"))
        p.wait("tensor", p.tok("dv"))
        p.wait("tensor", p.tok("ac"))
        for n in range(NCh):
            for j in range(4):
                hh = kv * 4 + j
                gcol = n * 16 + hh
                smt = p.op("tensor", lambda e, n=n, j=j: e.matmul(
                    out=ps_st, lhsT=kfm[:, n * 128:(n + 1) * 128],
                    rhs=qfm[:, j * L + n * 128: j * L + (n + 1) * 128],
                    start=True, stop=True), inc="pe")
                t1t = p.op("scalar", lambda e, gcol=gcol: e.activation(
                    out=T1[:], in_=T1[:],
                    func=AF.Identity, bias=Gv[:, gcol:gcol + 1], scale=0.0), inc="ac")
                p.wait("tensor", t1t)
                t2t = p.op("tensor", lambda e: e.transpose(
                    out=ps_trf[:], in_=T1[:], identity=idenff[:]), inc="pe")
                p.wait("scalar", t2t)
                p.wait("scalar", prev_yn1)
                t3t = p.op("scalar", lambda e, gcol=gcol: e.activation(
                    out=dmask[:], in_=ps_trf[:], func=AF.Identity,
                    bias=nGv[:, gcol:gcol + 1], scale=1.0), inc="ac")
                p.wait("gpsimd", t3t)
                t4t = p.op("gpsimd", lambda e: e.affine_select(
                    out=dmask[:], in_=dmask[:], pattern=[[1, 128]],
                    compare_op=ALU.is_ge, fill=fillreg["r"], base=0,
                    channel_multiplier=-1), inc="pool")
                p.wait("scalar", t4t)
                t4b = p.op("scalar", lambda e: e.activation(
                    out=dmask[:], in_=dmask[:], func=AF.Exp,
                    bias=0.0, scale=1.0), inc="ac")
                p.wait("vector", t4b)
                p.wait("vector", smt)
                t5t = p.op("vector", lambda e: e.tensor_tensor(
                    out=mt[:], in0=ps_st, in1=dmask[:], op=ALU.mult), inc="dv")
                p.wait("scalar", prev_cn)
                xc1 = p.op("scalar", lambda e, n=n, gcol=gcol: e.activation(
                    out=xct[:], in_=vtok[:, n * 128:(n + 1) * 128], func=AF.Identity,
                    bias=0.0, scale=dtv[:, gcol:gcol + 1]), inc="ac")
                kd2 = p.op("scalar", lambda e, n=n, gcol=gcol: e.activation(
                    out=kdt[:], in_=ktok[:, n * 128:(n + 1) * 128], func=AF.Identity,
                    bias=0.0, scale=edcv[:, gcol:gcol + 1]), inc="ac")
                p.wait("tensor", t5t)
                p.wait("tensor", xc1)
                p.wait("tensor", prev_ye2)
                p.wait("tensor", prev_ye1)
                ya = p.op("tensor", lambda e: e.matmul(
                    out=ps_yA, lhsT=mt[:], rhs=xct[:], start=True, stop=True), inc="pe")
                yb = p.op("tensor", lambda e, j=j: e.matmul(
                    out=ps_yB, lhsT=qfm[:, j * L + n * 128: j * L + (n + 1) * 128],
                    rhs=prevb[:, j * 128:(j + 1) * 128], start=True, stop=True), inc="pe")
                p.wait("tensor", kd2)
                p.wait("tensor", prev_su)
                cn = p.op("tensor", lambda e: e.matmul(
                    out=ps_C, lhsT=kdt[:], rhs=xct[:], start=True, stop=True), inc="pe")
                p.wait("vector", cn)
                su = p.op("vector", lambda e, j=j, gcol=gcol: e.scalar_tensor_tensor(
                    out=prevC[:, j * 128:(j + 1) * 128],
                    in0=prevC[:, j * 128:(j + 1) * 128],
                    scalar=deccv[:, gcol:gcol + 1],
                    in1=ps_C, op0=ALU.mult, op1=ALU.add), inc="dv")
                p.wait("scalar", su)
                p.wait("scalar", yb)
                pc = p.op("scalar", lambda e, j=j: e.copy(
                    out=prevb[:, j * 128:(j + 1) * 128],
                    in_=prevC[:, j * 128:(j + 1) * 128]), inc="ac")
                p.wait("scalar", yb)
                ye1 = p.op("scalar", lambda e, gcol=gcol: e.activation(
                    out=ytmp[:], in_=ps_yB, func=AF.Identity, bias=0.0,
                    scale=egcv[:, gcol:gcol + 1]), inc="ac")
                p.wait("vector", ye1)
                p.wait("vector", ya)
                ye2 = p.op("vector", lambda e: e.tensor_tensor(
                    out=ytile[:], in0=ps_yA, in1=ytmp[:], op=ALU.add), inc="dv")
                p.wait("scalar", ye2)
                p.wait("scalar", prev_yn1)
                p.wait("scalar", cn)
                rs1 = p.op("scalar", lambda e: e.activation(
                    out=yjunk[:], in_=ytile[:], func=AF.Square, bias=0.0, scale=1.0,
                    accum_out=ccol[:, 0:1]), inc="ac")
                p.wait("vector", rs1)
                rs2 = p.op("vector", lambda e: e.tensor_scalar(
                    out=ccol[:, 0:1], in0=ccol[:, 0:1], scalar1=1.0 / 128.0,
                    scalar2=RMS_EPS, op0=ALU.mult, op1=ALU.add), inc="dv")
                p.wait("scalar", rs2)
                rs3 = p.op("scalar", lambda e: e.activation(
                    out=ccol[:, 0:1], in_=ccol[:, 0:1], func=AF.Ln,
                    bias=0.0, scale=1.0), inc="ac")
                p.wait("scalar", rs3)
                rs4 = p.op("scalar", lambda e: e.activation(
                    out=ccol[:, 0:1], in_=ccol[:, 0:1], func=AF.Exp,
                    bias=0.0, scale=-0.5), inc="ac")
                p.wait("vector", rs4)
                yn1 = p.op("vector", lambda e: e.scalar_tensor_tensor(
                    out=ytile[:], in0=ytile[:], scalar=ccol[:, 0:1], in1=gnb[:],
                    op0=ALU.mult, op1=ALU.mult), inc="dv")
                p.wait("vector", yn1)
                if yg_st is not None:
                    p.wait("vector", yg_st)
                yg1 = p.op("vector", lambda e, j=j, n=n: e.tensor_tensor(
                    out=ygstg[:, j * 128:(j + 1) * 128], in0=ytile[:],
                    in1=gtok[:, n * 512 + j * 128: n * 512 + (j + 1) * 128],
                    op=ALU.mult), inc="dv")
                prev_su, prev_yn1, prev_cn = su, yn1, cn
                prev_ye1, prev_ye2 = ye1, ye2
            p.wait("sync", p.tok("dv"))
            yg_st = p.dma("sync", yg_dst(n, kv), ygstg[:], f"yst{n % 2}")
    p.wait("sync", p.tok("yst0"))
    p.wait("sync", p.tok("yst1"))
    if ctx is None:
        p.emit()
        return nc


def stage_C(inputs, x2_full):
    ident = np.eye(128, dtype=BF)
    identf = np.eye(128, dtype=np.float32)
    trif = np.ascontiguousarray(np.triu(np.ones((128, 128), np.float32)))
    WqT = np.ascontiguousarray(np.asarray(inputs["L1_attn_Wq"]).T)
    WkT = np.ascontiguousarray(np.asarray(inputs["L1_attn_Wk"]).T)
    WvT = np.ascontiguousarray(np.asarray(inputs["L1_attn_Wv"]).T)
    WgT = np.ascontiguousarray(np.asarray(inputs["L1_attn_Wg"]).T)
    WinT = np.ascontiguousarray(np.asarray(inputs["L1_attn_Win"]).T)
    cos, sin = host_rope_tables(np.arange(L))
    in_maps = []
    for c in range(NCORE):
        b, s = c // 2, c % 2
        qsl = slice(s * 2048, (s + 1) * 2048)
        kvsl = slice(s * 512, (s + 1) * 512)
        hsl = slice(s * 16, (s + 1) * 16)
        in_maps.append({
            "x2f": np.ascontiguousarray(x2_full[b]),
            "wqT": np.ascontiguousarray(WqT[:, qsl]),
            "wkT": np.ascontiguousarray(WkT[:, kvsl]),
            "wvT": np.ascontiguousarray(WvT[:, kvsl]),
            "wgT": np.ascontiguousarray(WgT[:, qsl]),
            "winT": np.ascontiguousarray(WinT[:, hsl]),
            "dtb": np.ascontiguousarray(np.asarray(inputs["L1_attn_dt_bias"])[None, hsl]),
            "alog": np.ascontiguousarray(np.asarray(inputs["L1_attn_A_log_bias"])[None, hsl]),
            "gnw": np.ascontiguousarray(np.asarray(inputs["L1_attn_gnorm_w"])[None, :]),
            "cosT": cos, "sinT": sin,
            "ident": ident, "identf": identf, "trif": trif,
        })
    return in_maps




def build_launch_D(ctx=None):
    if ctx is None:
        nc = bass.Bass("TRN2", target_bir_lowering=False, num_devices=NCORE)
        p = Prog(nc)
        x2 = nc.dram_tensor("x2", [TPC, DM], F32, kind="ExternalInput").ap()
        yg = nc.dram_tensor("yg", [TPC, DM], BF16, kind="ExternalInput").ap()
        woT = nc.dram_tensor("woT", [DM, DM], F32, kind="ExternalInput").ap()
        w1T = nc.dram_tensor("w1T", [DM, DM], F32, kind="ExternalInput").ap()
        b1 = nc.dram_tensor("b1", [1, DM], F32, kind="ExternalInput").ap()
        art = nc.dram_tensor("art", [DM, 80], F32, kind="ExternalInput").ap()
        bpT = nc.dram_tensor("bpT", [8, 128, DM], F32, kind="ExternalInput").ap()
        ident = nc.dram_tensor("ident", [128, 128], BF16, kind="ExternalInput").ap()
        x4o = nc.dram_tensor("x4o", [TPC, DM], F32, kind="ExternalOutput").ap()
        x2_src = lambda tt: (lambda tt=tt: x2[tt * 128:(tt + 1) * 128, :])
        yg_srcs = lambda tt: [(slice(0, DM),
                               lambda tt=tt: yg[tt * 128:(tt + 1) * 128, :])]
    else:
        nc, p = ctx["nc"], ctx["p"]
        woT, w1T, b1, art, bpT, ident = (ctx["woT"], ctx["w1T"], ctx["b1"],
                                         ctx["art"], ctx["bpT"], ctx["ident"])
        x4o = ctx["x4o"]
        x2_src = ctx["x2_src"]
        yg_srcs = ctx["yg_srcs"]
    ygTd = nc.dram_tensor("ygTd", [32, 128, TPC], BF16, kind="Internal").ap()
    xT3d = nc.dram_tensor("xT3d", [32, 128, TPC], BF16, kind="Internal").ap()

    al = ctx["al"] if ctx else nc.alloc_sbuf_tensor
    identb = al("identb", [128, 128], BF16)
    z = al("z", [128, 8 * DM], F32)
    scr = al("scr", [128, 2688], F32)
    artb = al("artb", [128, 32 * 80], BF16)
    hgT = al("hgT", [128, 8 * 1024], BF16)
    xw = al("xw", [128, 2 * 4096], BF16)
    stg = xw[:, 0:DM]
    xgt_alias = xw[:, DM:2 * DM]
    wfs = al("wfs", [128, 2 * 2048], F32)
    wbs = al("wbs", [128, 2 * 2048], BF16)
    bp_b = al("bp_b", [128, 2 * 512], BF16)
    brb = al("brb", [1, 512], BF16)
    ones1b = al("ones1b", [1, 128], BF16)
    stat = al("stat", [128, 48], F32)
    rh_f = al("rh_f", [128, 64], F32)
    lg = al("lg", [128, 16], F32)
    pex = al("pex", [128, 16], F32)
    s1 = al("s1", [128, 16], F32)
    mskd = al("mskd", [128, 16], F32)
    comb = al("comb", [128, 16], F32)
    het = al("het", [128, 64], BF16)
    ccol = al("ccol", [128, 8], F32)

    ps = ctx["PS"] if ctx else nc.alloc_psum_tensor("ps", [128, 4096], F32)
    ps_trv = ps.bitcast(BF16)[:, 0:128]
    ps_aux = ps[:, 512:592]

    ZS, ZSS, RSTD, NEGM, MEAN, VAR = 0, 8, 16, 24, 32, 40

    def ln_stats(tt, ztile):
        p.wait("vector", p.tok("ac"))
        p.wait("scalar", p.tok("dv"))
        t1 = p.op("vector", lambda e: e.reduce_sum(
            out=stat[:, ZS + tt:ZS + tt + 1], in_=ztile, axis=AX.X), inc="dv")
        t2 = p.op("scalar", lambda e: e.activation(
            out=stg[:, 0:DM], in_=ztile, func=AF.Square,
            bias=0.0, scale=1.0, accum_out=stat[:, ZSS + tt:ZSS + tt + 1]), inc="ac")
        p.wait("vector", t1)
        p.wait("vector", t2)
        t3 = p.op("vector", lambda e: e.tensor_scalar(
            out=stat[:, MEAN:MEAN + 1], in0=stat[:, ZS + tt:ZS + tt + 1],
            scalar1=1.0 / DM, scalar2=None, op0=ALU.mult), inc="dv")
        p.wait("vector", t3)
        t4 = p.op("vector", lambda e: e.tensor_scalar(
            out=stat[:, VAR:VAR + 1], in0=stat[:, ZSS + tt:ZSS + tt + 1],
            scalar1=1.0 / DM, scalar2=None, op0=ALU.mult), inc="dv")
        p.wait("scalar", t3)
        t5 = p.op("scalar", lambda e: e.activation(
            out=stat[:, MEAN + 1:MEAN + 2], in_=stat[:, MEAN:MEAN + 1],
            func=AF.Square, bias=0.0, scale=1.0), inc="ac")
        p.wait("vector", t4)
        p.wait("vector", t5)
        t6 = p.op("vector", lambda e: e.tensor_tensor(
            out=stat[:, VAR:VAR + 1], in0=stat[:, VAR:VAR + 1],
            in1=stat[:, MEAN + 1:MEAN + 2], op=ALU.subtract), inc="dv")
        p.wait("vector", t6)
        t6b = p.op("vector", lambda e: e.tensor_scalar(
            out=stat[:, VAR:VAR + 1], in0=stat[:, VAR:VAR + 1],
            scalar1=LN_EPS, scalar2=None, op0=ALU.add), inc="dv")
        p.wait("scalar", t6b)
        t7 = p.op("scalar", lambda e: e.activation(
            out=stat[:, VAR + 1:VAR + 2], in_=stat[:, VAR:VAR + 1],
            func=AF.Ln, bias=0.0, scale=1.0), inc="ac")
        p.wait("scalar", t7)
        t8 = p.op("scalar", lambda e: e.activation(
            out=stat[:, RSTD + tt:RSTD + tt + 1], in_=stat[:, VAR + 1:VAR + 2],
            func=AF.Exp, bias=0.0, scale=-0.5), inc="ac")
        p.wait("vector", t8)
        p.wait("scalar", t8)
        t9 = p.op("vector", lambda e: e.tensor_tensor(
            out=stat[:, NEGM + tt:NEGM + tt + 1], in0=stat[:, MEAN:MEAN + 1],
            in1=stat[:, RSTD + tt:RSTD + tt + 1], op=ALU.mult), inc="dv")
        p.wait("vector", t9)
        t10 = p.op("vector", lambda e: e.tensor_scalar(
            out=stat[:, NEGM + tt:NEGM + tt + 1], in0=stat[:, NEGM + tt:NEGM + tt + 1],
            scalar1=-1.0, scalar2=None, op0=ALU.mult), inc="dv")
        return t10

    p.dma("sync", identb[:], ident, "lc")
    mzt = p.op("vector", lambda e: e.memset(ones1b[:], 1.0), inc="dv")
    ct = p.dma("gpsimd", artb[:].rearrange("p (c r) -> p c r", c=32),
               art.rearrange("(c p) r -> p c r", p=128), "la")
    p.wait("tensor", ct)

    ygt = xgt_alias
    p.wait("tensor", p.tok("lc"))
    for tt in range(8):
        p.dma("sync", z[:, tt * DM:(tt + 1) * DM], x2_src(tt), f"lz{tt % 2}")
        for dsl, srcfn in yg_srcs(tt):
            lt = p.dma("sync", ygt[:, dsl], srcfn, f"ly{tt % 2}")
        p.wait("tensor", lt)
        p.wait("vector", p.tok("xtw0"))
        p.wait("vector", p.tok("xtw1"))
        for c in range(32):
            trt = p.op("tensor", lambda e, c=c: e.transpose(
                out=ps_trv, in_=ygt[:, c * 128:(c + 1) * 128], identity=identb[:]),
                inc="pe")
            p.wait("vector", trt)
            cpt = p.op("vector", lambda e, c=c: e.tensor_copy(
                out=stg[:, c * 128:(c + 1) * 128], in_=ps_trv), inc="dv")
            p.wait("tensor", cpt)
        p.wait("sync", p.tok("dv"))
        sttk = p.dma("sync", ygTd[:, :, tt * 128:(tt + 1) * 128].transpose([1, 0, 2]),
                     stg.rearrange("p (c i) -> p c i", c=32), f"xtw{tt % 2}")
        p.wait("sync", p.tok("ac"))
        p.wait("sync", p.tok("pe"))
    p.wait("sync", p.tok("xtw0"))
    p.wait("sync", p.tok("xtw1"))
    p.wait("scalar", p.tok("xtw0"))
    p.wait("scalar", p.tok("xtw1"))
    wu = [None, None]
    ev_tok = None
    xu = None
    for osp in range(8):
        for dg in range(8):
            db = dg % 2
            p.wait("scalar", wu[db])
            xl = p.op("scalar", lambda e, dg=dg, db=db: e.dma_start(
                out=xw[:, db * 4096:(db + 1) * 4096].rearrange("p (c m) -> p c m", c=4),
                in_=ygTd[dg * 4:(dg + 1) * 4].transpose([1, 0, 2])), inc=f"xl{db}", k=16)
            p.wait("gpsimd", wu[db])
            wc = p.dma("gpsimd", wbs[:, db * 2048:(db + 1) * 2048]
                       .rearrange("p (c o) -> p c o", c=4),
                       woT[dg * 512:(dg + 1) * 512, osp * 512:(osp + 1) * 512]
                       .rearrange("(c p) o -> p c o", p=128), f"wl{db}")
            p.wait("tensor", wc)
            p.wait("tensor", xl)
            if dg < 2 and ev_tok is not None:
                p.wait("tensor", ev_tok)
            for dc in range(4):
                for tt in range(8):
                    mmt = p.op("tensor", lambda e, tt=tt, dc=dc, db=db, dg=dg: e.matmul(
                        out=ps[:, tt * 512:(tt + 1) * 512],
                        lhsT=xw[:, db * 4096 + dc * 1024 + tt * 128:
                                   db * 4096 + dc * 1024 + (tt + 1) * 128],
                        rhs=wbs[:, db * 2048 + dc * 512: db * 2048 + (dc + 1) * 512],
                        start=(dg == 0 and dc == 0), stop=(dg == 7 and dc == 3)),
                        inc="pe" if (tt == 7 and dc == 3) else None)
            wu[db] = mmt
        for tt in range(8):
            p.wait("vector", p.tok("pe"))
            p.wait("vector", p.tok("lz0"))
            p.wait("vector", p.tok("lz1"))
            ev_tok = p.op("vector", lambda e, tt=tt, osp=osp: e.tensor_tensor(
                out=z[:, tt * DM + osp * 512: tt * DM + (osp + 1) * 512],
                in0=ps[:, tt * 512:(tt + 1) * 512],
                in1=z[:, tt * DM + osp * 512: tt * DM + (osp + 1) * 512],
                op=ALU.add), inc="dv")
    x3b = xgt_alias
    aux_ev = None
    stg_user = None
    for tt in range(8):
        p.wait("vector", p.tok("dv"))
        p.wait("scalar", stg_user)
        sttok = ln_stats(tt, z[:, tt * DM:(tt + 1) * DM])
        p.wait("scalar", sttok)
        p.wait("scalar", p.tok("pool"))
        x3t = p.op("scalar", lambda e, tt=tt: e.activation(
            out=x3b, in_=z[:, tt * DM:(tt + 1) * DM], func=AF.Identity,
            bias=stat[:, NEGM + tt:NEGM + tt + 1],
            scale=stat[:, RSTD + tt:RSTD + tt + 1]), inc="ac")
        p.wait("tensor", x3t)
        if stg_user is not None:
            p.wait("vector", stg_user)
        for c in range(32):
            trt = p.op("tensor", lambda e, c=c: e.transpose(
                out=ps_trv, in_=x3b[:, c * 128:(c + 1) * 128], identity=identb[:]),
                inc="pe")
            p.wait("vector", trt)
            cpt = p.op("vector", lambda e, c=c: e.tensor_copy(
                out=stg[:, c * 128:(c + 1) * 128], in_=ps_trv), inc="dv")
            p.wait("tensor", cpt)
        p.wait("sync", p.tok("dv"))
        stg_user = p.dma("sync", xT3d[:, :, tt * 128:(tt + 1) * 128].transpose([1, 0, 2]),
                         stg.rearrange("p (c i) -> p c i", c=32), f"xt3{tt % 2}")
        p.wait("tensor", aux_ev)
        for d in range(32):
            amt = p.op("tensor", lambda e, d=d: e.matmul(
                out=ps_aux, lhsT=stg[:, d * 128:(d + 1) * 128],
                rhs=artb[:, d * 80:(d + 1) * 80],
                start=(d == 0), stop=(d == 31)), inc="pe" if d == 31 else None)
        p.wait("scalar", amt)
        a1 = p.op("scalar", lambda e: e.copy(out=rh_f[:], in_=ps_aux[:, 0:64]), inc="ac")
        aux_ev = p.op("scalar", lambda e: e.copy(out=lg[:], in_=ps_aux[:, 64:80]), inc="ac")
        p.wait("vector", aux_ev)
        q1 = p.op("vector", lambda e: e.reduce_max(out=ccol[:, 0:1], in_=lg[:],
                                                   axis=AX.X, negate=True), inc="dv")
        p.wait("scalar", q1)
        q2 = p.op("scalar", lambda e: e.activation(out=pex[:], in_=lg[:], func=AF.Exp,
                                                   bias=ccol[:, 0:1], scale=1.0,
                                                   accum_out=ccol[:, 1:2]), inc="ac")
        p.wait("vector", q2)
        q3 = p.op("vector", lambda e: e.tensor_scalar(out=ccol[:, 2:3], in0=ccol[:, 0:1],
                                                      scalar1=-1.0, scalar2=None,
                                                      op0=ALU.mult), inc="dv")
        p.wait("scalar", q3)
        q4 = p.op("scalar", lambda e: e.activation(out=s1[:], in_=lg[:], func=AF.Sign,
                                                   bias=ccol[:, 2:3], scale=-1.0), inc="ac")
        p.wait("vector", q4)
        q5 = p.op("vector", lambda e: e.tensor_tensor(
            out=mskd[:], in0=lg[:], in1=s1[:], op=ALU.mult), inc="dv")
        q5b = p.op("vector", lambda e: e.tensor_scalar(
            out=s1[:], in0=s1[:], scalar1=-1.0, scalar2=64.0,
            op0=ALU.add, op1=ALU.mult), inc="dv")
        p.wait("vector", q5b)
        q6 = p.op("vector", lambda e: e.tensor_tensor(out=mskd[:], in0=mskd[:],
                                                      in1=s1[:], op=ALU.add), inc="dv")
        p.wait("vector", q6)
        q7 = p.op("vector", lambda e: e.reduce_max(out=ccol[:, 3:4], in_=mskd[:],
                                                   axis=AX.X), inc="dv")
        p.wait("scalar", q7)
        q8 = p.op("scalar", lambda e: e.activation(out=s1[:], in_=lg[:], func=AF.Sign,
                                                   bias=ccol[:, 3:4], scale=-1.0), inc="ac")
        p.wait("vector", q8)
        q9 = p.op("vector", lambda e: e.tensor_scalar(out=s1[:], in0=s1[:],
                                                      scalar1=-1.0, scalar2=1.0,
                                                      op0=ALU.mult, op1=ALU.add), inc="dv")
        p.wait("vector", q9)
        q10 = p.op("vector", lambda e: e.tensor_scalar(out=s1[:], in0=s1[:],
                                                       scalar1=1.0, scalar2=None,
                                                       op0=ALU.min), inc="dv")
        p.wait("vector", q10)
        q11 = p.op("vector", lambda e: e.tensor_tensor(out=pex[:], in0=pex[:],
                                                       in1=s1[:], op=ALU.mult), inc="dv")
        p.wait("vector", q11)
        q12 = p.op("vector", lambda e: e.reduce_sum(out=ccol[:, 4:5], in_=pex[:],
                                                    axis=AX.X), inc="dv")
        p.wait("vector", q12)
        q13 = p.op("vector", lambda e: e.scalar_tensor_tensor(
            out=ccol[:, 5:6], in0=ccol[:, 1:2], scalar=1e-6, in1=ccol[:, 4:5],
            op0=ALU.mult, op1=ALU.add), inc="dv")
        p.wait("vector", q13)
        q14 = p.op("vector", lambda e: e.reciprocal(out=ccol[:, 5:6], in_=ccol[:, 5:6]),
                   inc="dv")
        p.wait("vector", q14)
        q15 = p.op("vector", lambda e: e.tensor_scalar(out=ccol[:, 5:6], in0=ccol[:, 5:6],
                                                       scalar1=0.5, scalar2=None,
                                                       op0=ALU.mult), inc="dv")
        p.wait("vector", q15)
        q16 = p.op("vector", lambda e: e.tensor_scalar(out=comb[:], in0=pex[:],
                                                       scalar1=ccol[:, 5:6], scalar2=None,
                                                       op0=ALU.mult), inc="dv")
        p.wait("vector", a1)
        p.wait("vector", q16)
        for ee in range(16):
            he = p.op("vector", lambda e, ee=ee: e.tensor_scalar(
                out=het[:], in0=rh_f[:], scalar1=comb[:, ee:ee + 1], scalar2=None,
                op0=ALU.mult), inc="dv")
            p.wait("tensor", he)
            trt = p.op("tensor", lambda e: e.transpose(
                out=ps_trv[0:64, :], in_=het[:], identity=identb[:]), inc="pe")
            p.wait("scalar", trt)
            hct = p.op("scalar", lambda e, ee=ee, tt=tt: e.copy(
                out=hgT[(ee % 2) * 64:(ee % 2) * 64 + 64,
                        (ee // 2) * 1024 + tt * 128: (ee // 2) * 1024 + (tt + 1) * 128],
                in_=ps_trv[0:64, :]), inc="ac")
            p.wait("vector", hct)
            p.wait("tensor", hct)
    p.wait("sync", p.tok("xt30"))
    p.wait("sync", p.tok("xt31"))
    p.wait("scalar", p.tok("xt30"))
    p.wait("scalar", p.tok("xt31"))
    wu = [None, None]
    ev_tok = None
    bp_user = None
    xu = None
    prev_bmt = None
    for osp in range(8):
        p.wait("sync", p.tok("dv"))
        brl = p.dma("sync", scr[0:1, 1536:1536 + 512],
                    b1[0:1, osp * 512:(osp + 1) * 512], f"brl{osp % 2}")
        p.wait("scalar", brl)
        p.wait("scalar", prev_bmt)
        brc = p.op("scalar", lambda e: e.copy(out=brb[:], in_=scr[0:1, 1536:1536 + 512]),
                   inc="ac")
        for dg in range(8):
            db = dg % 2
            p.wait("scalar", wu[db])
            xl = p.op("scalar", lambda e, dg=dg, db=db: e.dma_start(
                out=xw[:, db * 4096:(db + 1) * 4096].rearrange("p (c m) -> p c m", c=4),
                in_=xT3d[dg * 4:(dg + 1) * 4].transpose([1, 0, 2])), inc=f"x2l{db}", k=16)
            p.wait("gpsimd", wu[db])
            wc = p.dma("gpsimd", wbs[:, db * 2048:(db + 1) * 2048]
                       .rearrange("p (c o) -> p c o", c=4),
                       w1T[dg * 512:(dg + 1) * 512, osp * 512:(osp + 1) * 512]
                       .rearrange("(c p) o -> p c o", p=128), f"w2{db}")
            p.wait("tensor", wc)
            p.wait("tensor", xl)
            if dg < 2 and ev_tok is not None:
                p.wait("tensor", ev_tok)
            for dc in range(4):
                for tt in range(8):
                    mmt = p.op("tensor", lambda e, tt=tt, dc=dc, db=db, dg=dg: e.matmul(
                        out=ps[:, tt * 512:(tt + 1) * 512],
                        lhsT=xw[:, db * 4096 + dc * 1024 + tt * 128:
                                   db * 4096 + dc * 1024 + (tt + 1) * 128],
                        rhs=wbs[:, db * 2048 + dc * 512: db * 2048 + (dc + 1) * 512],
                        start=(dg == 0 and dc == 0), stop=False),
                        inc="pe" if (tt == 7 and dc == 3) else None)
            wu[db] = mmt
        p.wait("gpsimd", p.tok("pe"))
        for fp in range(4):
            bts = []
            for ff in range(2):
                f = fp * 2 + ff
                bts.append(p.dma("gpsimd", bp_b[:, ff * 512:(ff + 1) * 512],
                                 bpT[f][:, osp * 512:(osp + 1) * 512], f"lb{f % 2}"))
            p.wait("tensor", bts[0])
            p.wait("tensor", bts[1])
            for tt in range(8):
                dmm = p.op("tensor", lambda e, tt=tt, fp=fp: e.matmul(
                    out=ps[:, tt * 512:(tt + 1) * 512],
                    lhsT=hgT[:, (fp * 2) * 1024 + tt * 128: (fp * 2) * 1024 + (tt + 1) * 128],
                    rhs=bp_b[:, 0:512],
                    start=False, stop=False), inc=None)
                dmm = p.op("tensor", lambda e, tt=tt, fp=fp: e.matmul(
                    out=ps[:, tt * 512:(tt + 1) * 512],
                    lhsT=hgT[:, (fp * 2 + 1) * 1024 + tt * 128: (fp * 2 + 1) * 1024 + (tt + 1) * 128],
                    rhs=bp_b[:, 512:1024],
                    start=False, stop=False),
                    inc="pe" if tt == 7 else None)
            p.wait("gpsimd", dmm)
        for tt in range(8):
            if tt == 0:
                p.wait("tensor", brc)
            bmt = p.op("tensor", lambda e, tt=tt: e.matmul(
                out=ps[:, tt * 512:(tt + 1) * 512],
                lhsT=ones1b[:], rhs=brb[:],
                start=False, stop=True), inc="pe")
            prev_bmt = bmt
            p.wait("scalar", bmt)
            x3o = p.op("scalar", lambda e, tt=tt, osp=osp: e.activation(
                out=scr[:, 1024:1536], in_=z[:, tt * DM + osp * 512: tt * DM + (osp + 1) * 512],
                func=AF.Identity, bias=stat[:, NEGM + tt:NEGM + tt + 1],
                scale=stat[:, RSTD + tt:RSTD + tt + 1]), inc="ac")
            p.wait("vector", x3o)
            ev_tok = p.op("vector", lambda e, tt=tt, osp=osp: e.tensor_tensor(
                out=z[:, tt * DM + osp * 512: tt * DM + (osp + 1) * 512],
                in0=ps[:, tt * 512:(tt + 1) * 512], in1=scr[:, 1024:1536],
                op=ALU.add), inc="dv")
            p.wait("scalar", ev_tok)
    out_f = None
    ot_prev = None
    for tt in range(8):
        p.wait("vector", p.tok("dv"))
        p.wait("scalar", p.tok("dv"))
        p.wait("scalar", p.tok("pool"))
        sttok = ln_stats(tt, z[:, tt * DM:(tt + 1) * DM])
        p.wait("scalar", sttok)
        slots = [scr[:, 0:512], scr[:, 512:1024]]
        for oq in range(8):
            sl = slots[oq % 2]
            p.wait("scalar", ot_prev[oq % 2] if isinstance(ot_prev, list) else ot_prev)
            oft = p.op("scalar", lambda e, tt=tt, oq=oq, sl=sl: e.activation(
                out=sl, in_=z[:, tt * DM + oq * 512: tt * DM + (oq + 1) * 512],
                func=AF.Identity,
                bias=stat[:, NEGM + tt:NEGM + tt + 1],
                scale=stat[:, RSTD + tt:RSTD + tt + 1]), inc="ac")
            p.wait("sync", oft)
            tokx = p.dma("sync",
                         x4o[tt * 128:(tt + 1) * 128, oq * 512:(oq + 1) * 512],
                         sl, f"ox{oq % 2}")
            if not isinstance(ot_prev, list):
                ot_prev = [None, None]
            ot_prev[oq % 2] = tokx
    p.wait("sync", p.tok("ox0"))
    p.wait("sync", p.tok("ox1"))
    if ctx is None:
        p.emit()
        return nc


def stage_D(inputs, x2_full, yg_full):
    ident = np.eye(128, dtype=BF)
    woT = np.ascontiguousarray(np.asarray(inputs["L1_attn_Wo"]).T)
    w1T = np.ascontiguousarray(np.asarray(inputs["L1_ffn_base_w"]).T)
    A = np.asarray(inputs["L1_ffn_A"])
    rw = np.asarray(inputs["L1_ffn_router_w"])
    art = np.ascontiguousarray(np.concatenate([A.T, rw.T], axis=1))
    Bm = np.asarray(inputs["L1_ffn_B"])
    bpT = np.ascontiguousarray(
        np.stack([np.concatenate([Bm[2 * f].T, Bm[2 * f + 1].T], axis=0)
                  for f in range(8)]))
    b1 = np.ascontiguousarray(np.asarray(inputs["L1_ffn_base_b"])[None, :])
    in_maps = []
    for c in range(NCORE):
        b, s = c // 2, c % 2
        tsl = slice(s * 1024, (s + 1) * 1024)
        in_maps.append({
            "x2": np.ascontiguousarray(x2_full[b][tsl]),
            "yg": np.ascontiguousarray(yg_full[b][tsl]),
            "woT": woT, "w1T": w1T, "b1": b1, "art": art, "bpT": bpT,
            "ident": ident,
        })
    return in_maps




MAGIC = [0x5A5A0001, 0x5A5A0002, 0x5A5A0003]


def build_fused():
    nc = bass.Bass("TRN2", target_bir_lowering=False, num_devices=NCORE,
                   detect_race_conditions=False)
    p = Prog(nc)
    EI = lambda n, s, d: nc.dram_tensor(n, s, d, kind="ExternalInput").ap()
    x0 = EI("x0", [L, DM], F32)
    x0own = EI("x0own", [TPC, DM], F32)
    wT = {k: EI(f"w{k}T", [DM, 2048], F32) for k in "qkv"}
    bias = {k: EI(f"b{k}", [2048], F32) for k in "qkv"}
    bwT = EI("bwT", [3, 16], F32)
    bivmat = EI("bivmat", [3, 64], F32)
    revtile = EI("revtile", [16, 64], F32)
    scdelta = EI("scdelta", [16, 64], F32)
    ident = EI("ident", [128, 128], BF16)
    identf = EI("identf", [128, 128], F32)
    trif = EI("trif", [128, 128], F32)
    w0T = EI("w0T", [DM, DM], F32)
    bias0 = EI("bias0", [1, DM], F32)
    a0T = EI("a0T", [DM, 256], F32)
    b0cat = EI("b0cat", [256, DM], F32)
    coefr = EI("coefr", [256], F32)
    sclr = EI("sclr", [256], F32)
    wq1T = EI("wq1T", [DM, 2048], F32)
    wk1T = EI("wk1T", [DM, 512], F32)
    wv1T = EI("wv1T", [DM, 512], F32)
    wg1T = EI("wg1T", [DM, 2048], F32)
    win1T = EI("win1T", [DM, 16], F32)
    dtb = EI("dtb", [1, 16], F32)
    alog = EI("alog", [1, 16], F32)
    gnw = EI("gnw", [1, 128], F32)
    cosT = EI("cosT", [128, L], F32)
    sinT = EI("sinT", [128, L], F32)
    woT = EI("woT", [DM, DM], F32)
    w1T = EI("w1T", [DM, DM], F32)
    b1 = EI("b1", [1, DM], F32)
    art = EI("art", [DM, 80], F32)
    bpT = EI("bpT", [8, 128, DM], F32)
    svin = EI("sv", [1, 1], U32)
    x4o = nc.dram_tensor("x4o", [TPC, DM], F32, kind="ExternalOutput").ap()
    sh_attn = nc.dram_tensor("sh_attn", [2, 2, 1024, 2048], BF16,
                             kind="Internal", addr_space="Shared").ap()
    sh_x2 = nc.dram_tensor("sh_x2", [2, 1024, DM], F32,
                           kind="Internal", addr_space="Shared").ap()
    sh_yg = nc.dram_tensor("sh_yg", [2, 2, 1024, 2048], BF16,
                           kind="Internal", addr_space="Shared").ap()
    sh_flag = nc.dram_tensor("sh_flag", [3, 2, 1, 1], U32,
                             kind="Internal", addr_space="Shared").ap()
    PS = nc.alloc_psum_tensor("PS", [128, 4096], F32)

    sv_sb = nc.alloc_sbuf_tensor("sv_sb", [1, 1], U32)
    magic_sb = nc.alloc_sbuf_tensor("magic_sb", [1, 4], U32)
    flag_rx = nc.alloc_sbuf_tensor("flag_rx", [1, 4], U32)
    ARENA0 = (nc.sbuf_base + 512 + 31) // 32 * 32

    def mkal(prefix):
        off = [ARENA0]
        def al(name, shape, dtype):
            szmap = {F32: 4, BF16: 2, U32: 4}
            per = int(np.prod(shape[1:])) * szmap[dtype]
            t = nc.alloc_sbuf_tensor_at(f"{prefix}_{name}", shape, dtype, offset=off[0])
            off[0] += (per + 31) // 32 * 32
            assert off[0] <= nc.sbuf_top, f"{prefix}_{name} arena overflow {off[0]}"
            return t
        return al

    svh = {}
    lvt = p.dma("sync", sv_sb[:], svin, "lsv")
    p.wait("sync", lvt)
    p.raw("sync", lambda e: svh.__setitem__("v", e.value_load(sv_sb[0:1, 0:1])))
    for k in range(3):
        p.op("vector", lambda e, k=k: e.memset(magic_sb[0:1, k:k + 1], MAGIC[k]),
             inc="mgk")
    attn_loc = nc.dram_tensor("attn_loc", [L, 2048], BF16, kind="Internal").ap()
    attn_in = nc.dram_tensor("attn_in", [TPC, DM], BF16, kind="Internal").ap()
    x2_loc = nc.dram_tensor("x2_loc", [TPC, DM], F32, kind="Internal").ap()
    ygo_loc = nc.dram_tensor("ygo_loc", [L, 2048], BF16, kind="Internal").ap()
    yg_loc = nc.dram_tensor("yg_loc", [TPC, DM], BF16, kind="Internal").ap()

    def set_flag(k):
        p.wait("sync", p.tok("mgk"))
        p.dma("sync", lambda k=k: sh_flag[k][svh["v"]], magic_sb[0:1, k:k + 1],
              f"fw{k}")
        p.wait("sync", p.tok(f"fw{k}"))

    def poll_flag(k):
        psem = p.sem(f"poll{k}")
        def go(e, k=k, psem=psem):
            with e.register(f"pf{k}") as f, e.register(f"pc{k}") as cnt, \
                 e.register(f"pn{k}") as nd:
                e.reg_mov(f, 0)
                e.reg_mov(cnt, 0)
                e.reg_mov(nd, 1)
                with e.While(nd):
                    e.dma_start(out=flag_rx[0:1, k:k + 1],
                                in_=sh_flag[k][1 - svh["v"]]).then_inc(psem, 16)
                    e.reg_add(cnt, cnt, 16)
                    e.wait_ge(psem, cnt)
                    e.load(f, flag_rx[0:1, k:k + 1])
                    e.reg_alu(nd, f, MAGIC[k], ALU.not_equal)
        p.raw("sync", go)

    ctxA = dict(nc=nc, p=p, al=mkal("A"), PS=PS, x0=x0, wT=wT, bias=bias,
                bwT=bwT, bivmat=bivmat, revtile=revtile, scdelta=scdelta,
                ident=ident,
                attn_dst=lambda qt, g: (lambda qt=qt, g=g: attn_loc
                                        [qt * 128:(qt + 1) * 128,
                                         g * 512:(g + 1) * 512]))
    build_launch_A(ctxA)
    pub0 = p.dma("sync", lambda: sh_attn[svh["v"]],
                 attn_loc.rearrange("(a b) c -> a b c", b=1024), "pub0")
    p.wait("sync", pub0)
    set_flag(0)
    p.barrier("AB")
    poll_flag(0)
    g0 = p.dma("sync", attn_in[:, 0:2048], lambda: sh_attn[0][svh["v"]], "gat0")
    g1 = p.dma("sync", attn_in[:, 2048:4096], lambda: sh_attn[1][svh["v"]], "gat1")
    p.wait("sync", g0)
    p.wait("sync", g1)
    ctxB = dict(nc=nc, p=p, al=mkal("B"), PS=PS, x0own=x0own, w0T=w0T,
                bias0=bias0, a0T=a0T, b0cat=b0cat, coefr=coefr, sclr=sclr,
                ident=ident,
                attn_srcs=lambda tt: [(slice(0, DM),
                                       lambda tt=tt: attn_in[tt * 128:(tt + 1) * 128, :])],
                x2_dst=lambda tt: (lambda tt=tt: x2_loc[tt * 128:(tt + 1) * 128, :]))
    build_launch_B(ctxB)
    pub1 = p.dma("sync", lambda: sh_x2[svh["v"]], x2_loc, "pub1")
    p.wait("sync", pub1)
    set_flag(1)
    p.barrier("BC")
    poll_flag(1)
    ctxC = dict(nc=nc, p=p, al=mkal("C"), PS=PS,
                x2f=sh_x2.rearrange("a b c -> (a b) c"),
                wq1T=wq1T, wk1T=wk1T, wv1T=wv1T, wg1T=wg1T, win1T=win1T,
                dtb=dtb, alog=alog, gnw=gnw, cosT=cosT, sinT=sinT,
                ident=ident, identf=identf, trif=trif,
                yg_dst=lambda n, kv: (lambda n=n, kv=kv: ygo_loc
                                      [n * 128:(n + 1) * 128,
                                       kv * 512:(kv + 1) * 512]))
    build_launch_C(ctxC)
    pub2 = p.dma("sync", lambda: sh_yg[svh["v"]],
                 ygo_loc.rearrange("(a b) c -> a b c", b=1024), "pub2")
    p.wait("sync", pub2)
    set_flag(2)
    p.barrier("CD")
    poll_flag(2)
    g2 = p.dma("sync", yg_loc[:, 0:2048], lambda: sh_yg[0][svh["v"]], "gyg0")
    g3 = p.dma("sync", yg_loc[:, 2048:4096], lambda: sh_yg[1][svh["v"]], "gyg1")
    p.wait("sync", g2)
    p.wait("sync", g3)
    ctxD = dict(nc=nc, p=p, al=mkal("D"), PS=PS, woT=woT, w1T=w1T, b1=b1,
                art=art, bpT=bpT, ident=ident, x4o=x4o,
                x2_src=lambda tt: (lambda tt=tt: x2_loc[tt * 128:(tt + 1) * 128, :]),
                yg_srcs=lambda tt: [(slice(0, DM),
                                     lambda tt=tt: yg_loc[tt * 128:(tt + 1) * 128, :])])
    build_launch_D(ctxD)
    p.emit()
    return nc


def stage_fused(inputs):
    mA = stage_A(inputs)
    dummy_attn = np.zeros((B, L, DM), BF)
    dummy_x2 = np.zeros((B, L, DM), np.float32)
    dummy_yg = np.zeros((B, L, DM), BF)
    mB = stage_B(inputs, dummy_attn, inputs["L0_attn_x"])
    mC = stage_C(inputs, dummy_x2)
    mD = stage_D(inputs, dummy_x2, dummy_yg)
    in_maps = []
    for c in range(NCORE):
        m = {}
        m.update({k: v for k, v in mA[c].items()})
        m["x0own"] = mB[c]["x0"]
        for k in ["w0T", "bias0", "a0T", "b0cat", "coefr", "sclr"]:
            m[k] = mB[c][k]
        for k_new, k_old in [("wq1T", "wqT"), ("wk1T", "wkT"), ("wv1T", "wvT"),
                             ("wg1T", "wgT"), ("win1T", "winT")]:
            m[k_new] = mC[c][k_old]
        for k in ["dtb", "alog", "gnw", "cosT", "sinT", "identf", "trif"]:
            m[k] = mC[c][k]
        for k in ["woT", "w1T", "b1", "art", "bpT"]:
            m[k] = mD[c][k]
        m["sv"] = np.array([[c % 2]], dtype=np.uint32)
        in_maps.append(m)
    return in_maps


def kernel_fused(**inputs):
    inputs = {k: np.asarray(v) for k, v in inputs.items()}
    nc = _get_nc("F", build_fused)
    res = run_bass_kernel_spmd(nc, stage_fused(inputs), core_ids=list(range(NCORE)))
    out = np.concatenate(
        [np.concatenate([np.asarray(res.results[2 * b]["x4o"]),
                         np.asarray(res.results[2 * b + 1]["x4o"])], axis=0)[None]
         for b in range(B)], axis=0)
    return out


_NC_CACHE = {}


def _get_nc(name, builder):
    if name not in _NC_CACHE:
        _NC_CACHE[name] = builder()
    return _NC_CACHE[name]


def kernel(**inputs):
    return kernel_fused(**inputs)


def kernel_unfused(**inputs):
    inputs = {k: np.asarray(v) for k, v in inputs.items()}

    ncA = _get_nc("A", build_launch_A)
    resA = run_bass_kernel_spmd(ncA, stage_A(inputs), core_ids=list(range(NCORE)))
    attn_full = np.concatenate(
        [np.concatenate([np.asarray(resA.results[2 * b]["attn_out"]),
                         np.asarray(resA.results[2 * b + 1]["attn_out"])], axis=1)[None]
         for b in range(B)], axis=0)

    ncB = _get_nc("B", build_launch_B)
    resB = run_bass_kernel_spmd(ncB, stage_B(inputs, attn_full, inputs["L0_attn_x"]),
                                core_ids=list(range(NCORE)))
    x2_full = np.concatenate(
        [np.concatenate([np.asarray(resB.results[2 * b]["x2o"]),
                         np.asarray(resB.results[2 * b + 1]["x2o"])], axis=0)[None]
         for b in range(B)], axis=0)

    ncC = _get_nc("C", build_launch_C)
    resC = run_bass_kernel_spmd(ncC, stage_C(inputs, x2_full), core_ids=list(range(NCORE)))
    yg_full = np.concatenate(
        [np.concatenate([np.asarray(resC.results[2 * b]["ygo"]),
                         np.asarray(resC.results[2 * b + 1]["ygo"])], axis=1)[None]
         for b in range(B)], axis=0)

    ncD = _get_nc("D", build_launch_D)
    resD = run_bass_kernel_spmd(ncD, stage_D(inputs, x2_full, yg_full),
                                core_ids=list(range(NCORE)))
    out = np.concatenate(
        [np.concatenate([np.asarray(resD.results[2 * b]["x4o"]),
                         np.asarray(resD.results[2 * b + 1]["x4o"])], axis=0)[None]
         for b in range(B)], axis=0)
    return out
